# version 12
# baseline (speedup 1.0000x reference)
import numpy as np
import ml_dtypes
import concourse.bass as bass
import concourse.mybir as mybir
from concourse.bass_utils import run_bass_kernel_spmd
from contextlib import ExitStack

F32 = mybir.dt.float32
BF16 = mybir.dt.bfloat16
AF = mybir.ActivationFunctionType
ALU = mybir.AluOpType
AX = mybir.AxisListType

D = 1024
DI = 2048
DS = 16
DTR = 64
RK = 16
NH = 16
HD = 64
NE = 8
FF = 4096
EPS = 1e-5
NEG = -1e30



class Prog:

    def __init__(self):
        self.ops = []
        self.lastw = {}
        self.readers = {}
        self.qcount = {}
        self.alias = {}

    def op(self, eng, emit, reads=(), writes=(), q=None, sgrp=None, tag=""):
        reads = tuple(self.alias.get(r, r) for r in reads)
        writes = tuple(self.alias.get(w, w) for w in writes)
        i = len(self.ops)
        rec = dict(i=i, eng=eng, emit=emit, q=q, sgrp=sgrp, qidx=None, inc=False,
                   qwaits={}, ewaits={}, sdep=-1, tag=tag)
        if q is not None:
            if sgrp is None:
                sgrp = q + ":" + (writes[0] if writes else "misc")
            rec["sgrp"] = sgrp
            self.qcount[sgrp] = self.qcount.get(sgrp, 0) + 1
            rec["qidx"] = self.qcount[sgrp]
        deps = set()
        for r in reads:
            d = self.lastw.get(r)
            if d is not None:
                deps.add(d)
        for w in writes:
            d = self.lastw.get(w)
            if d is not None:
                deps.add(d)
            deps.update(self.readers.get(w, ()))
        deps.discard(i)
        for d in deps:
            dr = self.ops[d]
            if dr["q"] is not None:
                rec["qwaits"][dr["sgrp"]] = max(rec["qwaits"].get(dr["sgrp"], 0), 16 * dr["qidx"])
            elif dr["eng"] == eng and q is None and eng == "pe":
                pass
            elif dr["eng"] == eng:
                rec["sdep"] = max(rec["sdep"], d)
            else:
                dr["inc"] = True
                rec["ewaits"][dr["eng"]] = max(rec["ewaits"].get(dr["eng"], -1), d)
        for r in reads:
            self.readers.setdefault(r, []).append(i)
        for w in writes:
            self.lastw[w] = i
            self.readers[w] = []
        self.ops.append(rec)
        return i

    def emit_all(self, block, esem, qsem):
        incnum = {}
        counts = {}
        for rec in self.ops:
            if rec["inc"] and rec["q"] is None:
                counts[rec["eng"]] = counts.get(rec["eng"], 0) + 1
                incnum[rec["i"]] = counts[rec["eng"]]
        self.final_inc_counts = counts
        self.final_q_counts = {g: 16 * n for g, n in self.qcount.items()}

        streams = {e: [rec for rec in self.ops if rec["eng"] == e]
                   for e in ("pe", "act", "dve", "pool", "sp")}

        def run_stream(eng_name):
            def body(eng):
                wm_q = {}
                wm_e = {}
                last_drain = -1
                prev_idx = -1
                for rec in streams[eng_name]:
                    for qn, val in sorted(rec["qwaits"].items()):
                        if wm_q.get(qn, 0) < val:
                            eng.wait_ge(qsem[qn], val)
                            wm_q[qn] = val
                    for en, didx in sorted(rec["ewaits"].items()):
                        val = incnum[didx]
                        if wm_e.get(en, 0) < val:
                            eng.wait_ge(esem[en], val)
                            wm_e[en] = val
                    if rec["sdep"] > last_drain:
                        eng.drain()
                        last_drain = prev_idx
                    inst = rec["emit"](eng)
                    if rec["q"] is not None:
                        inst.then_inc(qsem[rec["sgrp"]], 16)
                    elif rec["inc"]:
                        inst.then_inc(esem[eng_name], 1)
                    prev_idx = rec["i"]
            return body

        block.tensor(run_stream("pe"))
        block.scalar(run_stream("act"))
        block.vector(run_stream("dve"))
        block.gpsimd(run_stream("pool"))
        block.sync(run_stream("sp"))

        def finale(eng):
            for qn, val in sorted(self.final_q_counts.items()):
                if val:
                    eng.wait_ge(qsem[qn], val)
            for en, val in sorted(self.final_inc_counts.items()):
                eng.wait_ge(esem[en], val)
        block.sync(finale)


class KB:

    def __init__(self, nc, T, dbg_names=()):
        self.nc = nc
        self.T = T
        self.NTT = T // 128
        self.p = Prog()
        self.st = ExitStack()
        self.sb = {}
        self.ps = {}
        self.dram_in = {}
        self.dram_out = {}
        self.dbg_names = set(dbg_names)
        self.sbuf_bytes = 0

    def din(self, name, shape, dt=F32):
        h = self.nc.dram_tensor(name, list(shape), dt, kind="ExternalInput")
        self.dram_in[name] = h.ap()
        return self.dram_in[name]

    def dout(self, name, shape, dt=F32):
        h = self.nc.dram_tensor(name, list(shape), dt, kind="ExternalOutput")
        self.dram_out[name] = h.ap()
        return self.dram_out[name]

    def t(self, name, shape, dt=BF16):
        h = self.st.enter_context(self.nc.sbuf_tensor(name, list(shape), dt))
        self.sb[name] = h
        esz = {F32: 4, BF16: 2}[dt]
        self.sbuf_bytes += int(np.prod(shape[1:])) * esz
        return h

    def pt(self, name, shape, dt=F32):
        h = self.st.enter_context(self.nc.psum_tensor(name, list(shape), dt))
        self.ps[name] = h
        return h

    def dma(self, qname, out_ap, in_ap, reads=(), writes=(), sgrp=None, tag=""):
        def emit(eng):
            return eng.dma_start(out=out_ap, in_=in_ap)
        self.p.op(qname, emit, reads=reads, writes=writes, q=qname, sgrp=sgrp, tag=tag)

    def mm(self, out_ap, lhsT_ap, rhs_ap, psum_key, start, stop, reads=(), tag=""):
        def emit(eng):
            return eng.matmul(out_ap, lhsT_ap, rhs_ap, start=start, stop=stop)
        self.p.op("pe", emit, reads=tuple(reads), writes=(psum_key,), tag=tag)

    def tr(self, out_ap, in_ap, ident_ap, psum_key, reads=(), tag=""):
        def emit(eng):
            return eng.transpose(out_ap, in_ap, ident_ap)
        self.p.op("pe", emit, reads=tuple(reads), writes=(psum_key,), tag=tag)

    def act(self, out_ap, in_ap, func, reads=(), writes=(), scale=None, bias=None,
            accum=None, tag=""):
        if func == AF.Copy and bias is not None and not isinstance(bias, float):
            func = AF.Identity
        kw = {}
        if scale is not None:
            kw["scale"] = scale
        if bias is not None:
            kw["bias"] = bias
        if accum is not None:
            kw["accum_out"] = accum

        def emit(eng):
            return eng.activation(out_ap, in_ap, func, **kw)
        self.p.op("act", emit, reads=tuple(reads), writes=tuple(writes), tag=tag)

    def ve(self, method, *args, reads=(), writes=(), eng_name="dve", tag="", **kw):
        def emit(eng):
            return getattr(eng, method)(*args, **kw)
        self.p.op(eng_name, emit, reads=tuple(reads), writes=tuple(writes), tag=tag)

    def dbg(self, name, ap, shape, dt=F32, reads=()):
        if name not in self.dbg_names:
            return
        o = self.dout("dbg_" + name, shape, dt)
        self.dma("sp", o, ap, reads=tuple(reads), sgrp="dbg_" + name, tag="dbg" + name)



def ln_stats(kb, src_fn, n_tiles, tag, skey, keys_read):
    sm = kb.sb["stats"]
    bn = kb.sb["spB"]
    K = "stats"
    for tt in range(n_tiles):
        full = src_fn(tt)
        kb.ve("bn_stats", bn[:, tt * 12:tt * 12 + 6], full[:, 0:512],
              reads=keys_read + ("spB",), writes=("spB",), tag=tag + "bs")
        kb.ve("bn_stats", bn[:, tt * 12 + 6:tt * 12 + 12], full[:, 512:1024],
              reads=keys_read + ("spB",), writes=("spB",), tag=tag + "bs")
        kb.ve("bn_aggr", sm[:, 4 * tt:4 * tt + 2], bn[:, tt * 12:tt * 12 + 12],
              reads=("spB",), writes=(K,), tag=tag + "ba")
    var_v = sm[:, :].rearrange("p (t c) -> p t c", c=4)[:, 0:n_tiles, 1]
    lnv_v = sm[:, :].rearrange("p (t c) -> p t c", c=4)[:, 0:n_tiles, 2]
    rst_v = sm[:, :].rearrange("p (t c) -> p t c", c=4)[:, 0:n_tiles, 3]
    kb.ve("tensor_scalar_add", lnv_v, var_v, EPS, reads=(K,), writes=(K,), tag=tag + "ep")
    kb.act(lnv_v, lnv_v, AF.Ln, reads=(K,), writes=(K,), tag=tag + "ln")
    kb.act(rst_v, lnv_v, AF.Exp, scale=-0.5, reads=(K,), writes=(K,), tag=tag + "ex")
    return K


def ln_mu_rstd(kb, skey, tt):
    sm = kb.sb["stats"]
    return sm[:, 4 * tt:4 * tt + 1], sm[:, 4 * tt + 3:4 * tt + 4]


def tm_to_fm(kb, src_ap_fn, dst, dst_key, n_tt, tt0_tok, ndt, tag, keys_read,
             scale=None, bias=None, dt0=0, view_d=None):
    ident = kb.sb["identb"]
    for tt in range(n_tt):
        for dt in range(ndt):
            pk = "pt%d" % ((tt * ndt + dt) % 2)
            pt = kb.ps[pk]
            kb.tr(pt[:, 0:128], src_ap_fn(tt, dt), ident[:, :], pk,
                  reads=keys_read + ("identb",), tag=tag + "tr")
            dv = dst[:, :].rearrange("p (d t) -> p d t", d=(view_d or ndt))[
                :, dt0 + dt, tt0_tok + tt * 128: tt0_tok + (tt + 1) * 128]
            kb.act(dv, pt[:, 0:128], AF.Copy, scale=scale, bias=bias,
                   reads=(pk,), writes=(dst_key, pk), tag=tag + "ev")


def stage_a(kb, P):
    T, NTT = kb.T, kb.NTT
    TcA = min(512, T)
    NCH = T // TcA
    NTC = TcA // 128
    sb = kb.sb
    ps = kb.ps
    xtm = sb["xtm"]
    G = {k: sb[k] for k in ("G1", "G2", "G3", "G4", "G5", "G6", "G7", "G8", "G9")}
    u = G["G1"]
    du = G["G2"]
    xsr = G["G7"]
    bc = G["G4"]
    ya = G["G5"]
    xnf = G["G3"]
    zb = G["G6"]
    xnt = G["G8"]

    uv = u[:, :].rearrange("p (d t) -> p d t", d=16)
    duv = du[:, :].rearrange("p (d t) -> p d t", d=16)
    xsrv = xsr[:, :].rearrange("p (d t) -> p d t", d=16)
    bcv = bc[:, :].rearrange("p (d t) -> p d t", d=16)
    yav = ya[:, :].rearrange("p (d t) -> p d t", d=16)
    xnfv = xnf[:, :].rearrange("p (d t) -> p d t", d=16)
    hsl = xnf[:, 4096:8192].rearrange("p (n t) -> p n t", n=8)
    zbv = zb[:, :].rearrange("p (d t) -> p d t", d=16)
    ball = zb[:, 0:8 * TcA].rearrange("p (n t) -> p n t", n=8)
    asl = zb[:, 4096:4096 + 2 * TcA].rearrange("p (n t) -> p n t", n=2)
    xntv = xnt[:, :].rearrange("p (t d) -> p t d", d=1024)
    dlt = lambda dt, w: xsrv[:, dt, 3:3 + w]
    yak = lambda dt: G5K[dt // 4]
    zbk = lambda ot: G6K[ot // 4]

    stk = ln_stats(kb, lambda tt: xtm[:, :].rearrange("p (t d) -> p t d", d=1024)[:, tt, :],
                   NTT, "lnA", "A", tuple("xt%d" % i for i in range(NTT)))

    for c in range(NCH):
        for tl in range(NTC):
            tt = c * NTC + tl
            mu, rs = ln_mu_rstd(kb, "A", tt)
            src = xtm[:, :].rearrange("p (t d) -> p t d", d=1024)[:, tt, :]
            kb.ve("tensor_scalar", xntv[:, tl, :], src, mu, rs, ALU.subtract, ALU.mult,
                  reads=(xtk(tt), stk), writes=("G8r%d" % tl,), tag="axn")
        tm_to_fm(kb, lambda tl, dt: xntv[:, tl, dt * 128:(dt + 1) * 128],
                 xnf, "G3a", NTC, 0, 8, "axf",
                 ("G8r0", "G8r1", "G8r2", "G8r3"), view_d=16)

        for kt in range(8):
            kb.mm(ps["p4"][0:16, 0:TcA], sb["laiT"][:, :].rearrange(
                "p (k r) -> p k r", k=8)[:, kt, :], xnfv[:, kt, 0:TcA],
                "p4", kt == 0, kt == 7, reads=("laiT", "G3a"), tag="axl")
        kb.ve("tensor_copy", sb["xl"][0:16, 0:TcA], ps["p4"][0:16, 0:TcA],
              reads=("p4",), writes=("xl", "p4"), tag="axl_ev")

        for ot in range(16):
            wbuf = "wsA%d" % (ot % 2)
            kb.dma("pool", sb[wbuf][:, :].rearrange("p (k o) -> p k o", k=8),
                   P["w_inT"][ot], writes=(wbuf,), tag="awin")
            pk = "p%d" % (ot % 2)
            for kt in range(8):
                kb.mm(ps[pk][:, 0:TcA], sb[wbuf][:, :].rearrange(
                    "p (k o) -> p k o", k=8)[:, kt, :], xnfv[:, kt, 0:TcA],
                    pk, kt == 0, False, reads=(wbuf, "G3a"), tag="axz")
            kb.mm(ps[pk][:, 0:TcA], sb["lbi_s"][0:16, ot * 128:(ot + 1) * 128],
                  sb["xl"][0:16, 0:TcA], pk, False, True,
                  reads=("lbi_s", "xl"), tag="axzl")
            kb.act(xsrv[:, ot, 3:3 + TcA], ps[pk][:, 0:TcA], AF.Copy,
                   reads=(pk,), writes=("G7", pk), tag="axz_ev")

        if c == 0:
            kb.ve("memset", xsrv[:, :, 0:3], 0.0, writes=("G7",), tag="aconv0")
        for dt in range(16):
            dacc = sb["dacc"]
            cwt = sb["cw"][:, :].rearrange("p (d k) -> p d k", d=16)
            kb.ve("tensor_scalar", dacc[:, 0:TcA], xsrv[:, dt, 0:TcA],
                  cwt[:, dt, 0:1], None, ALU.mult,
                  reads=("G7", "cw"), writes=("spA",), tag="acv")
            for k in range(1, 4):
                kb.ve("scalar_tensor_tensor", dacc[:, 0:TcA], xsrv[:, dt, k:k + TcA],
                      cwt[:, dt, k:k + 1], dacc[:, 0:TcA], ALU.mult, ALU.add,
                      reads=("G7", "cw", "spA"), writes=("spA",), tag="acv")
            kb.act(uv[:, dt, 0:TcA], dacc[:, 0:TcA], AF.Silu,
                   bias=sb["cb"][:, dt:dt + 1],
                   reads=("spA", "cb"), writes=("G1",), tag="acs")
        if c + 1 < NCH:
            kb.ve("tensor_copy", xsrv[:, :, 0:3], xsrv[:, :, TcA:TcA + 3],
                  reads=("G7",), writes=("G7",), tag="ahalo")

        for kt in range(16):
            kb.mm(ps["p4"][0:96, 0:TcA], sb["wxpT"][:, :].rearrange(
                "p (d r) -> p d r", d=16)[:, kt, :], uv[:, kt, 0:TcA],
                "p4", kt == 0, kt == 15, reads=("wxpT", "G1"), tag="axd")
        kb.ve("tensor_copy", sb["xdbl"][0:96, 0:TcA], ps["p4"][0:96, 0:TcA],
              reads=("p4",), writes=("xdbl", "p4"), tag="axd_ev")

        for dt in range(16):
            pk = "p%d" % (dt % 2)
            kb.mm(ps[pk][:, 0:TcA], sb["wdtT"][0:64, dt * 128:(dt + 1) * 128],
                  sb["xdbl"][0:64, 0:TcA], pk, True, True,
                  reads=("wdtT", "xdbl"), tag="adl")
            spA, spB = sb["spA"], sb["spB"]
            bdt1 = sb["bdt"][:, dt:dt + 1]
            kb.act(spA[:, 0:TcA], ps[pk][:, 0:TcA], AF.Abs, bias=bdt1,
                   reads=(pk, "bdt"), writes=("spA",), tag="asp1")
            kb.act(spB[:, 0:TcA], spA[:, 0:TcA], AF.Exp, scale=-1.0,
                   reads=("spA",), writes=("spB",), tag="asp2")
            kb.act(spA[:, 0:TcA], spB[:, 0:TcA], AF.Ln, bias=1.0,
                   reads=("spB",), writes=("spA",), tag="asp3")
            kb.act(spB[:, 0:TcA], ps[pk][:, 0:TcA], AF.Relu, bias=bdt1,
                   reads=(pk, "bdt"), writes=("spB", pk), tag="asp4")
            kb.ve("tensor_add", dlt(dt, TcA), spA[:, 0:TcA], spB[:, 0:TcA],
                  reads=("spA", "spB"), writes=("G7",), tag="asp5")
            kb.ve("tensor_mul", duv[:, dt, 0:TcA], dlt(dt, TcA), uv[:, dt, 0:TcA],
                  reads=("G7", "G1"), writes=("G2",), tag="adu")
            kb.ve("tensor_scalar", yav[:, dt, 0:TcA], uv[:, dt, 0:TcA],
                  sb["Dp"][:, dt:dt + 1], None, ALU.mult,
                  reads=("G1", "Dp"), writes=(yak(dt),), tag="ayi")

        if c == 0:
            kb.ve("memset", sb["hcar"][:, :], 0.0, writes=("hcar",), tag="ahc0")
        kb.dma("pool", P["scr_bc"][:, 0:TcA], sb["xdbl"][64:96, 0:TcA],
               reads=("xdbl",), writes=("scrBC",), tag="abc_st")
        pb8 = zb[:, 8 * TcA:16 * TcA].rearrange("p (n t) -> p n t", n=8)
        g6r = lambda lo, hi: tuple(G6K[r] for r in range(lo // 2048, (hi - 1) // 2048 + 1))
        ballk = g6r(0, 8 * TcA)
        pbak = g6r(8 * TcA, 12 * TcA)
        pbbk = g6r(12 * TcA, 16 * TcA)
        aslg = sb["G9"][:, 0:2 * TcA].rearrange("p (n t) -> p n t", n=2)
        pend = []
        for g in range(2):
            kb.dma("pool", bcv[:, 0:8, 0:TcA],
                   P["scr_bc"][g * 8:g * 8 + 8, 0:TcA][None, :, :].broadcast_to([128, 8, TcA]),
                   reads=("scrBC",), writes=("G4a",), tag="abc_ldB")
            kb.dma("pool", bcv[:, 8:16, 0:TcA],
                   P["scr_bc"][16 + g * 8:16 + g * 8 + 8, 0:TcA][None, :, :].broadcast_to([128, 8, TcA]),
                   reads=("scrBC",), writes=("G4b",), tag="abc_ldC")
            for dt in range(16):
                kb.ve("tensor_mul", ball[:, :, 0:TcA],
                      duv[:, dt:dt + 1, 0:TcA].broadcast_to([128, 8, TcA]),
                      bcv[:, 0:8, 0:TcA],
                      reads=("G2", "G4a"), writes=ballk, tag="asb")
                if pend:
                    pend.pop()()
                for n in range(8):
                    nn = g * 8 + n
                    par = n % 2
                    kb.act(aslg[:, par, 0:TcA], dlt(dt, TcA), AF.Exp,
                           scale=sb["Aneg"][:, :].rearrange("p (d n) -> p d n", d=16)[:, dt, nn:nn + 1],
                           reads=("G7", "Aneg"), writes=("g9p%d" % par,), tag="asa")
                    kb.ve("tensor_tensor_scan", hsl[:, n, 0:TcA], aslg[:, par, 0:TcA],
                          ball[:, n, 0:TcA],
                          sb["hcar"][:, dt * 16 + nn:dt * 16 + nn + 1],
                          ALU.mult, ALU.add,
                          reads=("g9p%d" % par,) + ballk + ("hcar",),
                          writes=("hs%d" % n,), tag="asc")
                kb.act(sb["hcar"][:, dt * 16 + g * 8:dt * 16 + g * 8 + 8],
                       hsl[:, :, TcA - 1], AF.Copy,
                       reads=tuple("hs%d" % i for i in range(8)), writes=("hcar",), tag="ahc")
                kb.ve("tensor_mul", pb8[:, 0:4, 0:TcA], hsl[:, 0:4, 0:TcA], bcv[:, 8:12, 0:TcA],
                      reads=("hs0", "hs1", "hs2", "hs3", "G4b"),
                      writes=pbak, eng_name="pool", tag="asya")
                kb.ve("tensor_mul", pb8[:, 4:8, 0:TcA], hsl[:, 4:8, 0:TcA], bcv[:, 12:16, 0:TcA],
                      reads=("hs4", "hs5", "hs6", "hs7", "G4b"),
                      writes=pbbk, eng_name="pool", tag="asyb")
                kb.ve("tensor_add", pb8[:, 0:4, 0:TcA], pb8[:, 0:4, 0:TcA], pb8[:, 4:8, 0:TcA],
                      reads=pbak + pbbk, writes=pbak, eng_name="pool", tag="atr1")
                kb.mm(ps["p2"][0:16, 0:16], sb["onesb"][0:1, 0:16], sb["onesb"][0:1, 0:16],
                      "p2", True, True, reads=("onesb", "hs7"), tag="awarm")

                def tail(dt=dt):
                    kb.ve("tensor_add", pb8[:, 0:2, 0:TcA], pb8[:, 0:2, 0:TcA],
                          pb8[:, 2:4, 0:TcA], reads=pbak, writes=pbak, tag="atr2")
                    kb.ve("tensor_add", pb8[:, 0, 0:TcA], pb8[:, 0, 0:TcA],
                          pb8[:, 1, 0:TcA], reads=pbak, writes=pbak, tag="atr3")
                    kb.ve("tensor_add", yav[:, dt, 0:TcA], yav[:, dt, 0:TcA],
                          pb8[:, 0, 0:TcA], reads=(yak(dt),) + pbak, writes=(yak(dt),), tag="atr4")
                pend.append(tail)
        pend.pop()()

        for ot in range(16):
            wbuf = "wsA%d" % (ot % 2)
            kb.dma("pool", sb[wbuf][:, :].rearrange("p (k o) -> p k o", k=8),
                   P["w_inT"][16 + ot], writes=(wbuf,), tag="awz")
            pk = "p%d" % (ot % 2)
            for kt in range(8):
                kb.mm(ps[pk][:, 0:TcA], sb[wbuf][:, :].rearrange(
                    "p (k o) -> p k o", k=8)[:, kt, :], xnfv[:, kt, 0:TcA],
                    pk, kt == 0, False, reads=(wbuf, "G3a"), tag="az")
            kb.mm(ps[pk][:, 0:TcA], sb["lbi_s"][0:16, (16 + ot) * 128:(17 + ot) * 128],
                  sb["xl"][0:16, 0:TcA], pk, False, True,
                  reads=("lbi_s", "xl"), tag="azl")
            kb.act(zbv[:, ot, 0:TcA], ps[pk][:, 0:TcA], AF.Silu,
                   reads=(pk,), writes=(zbk(ot), pk), tag="az_ev")

        for tb in range(NTC):
            ygv = xnt[:, :].rearrange("p (d t) -> p d t", d=16)
            for dt in range(16):
                kb.ve("tensor_mul", ygv[:, dt, 0:128], yav[:, dt, tb * 128:(tb + 1) * 128],
                      zbv[:, dt, tb * 128:(tb + 1) * 128],
                      reads=(yak(dt), zbk(dt)), writes=("G8r%d" % (dt // 4),), tag="agate")
            for kt in range(16):
                kb.mm(ps["p4"][0:16, 0:128], sb["laoT"][:, :].rearrange(
                    "p (d r) -> p d r", d=16)[:, kt, :], ygv[:, kt, 0:128],
                    "p4", kt == 0, kt == 15, reads=("laoT", "G8r%d" % (kt // 4)), tag="aylo")
            kb.ve("tensor_copy", sb["ylo"][0:16, 0:128], ps["p4"][0:16, 0:128],
                  reads=("p4",), writes=("ylo", "p4"), tag="aylo_ev")
            for oh in range(2):
                pk = "p%d" % oh
                for kt in range(16):
                    wbuf = "wsA%d" % (kt % 2)
                    kb.dma("pool", sb[wbuf][:, 0:512],
                           P["w_outT"][kt, oh], writes=(wbuf,), tag="awo")
                    kb.mm(ps[pk][:, 0:512], ygv[:, kt, 0:128], sb[wbuf][:, 0:512],
                          pk, kt == 0, False, reads=("G8r%d" % (kt // 4), wbuf), tag="aop")
                kb.mm(ps[pk][:, 0:512], sb["ylo"][0:16, 0:128],
                      sb["lbo_s"][0:16, oh * 512:(oh + 1) * 512], pk, False, True,
                      reads=("ylo", "lbo_s"), tag="aopl")
                xv = xtm[:, :].rearrange("p (t d) -> p t d", d=1024)
                tt = c * NTC + tb
                kb.ve("tensor_add", xv[:, tt, oh * 512:(oh + 1) * 512],
                      xv[:, tt, oh * 512:(oh + 1) * 512], ps[pk][:, 0:512],
                      reads=(xtk(tt), pk), writes=(xtk(tt), pk), tag="aox")
    kb.dbg("x_after_a", xtm[:, :], [128, NTT * 1024], BF16, reads=tuple("xt%d" % i for i in range(NTT)))


def build_program(T, stages=("a",), dbg_names=()):
    nc = bass.Bass("TRN2", target_bir_lowering=False, debug=False)
    kb = KB(nc, T, dbg_names)
    NTT = kb.NTT
    TcA = min(512, T)

    P = {}
    for nm, shp in [
        ("x", [T, D]), ("w_inT", [32, 128, 8, 128]), ("laiT", [D, RK]), ("lbiT", [RK, 2 * DI]),
        ("mask_in", [2 * DI]), ("cw", [128, 16, 4]), ("cb", [128, 16]), ("wxpT", [DI, 96]),
        ("wdtT", [DTR, DI]), ("bdt", [128, 16]), ("Alog", [128, 16, DS]), ("Dp", [128, 16]),
        ("w_outT", [16, 2, 128, 512]), ("laoT", [DI, RK]), ("lboT", [RK, D]), ("mask_out", [D]),
        ("nu", [128, 8]), ("th", [128, 8]), ("gam", [128, 8]), ("Dlru", [1, D]),
        ("BreT", [8, 128, 8, 128]), ("BimT", [8, 128, 8, 128]),
        ("CreT", [8, 2, 128, 512]), ("CimT", [8, 2, 128, 512]),
        ("ln1g", [128, 8]), ("ln1b", [128, 8]), ("ln2g", [128, 8]), ("ln2b", [128, 8]),
        ("Wq", [8, 128, 8, 128]), ("Wk", [8, 128, 8, 128]), ("Wv", [8, 128, 8, 128]),
        ("Wp", [8, 2, 128, 512]),
        ("bq", [128, 8]), ("bk", [128, 8]), ("bv", [1, D]), ("bp", [1, D]),
        ("gw", [D, NE]), ("gb", [NE]), ("W1", [NE, 8, 128, 8, 512]), ("b1", [128, NE, 32]),
        ("W2", [NE, 4, 2, 128, 8, 512]), ("b2", [NE, D]),
    ]:
        P[nm] = kb.din("i_" + nm, shp, F32)
    for nm, shp, dt in [("identb", [128, 128], BF16), ("maskA", [128, 128], F32),
                        ("triB", [128, 128], BF16), ("triN", [128, 128], BF16),
                        ("iota8", [128, NE], F32)]:
        P[nm] = kb.din("i_" + nm, shp, dt)
    OUT = kb.dout("out", [T, D], F32)
    SCR_BC = kb.nc.dram_tensor("scr_bc", [32, min(512, T)], BF16, kind="Internal").ap()
    P["scr_bc"] = SCR_BC

    kb.t("xtm", [128, NTT * 1024], BF16)
    for g in ("G1", "G2", "G3", "G4", "G5", "G6"):
        kb.t(g, [128, 8192], BF16)
    kb.t("G7", [128, 16 * (TcA + 3)], BF16)
    kb.t("G8", [128, 4096], BF16)
    kb.t("G9", [128, 4096], BF16)
    kb.t("wsA0", [128, 8 * 128], BF16)
    kb.t("wsA1", [128, 8 * 128], BF16)
    kb.t("identb", [128, 128], BF16)
    kb.t("maskA", [128, 128], F32)
    kb.t("triB", [128, 128], BF16)
    kb.t("triN", [128, 128], BF16)
    kb.t("iota8", [128, NE], F32)
    kb.t("stats", [128, 4 * NTT], F32)
    kb.t("onesf", [1, 128], F32)
    kb.t("onesb", [1, 128], BF16)
    kb.t("onesn", [1, 128], BF16)
    kb.t("onescol", [128, 1], BF16)
    kb.t("onescf", [128, 1], F32)
    kb.t("laiT", [128, 8 * RK], BF16)
    kb.t("bsml", [128, 4096], BF16)
    kb.t("cw", [128, 16 * 4], F32)
    kb.t("cb", [128, 16], F32)
    kb.t("wxpT", [128, 16 * 96], BF16)
    kb.t("wdtT", [DTR, DI], BF16)
    kb.t("bdt", [128, 16], F32)
    kb.t("Aneg", [128, 16 * DS], F32)
    kb.t("Dp", [128, 16], F32)
    kb.t("laoT", [128, 16 * RK], BF16)
    kb.t("lbo_s", [RK, D], BF16)
    kb.t("xl", [RK, TcA], BF16)
    kb.t("ylo", [RK, 128], BF16)
    kb.t("xdbl", [128, TcA], BF16)
    kb.t("spA", [128, max(TcA, 1024)], F32)
    kb.t("spB", [128, max(TcA, 1024)], F32)
    kb.t("hcar", [128, 16 * DS], F32)
    kb.t("rows", [1, 2048], BF16)
    kb.t("wsV", [128, 8 * 256], BF16)
    kb.t("attsh", [128, 32], BF16)
    kb.t("ascr", [128, 16], F32)
    kb.t("lgt", [128, 8 * NTT], F32)
    kb.t("wfl", [128, 8 * NTT], F32)
    kb.t("e8", [128, NE], F32)
    kb.t("gwt", [128, 8 * NE], BF16)
    kb.t("b1pp", [128, NE * 32], BF16)
    kb.t("bq", [128, 8], F32)
    kb.t("bk8", [128, 8], F32)
    kb.t("ln1g", [128, 8], F32)
    kb.t("ln1b", [128, 8], F32)
    kb.t("ln2g", [128, 8], F32)
    kb.t("ln2b", [128, 8], F32)

    bs = kb.sb["bsml"]
    bsf = bs.bitcast(F32)
    kb.sb["lbi_s"] = bs[0:128, :]
    kb.sb["lpr"] = bs[:, 0:1024]
    kb.sb["lpi"] = bs[:, 1024:2048]
    kb.sb["lh"] = bsf[:, 1024:1280]
    kb.sb["lam"] = bsf[:, 1280:1344]
    kb.sb["nuv"] = bsf[:, 1344:1352]
    kb.sb["thv"] = bsf[:, 1352:1360]
    kb.sb["gamv"] = bsf[:, 1360:1368]
    kb.sb["gexp"] = bsf[:, 1368:1376]
    kb.sb["dacc"] = kb.sb["spA"][:, 512:1024]
    for k in ("lbi_s", "lpr", "lpi", "lh", "lam", "nuv", "thv", "gamv", "gexp"):
        kb.p.alias[k] = "bsml"
    kb.p.alias["dacc"] = "spA"
    for i in range(6):
        kb.pt("p%d" % i, [128, 512], F32)
    kb.pt("pt0", [128, 128], BF16)
    kb.pt("pt1", [128, 128], BF16)

    pre = [
        ("sp", "identb", P["identb"], None), ("sp", "maskA", P["maskA"], None),
        ("sp", "triB", P["triB"], None), ("sp", "triN", P["triN"], None),
        ("sp", "iota8", P["iota8"], None),
    ]
    for q, nm, src, re_ in pre:
        kb.dma(q, kb.sb[nm][:, :], src, writes=(nm,), tag="pre" + nm)
    kb.dma("pool", kb.sb["xtm"][:, :].rearrange("p (t d) -> p t d", d=1024),
           P["x"].rearrange("(t p) d -> p t d", p=128),
           writes=tuple("xt%d" % i for i in range(NTT)), sgrp="prex", tag="prex")
    kb.dma("pool", kb.sb["laiT"][:, :].rearrange("p (k r) -> p k r", k=8),
           P["laiT"].rearrange("(k p) r -> p k r", p=128), writes=("laiT",), tag="plai")
    kb.dma("pool", kb.sb["G1"][0:RK, 0:2 * DI], P["lbiT"], writes=("G1",), tag="plbi")
    kb.dma("pool", kb.sb["G2"][0:RK, 0:2 * DI], P["mask_in"][None, :].broadcast_to([RK, 2 * DI]), writes=("G2",), tag="pmin")
    kb.dma("sp", kb.sb["cw"][:, :].rearrange("p (d k) -> p d k", d=16),
           P["cw"], writes=("cw",), tag="pcw")
    kb.dma("sp", kb.sb["cb"][:, :], P["cb"], writes=("cb",), tag="pcb")
    kb.dma("pool", kb.sb["wxpT"][:, :].rearrange("p (d r) -> p d r", d=16),
           P["wxpT"].rearrange("(d p) r -> p d r", p=128), writes=("wxpT",), tag="pwxp")
    kb.dma("pool", kb.sb["wdtT"][0:DTR, :], P["wdtT"], writes=("wdtT",), tag="pwdt")
    kb.dma("sp", kb.sb["bdt"][:, :], P["bdt"], writes=("bdt",), tag="pbdt")
    kb.dma("sp", kb.sb["spA"].bitcast(F32)[:, 0:256].rearrange("p (d n) -> p d n", d=16)
           if False else kb.sb["spA"][:, 0:256].rearrange("p (d n) -> p d n", d=16),
           P["Alog"], writes=("spA",), tag="pal")
    kb.dma("sp", kb.sb["Dp"][:, :], P["Dp"], writes=("Dp",), tag="pdp")
    kb.dma("pool", kb.sb["laoT"][:, :].rearrange("p (k r) -> p k r", k=16),
           P["laoT"].rearrange("(k p) r -> p k r", p=128), writes=("laoT",), tag="plao")
    kb.dma("pool", kb.sb["G1"][0:RK, 4096:4096 + D], P["lboT"], writes=("G1",), tag="plbo")
    kb.dma("pool", kb.sb["G2"][0:RK, 4096:4096 + D], P["mask_out"][None, :].broadcast_to([RK, D]), writes=("G2",), tag="pmout")
    kb.ve("memset", kb.sb["onesf"][:, :], 1.0, writes=("onesf",), tag="ponef")
    kb.ve("memset", kb.sb["e8"][:, :], 8.0, writes=("e8",), tag="pe8")
    kb.dma("sp", kb.sb["bq"][:, :], P["bq"], writes=("bq",), tag="pbq")
    kb.dma("sp", kb.sb["bk8"][:, :], P["bk"], writes=("bk8",), tag="pbk")
    kb.ve("tensor_scalar_mul", kb.sb["bk8"][:, :], kb.sb["bk8"][:, :], 0.125,
          reads=("bk8",), writes=("bk8",), tag="pbk8")
    for nmv in ("ln1g", "ln1b", "ln2g", "ln2b"):
        kb.dma("sp", kb.sb[nmv][:, :], P[nmv], writes=(nmv,), tag="p" + nmv)
    kb.dma("pool", kb.sb["rows"][0:1, 0:D], P["bp"], writes=("rows",), sgrp="r_bp", tag="pbp")
    kb.dma("pool", kb.sb["attsh"][0:1, 24:24 + NE], P["gb"][None, :], writes=("attsh",), sgrp="r_gb", tag="pgb")
    kb.dma("pool", kb.sb["gwt"][:, :].rearrange("p (k e) -> p k e", k=8),
           P["gw"].rearrange("(k p) e -> p k e", p=128), writes=("gwt",), tag="pgw")
    kb.dma("pool", kb.sb["b1pp"][:, :].rearrange("p (e h) -> p e h", e=NE),
           P["b1"], writes=("b1pp",), tag="pb1")
    kb.ve("memset", kb.sb["onesb"][:, :], 1.0, writes=("onesb",), tag="poneb")
    kb.ve("memset", kb.sb["onesn"][:, :], -1.0, writes=("onesn",), tag="ponen")
    kb.ve("memset", kb.sb["onescol"][:, :], 1.0, writes=("onescol",), tag="ponec")
    kb.ve("memset", kb.sb["onescf"][:, :], 1.0, writes=("onescf",), tag="ponecf")
    kb.act(kb.sb["Aneg"][:, :], kb.sb["spA"][:, 0:256], AF.Exp, reads=("spA",),
           writes=("Aneg",), tag="pan1")
    kb.ve("tensor_scalar_mul", kb.sb["Aneg"][:, :], kb.sb["Aneg"][:, :], -1.0,
          reads=("Aneg",), writes=("Aneg",), tag="pan2")
    kb.ve("scalar_tensor_tensor", kb.sb["lbi_s"][0:RK, :], kb.sb["G1"][0:RK, 0:2 * DI], 2.0,
          kb.sb["G2"][0:RK, 0:2 * DI], ALU.mult, ALU.mult,
          reads=("G1", "G2"), writes=("lbi_s",), tag="plbis")
    kb.ve("scalar_tensor_tensor", kb.sb["lbo_s"][0:RK, :], kb.sb["G1"][0:RK, 4096:4096 + D], 2.0,
          kb.sb["G2"][0:RK, 4096:4096 + D], ALU.mult, ALU.mult,
          reads=("G1", "G2"), writes=("lbo_s",), tag="plbos")

    if "a" in stages:
        stage_a(kb, P)
    if "b" in stages:
        stage_ffn(kb, P, 1)
    if "c" in stages:
        stage_lru(kb, P)
    if "b2" in stages:
        stage_ffn(kb, P, 2)

    kb.dma("pool", OUT.rearrange("(t p) d -> p t d", p=128),
           kb.sb["xtm"][:, :].rearrange("p (t d) -> p t d", d=1024),
           reads=tuple("xt%d" % i for i in range(NTT)), sgrp="outx", tag="outx")

    esem = {}
    qsem = {}
    for e in ("pe", "act", "dve", "pool", "sp"):
        esem[e] = kb.st.enter_context(nc.semaphore("se_" + e))
    for g in sorted(kb.p.qcount):
        qsem[g] = kb.st.enter_context(nc.semaphore("sq_" + g))
    with nc.Block() as block:
        kb.p.emit_all(block, esem, qsem)
    kb.st.close()
    assert kb.sbuf_bytes <= 213 * 1024, f"sbuf over budget: {kb.sbuf_bytes}"
    return nc, kb


def _til8(w):
    K, N = w.shape
    return np.ascontiguousarray(w.reshape(K // 128, 128, N // 128, 128).transpose(2, 1, 0, 3))


def host_inputs(inputs, T=2048):
    g = lambda k: np.ascontiguousarray(np.asarray(inputs["L0_" + k], np.float32))
    com = {
        "w_inT": np.ascontiguousarray(g("attn_a_w_in").T.reshape(8, 128, 32, 128).transpose(2, 1, 0, 3)),
        "laiT": np.ascontiguousarray(g("attn_a_lora_A_in").T),
        "lbiT": np.ascontiguousarray(g("attn_a_lora_B_in").T),
        "mask_in": g("attn_a_mask_in"),
        "cw": np.ascontiguousarray(g("attn_a_conv_w")[:, 0, :].reshape(16, 128, 4).transpose(1, 0, 2)),
        "cb": np.ascontiguousarray(g("attn_a_conv_b").reshape(16, 128).T),
        "wxpT": np.ascontiguousarray(g("attn_a_w_xproj").T),
        "wdtT": np.ascontiguousarray(g("attn_a_w_dt").T),
        "bdt": np.ascontiguousarray(g("attn_a_b_dt").reshape(16, 128).T),
        "Alog": np.ascontiguousarray(g("attn_a_A_log").reshape(16, 128, 16).transpose(1, 0, 2)),
        "Dp": np.ascontiguousarray(g("attn_a_Dp").reshape(16, 128).T),
        "w_outT": np.ascontiguousarray(g("attn_a_w_out").T.reshape(16, 128, 2, 512).transpose(0, 2, 1, 3)),
        "laoT": np.ascontiguousarray(g("attn_a_lora_A_out").T),
        "lboT": np.ascontiguousarray(g("attn_a_lora_B_out").T),
        "mask_out": g("attn_a_mask_out"),
        "nu": np.ascontiguousarray(g("attn_b_nu_log").reshape(8, 128).T), "th": np.ascontiguousarray(g("attn_b_theta_log").reshape(8, 128).T),
        "gam": np.ascontiguousarray(g("attn_b_gamma_log").reshape(8, 128).T), "Dlru": g("attn_b_D")[None, :],
        "BreT": _til8(np.ascontiguousarray(g("attn_b_B_re").T)),
        "BimT": _til8(np.ascontiguousarray(g("attn_b_B_im").T)),
        "CreT": np.ascontiguousarray(g("attn_b_C_re").T.reshape(8, 128, 2, 512).transpose(0, 2, 1, 3)),
        "CimT": np.ascontiguousarray(g("attn_b_C_im").T.reshape(8, 128, 2, 512).transpose(0, 2, 1, 3)),
        "ln1g": np.ascontiguousarray(g("ffn_ln1_g").reshape(8, 128).T), "ln1b": np.ascontiguousarray(g("ffn_ln1_b").reshape(8, 128).T),
        "ln2g": np.ascontiguousarray(g("ffn_ln2_g").reshape(8, 128).T), "ln2b": np.ascontiguousarray(g("ffn_ln2_b").reshape(8, 128).T),
        "Wq": _til8(g("ffn_Wq")), "Wk": _til8(g("ffn_Wk")), "Wv": _til8(g("ffn_Wv")),
        "Wp": np.ascontiguousarray(g("ffn_Wp").reshape(8, 128, 2, 512).transpose(0, 2, 1, 3)),
        "bq": np.ascontiguousarray(g("ffn_bq").reshape(8, 128).T), "bk": np.ascontiguousarray(g("ffn_bk").reshape(8, 128).T), "bv": g("ffn_bv")[None, :], "bp": g("ffn_bp")[None, :],
        "gw": g("ffn_gate_W"), "gb": g("ffn_gate_b"),
        "W1": np.ascontiguousarray(g("ffn_exp_W1").reshape(NE, 8, 128, 8, 512).transpose(0, 3, 2, 1, 4)), "b1": np.ascontiguousarray(g("ffn_exp_b1").reshape(8, 32, 128).transpose(2, 0, 1)),
        "W2": np.ascontiguousarray(g("ffn_exp_W2").reshape(NE, 4, 8, 128, 2, 512).transpose(0, 1, 4, 3, 2, 5)), "b2": g("ffn_exp_b2"),
        "i_identb": np.eye(128, dtype=ml_dtypes.bfloat16),
        "i_maskA": np.where(np.tril(np.ones((128, 128), bool)), 0.0, NEG).astype(np.float32),
        "i_triB": np.where(np.triu(np.ones((128, 128), bool)), 1.0, 0.0).astype(ml_dtypes.bfloat16),
        "i_triN": np.where(np.triu(np.ones((128, 128), bool)), 0.0, -1e30).astype(ml_dtypes.bfloat16),
        "i_iota8": np.broadcast_to(np.arange(NE, dtype=np.float32), (128, NE)).copy(),
    }
    com = {("i_" + k if not k.startswith("i_") else k): v for k, v in com.items()}
    x = np.asarray(inputs["L0_attn_a_x"], np.float32)
    maps = []
    for b in range(x.shape[0]):
        m = dict(com)
        m["i_x"] = np.ascontiguousarray(x[b, :T])
        maps.append(m)
    return maps


_CACHE = {}


def kernel(**inputs):
    T = 2048
    if "prog" not in _CACHE:
        _CACHE["prog"] = build_program(T, stages=("a", "b", "c", "b2"))
    nc, kb = _CACHE["prog"]
    in_maps = host_inputs(inputs, T)
    res = run_bass_kernel_spmd(nc, in_maps, core_ids=list(range(8)))
    outs = [np.asarray(r["out"], np.float32) for r in res.results]
    full = np.stack(outs, 0)
    return full


def xtk(tt):
    return "xt%d" % tt


def _htile(kb, tt):
    g = "G1" if tt < 8 else "G2"
    return kb.sb[g], g, (tt % 8) * 1024


def _fm(kb, dt):
    g = "G3" if dt < 4 else "G4"
    k = (("G3a",), ("G3a",), ("hs0", "hs1", "hs2", "hs3"), ("hs4", "hs5", "hs6", "hs7"),
         ("G4a",), ("G4a",), ("G4b",), ("G4b",))[dt]
    return kb.sb[g], k, (dt % 4) * 2048


def stage_ffn(kb, P, ci):
    T, NTT = kb.T, kb.NTT
    sb, ps = kb.sb, kb.ps
    xtm = sb["xtm"]
    xv = xtm[:, :].rearrange("p (t d) -> p t d", d=1024)
    QCW = min(512, T)
    NQC = T // QCW
    cg = "c%d_" % ci

    kb.dma("pool", sb["G7"][0:1, 3 * T + 1024:3 * T + 2048], P["bv"], writes=("G7",),
           sgrp="r_bv", tag=cg + "pbv")
    ln_stats(kb, lambda tt: xv[:, tt, :], NTT, cg + "ln0", "F0", tuple("xt%d" % i for i in range(NTT)))
    for tt in range(NTT):
        mu, rs = ln_mu_rstd(kb, "F0", tt)
        ht, hk, co = _htile(kb, tt)
        kb.ve("tensor_scalar", ht[:, co:co + 1024], xv[:, tt, :], mu, rs,
              ALU.subtract, ALU.mult, reads=(xtk(tt), "stats"), writes=(hk,), tag=cg + "h")

    ln_stats(kb, lambda tt: _htile(kb, tt)[0][:, _htile(kb, tt)[2]:_htile(kb, tt)[2] + 1024],
             NTT, cg + "ln1", "F1", ("G1", "G2"))
    tscr = sb["G9"]
    for tt in range(NTT):
        mu, rs = ln_mu_rstd(kb, "F1", tt)
        ht, hk, co = _htile(kb, tt)
        kb.ve("tensor_scalar", tscr[:, 0:1024], ht[:, co:co + 1024], mu, rs,
              ALU.subtract, ALU.mult, reads=(hk, "stats"), writes=("g9t",), tag=cg + "n1")
        for dt in range(8):
            pk = "pt%d" % (dt % 2)
            kb.tr(ps[pk][:, 0:128], tscr[:, dt * 128:(dt + 1) * 128], sb["identb"][:, :],
                  pk, reads=("g9t", "identb"), tag=cg + "t1")
            ft, fk, fo = _fm(kb, dt)
            kb.act(ft[:, fo + tt * 128:fo + (tt + 1) * 128], ps[pk][:, 0:128], AF.Copy,
                   scale=sb["ln1g"][:, dt:dt + 1], bias=sb["ln1b"][:, dt:dt + 1],
                   reads=(pk, "ln1g", "ln1b"), writes=fk + (pk,), tag=cg + "t1e")

    rows = sb["rows"]
    for hg in range(4):
        for wname, dstk in (("Wq", "G5"), ("Wk", "G6")):
            dkeys = G5K if dstk == "G5" else G6K
            for ot2 in range(2):
                og = hg * 2 + ot2
                wbuf = "wsA%d" % (ot2 % 2)
                kb.dma("pool", sb[wbuf][:, :].rearrange("p (k o) -> p k o", k=8),
                       P[wname][og].rearrange("p k o -> p k o"), writes=(wbuf,), tag=cg + "wqk")
                for qc in range(NQC):
                    pk = "p%d" % (qc % 2)
                    for kt in range(8):
                        ft, fk, fo = _fm(kb, kt)
                        kb.mm(ps[pk][:, 0:QCW], sb[wbuf][:, :].rearrange(
                            "p (k o) -> p k o", k=8)[:, kt, :],
                            ft[:, fo + qc * QCW:fo + (qc + 1) * QCW],
                            pk, kt == 0, kt == 7, reads=(wbuf,) + fk, tag=cg + "qk")
                    for hh in range(2):
                        hl = ot2 * 2 + hh
                        if wname == "Wq":
                            sc, bi = 1.0, sb["bq"][hh * 64:hh * 64 + 64, og:og + 1]
                        else:
                            sc, bi = 0.125, sb["bk8"][hh * 64:hh * 64 + 64, og:og + 1]
                        kb.act(sb[dstk][0:64, hl * T + qc * QCW:hl * T + (qc + 1) * QCW],
                               ps[pk][hh * 64:hh * 64 + 64, 0:QCW], AF.Copy,
                               scale=sc, bias=bi,
                               reads=(pk, "bq", "bk8"), writes=dkeys + (pk,), tag=cg + "qke")
        g8v = sb["G8"][:, :].rearrange("p (t o) -> p t o", o=256)
        for _vt in range(2):
            kb.dma("pool", sb["wsV"][:, _vt * 1024:(_vt + 1) * 1024].rearrange(
                "p (k o) -> p k o", k=8),
                P["Wv"][hg * 2 + _vt].rearrange("p k o -> p k o"),
                writes=("wsV%d" % _vt,), tag=cg + "wvl")
        for tb in range(NTT):
            pk = "p%d" % (tb % 2)
            for kt in range(8):
                ft, fk, fo = _fm(kb, kt)
                for vt in range(2):
                    kb.mm(ps[pk][:, vt * 128:(vt + 1) * 128],
                          ft[:, fo + tb * 128:fo + (tb + 1) * 128],
                          sb["wsV"][:, vt * 1024 + kt * 128:vt * 1024 + (kt + 1) * 128],
                          pk, kt == 0 and vt == 0, False,
                          reads=fk + ("wsV%d" % vt,), tag=cg + "v")
            kb.mm(ps[pk][:, 0:256], sb["onesb"][0:1, :],
                  sb["G7"][0:1, 3 * T + 1024 + hg * 256:3 * T + 1024 + (hg + 1) * 256],
                  pk, False, True, reads=("onesb", "G7"), tag=cg + "vb")
            kb.ve("tensor_copy", g8v[:, tb, :], ps[pk][:, 0:256],
                  reads=(pk,), writes=("G8r%d" % (tb // 4), pk), tag=cg + "ve")

        for hl in range(4):
            h = hg * 4 + hl
            q0c = hl * T
            g9f = sb["G9"].bitcast(F32)
            kb.act(g9f[0:64, 0:T], sb["G6"][0:64, q0c:q0c + T], AF.Square,
                   reads=G6K, writes=("g9t", "g9p0", "g9p1", "g9p2", "g9p3"), tag=cg + "ksq")
            NQB = T // 512 if T >= 512 else 1
            QW = min(512, T)
            for c_ in range(T // QW):
                kb.mm(ps["p5"][0:1, 0:QW],
                      sb["onescf"][0:64, 0:1], g9f[0:64, c_ * QW:(c_ + 1) * QW],
                      "p5", True, True, reads=("onescf", "g9t"), tag=cg + "ksum")
                kb.ve("reduce_max", sb["spB"][0:1, c_:c_ + 1], ps["p5"][0:1, 0:QW],
                      axis=AX.X, reads=("p5",), writes=("spB",), tag=cg + "kmax")
            kb.ve("reduce_max", sb["spB"][0:1, 8:9], sb["spB"][0:1, 0:T // QW],
                  axis=AX.X, reads=("spB",), writes=("spB",), tag=cg + "kmax2")
            kb.act(g9f[0:64, 0:T], sb["G5"][0:64, q0c:q0c + T], AF.Square,
                   reads=G5K, writes=("g9t", "g9p0", "g9p1", "g9p2", "g9p3"), tag=cg + "qsq")
            for c_ in range(T // QW):
                kb.mm(ps["p5"][0:1, 0:QW],
                      sb["onescf"][0:64, 0:1], g9f[0:64, c_ * QW:(c_ + 1) * QW],
                      "p5", True, True, reads=("onescf", "g9t"), tag=cg + "qsum")
                kb.act(sb["G7"][0:1, 2 * T + c_ * QW:2 * T + (c_ + 1) * QW],
                       ps["p5"][0:1, 0:QW], AF.Sqrt, scale=sb["spB"][0:1, 8:9],
                       reads=("p5", "spB"), writes=("G7",), tag=cg + "qsh")

            for qc in range(NQC):
                qlo = qc * QCW
                po = "p4"
                nkb = qc * (QCW // 128) + (QCW // 128)

                def st_mm(kb_i):
                    k0 = kb_i * 128
                    q0 = max(qlo, k0)
                    w = qlo + QCW - q0
                    pk = "p%d" % (2 + kb_i % 2)
                    kb.mm(ps[pk][:, 0:w], sb["G6"][0:64, q0c + k0:q0c + k0 + 128],
                          sb["G5"][0:64, q0c + q0:q0c + qlo + QCW],
                          pk, True, False, reads=G5K + G6K, tag=cg + "pb")
                    kb.mm(ps[pk][:, 0:w], sb["onesn"][0:1, :],
                          sb["G7"][0:1, 2 * T + q0:2 * T + qlo + QCW], pk, False, q0 != k0,
                          reads=("onesn", "G7"), tag=cg + "pbs")
                    if q0 == k0:
                        kb.mm(ps[pk][:, 0:128], sb["identb"][:, :], sb["triN"][:, :],
                              pk, False, True, reads=("identb", "triN"), tag=cg + "pbm")
                    pb = "g9p%d" % (kb_i % 4)
                    pbuf = sb["G9"][:, 1024 + (kb_i % 4) * 512:1024 + (kb_i % 4) * 512 + 512]
                    kb.act(pbuf[:, 0:w], ps[pk][:, 0:w], AF.Exp,
                           reads=(pk,), writes=(pb, pk), tag=cg + "pbe")

                def pv_mm(kb_i):
                    k0 = kb_i * 128
                    q0 = max(qlo, k0)
                    w = qlo + QCW - q0
                    pb = "g9p%d" % (kb_i % 4)
                    pbuf = sb["G9"][:, 1024 + (kb_i % 4) * 512:1024 + (kb_i % 4) * 512 + 512]
                    kb.mm(ps[po][0:64, q0 - qlo:QCW], g8v[:, kb_i, hl * 64:(hl + 1) * 64],
                          pbuf[:, 0:w], po, kb_i == 0, kb_i == nkb - 1,
                          reads=("G8r%d" % (kb_i // 4), pb), tag=cg + "pv")
                    kb.mm(ps["p5"][0:1, q0 - qlo:QCW], sb["onescol"][:, 0:1],
                          pbuf[:, 0:w], "p5", kb_i == 0, kb_i == nkb - 1,
                          reads=("onescol", pb), tag=cg + "psum")

                for kb_i in range(nkb + 1):
                    if kb_i < nkb:
                        st_mm(kb_i)
                    if kb_i >= 1:
                        pv_mm(kb_i - 1)
                kb.ve("reciprocal", sb["spB"][0:1, 512:512 + QCW],
                      ps["p5"][0:1, 0:QCW], reads=("p5",), writes=("spB", "p5"), tag=cg + "prc")
                pk = "p%d" % (2 + qc % 2)
                kb.mm(ps[pk][0:64, 0:QCW], sb["onesf"][0:1, 0:64],
                      sb["spB"][0:1, 512:512 + QCW], pk, True, True,
                      reads=("onesf", "spB"), tag=cg + "pbc")
                kb.act(sb["spA"][0:64, 0:QCW], ps[pk][0:64, 0:QCW],
                       AF.Copy, reads=(pk,), writes=("spA", pk), tag=cg + "pbce")
                dtg = hl // 2
                rbase = (hl % 2) * 64
                kb.ve("tensor_mul", sb["G7"][rbase:rbase + 64, dtg * T + qlo:dtg * T + qlo + QCW],
                      ps[po][0:64, 0:QCW], sb["spA"][0:64, 0:QCW],
                      reads=(po, "spA"), writes=("G7", po), tag=cg + "oev")
        for par in range(2):
            kb.dma("pool", sb["G5"][:, par * 2048:par * 2048 + 1024].rearrange(
                "p (a o) -> p a o", a=2),
                P["Wp"][2 * hg:2 * hg + 2, par].rearrange("a p o -> p a o"),
                writes=("G5q%d" % par,), tag=cg + "wp")
            for tb in range(NTT):
                pk = "p%d" % (tb % 2)
                for dtg in range(2):
                    kb.mm(ps[pk][:, 0:512],
                          sb["G7"][0:128, dtg * T + tb * 128:dtg * T + (tb + 1) * 128],
                          sb["G5"][:, par * 2048 + dtg * 512:par * 2048 + dtg * 512 + 512],
                          pk, dtg == 0, dtg == 1 and hg != 0,
                          reads=("G7", "G5q%d" % par), tag=cg + "pj")
                if hg == 0:
                    kb.mm(ps[pk][:, 0:512], sb["onesb"][0:1, :],
                          rows[0:1, par * 512:(par + 1) * 512], pk,
                          False, True, reads=("onesb", "rows"), tag=cg + "pjb")
                ht, hk, co = _htile(kb, tb)
                kb.ve("tensor_add", ht[:, co + par * 512:co + (par + 1) * 512],
                      ht[:, co + par * 512:co + (par + 1) * 512], ps[pk][:, 0:512],
                      reads=(hk, pk), writes=(hk, pk), tag=cg + "pje")

    ln_stats(kb, lambda tt: _htile(kb, tt)[0][:, _htile(kb, tt)[2]:_htile(kb, tt)[2] + 1024],
             NTT, cg + "ln2", "F2", ("G1", "G2"))
    for tt in range(NTT):
        mu, rs = ln_mu_rstd(kb, "F2", tt)
        ht, hk, co = _htile(kb, tt)
        kb.ve("tensor_scalar", tscr[:, 0:1024], ht[:, co:co + 1024], mu, rs,
              ALU.subtract, ALU.mult, reads=(hk, "stats"), writes=("g9t",), tag=cg + "n2")
        for dt in range(8):
            pk = "pt%d" % (dt % 2)
            kb.tr(ps[pk][:, 0:128], tscr[:, dt * 128:(dt + 1) * 128], sb["identb"][:, :],
                  pk, reads=("g9t", "identb"), tag=cg + "t2")
            ft, fk, fo = _fm(kb, dt)
            kb.act(ft[:, fo + tt * 128:fo + (tt + 1) * 128], ps[pk][:, 0:128], AF.Copy,
                   scale=sb["ln2g"][:, dt:dt + 1], bias=sb["ln2b"][:, dt:dt + 1],
                   reads=(pk, "ln2g", "ln2b"), writes=fk + (pk,), tag=cg + "t2e")
        kb.ve("tensor_add", xv[:, tt, :], xv[:, tt, :], ht[:, co:co + 1024],
              reads=(xtk(tt), hk), writes=(xtk(tt),), tag=cg + "xf")

    for tt in range(NTT):
        pk = "p%d" % (tt % 2)
        for kt in range(8):
            ft, fk, fo = _fm(kb, kt)
            kb.mm(ps[pk][:, 0:NE], ft[:, fo + tt * 128:fo + (tt + 1) * 128],
                  sb["gwt"][:, :].rearrange("p (k e) -> p k e", k=8)[:, kt, :],
                  pk, kt == 0, False, reads=("gwt",) + fk, tag=cg + "g")
        kb.mm(ps[pk][:, 0:NE], sb["onesb"][0:1, :], sb["attsh"][0:1, 24:24 + NE],
              pk, False, True, reads=("onesb", "attsh"), tag=cg + "gb")
        kb.ve("tensor_copy", sb["lgt"][:, tt * NE:(tt + 1) * NE], ps[pk][:, 0:NE],
              reads=(pk,), writes=("lgt", pk), tag=cg + "ge")
    kb.ve("tensor_scalar_add", sb["ascr"][:, 8:16], sb["iota8"][:, :], -8.0,
          reads=("iota8",), writes=("ascr",), tag=cg + "iom8")
    for tt in range(NTT):
        g_ = sb["spA"][:, 960:1008]
        lg = sb["lgt"][:, tt * NE:(tt + 1) * NE]
        wf = sb["wfl"][:, tt * NE:(tt + 1) * NE]
        kb.ve("reduce_max", g_[:, 0:1], lg, axis=AX.X, reads=("lgt",), writes=("spA",), tag=cg + "m1")
        kb.ve("tensor_scalar", g_[:, 8:16], lg, g_[:, 0:1], None, ALU.is_equal,
              reads=("lgt", "spA"), writes=("spA",), tag=cg + "eq1")
        kb.ve("scalar_tensor_tensor", g_[:, 16:24], g_[:, 8:16], 1.0, sb["ascr"][:, 8:16],
              ALU.mult, ALU.mult, reads=("spA", "ascr"), writes=("spA",), tag=cg + "sel1a")
        kb.ve("tensor_scalar_add", g_[:, 16:24], g_[:, 16:24], 8.0,
              reads=("spA",), writes=("spA",), tag=cg + "sel1b")
        kb.ve("tensor_reduce", g_[:, 1:2], g_[:, 16:24], AX.X, ALU.min,
              reads=("spA",), writes=("spA",), tag=cg + "i1")
        kb.ve("tensor_scalar", g_[:, 8:16], sb["iota8"][:, :], g_[:, 1:2], None, ALU.is_equal,
              reads=("spA", "iota8"), writes=("spA",), tag=cg + "es1")
        kb.ve("scalar_tensor_tensor", g_[:, 24:32], g_[:, 8:16], NEG, lg,
              ALU.mult, ALU.add, reads=("spA", "lgt"), writes=("spA",), tag=cg + "lg2")
        kb.ve("reduce_max", g_[:, 2:3], g_[:, 24:32], axis=AX.X,
              reads=("spA",), writes=("spA",), tag=cg + "m2")
        kb.ve("tensor_scalar", g_[:, 32:40], g_[:, 24:32], g_[:, 2:3], None, ALU.is_equal,
              reads=("spA",), writes=("spA",), tag=cg + "eq2")
        kb.ve("scalar_tensor_tensor", g_[:, 16:24], g_[:, 32:40], 1.0, sb["ascr"][:, 8:16],
              ALU.mult, ALU.mult, reads=("spA", "ascr"), writes=("spA",), tag=cg + "sel2a")
        kb.ve("tensor_scalar_add", g_[:, 16:24], g_[:, 16:24], 8.0,
              reads=("spA",), writes=("spA",), tag=cg + "sel2b")
        kb.ve("tensor_reduce", g_[:, 3:4], g_[:, 16:24], AX.X, ALU.min,
              reads=("spA",), writes=("spA",), tag=cg + "i2")
        kb.ve("tensor_scalar", g_[:, 32:40], sb["iota8"][:, :], g_[:, 3:4], None, ALU.is_equal,
              reads=("spA", "iota8"), writes=("spA",), tag=cg + "es2")
        kb.ve("tensor_sub", g_[:, 4:5], g_[:, 0:1], g_[:, 2:3],
              reads=("spA",), writes=("spA",), tag=cg + "d12")
        kb.act(g_[:, 5:6], g_[:, 4:5], AF.Sigmoid, reads=("spA",), writes=("spA",), tag=cg + "w1")
        kb.act(g_[:, 6:7], g_[:, 4:5], AF.Sigmoid, scale=-1.0,
               reads=("spA",), writes=("spA",), tag=cg + "w2")
        kb.ve("tensor_scalar", g_[:, 40:48], g_[:, 8:16], g_[:, 5:6], None, ALU.mult,
              reads=("spA",), writes=("spA",), tag=cg + "wa")
        kb.ve("scalar_tensor_tensor", wf, g_[:, 32:40], g_[:, 6:7], g_[:, 40:48],
              ALU.mult, ALU.add, reads=("spA",), writes=("wfl",), tag=cg + "wb")

    TCW = min(512, T)
    NTCH = T // TCW
    midv1 = sb["G1"][:, :].rearrange("p (h t) -> p h t", h=16)
    midv2 = sb["G2"][:, :].rearrange("p (h t) -> p h t", h=16)
    for e in range(NE):
        kb.dma("pool", sb["G7"][0:1, 3 * T:3 * T + 1024], P["b2"][e:e + 1, :],
               writes=("G7",), sgrp="r_b2", tag=cg + "b2l")
        for tc in range(NTCH):
            for o8 in range(8):
                woff = (o8 % 2) * 4096
                kb.dma("pool", sb["G5"][:, woff:woff + 4096].rearrange(
                    "p (k h) -> p k h", k=8),
                    P["W1"][e, o8], writes=G5K[2 * (o8 % 2):2 * (o8 % 2) + 2],
                    sgrp="w1l%d" % (o8 % 2), tag=cg + "w1l")
                for htl in range(4):
                    ht = o8 * 4 + htl
                    pk = "p%d" % (4 + ht % 2)
                    for kt in range(8):
                        ft, fk, fo = _fm(kb, kt)
                        kb.mm(ps[pk][:, 0:TCW], sb["G5"][:, woff + kt * 512 + htl * 128:
                                                         woff + kt * 512 + (htl + 1) * 128],
                              ft[:, fo + tc * TCW:fo + (tc + 1) * TCW], pk,
                              kt == 0, kt == 7,
                              reads=G5K[2 * (o8 % 2):2 * (o8 % 2) + 2] + fk, tag=cg + "m1")
                    mv = midv1 if ht < 16 else midv2
                    mk = "G1" if ht < 16 else "G2"
                    kb.act(mv[:, ht % 16, 0:TCW], ps[pk][:, 0:TCW], AF.Gelu,
                           bias=sb["b1pp"][:, :].rearrange("p (e h) -> p e h", e=NE)[:, e, ht:ht + 1],
                           reads=(pk, "b1pp"), writes=(mk, pk), tag=cg + "m1e")
            for oh in range(2):
                for ktg in range(4):
                    woff = (ktg % 2) * 4096
                    kb.dma("pool", sb["G6"][:, woff:woff + 4096].rearrange(
                        "p (k o) -> p k o", k=8),
                        P["W2"][e, ktg, oh], writes=G6K[2 * (ktg % 2):2 * (ktg % 2) + 2],
                        sgrp="w2l%d" % (ktg % 2), tag=cg + "w2l")
                    for kt in range(8):
                        htg = ktg * 8 + kt
                        mv = midv1 if htg < 16 else midv2
                        mk = "G1" if htg < 16 else "G2"
                        for tb in range(TCW // 128):
                            pk = "p%d" % (tb % 4)
                            kb.mm(ps[pk][:, 0:512],
                                  mv[:, htg % 16, tb * 128:(tb + 1) * 128],
                                  sb["G6"][:, woff + kt * 512:woff + (kt + 1) * 512],
                                  pk, htg == 0, False,
                                  reads=G6K[2 * (ktg % 2):2 * (ktg % 2) + 2] + (mk,), tag=cg + "m2")
                for tb in range(TCW // 128):
                    pk = "p%d" % (tb % 4)
                    kb.mm(ps[pk][:, 0:512], sb["onesb"][0:1, :],
                          sb["G7"][0:1, 3 * T + oh * 512:3 * T + (oh + 1) * 512], pk,
                          False, True, reads=("onesb", "G7"), tag=cg + "m2b")
                    tt = tc * (TCW // 128) + tb
                    kb.ve("scalar_tensor_tensor", xv[:, tt, oh * 512:(oh + 1) * 512],
                          ps[pk][:, 0:512], sb["wfl"][:, tt * NE + e:tt * NE + e + 1],
                          xv[:, tt, oh * 512:(oh + 1) * 512], ALU.mult, ALU.add,
                          reads=(pk, "wfl", xtk(tt)), writes=(xtk(tt), pk), tag=cg + "m2x")
    if ci == 1:
        kb.dbg("x_after_f1", xtm[:, :], [128, NTT * 1024], BF16, reads=tuple("xt%d" % i for i in range(NTT)))


def _bu(kb, im, pt):
    T = kb.T
    if not im:
        gk = G5K if pt < 4 else G6K
        g = "G5" if pt < 4 else "G6"
        reg = ((pt % 4) * T) // 2048
        return kb.sb[g], gk[reg], (pt % 4) * T
    if pt < 4:
        return kb.sb["G7"], "G7", pt * T
    g = "G8" if pt < 6 else "G9"
    return kb.sb[g], g, (pt % 2) * T


G5K = ("G5q0", "G5q1", "G5q2", "G5q3")
G6K = ("G6q0", "G6q1", "G6q2", "G6q3")

_LREG = [("G5", 0, 4, 0), ("G6", 0, 4, 4), ("G7", 1, 4, 0), ("G8", 1, 2, 4), ("G9", 1, 2, 6)]
_LKEY = {"G5": G5K, "G6": G6K, "G7": ("G7",), "G8": ("G8",), "G9": ("G9",)}


def stage_lru(kb, P):
    T, NTT = kb.T, kb.NTT
    sb, ps = kb.sb, kb.ps
    xtm = sb["xtm"]
    xv = xtm[:, :].rearrange("p (t d) -> p t d", d=1024)
    NCH = T // 128
    lam = sb["lam"]
    cg = "l_"

    kb.dma("sp", sb["nuv"][:, :], P["nu"], writes=("nuv",), sgrp="c_nu", tag=cg + "pnu")
    kb.dma("sp", sb["thv"][:, :], P["th"], writes=("thv",), sgrp="c_th", tag=cg + "pth")
    kb.dma("sp", sb["gamv"][:, :], P["gam"], writes=("gamv",), sgrp="c_gam", tag=cg + "pgam")
    kb.act(lam[:, 16:24], sb["nuv"][:, :], AF.Exp, reads=("nuv",), writes=("lam",), tag=cg + "e1")
    kb.act(lam[:, 24:32], lam[:, 16:24], AF.Exp, scale=-1.0,
           reads=("lam",), writes=("lam",), tag=cg + "r")
    kb.act(lam[:, 32:40], sb["thv"][:, :], AF.Exp, reads=("thv",), writes=("lam",), tag=cg + "th")
    kb.act(lam[:, 40:48], lam[:, 32:40], AF.Sin, scale=0.5,
           reads=("lam",), writes=("lam",), tag=cg + "s2")
    kb.act(lam[:, 48:56], lam[:, 32:40], AF.Sin, scale=0.25,
           reads=("lam",), writes=("lam",), tag=cg + "s4")
    kb.act(lam[:, 56:64], lam[:, 40:48], AF.Square, reads=("lam",), writes=("lam",), tag=cg + "q2")
    kb.ve("tensor_scalar", lam[:, 56:64], lam[:, 56:64], -2.0, 1.0, ALU.mult, ALU.add,
          reads=("lam",), writes=("lam",), tag=cg + "cos")
    kb.act(lam[:, 48:56], lam[:, 48:56], AF.Square, reads=("lam",), writes=("lam",), tag=cg + "q4")
    kb.ve("tensor_scalar", lam[:, 48:56], lam[:, 48:56], -2.0, 1.0, ALU.mult, ALU.add,
          reads=("lam",), writes=("lam",), tag=cg + "c2")
    kb.ve("tensor_mul", lam[:, 40:48], lam[:, 40:48], lam[:, 48:56],
          reads=("lam",), writes=("lam",), tag=cg + "sc")
    kb.ve("tensor_scalar_mul", lam[:, 40:48], lam[:, 40:48], 2.0,
          reads=("lam",), writes=("lam",), tag=cg + "sin")
    kb.ve("tensor_mul", lam[:, 0:8], lam[:, 24:32], lam[:, 56:64],
          reads=("lam",), writes=("lam",), tag=cg + "lr")
    kb.ve("tensor_mul", lam[:, 8:16], lam[:, 24:32], lam[:, 40:48],
          reads=("lam",), writes=("lam",), tag=cg + "li")
    kb.act(sb["gexp"][:, :], sb["gamv"][:, :], AF.Exp, reads=("gamv",), writes=("gexp",), tag=cg + "g")
    kb.dma("pool", kb.sb["rows"][0:1, 1024:2048], P["Dlru"], writes=("rows",), sgrp="r_dl", tag=cg + "dl")
    kb.mm(ps["p5"][:, 0:512], sb["onesb"][0:1, :], sb["rows"][0:1, 1024:1536], "p5",
          True, True, reads=("onesb", "rows"), tag=cg + "db1")
    kb.act(sb["wsV"][:, 0:512], ps["p5"][:, 0:512], AF.Copy, reads=("p5",),
           writes=("wsV0", "p5"), tag=cg + "db1e")
    kb.mm(ps["p5"][:, 0:512], sb["onesb"][0:1, :], sb["rows"][0:1, 1536:2048], "p5",
          True, True, reads=("onesb", "rows"), tag=cg + "db2")
    kb.act(sb["wsV"][:, 512:1024], ps["p5"][:, 0:512], AF.Copy, reads=("p5",),
           writes=("wsV0", "p5"), tag=cg + "db2e")

    lprv = sb["lpr"].rearrange("p (e j) -> p e j", e=8)
    lpiv = sb["lpi"].rearrange("p (e j) -> p e j", e=8)
    kb.ve("tensor_copy", lprv[:, :, 0:1], lam[:, 0:8][:, :, None],
          reads=("lam",), writes=("lpr",), tag=cg + "p0r")
    kb.ve("tensor_copy", lpiv[:, :, 0:1], lam[:, 8:16][:, :, None],
          reads=("lam",), writes=("lpi",), tag=cg + "p0i")
    w = 1
    while w < 128:
        m = min(w, 128 - w)
        ar = lprv[:, :, w - 1:w].broadcast_to([128, 8, m])
        ai = lpiv[:, :, w - 1:w].broadcast_to([128, 8, m])
        sA = sb["spA"].rearrange("p (e j) -> p e j", e=8)[:, :, 0:m]
        sB = sb["spB"].rearrange("p (e j) -> p e j", e=8)[:, :, 0:m]
        kb.ve("tensor_mul", sA, lprv[:, :, 0:m], ar, reads=("lpr",), writes=("spA", "sC", "sD"), tag=cg + "pd1")
        kb.ve("tensor_mul", sB, lpiv[:, :, 0:m], ai, reads=("lpr", "lpi"), writes=("spB",), tag=cg + "pd2")
        kb.ve("tensor_sub", sA, sA, sB, reads=("spA", "sC", "sD", "spB"), writes=("spA", "sC", "sD"), tag=cg + "pd3")
        kb.ve("tensor_mul", sB, lprv[:, :, 0:m], ai, reads=("lpr", "lpi"), writes=("spB",), tag=cg + "pd4")
        kb.ve("tensor_copy", lprv[:, :, w:w + m], sA, reads=("spA", "sC", "sD"), writes=("lpr",), tag=cg + "pd5")
        kb.ve("tensor_mul", sA, lpiv[:, :, 0:m], ar, reads=("lpi",), writes=("spA", "sC", "sD"), tag=cg + "pd6")
        kb.ve("tensor_add", lpiv[:, :, w:w + m], sA, sB, reads=("spA", "sC", "sD", "spB"), writes=("lpi",), tag=cg + "pd7")
        w *= 2

    ln_stats(kb, lambda tt: xv[:, tt, :], NTT, cg + "ln", "L", tuple("xt%d" % i for i in range(NTT)))
    for tt in range(NTT):
        mu, rs = ln_mu_rstd(kb, "L", tt)
        ht, hk, co = _htile(kb, tt)
        kb.ve("tensor_scalar", ht[:, co:co + 1024], xv[:, tt, :], mu, rs,
              ALU.subtract, ALU.mult, reads=(xtk(tt), "stats"), writes=(hk,), tag=cg + "n")
        for dt in range(8):
            pk = "pt%d" % (dt % 2)
            kb.tr(ps[pk][:, 0:128], ht[:, co + dt * 128:co + (dt + 1) * 128],
                  sb["identb"][:, :], pk, reads=(hk, "identb"), tag=cg + "t")
            ft, fk, fo = _fm(kb, dt)
            kb.act(ft[:, fo + tt * 128:fo + (tt + 1) * 128], ps[pk][:, 0:128], AF.Copy,
                   reads=(pk,), writes=fk + (pk,), tag=cg + "te")

    for pt in range(8):
        for im in range(2):
            wsb = "wsA%d" % im
            src = P["BimT"] if im else P["BreT"]
            kb.dma("pool", sb[wsb][:, :].rearrange("p (k o) -> p k o", k=8),
                   src[pt], writes=(wsb,), tag=cg + "bw")
            bt, bk_, bco = _bu(kb, im, pt)
            BW = min(512, T)
            for tc in range(T // BW):
                pk = "p%d" % (2 * im + tc % 2)
                for kt in range(8):
                    ft, fk, fo = _fm(kb, kt)
                    kb.mm(ps[pk][:, 0:BW], sb[wsb][:, :].rearrange(
                        "p (k o) -> p k o", k=8)[:, kt, :],
                        ft[:, fo + tc * BW:fo + (tc + 1) * BW], pk, kt == 0, kt == 7,
                        reads=(wsb,) + fk, tag=cg + "bu")
                kb.act(bt[:, bco + tc * BW:bco + (tc + 1) * BW], ps[pk][:, 0:BW],
                       AF.Copy, scale=sb["gexp"][:, pt:pt + 1],
                       reads=(pk, "gexp"), writes=(bk_, pk), tag=cg + "bue")

    kb.dbg("bu_re", sb["G5"][:, 0:4 * T], [128, 4 * T], BF16, reads=G5K)
    kb.dbg("bu_im", sb["G7"][:, 0:4 * T], [128, 4 * T], BF16, reads=("G7",))
    lh = sb["lh"]
    kb.ve("memset", lh[:, :], 0.0, writes=("lhr", "lhi"), tag=cg + "h0")
    hre = lh[:, 0:128].rearrange("p (e c) -> p e c", e=8)[:, :, 0:NCH]
    him = lh[:, 128:256].rearrange("p (e c) -> p e c", e=8)[:, :, 0:NCH]
    lrv = lam[:, 0:8][:, :, None].broadcast_to([128, 8, NCH])
    liv = lam[:, 8:16][:, :, None].broadcast_to([128, 8, NCH])
    sA8 = sb["spA"][:, 0:128].rearrange("p (e c) -> p e c", e=8)[:, :, 0:NCH]
    sB8 = sb["spB"][:, 0:128].rearrange("p (e c) -> p e c", e=8)[:, :, 0:NCH]
    sC8 = sb["spA"][:, 128:256].rearrange("p (e c) -> p e c", e=8)[:, :, 0:NCH]

    def rview(g, npt, bpt, j):
        return sb[g][:, :].rearrange("p (a c j) -> p a c j", a=8192 // 128 // 8 if False else None)

    def bu_col(g, npt, j):
        return sb[g][:, 0:npt * T].rearrange("p (a c j) -> p a c j", a=npt, j=128)[:, :, 0:NCH, j]

    sD8 = sb["dacc"][:, 128:256].rearrange("p (e c) -> p e c", e=8)[:, :, 0:NCH]
    for j in range(128):
        kb.ve("tensor_mul", sA8, hre, lrv, reads=("lhr", "lam"), writes=("spA",), tag=cg + "a1")
        kb.ve("tensor_mul", sC8, him, liv, reads=("lhi", "lam"), writes=("sC",), tag=cg + "a2")
        kb.ve("tensor_sub", sA8, sA8, sC8, reads=("spA", "sC"), writes=("spA",), tag=cg + "a3")
        kb.ve("tensor_mul", sB8, him, lrv, reads=("lhi", "lam"), writes=("spB",),
              eng_name="pool", tag=cg + "b1")
        kb.ve("tensor_mul", sD8, hre, liv, reads=("lhr", "lam"), writes=("sD",),
              eng_name="pool", tag=cg + "b2")
        kb.ve("tensor_add", sB8, sB8, sD8, reads=("spB", "sD"), writes=("spB",),
              eng_name="pool", tag=cg + "b3")
        for g, im, npt, bpt in _LREG:
            hv = (him if im else hre)[:, bpt:bpt + npt, :]
            tv = (sB8 if im else sA8)[:, bpt:bpt + npt, :]
            bj = bu_col(g, npt, j)
            kb.ve("tensor_add", hv, tv, bj, reads=("spB" if im else "spA",) + _LKEY[g],
                  writes=("lhi" if im else "lhr",),
                  eng_name="pool" if im else "dve", tag=cg + "hu")
            kb.act(bj, hv, AF.Copy, reads=("lhi" if im else "lhr",),
                   writes=_LKEY[g], tag=cg + "xw")
        if j % 4 == 0:
            kb.mm(ps["p5"][0:16, 0:16], sb["onesb"][0:1, 0:16], sb["onesb"][0:1, 0:16],
                  "p5", True, True, reads=("onesb", "lhr"), tag=cg + "warm")

    xer = hre
    xei = him
    kb.ve("tensor_copy", lam[:, 16:24], lam[:, 0:8], reads=("lam",), writes=("lam",), tag=cg + "q0")
    kb.ve("tensor_copy", lam[:, 24:32], lam[:, 8:16], reads=("lam",), writes=("lam",), tag=cg + "q0i")
    for _sq in range(7):
        kb.ve("tensor_mul", lam[:, 32:40], lam[:, 16:24], lam[:, 16:24],
              reads=("lam",), writes=("lam",), tag=cg + "sq1")
        kb.ve("tensor_mul", lam[:, 40:48], lam[:, 24:32], lam[:, 24:32],
              reads=("lam",), writes=("lam",), tag=cg + "sq2")
        kb.ve("tensor_mul", lam[:, 48:56], lam[:, 16:24], lam[:, 24:32],
              reads=("lam",), writes=("lam",), tag=cg + "sq3")
        kb.ve("tensor_sub", lam[:, 16:24], lam[:, 32:40], lam[:, 40:48],
              reads=("lam",), writes=("lam",), tag=cg + "sq4")
        kb.ve("tensor_scalar_mul", lam[:, 24:32], lam[:, 48:56], 2.0,
              reads=("lam",), writes=("lam",), tag=cg + "sq5")
    l128r = lam[:, 16:24][:, :, None]
    l128i = lam[:, 24:32][:, :, None]
    xer = hre
    xei = him
    for c in range(1, NCH):
        sA1 = sb["spA"].rearrange("p (e c) -> p e c", e=8)[:, :, 0:1]
        sB1 = sb["spB"].rearrange("p (e c) -> p e c", e=8)[:, :, 0:1]
        sA2 = sb["spA"].rearrange("p (e c) -> p e c", e=8)[:, :, 1:2]
        kb.ve("tensor_mul", sA1, xer[:, :, c - 1:c], l128r, reads=("lhr", "lam"), writes=("spA", "sC", "sD"), tag=cg + "xc1")
        kb.ve("tensor_mul", sB1, xei[:, :, c - 1:c], l128i, reads=("lhi", "lam"), writes=("spB",), tag=cg + "xc2")
        kb.ve("tensor_sub", sA1, sA1, sB1, reads=("spA", "sC", "sD", "spB"), writes=("spA", "sC", "sD"), tag=cg + "xc3")
        kb.ve("tensor_mul", sB1, xei[:, :, c - 1:c], l128r, reads=("lhi", "lam"), writes=("spB",), tag=cg + "xc5")
        kb.ve("tensor_mul", sA2, xer[:, :, c - 1:c], l128i, reads=("lhr", "lam"), writes=("spA", "sC", "sD"), tag=cg + "xc6")
        kb.ve("tensor_add", xer[:, :, c:c + 1], xer[:, :, c:c + 1], sA1,
              reads=("lhr", "spA", "sC", "sD"), writes=("lhr",), tag=cg + "xc4")
        kb.ve("tensor_add", sA2, sA2, sB1, reads=("spA", "sC", "sD", "spB"), writes=("spA", "sC", "sD"), tag=cg + "xc7")
        kb.ve("tensor_add", xei[:, :, c:c + 1], xei[:, :, c:c + 1], sA2,
              reads=("lhi", "spA", "sC", "sD"), writes=("lhi",), tag=cg + "xc8")

    for c in range(1, NCH):
        for g, im, npt, bpt in _LREG:
            xcol = sb[g][:, 0:npt * T].rearrange("p (a c j) -> p a c j", a=npt, j=128)[:, :, c, :]
            pr = lprv[:, bpt:bpt + npt, :]
            pi = lpiv[:, bpt:bpt + npt, :]
            XsR = hre[:, bpt:bpt + npt, c - 1:c].broadcast_to([128, npt, 128])
            XsI = him[:, bpt:bpt + npt, c - 1:c].broadcast_to([128, npt, 128])
            sA = sb["spA"].rearrange("p (e j) -> p e j", e=8)[:, 0:npt, :]
            sB = sb["spB"].rearrange("p (e j) -> p e j", e=8)[:, 0:npt, :]
            if not im:
                kb.ve("tensor_mul", sA, pr, XsR, reads=("lpr", "lhr"), writes=("spA", "sC", "sD"), tag=cg + "c1")
                kb.ve("tensor_mul", sB, pi, XsI, reads=("lpi", "lhi"), writes=("spB",), tag=cg + "c2")
                kb.ve("tensor_sub", sA, sA, sB, reads=("spA", "sC", "sD", "spB"), writes=("spA", "sC", "sD"), tag=cg + "c3")
            else:
                kb.ve("tensor_mul", sA, pr, XsI, reads=("lpr", "lhi"), writes=("spA", "sC", "sD"), tag=cg + "c4")
                kb.ve("tensor_mul", sB, pi, XsR, reads=("lpi", "lhr"), writes=("spB",), tag=cg + "c5")
                kb.ve("tensor_add", sA, sA, sB, reads=("spA", "sC", "sD", "spB"), writes=("spA", "sC", "sD"), tag=cg + "c6")
            kb.ve("tensor_add", xcol, xcol, sA, reads=_LKEY[g] + ("spA", "sC", "sD"),
                  writes=_LKEY[g], tag=cg + "c7")

    kb.dbg("xsc_re", sb["G5"][:, 0:4 * T], [128, 4 * T], BF16, reads=G5K)
    kb.dbg("xsc_im", sb["G7"][:, 0:4 * T], [128, 4 * T], BF16, reads=("G7",))
    for tt in range(NTT):
        ht, hk, co = _htile(kb, tt)
        kb.ve("tensor_mul", sb["spA"][:, 0:1024], ht[:, co:co + 1024],
              sb["wsV"][:, 0:1024], reads=(hk, "wsV0"), writes=("spA", "sC", "sD"), tag=cg + "du")
        kb.ve("tensor_add", xv[:, tt, :], xv[:, tt, :], sb["spA"][:, 0:1024],
              reads=(xtk(tt), "spA", "sC", "sD"), writes=(xtk(tt),), tag=cg + "dua")
    for tbg in range(NTT // 4 + (1 if NTT % 4 else 0)):
        tbs = range(tbg * 4, min(NTT, tbg * 4 + 4))
        for oh in range(2):
            for pt in range(8):
                wpar = pt % 2
                kb.dma("pool", sb["wsV"][:, wpar * 1024:wpar * 1024 + 512],
                       P["CreT"][pt, oh], writes=("wsV%d" % wpar,), tag=cg + "cr")
                kb.dma("pool", sb["wsV"][:, wpar * 1024 + 512:wpar * 1024 + 1024],
                       P["CimT"][pt, oh], writes=("wsV%d" % wpar,), tag=cg + "ci")
                kb.ve("tensor_scalar_mul", sb["wsV"][:, wpar * 1024 + 512:wpar * 1024 + 1024],
                      sb["wsV"][:, wpar * 1024 + 512:wpar * 1024 + 1024], -1.0,
                      reads=("wsV%d" % wpar,), writes=("wsV%d" % wpar,), tag=cg + "cin")
                for tb in tbs:
                    pk = "p%d" % (tb % 4)
                    rt, rk, rco = _bu(kb, 0, pt)
                    it, ik, ico = _bu(kb, 1, pt)
                    kb.mm(ps[pk][:, 0:512], rt[:, rco + tb * 128:rco + (tb + 1) * 128],
                          sb["wsV"][:, wpar * 1024:wpar * 1024 + 512], pk,
                          pt == 0, False, reads=(rk, "wsV%d" % wpar), tag=cg + "cx1")
                    kb.mm(ps[pk][:, 0:512], it[:, ico + tb * 128:ico + (tb + 1) * 128],
                          sb["wsV"][:, wpar * 1024 + 512:wpar * 1024 + 1024], pk,
                          False, pt == 7, reads=(ik, "wsV%d" % wpar), tag=cg + "cx2")
            for tb in tbs:
                pk = "p%d" % (tb % 4)
                kb.ve("tensor_add", xv[:, tb, oh * 512:(oh + 1) * 512],
                      xv[:, tb, oh * 512:(oh + 1) * 512], ps[pk][:, 0:512],
                      reads=(xtk(tb), pk), writes=(xtk(tb), pk), tag=cg + "cxe")
    kb.dbg("x_after_b", xtm[:, :], [128, NTT * 1024], BF16, reads=tuple("xt%d" % i for i in range(NTT)))


# version 13
# speedup vs baseline: 1.1470x; 1.1470x over previous
import numpy as np
import ml_dtypes
import concourse.bass as bass
import concourse.mybir as mybir
from concourse.bass_utils import run_bass_kernel_spmd
from contextlib import ExitStack

F32 = mybir.dt.float32
BF16 = mybir.dt.bfloat16
AF = mybir.ActivationFunctionType
ALU = mybir.AluOpType
AX = mybir.AxisListType

D = 1024
DI = 2048
DS = 16
DTR = 64
RK = 16
NH = 16
HD = 64
NE = 8
FF = 4096
EPS = 1e-5
NEG = -1e30



class Prog:

    def __init__(self):
        self.ops = []
        self.lastw = {}
        self.readers = {}
        self.qcount = {}
        self.alias = {}

    def op(self, eng, emit, reads=(), writes=(), q=None, sgrp=None, tag=""):
        reads = tuple(self.alias.get(r, r) for r in reads)
        writes = tuple(self.alias.get(w, w) for w in writes)
        i = len(self.ops)
        rec = dict(i=i, eng=eng, emit=emit, q=q, sgrp=sgrp, qidx=None, inc=False,
                   qwaits={}, ewaits={}, sdep=-1, tag=tag)
        if q is not None:
            if sgrp is None:
                sgrp = q + ":" + (writes[0] if writes else "misc")
            rec["sgrp"] = sgrp
            self.qcount[sgrp] = self.qcount.get(sgrp, 0) + 1
            rec["qidx"] = self.qcount[sgrp]
        deps = set()
        for r in reads:
            d = self.lastw.get(r)
            if d is not None:
                deps.add(d)
        for w in writes:
            d = self.lastw.get(w)
            if d is not None:
                deps.add(d)
            deps.update(self.readers.get(w, ()))
        deps.discard(i)
        for d in deps:
            dr = self.ops[d]
            if dr["q"] is not None:
                rec["qwaits"][dr["sgrp"]] = max(rec["qwaits"].get(dr["sgrp"], 0), 16 * dr["qidx"])
            elif dr["eng"] == eng and q is None and eng == "pe":
                pass
            elif dr["eng"] == eng:
                rec["sdep"] = max(rec["sdep"], d)
            else:
                dr["inc"] = True
                rec["ewaits"][dr["eng"]] = max(rec["ewaits"].get(dr["eng"], -1), d)
        for r in reads:
            self.readers.setdefault(r, []).append(i)
        for w in writes:
            self.lastw[w] = i
            self.readers[w] = []
        self.ops.append(rec)
        return i

    def emit_all(self, block, esem, qsem):
        incnum = {}
        counts = {}
        for rec in self.ops:
            if rec["inc"] and rec["q"] is None:
                counts[rec["eng"]] = counts.get(rec["eng"], 0) + 1
                incnum[rec["i"]] = counts[rec["eng"]]
        self.final_inc_counts = counts
        self.final_q_counts = {g: 16 * n for g, n in self.qcount.items()}

        streams = {e: [rec for rec in self.ops if rec["eng"] == e]
                   for e in ("pe", "act", "dve", "pool", "sp")}

        def run_stream(eng_name):
            def body(eng):
                wm_q = {}
                wm_e = {}
                last_drain = -1
                prev_idx = -1
                for rec in streams[eng_name]:
                    for qn, val in sorted(rec["qwaits"].items()):
                        if wm_q.get(qn, 0) < val:
                            eng.wait_ge(qsem[qn], val)
                            wm_q[qn] = val
                    for en, didx in sorted(rec["ewaits"].items()):
                        val = incnum[didx]
                        if wm_e.get(en, 0) < val:
                            eng.wait_ge(esem[en], val)
                            wm_e[en] = val
                    if rec["sdep"] > last_drain:
                        eng.drain()
                        last_drain = prev_idx
                    inst = rec["emit"](eng)
                    if rec["q"] is not None:
                        inst.then_inc(qsem[rec["sgrp"]], 16)
                    elif rec["inc"]:
                        inst.then_inc(esem[eng_name], 1)
                    prev_idx = rec["i"]
            return body

        block.tensor(run_stream("pe"))
        block.scalar(run_stream("act"))
        block.vector(run_stream("dve"))
        block.gpsimd(run_stream("pool"))
        block.sync(run_stream("sp"))

        def finale(eng):
            for qn, val in sorted(self.final_q_counts.items()):
                if val:
                    eng.wait_ge(qsem[qn], val)
            for en, val in sorted(self.final_inc_counts.items()):
                eng.wait_ge(esem[en], val)
        block.sync(finale)


class KB:

    def __init__(self, nc, T, dbg_names=()):
        self.nc = nc
        self.T = T
        self.NTT = T // 128
        self.p = Prog()
        self.st = ExitStack()
        self.sb = {}
        self.ps = {}
        self.dram_in = {}
        self.dram_out = {}
        self.dbg_names = set(dbg_names)
        self.sbuf_bytes = 0

    def din(self, name, shape, dt=F32):
        h = self.nc.dram_tensor(name, list(shape), dt, kind="ExternalInput")
        self.dram_in[name] = h.ap()
        return self.dram_in[name]

    def dout(self, name, shape, dt=F32):
        h = self.nc.dram_tensor(name, list(shape), dt, kind="ExternalOutput")
        self.dram_out[name] = h.ap()
        return self.dram_out[name]

    def t(self, name, shape, dt=BF16):
        h = self.st.enter_context(self.nc.sbuf_tensor(name, list(shape), dt))
        self.sb[name] = h
        esz = {F32: 4, BF16: 2}[dt]
        self.sbuf_bytes += int(np.prod(shape[1:])) * esz
        return h

    def pt(self, name, shape, dt=F32):
        h = self.st.enter_context(self.nc.psum_tensor(name, list(shape), dt))
        self.ps[name] = h
        return h

    def dma(self, qname, out_ap, in_ap, reads=(), writes=(), sgrp=None, tag=""):
        def emit(eng):
            return eng.dma_start(out=out_ap, in_=in_ap)
        self.p.op(qname, emit, reads=reads, writes=writes, q=qname, sgrp=sgrp, tag=tag)

    def mm(self, out_ap, lhsT_ap, rhs_ap, psum_key, start, stop, reads=(), tag=""):
        def emit(eng):
            return eng.matmul(out_ap, lhsT_ap, rhs_ap, start=start, stop=stop)
        self.p.op("pe", emit, reads=tuple(reads), writes=(psum_key,), tag=tag)

    def tr(self, out_ap, in_ap, ident_ap, psum_key, reads=(), tag=""):
        def emit(eng):
            return eng.transpose(out_ap, in_ap, ident_ap)
        self.p.op("pe", emit, reads=tuple(reads), writes=(psum_key,), tag=tag)

    def act(self, out_ap, in_ap, func, reads=(), writes=(), scale=None, bias=None,
            accum=None, tag=""):
        if func == AF.Copy and bias is not None and not isinstance(bias, float):
            func = AF.Identity
        kw = {}
        if scale is not None:
            kw["scale"] = scale
        if bias is not None:
            kw["bias"] = bias
        if accum is not None:
            kw["accum_out"] = accum

        def emit(eng):
            return eng.activation(out_ap, in_ap, func, **kw)
        self.p.op("act", emit, reads=tuple(reads), writes=tuple(writes), tag=tag)

    def ve(self, method, *args, reads=(), writes=(), eng_name="dve", tag="", **kw):
        def emit(eng):
            return getattr(eng, method)(*args, **kw)
        self.p.op(eng_name, emit, reads=tuple(reads), writes=tuple(writes), tag=tag)

    def dbg(self, name, ap, shape, dt=F32, reads=()):
        if name not in self.dbg_names:
            return
        o = self.dout("dbg_" + name, shape, dt)
        self.dma("sp", o, ap, reads=tuple(reads), sgrp="dbg_" + name, tag="dbg" + name)



def ln_stats(kb, src_fn, n_tiles, tag, skey, keys_read):
    sm = kb.sb["stats"]
    bn = kb.sb["spB"]
    K = "stats"
    for tt in range(n_tiles):
        full = src_fn(tt)
        kb.ve("bn_stats", bn[:, tt * 12:tt * 12 + 6], full[:, 0:512],
              reads=keys_read + ("spB",), writes=("spB",), tag=tag + "bs")
        kb.ve("bn_stats", bn[:, tt * 12 + 6:tt * 12 + 12], full[:, 512:1024],
              reads=keys_read + ("spB",), writes=("spB",), tag=tag + "bs")
        kb.ve("bn_aggr", sm[:, 4 * tt:4 * tt + 2], bn[:, tt * 12:tt * 12 + 12],
              reads=("spB",), writes=(K,), tag=tag + "ba")
    var_v = sm[:, :].rearrange("p (t c) -> p t c", c=4)[:, 0:n_tiles, 1]
    lnv_v = sm[:, :].rearrange("p (t c) -> p t c", c=4)[:, 0:n_tiles, 2]
    rst_v = sm[:, :].rearrange("p (t c) -> p t c", c=4)[:, 0:n_tiles, 3]
    kb.ve("tensor_scalar_add", lnv_v, var_v, EPS, reads=(K,), writes=(K,), tag=tag + "ep")
    kb.act(lnv_v, lnv_v, AF.Ln, reads=(K,), writes=(K,), tag=tag + "ln")
    kb.act(rst_v, lnv_v, AF.Exp, scale=-0.5, reads=(K,), writes=(K,), tag=tag + "ex")
    return K


def ln_mu_rstd(kb, skey, tt):
    sm = kb.sb["stats"]
    return sm[:, 4 * tt:4 * tt + 1], sm[:, 4 * tt + 3:4 * tt + 4]


def tm_to_fm(kb, src_ap_fn, dst, dst_key, n_tt, tt0_tok, ndt, tag, keys_read,
             scale=None, bias=None, dt0=0, view_d=None):
    ident = kb.sb["identb"]
    for tt in range(n_tt):
        for dt in range(ndt):
            pk = "pt%d" % ((tt * ndt + dt) % 2)
            pt = kb.ps[pk]
            kb.tr(pt[:, 0:128], src_ap_fn(tt, dt), ident[:, :], pk,
                  reads=keys_read + ("identb",), tag=tag + "tr")
            dv = dst[:, :].rearrange("p (d t) -> p d t", d=(view_d or ndt))[
                :, dt0 + dt, tt0_tok + tt * 128: tt0_tok + (tt + 1) * 128]
            kb.act(dv, pt[:, 0:128], AF.Copy, scale=scale, bias=bias,
                   reads=(pk,), writes=(dst_key, pk), tag=tag + "ev")


def stage_a(kb, P):
    T, NTT = kb.T, kb.NTT
    TcA = min(512, T)
    NCH = T // TcA
    NTC = TcA // 128
    sb = kb.sb
    ps = kb.ps
    xtm = sb["xtm"]
    G = {k: sb[k] for k in ("G1", "G2", "G3", "G4", "G5", "G6", "G7", "G8", "G9")}
    u = G["G1"]
    du = G["G2"]
    xsr = G["G7"]
    bc = G["G4"]
    ya = G["G5"]
    xnf = G["G3"]
    zb = G["G6"]
    xnt = G["G8"]

    uv = u[:, :].rearrange("p (d t) -> p d t", d=16)
    duv = du[:, :].rearrange("p (d t) -> p d t", d=16)
    xsrv = xsr[:, :].rearrange("p (d t) -> p d t", d=16)
    bcv = bc[:, :].rearrange("p (d t) -> p d t", d=16)
    yav = ya[:, :].rearrange("p (d t) -> p d t", d=16)
    xnfv = xnf[:, :].rearrange("p (d t) -> p d t", d=16)
    hsl = xnf[:, 4096:8192].rearrange("p (n t) -> p n t", n=8)
    zbv = zb[:, :].rearrange("p (d t) -> p d t", d=16)
    ball = zb[:, 0:8 * TcA].rearrange("p (n t) -> p n t", n=8)
    asl = zb[:, 4096:4096 + 2 * TcA].rearrange("p (n t) -> p n t", n=2)
    xntv = xnt[:, :].rearrange("p (t d) -> p t d", d=1024)
    dlt = lambda dt, w: xsrv[:, dt, 3:3 + w]
    yak = lambda dt: G5K[dt // 4]
    zbk = lambda ot: G6K[ot // 4]

    stk = ln_stats(kb, lambda tt: xtm[:, :].rearrange("p (t d) -> p t d", d=1024)[:, tt, :],
                   NTT, "lnA", "A", tuple("xt%d" % i for i in range(NTT)))

    for c in range(NCH):
        for tl in range(NTC):
            tt = c * NTC + tl
            mu, rs = ln_mu_rstd(kb, "A", tt)
            src = xtm[:, :].rearrange("p (t d) -> p t d", d=1024)[:, tt, :]
            kb.ve("tensor_scalar", xntv[:, tl, :], src, mu, rs, ALU.subtract, ALU.mult,
                  reads=(xtk(tt), stk), writes=("G8r%d" % tl,), tag="axn")
        tm_to_fm(kb, lambda tl, dt: xntv[:, tl, dt * 128:(dt + 1) * 128],
                 xnf, "G3a", NTC, 0, 8, "axf",
                 ("G8r0", "G8r1", "G8r2", "G8r3"), view_d=16)

        for kt in range(8):
            kb.mm(ps["p4"][0:16, 0:TcA], sb["laiT"][:, :].rearrange(
                "p (k r) -> p k r", k=8)[:, kt, :], xnfv[:, kt, 0:TcA],
                "p4", kt == 0, kt == 7, reads=("laiT", "G3a"), tag="axl")
        kb.ve("tensor_copy", sb["xl"][0:16, 0:TcA], ps["p4"][0:16, 0:TcA],
              reads=("p4",), writes=("xl", "p4"), tag="axl_ev")

        for ot in range(16):
            wbuf = "wsA%d" % (ot % 2)
            kb.dma("pool", sb[wbuf][:, :].rearrange("p (k o) -> p k o", k=8),
                   P["w_inT"][ot], writes=(wbuf,), tag="awin")
            pk = "p%d" % (ot % 2)
            for kt in range(8):
                kb.mm(ps[pk][:, 0:TcA], sb[wbuf][:, :].rearrange(
                    "p (k o) -> p k o", k=8)[:, kt, :], xnfv[:, kt, 0:TcA],
                    pk, kt == 0, False, reads=(wbuf, "G3a"), tag="axz")
            kb.mm(ps[pk][:, 0:TcA], sb["lbi_s"][0:16, ot * 128:(ot + 1) * 128],
                  sb["xl"][0:16, 0:TcA], pk, False, True,
                  reads=("lbi_s", "xl"), tag="axzl")
            kb.act(xsrv[:, ot, 3:3 + TcA], ps[pk][:, 0:TcA], AF.Copy,
                   reads=(pk,), writes=("G7", pk), tag="axz_ev")

        if c == 0:
            kb.ve("memset", xsrv[:, :, 0:3], 0.0, writes=("G7",), tag="aconv0")
        for dt in range(16):
            dacc = sb["dacc"]
            cwt = sb["cw"][:, :].rearrange("p (d k) -> p d k", d=16)
            kb.ve("tensor_scalar", dacc[:, 0:TcA], xsrv[:, dt, 0:TcA],
                  cwt[:, dt, 0:1], None, ALU.mult,
                  reads=("G7", "cw"), writes=("spA",), tag="acv")
            for k in range(1, 4):
                kb.ve("scalar_tensor_tensor", dacc[:, 0:TcA], xsrv[:, dt, k:k + TcA],
                      cwt[:, dt, k:k + 1], dacc[:, 0:TcA], ALU.mult, ALU.add,
                      reads=("G7", "cw", "spA"), writes=("spA",), tag="acv")
            kb.act(uv[:, dt, 0:TcA], dacc[:, 0:TcA], AF.Silu,
                   bias=sb["cb"][:, dt:dt + 1],
                   reads=("spA", "cb"), writes=("G1",), tag="acs")
        if c + 1 < NCH:
            kb.ve("tensor_copy", xsrv[:, :, 0:3], xsrv[:, :, TcA:TcA + 3],
                  reads=("G7",), writes=("G7",), tag="ahalo")

        for kt in range(16):
            kb.mm(ps["p4"][0:96, 0:TcA], sb["wxpT"][:, :].rearrange(
                "p (d r) -> p d r", d=16)[:, kt, :], uv[:, kt, 0:TcA],
                "p4", kt == 0, kt == 15, reads=("wxpT", "G1"), tag="axd")
        kb.ve("tensor_copy", sb["xdbl"][0:96, 0:TcA], ps["p4"][0:96, 0:TcA],
              reads=("p4",), writes=("xdbl", "p4"), tag="axd_ev")

        for dt in range(16):
            pk = "p%d" % (dt % 2)
            kb.mm(ps[pk][:, 0:TcA], sb["wdtT"][0:64, dt * 128:(dt + 1) * 128],
                  sb["xdbl"][0:64, 0:TcA], pk, True, True,
                  reads=("wdtT", "xdbl"), tag="adl")
            spA, spB = sb["spA"], sb["spB"]
            bdt1 = sb["bdt"][:, dt:dt + 1]
            kb.act(spA[:, 0:TcA], ps[pk][:, 0:TcA], AF.Abs, bias=bdt1,
                   reads=(pk, "bdt"), writes=("spA",), tag="asp1")
            kb.act(spB[:, 0:TcA], spA[:, 0:TcA], AF.Exp, scale=-1.0,
                   reads=("spA",), writes=("spB",), tag="asp2")
            kb.act(spA[:, 0:TcA], spB[:, 0:TcA], AF.Ln, bias=1.0,
                   reads=("spB",), writes=("spA",), tag="asp3")
            kb.act(spB[:, 0:TcA], ps[pk][:, 0:TcA], AF.Relu, bias=bdt1,
                   reads=(pk, "bdt"), writes=("spB", pk), tag="asp4")
            kb.ve("tensor_add", dlt(dt, TcA), spA[:, 0:TcA], spB[:, 0:TcA],
                  reads=("spA", "spB"), writes=("G7",), tag="asp5")
            kb.ve("tensor_mul", duv[:, dt, 0:TcA], dlt(dt, TcA), uv[:, dt, 0:TcA],
                  reads=("G7", "G1"), writes=("G2",), tag="adu")
            kb.ve("tensor_scalar", yav[:, dt, 0:TcA], uv[:, dt, 0:TcA],
                  sb["Dp"][:, dt:dt + 1], None, ALU.mult,
                  reads=("G1", "Dp"), writes=(yak(dt),), tag="ayi")

        if c == 0:
            kb.ve("memset", sb["hcar"][:, :], 0.0, writes=("hcar",), tag="ahc0")
        kb.dma("pool", P["scr_bc"][:, 0:TcA], sb["xdbl"][64:96, 0:TcA],
               reads=("xdbl",), writes=("scrBC",), tag="abc_st")
        pb8 = zb[:, 8 * TcA:16 * TcA].rearrange("p (n t) -> p n t", n=8)
        g6r = lambda lo, hi: tuple(G6K[r] for r in range(lo // 2048, (hi - 1) // 2048 + 1))
        ballk = g6r(0, 8 * TcA)
        pbak = g6r(8 * TcA, 12 * TcA)
        pbbk = g6r(12 * TcA, 16 * TcA)
        aslg = sb["G9"][:, 0:2 * TcA].rearrange("p (n t) -> p n t", n=2)
        pend = []
        for g in range(2):
            kb.dma("pool", bcv[:, 0:8, 0:TcA],
                   P["scr_bc"][g * 8:g * 8 + 8, 0:TcA][None, :, :].broadcast_to([128, 8, TcA]),
                   reads=("scrBC",), writes=("G4a",), tag="abc_ldB")
            kb.dma("pool", bcv[:, 8:16, 0:TcA],
                   P["scr_bc"][16 + g * 8:16 + g * 8 + 8, 0:TcA][None, :, :].broadcast_to([128, 8, TcA]),
                   reads=("scrBC",), writes=("G4b",), tag="abc_ldC")
            for dt in range(16):
                kb.ve("tensor_mul", ball[:, :, 0:TcA],
                      duv[:, dt:dt + 1, 0:TcA].broadcast_to([128, 8, TcA]),
                      bcv[:, 0:8, 0:TcA],
                      reads=("G2", "G4a"), writes=ballk, tag="asb")
                if len(pend) >= 2:
                    pend.pop(0)()
                for n in range(8):
                    nn = g * 8 + n
                    par = n % 2
                    kb.act(aslg[:, par, 0:TcA], dlt(dt, TcA), AF.Exp,
                           scale=sb["Aneg"][:, :].rearrange("p (d n) -> p d n", d=16)[:, dt, nn:nn + 1],
                           reads=("G7", "Aneg"), writes=("g9p%d" % par,), tag="asa")
                    kb.ve("tensor_tensor_scan", hsl[:, n, 0:TcA], aslg[:, par, 0:TcA],
                          ball[:, n, 0:TcA],
                          sb["hcar"][:, dt * 16 + nn:dt * 16 + nn + 1],
                          ALU.mult, ALU.add,
                          reads=("g9p%d" % par,) + ballk + ("hcar",),
                          writes=("hs%d" % n,), tag="asc")
                kb.act(sb["hcar"][:, dt * 16 + g * 8:dt * 16 + g * 8 + 8],
                       hsl[:, :, TcA - 1], AF.Copy,
                       reads=tuple("hs%d" % i for i in range(8)), writes=("hcar",), tag="ahc")
                kb.ve("tensor_mul", pb8[:, 0:4, 0:TcA], hsl[:, 0:4, 0:TcA], bcv[:, 8:12, 0:TcA],
                      reads=("hs0", "hs1", "hs2", "hs3", "G4b"),
                      writes=pbak, eng_name="pool", tag="asya")
                kb.ve("tensor_mul", pb8[:, 4:8, 0:TcA], hsl[:, 4:8, 0:TcA], bcv[:, 12:16, 0:TcA],
                      reads=("hs4", "hs5", "hs6", "hs7", "G4b"),
                      writes=pbbk, eng_name="pool", tag="asyb")
                kb.ve("tensor_add", pb8[:, 0:4, 0:TcA], pb8[:, 0:4, 0:TcA], pb8[:, 4:8, 0:TcA],
                      reads=pbak + pbbk, writes=pbak, eng_name="pool", tag="atr1")
                kb.mm(ps["p2"][0:16, 0:16], sb["onesb"][0:1, 0:16], sb["onesb"][0:1, 0:16],
                      "p2", True, True, reads=("onesb", "hs7"), tag="awarm")

                def tail(dt=dt):
                    kb.ve("tensor_add", pb8[:, 0:2, 0:TcA], pb8[:, 0:2, 0:TcA],
                          pb8[:, 2:4, 0:TcA], reads=pbak, writes=pbak, tag="atr2")
                    kb.ve("tensor_add", pb8[:, 0, 0:TcA], pb8[:, 0, 0:TcA],
                          pb8[:, 1, 0:TcA], reads=pbak, writes=pbak, tag="atr3")
                    kb.ve("tensor_add", yav[:, dt, 0:TcA], yav[:, dt, 0:TcA],
                          pb8[:, 0, 0:TcA], reads=(yak(dt),) + pbak, writes=(yak(dt),), tag="atr4")
                pend.append(tail)
        while pend:
            pend.pop(0)()

        for ot in range(16):
            wbuf = "wsA%d" % (ot % 2)
            kb.dma("pool", sb[wbuf][:, :].rearrange("p (k o) -> p k o", k=8),
                   P["w_inT"][16 + ot], writes=(wbuf,), tag="awz")
            pk = "p%d" % (ot % 2)
            for kt in range(8):
                kb.mm(ps[pk][:, 0:TcA], sb[wbuf][:, :].rearrange(
                    "p (k o) -> p k o", k=8)[:, kt, :], xnfv[:, kt, 0:TcA],
                    pk, kt == 0, False, reads=(wbuf, "G3a"), tag="az")
            kb.mm(ps[pk][:, 0:TcA], sb["lbi_s"][0:16, (16 + ot) * 128:(17 + ot) * 128],
                  sb["xl"][0:16, 0:TcA], pk, False, True,
                  reads=("lbi_s", "xl"), tag="azl")
            kb.act(zbv[:, ot, 0:TcA], ps[pk][:, 0:TcA], AF.Silu,
                   reads=(pk,), writes=(zbk(ot), pk), tag="az_ev")

        for tb in range(NTC):
            ygv = xnt[:, :].rearrange("p (d t) -> p d t", d=16)
            for dt in range(16):
                kb.ve("tensor_mul", ygv[:, dt, 0:128], yav[:, dt, tb * 128:(tb + 1) * 128],
                      zbv[:, dt, tb * 128:(tb + 1) * 128],
                      reads=(yak(dt), zbk(dt)), writes=("G8r%d" % (dt // 4),), tag="agate")
            for kt in range(16):
                kb.mm(ps["p4"][0:16, 0:128], sb["laoT"][:, :].rearrange(
                    "p (d r) -> p d r", d=16)[:, kt, :], ygv[:, kt, 0:128],
                    "p4", kt == 0, kt == 15, reads=("laoT", "G8r%d" % (kt // 4)), tag="aylo")
            kb.ve("tensor_copy", sb["ylo"][0:16, 0:128], ps["p4"][0:16, 0:128],
                  reads=("p4",), writes=("ylo", "p4"), tag="aylo_ev")
            for oh in range(2):
                pk = "p%d" % oh
                for kt in range(16):
                    wbuf = "wsA%d" % (kt % 2)
                    kb.dma("pool", sb[wbuf][:, 0:512],
                           P["w_outT"][kt, oh], writes=(wbuf,), tag="awo")
                    kb.mm(ps[pk][:, 0:512], ygv[:, kt, 0:128], sb[wbuf][:, 0:512],
                          pk, kt == 0, False, reads=("G8r%d" % (kt // 4), wbuf), tag="aop")
                kb.mm(ps[pk][:, 0:512], sb["ylo"][0:16, 0:128],
                      sb["lbo_s"][0:16, oh * 512:(oh + 1) * 512], pk, False, True,
                      reads=("ylo", "lbo_s"), tag="aopl")
                xv = xtm[:, :].rearrange("p (t d) -> p t d", d=1024)
                tt = c * NTC + tb
                kb.ve("tensor_add", xv[:, tt, oh * 512:(oh + 1) * 512],
                      xv[:, tt, oh * 512:(oh + 1) * 512], ps[pk][:, 0:512],
                      reads=(xtk(tt), pk), writes=(xtk(tt), pk), tag="aox")
    kb.dbg("x_after_a", xtm[:, :], [128, NTT * 1024], BF16, reads=tuple("xt%d" % i for i in range(NTT)))


def build_program(T, stages=("a",), dbg_names=()):
    nc = bass.Bass("TRN2", target_bir_lowering=False, debug=False)
    kb = KB(nc, T, dbg_names)
    NTT = kb.NTT
    TcA = min(512, T)

    P = {}
    for nm, shp in [
        ("x", [T, D]), ("w_inT", [32, 128, 8, 128]), ("laiT", [D, RK]), ("lbiT", [RK, 2 * DI]),
        ("mask_in", [2 * DI]), ("cw", [128, 16, 4]), ("cb", [128, 16]), ("wxpT", [DI, 96]),
        ("wdtT", [DTR, DI]), ("bdt", [128, 16]), ("Alog", [128, 16, DS]), ("Dp", [128, 16]),
        ("w_outT", [16, 2, 128, 512]), ("laoT", [DI, RK]), ("lboT", [RK, D]), ("mask_out", [D]),
        ("nu", [128, 8]), ("th", [128, 8]), ("gam", [128, 8]), ("Dlru", [1, D]),
        ("BreT", [8, 128, 8, 128]), ("BimT", [8, 128, 8, 128]),
        ("CreT", [8, 2, 128, 512]), ("CimT", [8, 2, 128, 512]),
        ("ln1g", [128, 8]), ("ln1b", [128, 8]), ("ln2g", [128, 8]), ("ln2b", [128, 8]),
        ("Wq", [8, 128, 8, 128]), ("Wk", [8, 128, 8, 128]), ("Wv", [8, 128, 8, 128]),
        ("Wp", [8, 2, 128, 512]),
        ("bq", [128, 8]), ("bk", [128, 8]), ("bv", [1, D]), ("bp", [1, D]),
        ("gw", [D, NE]), ("gb", [NE]), ("W1", [NE, 8, 128, 8, 512]), ("b1", [128, NE, 32]),
        ("W2", [NE, 4, 2, 128, 8, 512]), ("b2", [NE, D]),
    ]:
        P[nm] = kb.din("i_" + nm, shp, F32)
    for nm, shp, dt in [("identb", [128, 128], BF16), ("maskA", [128, 128], F32),
                        ("triB", [128, 128], BF16), ("triN", [128, 128], BF16),
                        ("iota8", [128, NE], F32)]:
        P[nm] = kb.din("i_" + nm, shp, dt)
    OUT = kb.dout("out", [T, D], F32)
    SCR_BC = kb.nc.dram_tensor("scr_bc", [32, min(512, T)], BF16, kind="Internal").ap()
    P["scr_bc"] = SCR_BC

    kb.t("xtm", [128, NTT * 1024], BF16)
    for g in ("G1", "G2", "G3", "G4", "G5", "G6"):
        kb.t(g, [128, 8192], BF16)
    kb.t("G7", [128, 16 * (TcA + 3)], BF16)
    kb.t("G8", [128, 4096], BF16)
    kb.t("G9", [128, 4096], BF16)
    kb.t("wsA0", [128, 8 * 128], BF16)
    kb.t("wsA1", [128, 8 * 128], BF16)
    kb.t("identb", [128, 128], BF16)
    kb.t("maskA", [128, 128], F32)
    kb.t("triB", [128, 128], BF16)
    kb.t("triN", [128, 128], BF16)
    kb.t("iota8", [128, NE], F32)
    kb.t("stats", [128, 4 * NTT], F32)
    kb.t("onesf", [1, 128], F32)
    kb.t("onesb", [1, 128], BF16)
    kb.t("onesn", [1, 128], BF16)
    kb.t("onescol", [128, 1], BF16)
    kb.t("onescf", [128, 1], F32)
    kb.t("laiT", [128, 8 * RK], BF16)
    kb.t("bsml", [128, 4096], BF16)
    kb.t("cw", [128, 16 * 4], F32)
    kb.t("cb", [128, 16], F32)
    kb.t("wxpT", [128, 16 * 96], BF16)
    kb.t("wdtT", [DTR, DI], BF16)
    kb.t("bdt", [128, 16], F32)
    kb.t("Aneg", [128, 16 * DS], F32)
    kb.t("Dp", [128, 16], F32)
    kb.t("laoT", [128, 16 * RK], BF16)
    kb.t("lbo_s", [RK, D], BF16)
    kb.t("xl", [RK, TcA], BF16)
    kb.t("ylo", [RK, 128], BF16)
    kb.t("xdbl", [128, TcA], BF16)
    kb.t("spA", [128, max(TcA, 1024)], F32)
    kb.t("spB", [128, max(TcA, 1024)], F32)
    kb.t("hcar", [128, 16 * DS], F32)
    kb.t("rows", [1, 2048], BF16)
    kb.t("wsV", [128, 8 * 256], BF16)
    kb.t("attsh", [128, 32], BF16)
    kb.t("ascr", [128, 16], F32)
    kb.t("lgt", [128, 8 * NTT], F32)
    kb.t("wfl", [128, 8 * NTT], F32)
    kb.t("e8", [128, NE], F32)
    kb.t("gwt", [128, 8 * NE], BF16)
    kb.t("b1pp", [128, NE * 32], BF16)
    kb.t("bq", [128, 8], F32)
    kb.t("bk8", [128, 8], F32)
    kb.t("ln1g", [128, 8], F32)
    kb.t("ln1b", [128, 8], F32)
    kb.t("ln2g", [128, 8], F32)
    kb.t("ln2b", [128, 8], F32)

    bs = kb.sb["bsml"]
    bsf = bs.bitcast(F32)
    kb.sb["lbi_s"] = bs[0:128, :]
    kb.sb["lpr"] = bs[:, 0:1024]
    kb.sb["lpi"] = bs[:, 1024:2048]
    kb.sb["lh"] = bsf[:, 1024:1280]
    kb.sb["lam"] = bsf[:, 1280:1344]
    kb.sb["nuv"] = bsf[:, 1344:1352]
    kb.sb["thv"] = bsf[:, 1352:1360]
    kb.sb["gamv"] = bsf[:, 1360:1368]
    kb.sb["gexp"] = bsf[:, 1368:1376]
    kb.sb["dacc"] = kb.sb["spA"][:, 512:1024]
    for k in ("lbi_s", "lpr", "lpi", "lh", "lam", "nuv", "thv", "gamv", "gexp"):
        kb.p.alias[k] = "bsml"
    kb.p.alias["dacc"] = "spA"
    for i in range(6):
        kb.pt("p%d" % i, [128, 512], F32)
    kb.pt("pt0", [128, 128], BF16)
    kb.pt("pt1", [128, 128], BF16)

    pre = [
        ("sp", "identb", P["identb"], None), ("sp", "maskA", P["maskA"], None),
        ("sp", "triB", P["triB"], None), ("sp", "triN", P["triN"], None),
        ("sp", "iota8", P["iota8"], None),
    ]
    for q, nm, src, re_ in pre:
        kb.dma(q, kb.sb[nm][:, :], src, writes=(nm,), tag="pre" + nm)
    kb.dma("pool", kb.sb["xtm"][:, :].rearrange("p (t d) -> p t d", d=1024),
           P["x"].rearrange("(t p) d -> p t d", p=128),
           writes=tuple("xt%d" % i for i in range(NTT)), sgrp="prex", tag="prex")
    kb.dma("pool", kb.sb["laiT"][:, :].rearrange("p (k r) -> p k r", k=8),
           P["laiT"].rearrange("(k p) r -> p k r", p=128), writes=("laiT",), tag="plai")
    kb.dma("pool", kb.sb["G1"][0:RK, 0:2 * DI], P["lbiT"], writes=("G1",), tag="plbi")
    kb.dma("pool", kb.sb["G2"][0:RK, 0:2 * DI], P["mask_in"][None, :].broadcast_to([RK, 2 * DI]), writes=("G2",), tag="pmin")
    kb.dma("sp", kb.sb["cw"][:, :].rearrange("p (d k) -> p d k", d=16),
           P["cw"], writes=("cw",), tag="pcw")
    kb.dma("sp", kb.sb["cb"][:, :], P["cb"], writes=("cb",), tag="pcb")
    kb.dma("pool", kb.sb["wxpT"][:, :].rearrange("p (d r) -> p d r", d=16),
           P["wxpT"].rearrange("(d p) r -> p d r", p=128), writes=("wxpT",), tag="pwxp")
    kb.dma("pool", kb.sb["wdtT"][0:DTR, :], P["wdtT"], writes=("wdtT",), tag="pwdt")
    kb.dma("sp", kb.sb["bdt"][:, :], P["bdt"], writes=("bdt",), tag="pbdt")
    kb.dma("sp", kb.sb["spA"].bitcast(F32)[:, 0:256].rearrange("p (d n) -> p d n", d=16)
           if False else kb.sb["spA"][:, 0:256].rearrange("p (d n) -> p d n", d=16),
           P["Alog"], writes=("spA",), tag="pal")
    kb.dma("sp", kb.sb["Dp"][:, :], P["Dp"], writes=("Dp",), tag="pdp")
    kb.dma("pool", kb.sb["laoT"][:, :].rearrange("p (k r) -> p k r", k=16),
           P["laoT"].rearrange("(k p) r -> p k r", p=128), writes=("laoT",), tag="plao")
    kb.dma("pool", kb.sb["G1"][0:RK, 4096:4096 + D], P["lboT"], writes=("G1",), tag="plbo")
    kb.dma("pool", kb.sb["G2"][0:RK, 4096:4096 + D], P["mask_out"][None, :].broadcast_to([RK, D]), writes=("G2",), tag="pmout")
    kb.ve("memset", kb.sb["onesf"][:, :], 1.0, writes=("onesf",), tag="ponef")
    kb.ve("memset", kb.sb["e8"][:, :], 8.0, writes=("e8",), tag="pe8")
    kb.dma("sp", kb.sb["bq"][:, :], P["bq"], writes=("bq",), tag="pbq")
    kb.dma("sp", kb.sb["bk8"][:, :], P["bk"], writes=("bk8",), tag="pbk")
    kb.ve("tensor_scalar_mul", kb.sb["bk8"][:, :], kb.sb["bk8"][:, :], 0.125,
          reads=("bk8",), writes=("bk8",), tag="pbk8")
    for nmv in ("ln1g", "ln1b", "ln2g", "ln2b"):
        kb.dma("sp", kb.sb[nmv][:, :], P[nmv], writes=(nmv,), tag="p" + nmv)
    kb.dma("pool", kb.sb["rows"][0:1, 0:D], P["bp"], writes=("rows",), sgrp="r_bp", tag="pbp")
    kb.dma("pool", kb.sb["attsh"][0:1, 24:24 + NE], P["gb"][None, :], writes=("attsh",), sgrp="r_gb", tag="pgb")
    kb.dma("pool", kb.sb["gwt"][:, :].rearrange("p (k e) -> p k e", k=8),
           P["gw"].rearrange("(k p) e -> p k e", p=128), writes=("gwt",), tag="pgw")
    kb.dma("pool", kb.sb["b1pp"][:, :].rearrange("p (e h) -> p e h", e=NE),
           P["b1"], writes=("b1pp",), tag="pb1")
    kb.ve("memset", kb.sb["onesb"][:, :], 1.0, writes=("onesb",), tag="poneb")
    kb.ve("memset", kb.sb["onesn"][:, :], -1.0, writes=("onesn",), tag="ponen")
    kb.ve("memset", kb.sb["onescol"][:, :], 1.0, writes=("onescol",), tag="ponec")
    kb.ve("memset", kb.sb["onescf"][:, :], 1.0, writes=("onescf",), tag="ponecf")
    kb.act(kb.sb["Aneg"][:, :], kb.sb["spA"][:, 0:256], AF.Exp, reads=("spA",),
           writes=("Aneg",), tag="pan1")
    kb.ve("tensor_scalar_mul", kb.sb["Aneg"][:, :], kb.sb["Aneg"][:, :], -1.0,
          reads=("Aneg",), writes=("Aneg",), tag="pan2")
    kb.ve("scalar_tensor_tensor", kb.sb["lbi_s"][0:RK, :], kb.sb["G1"][0:RK, 0:2 * DI], 2.0,
          kb.sb["G2"][0:RK, 0:2 * DI], ALU.mult, ALU.mult,
          reads=("G1", "G2"), writes=("lbi_s",), tag="plbis")
    kb.ve("scalar_tensor_tensor", kb.sb["lbo_s"][0:RK, :], kb.sb["G1"][0:RK, 4096:4096 + D], 2.0,
          kb.sb["G2"][0:RK, 4096:4096 + D], ALU.mult, ALU.mult,
          reads=("G1", "G2"), writes=("lbo_s",), tag="plbos")

    if "a" in stages:
        stage_a(kb, P)
    if "b" in stages:
        stage_ffn(kb, P, 1)
    if "c" in stages:
        stage_lru(kb, P)
    if "b2" in stages:
        stage_ffn(kb, P, 2)

    kb.dma("pool", OUT.rearrange("(t p) d -> p t d", p=128),
           kb.sb["xtm"][:, :].rearrange("p (t d) -> p t d", d=1024),
           reads=tuple("xt%d" % i for i in range(NTT)), sgrp="outx", tag="outx")

    esem = {}
    qsem = {}
    for e in ("pe", "act", "dve", "pool", "sp"):
        esem[e] = kb.st.enter_context(nc.semaphore("se_" + e))
    for g in sorted(kb.p.qcount):
        qsem[g] = kb.st.enter_context(nc.semaphore("sq_" + g))
    with nc.Block() as block:
        kb.p.emit_all(block, esem, qsem)
    kb.st.close()
    assert kb.sbuf_bytes <= 213 * 1024, f"sbuf over budget: {kb.sbuf_bytes}"
    return nc, kb


def _til8(w):
    K, N = w.shape
    return np.ascontiguousarray(w.reshape(K // 128, 128, N // 128, 128).transpose(2, 1, 0, 3))


def host_inputs(inputs, T=2048):
    g = lambda k: np.ascontiguousarray(np.asarray(inputs["L0_" + k], np.float32))
    com = {
        "w_inT": np.ascontiguousarray(g("attn_a_w_in").T.reshape(8, 128, 32, 128).transpose(2, 1, 0, 3)),
        "laiT": np.ascontiguousarray(g("attn_a_lora_A_in").T),
        "lbiT": np.ascontiguousarray(g("attn_a_lora_B_in").T),
        "mask_in": g("attn_a_mask_in"),
        "cw": np.ascontiguousarray(g("attn_a_conv_w")[:, 0, :].reshape(16, 128, 4).transpose(1, 0, 2)),
        "cb": np.ascontiguousarray(g("attn_a_conv_b").reshape(16, 128).T),
        "wxpT": np.ascontiguousarray(g("attn_a_w_xproj").T),
        "wdtT": np.ascontiguousarray(g("attn_a_w_dt").T),
        "bdt": np.ascontiguousarray(g("attn_a_b_dt").reshape(16, 128).T),
        "Alog": np.ascontiguousarray(g("attn_a_A_log").reshape(16, 128, 16).transpose(1, 0, 2)),
        "Dp": np.ascontiguousarray(g("attn_a_Dp").reshape(16, 128).T),
        "w_outT": np.ascontiguousarray(g("attn_a_w_out").T.reshape(16, 128, 2, 512).transpose(0, 2, 1, 3)),
        "laoT": np.ascontiguousarray(g("attn_a_lora_A_out").T),
        "lboT": np.ascontiguousarray(g("attn_a_lora_B_out").T),
        "mask_out": g("attn_a_mask_out"),
        "nu": np.ascontiguousarray(g("attn_b_nu_log").reshape(8, 128).T), "th": np.ascontiguousarray(g("attn_b_theta_log").reshape(8, 128).T),
        "gam": np.ascontiguousarray(g("attn_b_gamma_log").reshape(8, 128).T), "Dlru": g("attn_b_D")[None, :],
        "BreT": _til8(np.ascontiguousarray(g("attn_b_B_re").T)),
        "BimT": _til8(np.ascontiguousarray(g("attn_b_B_im").T)),
        "CreT": np.ascontiguousarray(g("attn_b_C_re").T.reshape(8, 128, 2, 512).transpose(0, 2, 1, 3)),
        "CimT": np.ascontiguousarray(g("attn_b_C_im").T.reshape(8, 128, 2, 512).transpose(0, 2, 1, 3)),
        "ln1g": np.ascontiguousarray(g("ffn_ln1_g").reshape(8, 128).T), "ln1b": np.ascontiguousarray(g("ffn_ln1_b").reshape(8, 128).T),
        "ln2g": np.ascontiguousarray(g("ffn_ln2_g").reshape(8, 128).T), "ln2b": np.ascontiguousarray(g("ffn_ln2_b").reshape(8, 128).T),
        "Wq": _til8(g("ffn_Wq")), "Wk": _til8(g("ffn_Wk")), "Wv": _til8(g("ffn_Wv")),
        "Wp": np.ascontiguousarray(g("ffn_Wp").reshape(8, 128, 2, 512).transpose(0, 2, 1, 3)),
        "bq": np.ascontiguousarray(g("ffn_bq").reshape(8, 128).T), "bk": np.ascontiguousarray(g("ffn_bk").reshape(8, 128).T), "bv": g("ffn_bv")[None, :], "bp": g("ffn_bp")[None, :],
        "gw": g("ffn_gate_W"), "gb": g("ffn_gate_b"),
        "W1": np.ascontiguousarray(g("ffn_exp_W1").reshape(NE, 8, 128, 8, 512).transpose(0, 3, 2, 1, 4)), "b1": np.ascontiguousarray(g("ffn_exp_b1").reshape(8, 32, 128).transpose(2, 0, 1)),
        "W2": np.ascontiguousarray(g("ffn_exp_W2").reshape(NE, 4, 8, 128, 2, 512).transpose(0, 1, 4, 3, 2, 5)), "b2": g("ffn_exp_b2"),
        "i_identb": np.eye(128, dtype=ml_dtypes.bfloat16),
        "i_maskA": np.where(np.tril(np.ones((128, 128), bool)), 0.0, NEG).astype(np.float32),
        "i_triB": np.where(np.triu(np.ones((128, 128), bool)), 1.0, 0.0).astype(ml_dtypes.bfloat16),
        "i_triN": np.where(np.triu(np.ones((128, 128), bool)), 0.0, -1e30).astype(ml_dtypes.bfloat16),
        "i_iota8": np.broadcast_to(np.arange(NE, dtype=np.float32), (128, NE)).copy(),
    }
    com = {("i_" + k if not k.startswith("i_") else k): v for k, v in com.items()}
    x = np.asarray(inputs["L0_attn_a_x"], np.float32)
    maps = []
    for b in range(x.shape[0]):
        m = dict(com)
        m["i_x"] = np.ascontiguousarray(x[b, :T])
        maps.append(m)
    return maps


_CACHE = {}


def kernel(**inputs):
    T = 2048
    if "prog" not in _CACHE:
        _CACHE["prog"] = build_program(T, stages=("a", "b", "c", "b2"))
    nc, kb = _CACHE["prog"]
    in_maps = host_inputs(inputs, T)
    res = run_bass_kernel_spmd(nc, in_maps, core_ids=list(range(8)))
    outs = [np.asarray(r["out"], np.float32) for r in res.results]
    full = np.stack(outs, 0)
    return full


def xtk(tt):
    return "xt%d" % tt


def _htile(kb, tt):
    g = "G1" if tt < 8 else "G2"
    return kb.sb[g], g, (tt % 8) * 1024


def _fm(kb, dt):
    g = "G3" if dt < 4 else "G4"
    k = (("G3a",), ("G3a",), ("hs0", "hs1", "hs2", "hs3"), ("hs4", "hs5", "hs6", "hs7"),
         ("G4a",), ("G4a",), ("G4b",), ("G4b",))[dt]
    return kb.sb[g], k, (dt % 4) * 2048


def stage_ffn(kb, P, ci):
    T, NTT = kb.T, kb.NTT
    sb, ps = kb.sb, kb.ps
    xtm = sb["xtm"]
    xv = xtm[:, :].rearrange("p (t d) -> p t d", d=1024)
    QCW = min(512, T)
    NQC = T // QCW
    cg = "c%d_" % ci

    kb.dma("pool", sb["G7"][0:1, 3 * T + 1024:3 * T + 2048], P["bv"], writes=("G7",),
           sgrp="r_bv", tag=cg + "pbv")
    ln_stats(kb, lambda tt: xv[:, tt, :], NTT, cg + "ln0", "F0", tuple("xt%d" % i for i in range(NTT)))
    for tt in range(NTT):
        mu, rs = ln_mu_rstd(kb, "F0", tt)
        ht, hk, co = _htile(kb, tt)
        kb.ve("tensor_scalar", ht[:, co:co + 1024], xv[:, tt, :], mu, rs,
              ALU.subtract, ALU.mult, reads=(xtk(tt), "stats"), writes=(hk,), tag=cg + "h")

    ln_stats(kb, lambda tt: _htile(kb, tt)[0][:, _htile(kb, tt)[2]:_htile(kb, tt)[2] + 1024],
             NTT, cg + "ln1", "F1", ("G1", "G2"))
    tscr = sb["G9"]
    for tt in range(NTT):
        mu, rs = ln_mu_rstd(kb, "F1", tt)
        ht, hk, co = _htile(kb, tt)
        kb.ve("tensor_scalar", tscr[:, 0:1024], ht[:, co:co + 1024], mu, rs,
              ALU.subtract, ALU.mult, reads=(hk, "stats"), writes=("g9t",), tag=cg + "n1")
        for dt in range(8):
            pk = "pt%d" % (dt % 2)
            kb.tr(ps[pk][:, 0:128], tscr[:, dt * 128:(dt + 1) * 128], sb["identb"][:, :],
                  pk, reads=("g9t", "identb"), tag=cg + "t1")
            ft, fk, fo = _fm(kb, dt)
            kb.act(ft[:, fo + tt * 128:fo + (tt + 1) * 128], ps[pk][:, 0:128], AF.Copy,
                   scale=sb["ln1g"][:, dt:dt + 1], bias=sb["ln1b"][:, dt:dt + 1],
                   reads=(pk, "ln1g", "ln1b"), writes=fk + (pk,), tag=cg + "t1e")

    rows = sb["rows"]
    for hg in range(4):
        for wname, dstk in (("Wq", "G5"), ("Wk", "G6")):
            dkeys = G5K if dstk == "G5" else G6K
            for ot2 in range(2):
                og = hg * 2 + ot2
                wbuf = "wsA%d" % (ot2 % 2)
                kb.dma("pool", sb[wbuf][:, :].rearrange("p (k o) -> p k o", k=8),
                       P[wname][og].rearrange("p k o -> p k o"), writes=(wbuf,), tag=cg + "wqk")
                for qc in range(NQC):
                    pk = "p%d" % (qc % 2)
                    for kt in range(8):
                        ft, fk, fo = _fm(kb, kt)
                        kb.mm(ps[pk][:, 0:QCW], sb[wbuf][:, :].rearrange(
                            "p (k o) -> p k o", k=8)[:, kt, :],
                            ft[:, fo + qc * QCW:fo + (qc + 1) * QCW],
                            pk, kt == 0, kt == 7, reads=(wbuf,) + fk, tag=cg + "qk")
                    for hh in range(2):
                        hl = ot2 * 2 + hh
                        if wname == "Wq":
                            sc, bi = 1.0, sb["bq"][hh * 64:hh * 64 + 64, og:og + 1]
                        else:
                            sc, bi = 0.125, sb["bk8"][hh * 64:hh * 64 + 64, og:og + 1]
                        kb.act(sb[dstk][0:64, hl * T + qc * QCW:hl * T + (qc + 1) * QCW],
                               ps[pk][hh * 64:hh * 64 + 64, 0:QCW], AF.Copy,
                               scale=sc, bias=bi,
                               reads=(pk, "bq", "bk8"), writes=dkeys + (pk,), tag=cg + "qke")
        g8v = sb["G8"][:, :].rearrange("p (t o) -> p t o", o=256)
        for _vt in range(2):
            kb.dma("pool", sb["wsV"][:, _vt * 1024:(_vt + 1) * 1024].rearrange(
                "p (k o) -> p k o", k=8),
                P["Wv"][hg * 2 + _vt].rearrange("p k o -> p k o"),
                writes=("wsV%d" % _vt,), tag=cg + "wvl")
        for tb in range(NTT):
            pk = "p%d" % (tb % 2)
            for kt in range(8):
                ft, fk, fo = _fm(kb, kt)
                for vt in range(2):
                    kb.mm(ps[pk][:, vt * 128:(vt + 1) * 128],
                          ft[:, fo + tb * 128:fo + (tb + 1) * 128],
                          sb["wsV"][:, vt * 1024 + kt * 128:vt * 1024 + (kt + 1) * 128],
                          pk, kt == 0 and vt == 0, False,
                          reads=fk + ("wsV%d" % vt,), tag=cg + "v")
            kb.mm(ps[pk][:, 0:256], sb["onesb"][0:1, :],
                  sb["G7"][0:1, 3 * T + 1024 + hg * 256:3 * T + 1024 + (hg + 1) * 256],
                  pk, False, True, reads=("onesb", "G7"), tag=cg + "vb")
            kb.ve("tensor_copy", g8v[:, tb, :], ps[pk][:, 0:256],
                  reads=(pk,), writes=("G8r%d" % (tb // 4), pk), tag=cg + "ve")

        for hl in range(4):
            h = hg * 4 + hl
            q0c = hl * T
            g9f = sb["G9"].bitcast(F32)
            kb.act(g9f[0:64, 0:T], sb["G6"][0:64, q0c:q0c + T], AF.Square,
                   reads=G6K, writes=("g9t", "g9p0", "g9p1", "g9p2", "g9p3"), tag=cg + "ksq")
            NQB = T // 512 if T >= 512 else 1
            QW = min(512, T)
            for c_ in range(T // QW):
                kb.mm(ps["p5"][0:1, 0:QW],
                      sb["onescf"][0:64, 0:1], g9f[0:64, c_ * QW:(c_ + 1) * QW],
                      "p5", True, True, reads=("onescf", "g9t"), tag=cg + "ksum")
                kb.ve("reduce_max", sb["spB"][0:1, c_:c_ + 1], ps["p5"][0:1, 0:QW],
                      axis=AX.X, reads=("p5",), writes=("spB",), tag=cg + "kmax")
            kb.ve("reduce_max", sb["spB"][0:1, 8:9], sb["spB"][0:1, 0:T // QW],
                  axis=AX.X, reads=("spB",), writes=("spB",), tag=cg + "kmax2")
            kb.act(g9f[0:64, 0:T], sb["G5"][0:64, q0c:q0c + T], AF.Square,
                   reads=G5K, writes=("g9t", "g9p0", "g9p1", "g9p2", "g9p3"), tag=cg + "qsq")
            for c_ in range(T // QW):
                kb.mm(ps["p5"][0:1, 0:QW],
                      sb["onescf"][0:64, 0:1], g9f[0:64, c_ * QW:(c_ + 1) * QW],
                      "p5", True, True, reads=("onescf", "g9t"), tag=cg + "qsum")
                kb.act(sb["G7"][0:1, 2 * T + c_ * QW:2 * T + (c_ + 1) * QW],
                       ps["p5"][0:1, 0:QW], AF.Sqrt, scale=sb["spB"][0:1, 8:9],
                       reads=("p5", "spB"), writes=("G7",), tag=cg + "qsh")

            for qc in range(NQC):
                qlo = qc * QCW
                po = "p4"
                nkb = qc * (QCW // 128) + (QCW // 128)

                def st_mm(kb_i):
                    k0 = kb_i * 128
                    q0 = max(qlo, k0)
                    w = qlo + QCW - q0
                    pk = "p%d" % (2 + kb_i % 2)
                    kb.mm(ps[pk][:, 0:w], sb["G6"][0:64, q0c + k0:q0c + k0 + 128],
                          sb["G5"][0:64, q0c + q0:q0c + qlo + QCW],
                          pk, True, False, reads=G5K + G6K, tag=cg + "pb")
                    kb.mm(ps[pk][:, 0:w], sb["onesn"][0:1, :],
                          sb["G7"][0:1, 2 * T + q0:2 * T + qlo + QCW], pk, False, q0 != k0,
                          reads=("onesn", "G7"), tag=cg + "pbs")
                    if q0 == k0:
                        kb.mm(ps[pk][:, 0:128], sb["identb"][:, :], sb["triN"][:, :],
                              pk, False, True, reads=("identb", "triN"), tag=cg + "pbm")
                    pb = "g9p%d" % (kb_i % 4)
                    pbuf = sb["G9"][:, 1024 + (kb_i % 4) * 512:1024 + (kb_i % 4) * 512 + 512]
                    kb.act(pbuf[:, 0:w], ps[pk][:, 0:w], AF.Exp,
                           reads=(pk,), writes=(pb, pk), tag=cg + "pbe")

                def pv_mm(kb_i):
                    k0 = kb_i * 128
                    q0 = max(qlo, k0)
                    w = qlo + QCW - q0
                    pb = "g9p%d" % (kb_i % 4)
                    pbuf = sb["G9"][:, 1024 + (kb_i % 4) * 512:1024 + (kb_i % 4) * 512 + 512]
                    kb.mm(ps[po][0:64, q0 - qlo:QCW], g8v[:, kb_i, hl * 64:(hl + 1) * 64],
                          pbuf[:, 0:w], po, kb_i == 0, kb_i == nkb - 1,
                          reads=("G8r%d" % (kb_i // 4), pb), tag=cg + "pv")
                    kb.mm(ps["p5"][0:1, q0 - qlo:QCW], sb["onescol"][:, 0:1],
                          pbuf[:, 0:w], "p5", kb_i == 0, kb_i == nkb - 1,
                          reads=("onescol", pb), tag=cg + "psum")

                for kb_i in range(nkb + 1):
                    if kb_i < nkb:
                        st_mm(kb_i)
                    if kb_i >= 1:
                        pv_mm(kb_i - 1)
                kb.ve("reciprocal", sb["spB"][0:1, 512:512 + QCW],
                      ps["p5"][0:1, 0:QCW], reads=("p5",), writes=("spB", "p5"), tag=cg + "prc")
                pk = "p%d" % (2 + qc % 2)
                kb.mm(ps[pk][0:64, 0:QCW], sb["onesf"][0:1, 0:64],
                      sb["spB"][0:1, 512:512 + QCW], pk, True, True,
                      reads=("onesf", "spB"), tag=cg + "pbc")
                kb.act(sb["spA"][0:64, 0:QCW], ps[pk][0:64, 0:QCW],
                       AF.Copy, reads=(pk,), writes=("spA", pk), tag=cg + "pbce")
                dtg = hl // 2
                rbase = (hl % 2) * 64
                kb.ve("tensor_mul", sb["G7"][rbase:rbase + 64, dtg * T + qlo:dtg * T + qlo + QCW],
                      ps[po][0:64, 0:QCW], sb["spA"][0:64, 0:QCW],
                      reads=(po, "spA"), writes=("G7", po), tag=cg + "oev")
        for par in range(2):
            kb.dma("pool", sb["G5"][:, par * 2048:par * 2048 + 1024].rearrange(
                "p (a o) -> p a o", a=2),
                P["Wp"][2 * hg:2 * hg + 2, par].rearrange("a p o -> p a o"),
                writes=("G5q%d" % par,), tag=cg + "wp")
            for tb in range(NTT):
                pk = "p%d" % (tb % 2)
                for dtg in range(2):
                    kb.mm(ps[pk][:, 0:512],
                          sb["G7"][0:128, dtg * T + tb * 128:dtg * T + (tb + 1) * 128],
                          sb["G5"][:, par * 2048 + dtg * 512:par * 2048 + dtg * 512 + 512],
                          pk, dtg == 0, dtg == 1 and hg != 0,
                          reads=("G7", "G5q%d" % par), tag=cg + "pj")
                if hg == 0:
                    kb.mm(ps[pk][:, 0:512], sb["onesb"][0:1, :],
                          rows[0:1, par * 512:(par + 1) * 512], pk,
                          False, True, reads=("onesb", "rows"), tag=cg + "pjb")
                ht, hk, co = _htile(kb, tb)
                kb.ve("tensor_add", ht[:, co + par * 512:co + (par + 1) * 512],
                      ht[:, co + par * 512:co + (par + 1) * 512], ps[pk][:, 0:512],
                      reads=(hk, pk), writes=(hk, pk), tag=cg + "pje")

    ln_stats(kb, lambda tt: _htile(kb, tt)[0][:, _htile(kb, tt)[2]:_htile(kb, tt)[2] + 1024],
             NTT, cg + "ln2", "F2", ("G1", "G2"))
    for tt in range(NTT):
        mu, rs = ln_mu_rstd(kb, "F2", tt)
        ht, hk, co = _htile(kb, tt)
        kb.ve("tensor_scalar", tscr[:, 0:1024], ht[:, co:co + 1024], mu, rs,
              ALU.subtract, ALU.mult, reads=(hk, "stats"), writes=("g9t",), tag=cg + "n2")
        for dt in range(8):
            pk = "pt%d" % (dt % 2)
            kb.tr(ps[pk][:, 0:128], tscr[:, dt * 128:(dt + 1) * 128], sb["identb"][:, :],
                  pk, reads=("g9t", "identb"), tag=cg + "t2")
            ft, fk, fo = _fm(kb, dt)
            kb.act(ft[:, fo + tt * 128:fo + (tt + 1) * 128], ps[pk][:, 0:128], AF.Copy,
                   scale=sb["ln2g"][:, dt:dt + 1], bias=sb["ln2b"][:, dt:dt + 1],
                   reads=(pk, "ln2g", "ln2b"), writes=fk + (pk,), tag=cg + "t2e")
        kb.ve("tensor_add", xv[:, tt, :], xv[:, tt, :], ht[:, co:co + 1024],
              reads=(xtk(tt), hk), writes=(xtk(tt),), tag=cg + "xf")

    for tt in range(NTT):
        pk = "p%d" % (tt % 2)
        for kt in range(8):
            ft, fk, fo = _fm(kb, kt)
            kb.mm(ps[pk][:, 0:NE], ft[:, fo + tt * 128:fo + (tt + 1) * 128],
                  sb["gwt"][:, :].rearrange("p (k e) -> p k e", k=8)[:, kt, :],
                  pk, kt == 0, False, reads=("gwt",) + fk, tag=cg + "g")
        kb.mm(ps[pk][:, 0:NE], sb["onesb"][0:1, :], sb["attsh"][0:1, 24:24 + NE],
              pk, False, True, reads=("onesb", "attsh"), tag=cg + "gb")
        kb.ve("tensor_copy", sb["lgt"][:, tt * NE:(tt + 1) * NE], ps[pk][:, 0:NE],
              reads=(pk,), writes=("lgt", pk), tag=cg + "ge")
    kb.ve("tensor_scalar_add", sb["ascr"][:, 8:16], sb["iota8"][:, :], -8.0,
          reads=("iota8",), writes=("ascr",), tag=cg + "iom8")
    for tt in range(NTT):
        g_ = sb["spA"][:, 960:1008]
        lg = sb["lgt"][:, tt * NE:(tt + 1) * NE]
        wf = sb["wfl"][:, tt * NE:(tt + 1) * NE]
        kb.ve("reduce_max", g_[:, 0:1], lg, axis=AX.X, reads=("lgt",), writes=("spA",), tag=cg + "m1")
        kb.ve("tensor_scalar", g_[:, 8:16], lg, g_[:, 0:1], None, ALU.is_equal,
              reads=("lgt", "spA"), writes=("spA",), tag=cg + "eq1")
        kb.ve("scalar_tensor_tensor", g_[:, 16:24], g_[:, 8:16], 1.0, sb["ascr"][:, 8:16],
              ALU.mult, ALU.mult, reads=("spA", "ascr"), writes=("spA",), tag=cg + "sel1a")
        kb.ve("tensor_scalar_add", g_[:, 16:24], g_[:, 16:24], 8.0,
              reads=("spA",), writes=("spA",), tag=cg + "sel1b")
        kb.ve("tensor_reduce", g_[:, 1:2], g_[:, 16:24], AX.X, ALU.min,
              reads=("spA",), writes=("spA",), tag=cg + "i1")
        kb.ve("tensor_scalar", g_[:, 8:16], sb["iota8"][:, :], g_[:, 1:2], None, ALU.is_equal,
              reads=("spA", "iota8"), writes=("spA",), tag=cg + "es1")
        kb.ve("scalar_tensor_tensor", g_[:, 24:32], g_[:, 8:16], NEG, lg,
              ALU.mult, ALU.add, reads=("spA", "lgt"), writes=("spA",), tag=cg + "lg2")
        kb.ve("reduce_max", g_[:, 2:3], g_[:, 24:32], axis=AX.X,
              reads=("spA",), writes=("spA",), tag=cg + "m2")
        kb.ve("tensor_scalar", g_[:, 32:40], g_[:, 24:32], g_[:, 2:3], None, ALU.is_equal,
              reads=("spA",), writes=("spA",), tag=cg + "eq2")
        kb.ve("scalar_tensor_tensor", g_[:, 16:24], g_[:, 32:40], 1.0, sb["ascr"][:, 8:16],
              ALU.mult, ALU.mult, reads=("spA", "ascr"), writes=("spA",), tag=cg + "sel2a")
        kb.ve("tensor_scalar_add", g_[:, 16:24], g_[:, 16:24], 8.0,
              reads=("spA",), writes=("spA",), tag=cg + "sel2b")
        kb.ve("tensor_reduce", g_[:, 3:4], g_[:, 16:24], AX.X, ALU.min,
              reads=("spA",), writes=("spA",), tag=cg + "i2")
        kb.ve("tensor_scalar", g_[:, 32:40], sb["iota8"][:, :], g_[:, 3:4], None, ALU.is_equal,
              reads=("spA", "iota8"), writes=("spA",), tag=cg + "es2")
        kb.ve("tensor_sub", g_[:, 4:5], g_[:, 0:1], g_[:, 2:3],
              reads=("spA",), writes=("spA",), tag=cg + "d12")
        kb.act(g_[:, 5:6], g_[:, 4:5], AF.Sigmoid, reads=("spA",), writes=("spA",), tag=cg + "w1")
        kb.act(g_[:, 6:7], g_[:, 4:5], AF.Sigmoid, scale=-1.0,
               reads=("spA",), writes=("spA",), tag=cg + "w2")
        kb.ve("tensor_scalar", g_[:, 40:48], g_[:, 8:16], g_[:, 5:6], None, ALU.mult,
              reads=("spA",), writes=("spA",), tag=cg + "wa")
        kb.ve("scalar_tensor_tensor", wf, g_[:, 32:40], g_[:, 6:7], g_[:, 40:48],
              ALU.mult, ALU.add, reads=("spA",), writes=("wfl",), tag=cg + "wb")

    TCW = min(512, T)
    NTCH = T // TCW
    midv1 = sb["G1"][:, :].rearrange("p (h t) -> p h t", h=16)
    midv2 = sb["G2"][:, :].rearrange("p (h t) -> p h t", h=16)
    for e in range(NE):
        kb.dma("pool", sb["G7"][0:1, 3 * T:3 * T + 1024], P["b2"][e:e + 1, :],
               writes=("G7",), sgrp="r_b2", tag=cg + "b2l")
        for tc in range(NTCH):
            for o8 in range(8):
                woff = (o8 % 2) * 4096
                kb.dma("pool", sb["G5"][:, woff:woff + 4096].rearrange(
                    "p (k h) -> p k h", k=8),
                    P["W1"][e, o8], writes=G5K[2 * (o8 % 2):2 * (o8 % 2) + 2],
                    sgrp="w1l%d" % (o8 % 2), tag=cg + "w1l")
                for htl in range(4):
                    ht = o8 * 4 + htl
                    pk = "p%d" % (4 + ht % 2)
                    for kt in range(8):
                        ft, fk, fo = _fm(kb, kt)
                        kb.mm(ps[pk][:, 0:TCW], sb["G5"][:, woff + kt * 512 + htl * 128:
                                                         woff + kt * 512 + (htl + 1) * 128],
                              ft[:, fo + tc * TCW:fo + (tc + 1) * TCW], pk,
                              kt == 0, kt == 7,
                              reads=G5K[2 * (o8 % 2):2 * (o8 % 2) + 2] + fk, tag=cg + "m1")
                    mv = midv1 if ht < 16 else midv2
                    mk = "G1" if ht < 16 else "G2"
                    kb.act(mv[:, ht % 16, 0:TCW], ps[pk][:, 0:TCW], AF.Gelu,
                           bias=sb["b1pp"][:, :].rearrange("p (e h) -> p e h", e=NE)[:, e, ht:ht + 1],
                           reads=(pk, "b1pp"), writes=(mk, pk), tag=cg + "m1e")
            for oh in range(2):
                for ktg in range(4):
                    woff = (ktg % 2) * 4096
                    kb.dma("pool", sb["G6"][:, woff:woff + 4096].rearrange(
                        "p (k o) -> p k o", k=8),
                        P["W2"][e, ktg, oh], writes=G6K[2 * (ktg % 2):2 * (ktg % 2) + 2],
                        sgrp="w2l%d" % (ktg % 2), tag=cg + "w2l")
                    for kt in range(8):
                        htg = ktg * 8 + kt
                        mv = midv1 if htg < 16 else midv2
                        mk = "G1" if htg < 16 else "G2"
                        for tb in range(TCW // 128):
                            pk = "p%d" % (tb % 4)
                            kb.mm(ps[pk][:, 0:512],
                                  mv[:, htg % 16, tb * 128:(tb + 1) * 128],
                                  sb["G6"][:, woff + kt * 512:woff + (kt + 1) * 512],
                                  pk, htg == 0, False,
                                  reads=G6K[2 * (ktg % 2):2 * (ktg % 2) + 2] + (mk,), tag=cg + "m2")
                for tb in range(TCW // 128):
                    pk = "p%d" % (tb % 4)
                    kb.mm(ps[pk][:, 0:512], sb["onesb"][0:1, :],
                          sb["G7"][0:1, 3 * T + oh * 512:3 * T + (oh + 1) * 512], pk,
                          False, True, reads=("onesb", "G7"), tag=cg + "m2b")
                    tt = tc * (TCW // 128) + tb
                    kb.ve("scalar_tensor_tensor", xv[:, tt, oh * 512:(oh + 1) * 512],
                          ps[pk][:, 0:512], sb["wfl"][:, tt * NE + e:tt * NE + e + 1],
                          xv[:, tt, oh * 512:(oh + 1) * 512], ALU.mult, ALU.add,
                          reads=(pk, "wfl", xtk(tt)), writes=(xtk(tt), pk), tag=cg + "m2x")
    if ci == 1:
        kb.dbg("x_after_f1", xtm[:, :], [128, NTT * 1024], BF16, reads=tuple("xt%d" % i for i in range(NTT)))


def _bu(kb, im, pt):
    T = kb.T
    if not im:
        gk = G5K if pt < 4 else G6K
        g = "G5" if pt < 4 else "G6"
        reg = ((pt % 4) * T) // 2048
        return kb.sb[g], gk[reg], (pt % 4) * T
    if pt < 4:
        return kb.sb["G7"], "G7", pt * T
    g = "G8" if pt < 6 else "G9"
    return kb.sb[g], g, (pt % 2) * T


G5K = ("G5q0", "G5q1", "G5q2", "G5q3")
G6K = ("G6q0", "G6q1", "G6q2", "G6q3")

_LREG = [("G5", 0, 4, 0), ("G6", 0, 4, 4), ("G7", 1, 4, 0), ("G8", 1, 2, 4), ("G9", 1, 2, 6)]
_LKEY = {"G5": G5K, "G6": G6K, "G7": ("G7",), "G8": ("G8",), "G9": ("G9",)}


def stage_lru(kb, P):
    T, NTT = kb.T, kb.NTT
    sb, ps = kb.sb, kb.ps
    xtm = sb["xtm"]
    xv = xtm[:, :].rearrange("p (t d) -> p t d", d=1024)
    NCH = T // 128
    lam = sb["lam"]
    cg = "l_"

    kb.dma("sp", sb["nuv"][:, :], P["nu"], writes=("nuv",), sgrp="c_nu", tag=cg + "pnu")
    kb.dma("sp", sb["thv"][:, :], P["th"], writes=("thv",), sgrp="c_th", tag=cg + "pth")
    kb.dma("sp", sb["gamv"][:, :], P["gam"], writes=("gamv",), sgrp="c_gam", tag=cg + "pgam")
    kb.act(lam[:, 16:24], sb["nuv"][:, :], AF.Exp, reads=("nuv",), writes=("lam",), tag=cg + "e1")
    kb.act(lam[:, 24:32], lam[:, 16:24], AF.Exp, scale=-1.0,
           reads=("lam",), writes=("lam",), tag=cg + "r")
    kb.act(lam[:, 32:40], sb["thv"][:, :], AF.Exp, reads=("thv",), writes=("lam",), tag=cg + "th")
    kb.act(lam[:, 40:48], lam[:, 32:40], AF.Sin, scale=0.5,
           reads=("lam",), writes=("lam",), tag=cg + "s2")
    kb.act(lam[:, 48:56], lam[:, 32:40], AF.Sin, scale=0.25,
           reads=("lam",), writes=("lam",), tag=cg + "s4")
    kb.act(lam[:, 56:64], lam[:, 40:48], AF.Square, reads=("lam",), writes=("lam",), tag=cg + "q2")
    kb.ve("tensor_scalar", lam[:, 56:64], lam[:, 56:64], -2.0, 1.0, ALU.mult, ALU.add,
          reads=("lam",), writes=("lam",), tag=cg + "cos")
    kb.act(lam[:, 48:56], lam[:, 48:56], AF.Square, reads=("lam",), writes=("lam",), tag=cg + "q4")
    kb.ve("tensor_scalar", lam[:, 48:56], lam[:, 48:56], -2.0, 1.0, ALU.mult, ALU.add,
          reads=("lam",), writes=("lam",), tag=cg + "c2")
    kb.ve("tensor_mul", lam[:, 40:48], lam[:, 40:48], lam[:, 48:56],
          reads=("lam",), writes=("lam",), tag=cg + "sc")
    kb.ve("tensor_scalar_mul", lam[:, 40:48], lam[:, 40:48], 2.0,
          reads=("lam",), writes=("lam",), tag=cg + "sin")
    kb.ve("tensor_mul", lam[:, 0:8], lam[:, 24:32], lam[:, 56:64],
          reads=("lam",), writes=("lam",), tag=cg + "lr")
    kb.ve("tensor_mul", lam[:, 8:16], lam[:, 24:32], lam[:, 40:48],
          reads=("lam",), writes=("lam",), tag=cg + "li")
    kb.act(sb["gexp"][:, :], sb["gamv"][:, :], AF.Exp, reads=("gamv",), writes=("gexp",), tag=cg + "g")
    kb.dma("pool", kb.sb["rows"][0:1, 1024:2048], P["Dlru"], writes=("rows",), sgrp="r_dl", tag=cg + "dl")
    kb.mm(ps["p5"][:, 0:512], sb["onesb"][0:1, :], sb["rows"][0:1, 1024:1536], "p5",
          True, True, reads=("onesb", "rows"), tag=cg + "db1")
    kb.act(sb["wsV"][:, 0:512], ps["p5"][:, 0:512], AF.Copy, reads=("p5",),
           writes=("wsV0", "p5"), tag=cg + "db1e")
    kb.mm(ps["p5"][:, 0:512], sb["onesb"][0:1, :], sb["rows"][0:1, 1536:2048], "p5",
          True, True, reads=("onesb", "rows"), tag=cg + "db2")
    kb.act(sb["wsV"][:, 512:1024], ps["p5"][:, 0:512], AF.Copy, reads=("p5",),
           writes=("wsV0", "p5"), tag=cg + "db2e")

    lprv = sb["lpr"].rearrange("p (e j) -> p e j", e=8)
    lpiv = sb["lpi"].rearrange("p (e j) -> p e j", e=8)
    kb.ve("tensor_copy", lprv[:, :, 0:1], lam[:, 0:8][:, :, None],
          reads=("lam",), writes=("lpr",), tag=cg + "p0r")
    kb.ve("tensor_copy", lpiv[:, :, 0:1], lam[:, 8:16][:, :, None],
          reads=("lam",), writes=("lpi",), tag=cg + "p0i")
    w = 1
    while w < 128:
        m = min(w, 128 - w)
        ar = lprv[:, :, w - 1:w].broadcast_to([128, 8, m])
        ai = lpiv[:, :, w - 1:w].broadcast_to([128, 8, m])
        sA = sb["spA"].rearrange("p (e j) -> p e j", e=8)[:, :, 0:m]
        sB = sb["spB"].rearrange("p (e j) -> p e j", e=8)[:, :, 0:m]
        kb.ve("tensor_mul", sA, lprv[:, :, 0:m], ar, reads=("lpr",), writes=("spA", "sC", "sD"), tag=cg + "pd1")
        kb.ve("tensor_mul", sB, lpiv[:, :, 0:m], ai, reads=("lpr", "lpi"), writes=("spB",), tag=cg + "pd2")
        kb.ve("tensor_sub", sA, sA, sB, reads=("spA", "sC", "sD", "spB"), writes=("spA", "sC", "sD"), tag=cg + "pd3")
        kb.ve("tensor_mul", sB, lprv[:, :, 0:m], ai, reads=("lpr", "lpi"), writes=("spB",), tag=cg + "pd4")
        kb.ve("tensor_copy", lprv[:, :, w:w + m], sA, reads=("spA", "sC", "sD"), writes=("lpr",), tag=cg + "pd5")
        kb.ve("tensor_mul", sA, lpiv[:, :, 0:m], ar, reads=("lpi",), writes=("spA", "sC", "sD"), tag=cg + "pd6")
        kb.ve("tensor_add", lpiv[:, :, w:w + m], sA, sB, reads=("spA", "sC", "sD", "spB"), writes=("lpi",), tag=cg + "pd7")
        w *= 2

    ln_stats(kb, lambda tt: xv[:, tt, :], NTT, cg + "ln", "L", tuple("xt%d" % i for i in range(NTT)))
    for tt in range(NTT):
        mu, rs = ln_mu_rstd(kb, "L", tt)
        ht, hk, co = _htile(kb, tt)
        kb.ve("tensor_scalar", ht[:, co:co + 1024], xv[:, tt, :], mu, rs,
              ALU.subtract, ALU.mult, reads=(xtk(tt), "stats"), writes=(hk,), tag=cg + "n")
        for dt in range(8):
            pk = "pt%d" % (dt % 2)
            kb.tr(ps[pk][:, 0:128], ht[:, co + dt * 128:co + (dt + 1) * 128],
                  sb["identb"][:, :], pk, reads=(hk, "identb"), tag=cg + "t")
            ft, fk, fo = _fm(kb, dt)
            kb.act(ft[:, fo + tt * 128:fo + (tt + 1) * 128], ps[pk][:, 0:128], AF.Copy,
                   reads=(pk,), writes=fk + (pk,), tag=cg + "te")

    for pt in range(8):
        for im in range(2):
            wsb = "wsA%d" % im
            src = P["BimT"] if im else P["BreT"]
            kb.dma("pool", sb[wsb][:, :].rearrange("p (k o) -> p k o", k=8),
                   src[pt], writes=(wsb,), tag=cg + "bw")
            bt, bk_, bco = _bu(kb, im, pt)
            BW = min(512, T)
            for tc in range(T // BW):
                pk = "p%d" % (2 * im + tc % 2)
                for kt in range(8):
                    ft, fk, fo = _fm(kb, kt)
                    kb.mm(ps[pk][:, 0:BW], sb[wsb][:, :].rearrange(
                        "p (k o) -> p k o", k=8)[:, kt, :],
                        ft[:, fo + tc * BW:fo + (tc + 1) * BW], pk, kt == 0, kt == 7,
                        reads=(wsb,) + fk, tag=cg + "bu")
                kb.act(bt[:, bco + tc * BW:bco + (tc + 1) * BW], ps[pk][:, 0:BW],
                       AF.Copy, scale=sb["gexp"][:, pt:pt + 1],
                       reads=(pk, "gexp"), writes=(bk_, pk), tag=cg + "bue")

    kb.dbg("bu_re", sb["G5"][:, 0:4 * T], [128, 4 * T], BF16, reads=G5K)
    kb.dbg("bu_im", sb["G7"][:, 0:4 * T], [128, 4 * T], BF16, reads=("G7",))
    lh = sb["lh"]
    kb.ve("memset", lh[:, :], 0.0, writes=("lhr", "lhi"), tag=cg + "h0")
    hre = lh[:, 0:128].rearrange("p (e c) -> p e c", e=8)[:, :, 0:NCH]
    him = lh[:, 128:256].rearrange("p (e c) -> p e c", e=8)[:, :, 0:NCH]
    lrv = lam[:, 0:8][:, :, None].broadcast_to([128, 8, NCH])
    liv = lam[:, 8:16][:, :, None].broadcast_to([128, 8, NCH])
    sA8 = sb["spA"][:, 0:128].rearrange("p (e c) -> p e c", e=8)[:, :, 0:NCH]
    sB8 = sb["spB"][:, 0:128].rearrange("p (e c) -> p e c", e=8)[:, :, 0:NCH]
    sC8 = sb["spA"][:, 128:256].rearrange("p (e c) -> p e c", e=8)[:, :, 0:NCH]

    def rview(g, npt, bpt, j):
        return sb[g][:, :].rearrange("p (a c j) -> p a c j", a=8192 // 128 // 8 if False else None)

    def bu_col(g, npt, j):
        return sb[g][:, 0:npt * T].rearrange("p (a c j) -> p a c j", a=npt, j=128)[:, :, 0:NCH, j]

    sD8 = sb["dacc"][:, 128:256].rearrange("p (e c) -> p e c", e=8)[:, :, 0:NCH]
    for j in range(128):
        kb.ve("tensor_mul", sA8, hre, lrv, reads=("lhr", "lam"), writes=("spA",), tag=cg + "a1")
        kb.ve("tensor_mul", sC8, him, liv, reads=("lhi", "lam"), writes=("sC",), tag=cg + "a2")
        kb.ve("tensor_sub", sA8, sA8, sC8, reads=("spA", "sC"), writes=("spA",), tag=cg + "a3")
        kb.ve("tensor_mul", sB8, him, lrv, reads=("lhi", "lam"), writes=("spB",),
              eng_name="pool", tag=cg + "b1")
        kb.ve("tensor_mul", sD8, hre, liv, reads=("lhr", "lam"), writes=("sD",),
              eng_name="pool", tag=cg + "b2")
        kb.ve("tensor_add", sB8, sB8, sD8, reads=("spB", "sD"), writes=("spB",),
              eng_name="pool", tag=cg + "b3")
        for g, im, npt, bpt in _LREG:
            hv = (him if im else hre)[:, bpt:bpt + npt, :]
            tv = (sB8 if im else sA8)[:, bpt:bpt + npt, :]
            bj = bu_col(g, npt, j)
            kb.ve("tensor_add", hv, tv, bj, reads=("spB" if im else "spA",) + _LKEY[g],
                  writes=("lhi" if im else "lhr",),
                  eng_name="pool" if im else "dve", tag=cg + "hu")
            kb.act(bj, hv, AF.Copy, reads=("lhi" if im else "lhr",),
                   writes=_LKEY[g], tag=cg + "xw")
        if j % 4 == 0:
            kb.mm(ps["p5"][0:16, 0:16], sb["onesb"][0:1, 0:16], sb["onesb"][0:1, 0:16],
                  "p5", True, True, reads=("onesb", "lhr"), tag=cg + "warm")

    xer = hre
    xei = him
    kb.ve("tensor_copy", lam[:, 16:24], lam[:, 0:8], reads=("lam",), writes=("lam",), tag=cg + "q0")
    kb.ve("tensor_copy", lam[:, 24:32], lam[:, 8:16], reads=("lam",), writes=("lam",), tag=cg + "q0i")
    for _sq in range(7):
        kb.ve("tensor_mul", lam[:, 32:40], lam[:, 16:24], lam[:, 16:24],
              reads=("lam",), writes=("lam",), tag=cg + "sq1")
        kb.ve("tensor_mul", lam[:, 40:48], lam[:, 24:32], lam[:, 24:32],
              reads=("lam",), writes=("lam",), tag=cg + "sq2")
        kb.ve("tensor_mul", lam[:, 48:56], lam[:, 16:24], lam[:, 24:32],
              reads=("lam",), writes=("lam",), tag=cg + "sq3")
        kb.ve("tensor_sub", lam[:, 16:24], lam[:, 32:40], lam[:, 40:48],
              reads=("lam",), writes=("lam",), tag=cg + "sq4")
        kb.ve("tensor_scalar_mul", lam[:, 24:32], lam[:, 48:56], 2.0,
              reads=("lam",), writes=("lam",), tag=cg + "sq5")
    l128r = lam[:, 16:24][:, :, None]
    l128i = lam[:, 24:32][:, :, None]
    xer = hre
    xei = him
    for c in range(1, NCH):
        sA1 = sb["spA"].rearrange("p (e c) -> p e c", e=8)[:, :, 0:1]
        sB1 = sb["spB"].rearrange("p (e c) -> p e c", e=8)[:, :, 0:1]
        sA2 = sb["spA"].rearrange("p (e c) -> p e c", e=8)[:, :, 1:2]
        kb.ve("tensor_mul", sA1, xer[:, :, c - 1:c], l128r, reads=("lhr", "lam"), writes=("spA", "sC", "sD"), tag=cg + "xc1")
        kb.ve("tensor_mul", sB1, xei[:, :, c - 1:c], l128i, reads=("lhi", "lam"), writes=("spB",), tag=cg + "xc2")
        kb.ve("tensor_sub", sA1, sA1, sB1, reads=("spA", "sC", "sD", "spB"), writes=("spA", "sC", "sD"), tag=cg + "xc3")
        kb.ve("tensor_mul", sB1, xei[:, :, c - 1:c], l128r, reads=("lhi", "lam"), writes=("spB",), tag=cg + "xc5")
        kb.ve("tensor_mul", sA2, xer[:, :, c - 1:c], l128i, reads=("lhr", "lam"), writes=("spA", "sC", "sD"), tag=cg + "xc6")
        kb.ve("tensor_add", xer[:, :, c:c + 1], xer[:, :, c:c + 1], sA1,
              reads=("lhr", "spA", "sC", "sD"), writes=("lhr",), tag=cg + "xc4")
        kb.ve("tensor_add", sA2, sA2, sB1, reads=("spA", "sC", "sD", "spB"), writes=("spA", "sC", "sD"), tag=cg + "xc7")
        kb.ve("tensor_add", xei[:, :, c:c + 1], xei[:, :, c:c + 1], sA2,
              reads=("lhi", "spA", "sC", "sD"), writes=("lhi",), tag=cg + "xc8")

    for c in range(1, NCH):
        for g, im, npt, bpt in _LREG:
            xcol = sb[g][:, 0:npt * T].rearrange("p (a c j) -> p a c j", a=npt, j=128)[:, :, c, :]
            pr = lprv[:, bpt:bpt + npt, :]
            pi = lpiv[:, bpt:bpt + npt, :]
            XsR = hre[:, bpt:bpt + npt, c - 1:c].broadcast_to([128, npt, 128])
            XsI = him[:, bpt:bpt + npt, c - 1:c].broadcast_to([128, npt, 128])
            sA = sb["spA"].rearrange("p (e j) -> p e j", e=8)[:, 0:npt, :]
            sB = sb["spB"].rearrange("p (e j) -> p e j", e=8)[:, 0:npt, :]
            if not im:
                kb.ve("tensor_mul", sA, pr, XsR, reads=("lpr", "lhr"), writes=("spA", "sC", "sD"), tag=cg + "c1")
                kb.ve("tensor_mul", sB, pi, XsI, reads=("lpi", "lhi"), writes=("spB",), tag=cg + "c2")
                kb.ve("tensor_sub", sA, sA, sB, reads=("spA", "sC", "sD", "spB"), writes=("spA", "sC", "sD"), tag=cg + "c3")
            else:
                kb.ve("tensor_mul", sA, pr, XsI, reads=("lpr", "lhi"), writes=("spA", "sC", "sD"), tag=cg + "c4")
                kb.ve("tensor_mul", sB, pi, XsR, reads=("lpi", "lhr"), writes=("spB",), tag=cg + "c5")
                kb.ve("tensor_add", sA, sA, sB, reads=("spA", "sC", "sD", "spB"), writes=("spA", "sC", "sD"), tag=cg + "c6")
            kb.ve("tensor_add", xcol, xcol, sA, reads=_LKEY[g] + ("spA", "sC", "sD"),
                  writes=_LKEY[g], tag=cg + "c7")

    kb.dbg("xsc_re", sb["G5"][:, 0:4 * T], [128, 4 * T], BF16, reads=G5K)
    kb.dbg("xsc_im", sb["G7"][:, 0:4 * T], [128, 4 * T], BF16, reads=("G7",))
    for tt in range(NTT):
        ht, hk, co = _htile(kb, tt)
        kb.ve("tensor_mul", sb["spA"][:, 0:1024], ht[:, co:co + 1024],
              sb["wsV"][:, 0:1024], reads=(hk, "wsV0"), writes=("spA", "sC", "sD"), tag=cg + "du")
        kb.ve("tensor_add", xv[:, tt, :], xv[:, tt, :], sb["spA"][:, 0:1024],
              reads=(xtk(tt), "spA", "sC", "sD"), writes=(xtk(tt),), tag=cg + "dua")
    for tbg in range(NTT // 4 + (1 if NTT % 4 else 0)):
        tbs = range(tbg * 4, min(NTT, tbg * 4 + 4))
        for oh in range(2):
            for pt in range(8):
                wpar = pt % 2
                kb.dma("pool", sb["wsV"][:, wpar * 1024:wpar * 1024 + 512],
                       P["CreT"][pt, oh], writes=("wsV%d" % wpar,), tag=cg + "cr")
                kb.dma("pool", sb["wsV"][:, wpar * 1024 + 512:wpar * 1024 + 1024],
                       P["CimT"][pt, oh], writes=("wsV%d" % wpar,), tag=cg + "ci")
                kb.ve("tensor_scalar_mul", sb["wsV"][:, wpar * 1024 + 512:wpar * 1024 + 1024],
                      sb["wsV"][:, wpar * 1024 + 512:wpar * 1024 + 1024], -1.0,
                      reads=("wsV%d" % wpar,), writes=("wsV%d" % wpar,), tag=cg + "cin")
                for tb in tbs:
                    pk = "p%d" % (tb % 4)
                    rt, rk, rco = _bu(kb, 0, pt)
                    it, ik, ico = _bu(kb, 1, pt)
                    kb.mm(ps[pk][:, 0:512], rt[:, rco + tb * 128:rco + (tb + 1) * 128],
                          sb["wsV"][:, wpar * 1024:wpar * 1024 + 512], pk,
                          pt == 0, False, reads=(rk, "wsV%d" % wpar), tag=cg + "cx1")
                    kb.mm(ps[pk][:, 0:512], it[:, ico + tb * 128:ico + (tb + 1) * 128],
                          sb["wsV"][:, wpar * 1024 + 512:wpar * 1024 + 1024], pk,
                          False, pt == 7, reads=(ik, "wsV%d" % wpar), tag=cg + "cx2")
            for tb in tbs:
                pk = "p%d" % (tb % 4)
                kb.ve("tensor_add", xv[:, tb, oh * 512:(oh + 1) * 512],
                      xv[:, tb, oh * 512:(oh + 1) * 512], ps[pk][:, 0:512],
                      reads=(xtk(tb), pk), writes=(xtk(tb), pk), tag=cg + "cxe")
    kb.dbg("x_after_b", xtm[:, :], [128, NTT * 1024], BF16, reads=tuple("xt%d" % i for i in range(NTT)))


# version 19
# speedup vs baseline: 1.1997x; 1.0459x over previous
import numpy as np
import ml_dtypes
import concourse.bass as bass
import concourse.mybir as mybir
from concourse.bass_utils import run_bass_kernel_spmd
from contextlib import ExitStack

F32 = mybir.dt.float32
BF16 = mybir.dt.bfloat16
AF = mybir.ActivationFunctionType
ALU = mybir.AluOpType
AX = mybir.AxisListType

D = 1024
DI = 2048
DS = 16
DTR = 64
RK = 16
NH = 16
HD = 64
NE = 8
FF = 4096
EPS = 1e-5
NEG = -1e30



class Prog:

    def __init__(self):
        self.ops = []
        self.lastw = {}
        self.readers = {}
        self.qcount = {}
        self.alias = {}

    def op(self, eng, emit, reads=(), writes=(), q=None, sgrp=None, tag=""):
        reads = tuple(self.alias.get(r, r) for r in reads)
        writes = tuple(self.alias.get(w, w) for w in writes)
        i = len(self.ops)
        rec = dict(i=i, eng=eng, emit=emit, q=q, sgrp=sgrp, qidx=None, inc=False,
                   qwaits={}, ewaits={}, sdep=-1, tag=tag)
        if q is not None:
            if sgrp is None:
                sgrp = q + ":" + (writes[0] if writes else "misc")
            rec["sgrp"] = sgrp
            self.qcount[sgrp] = self.qcount.get(sgrp, 0) + 1
            rec["qidx"] = self.qcount[sgrp]
        deps = set()
        for r in reads:
            d = self.lastw.get(r)
            if d is not None:
                deps.add(d)
        for w in writes:
            d = self.lastw.get(w)
            if d is not None:
                deps.add(d)
            deps.update(self.readers.get(w, ()))
        deps.discard(i)
        for d in deps:
            dr = self.ops[d]
            if dr["q"] is not None:
                rec["qwaits"][dr["sgrp"]] = max(rec["qwaits"].get(dr["sgrp"], 0), 16 * dr["qidx"])
            elif dr["eng"] == eng and q is None and eng == "pe":
                pass
            elif dr["eng"] == eng:
                rec["sdep"] = max(rec["sdep"], d)
            else:
                dr["inc"] = True
                rec["ewaits"][dr["eng"]] = max(rec["ewaits"].get(dr["eng"], -1), d)
        for r in reads:
            self.readers.setdefault(r, []).append(i)
        for w in writes:
            self.lastw[w] = i
            self.readers[w] = []
        self.ops.append(rec)
        return i

    def emit_all(self, block, esem, qsem):
        incnum = {}
        counts = {}
        for rec in self.ops:
            if rec["inc"] and rec["q"] is None:
                counts[rec["eng"]] = counts.get(rec["eng"], 0) + 1
                incnum[rec["i"]] = counts[rec["eng"]]
        self.final_inc_counts = counts
        self.final_q_counts = {g: 16 * n for g, n in self.qcount.items()}

        streams = {e: [rec for rec in self.ops if rec["eng"] == e]
                   for e in ("pe", "act", "dve", "pool", "sp")}

        def run_stream(eng_name):
            def body(eng):
                wm_q = {}
                wm_e = {}
                last_drain = -1
                prev_idx = -1
                for rec in streams[eng_name]:
                    for qn, val in sorted(rec["qwaits"].items()):
                        if wm_q.get(qn, 0) < val:
                            eng.wait_ge(qsem[qn], val)
                            wm_q[qn] = val
                    for en, didx in sorted(rec["ewaits"].items()):
                        val = incnum[didx]
                        if wm_e.get(en, 0) < val:
                            eng.wait_ge(esem[en], val)
                            wm_e[en] = val
                    if rec["sdep"] > last_drain:
                        eng.drain()
                        last_drain = prev_idx
                    inst = rec["emit"](eng)
                    if rec["q"] is not None:
                        inst.then_inc(qsem[rec["sgrp"]], 16)
                    elif rec["inc"]:
                        inst.then_inc(esem[eng_name], 1)
                    prev_idx = rec["i"]
            return body

        block.tensor(run_stream("pe"))
        block.scalar(run_stream("act"))
        block.vector(run_stream("dve"))
        block.gpsimd(run_stream("pool"))
        block.sync(run_stream("sp"))

        def finale(eng):
            for qn, val in sorted(self.final_q_counts.items()):
                if val:
                    eng.wait_ge(qsem[qn], val)
            for en, val in sorted(self.final_inc_counts.items()):
                eng.wait_ge(esem[en], val)
        block.sync(finale)


class KB:

    def __init__(self, nc, T, dbg_names=()):
        self.nc = nc
        self.T = T
        self.NTT = T // 128
        self.p = Prog()
        self.st = ExitStack()
        self.sb = {}
        self.ps = {}
        self.dram_in = {}
        self.dram_out = {}
        self.dbg_names = set(dbg_names)
        self.sbuf_bytes = 0

    def din(self, name, shape, dt=F32):
        h = self.nc.dram_tensor(name, list(shape), dt, kind="ExternalInput")
        self.dram_in[name] = h.ap()
        return self.dram_in[name]

    def dout(self, name, shape, dt=F32):
        h = self.nc.dram_tensor(name, list(shape), dt, kind="ExternalOutput")
        self.dram_out[name] = h.ap()
        return self.dram_out[name]

    def t(self, name, shape, dt=BF16):
        h = self.st.enter_context(self.nc.sbuf_tensor(name, list(shape), dt))
        self.sb[name] = h
        esz = {F32: 4, BF16: 2}[dt]
        self.sbuf_bytes += int(np.prod(shape[1:])) * esz
        return h

    def pt(self, name, shape, dt=F32):
        h = self.st.enter_context(self.nc.psum_tensor(name, list(shape), dt))
        self.ps[name] = h
        return h

    def dma(self, qname, out_ap, in_ap, reads=(), writes=(), sgrp=None, tag=""):
        def emit(eng):
            return eng.dma_start(out=out_ap, in_=in_ap)
        self.p.op(qname, emit, reads=reads, writes=writes, q=qname, sgrp=sgrp, tag=tag)

    def mm(self, out_ap, lhsT_ap, rhs_ap, psum_key, start, stop, reads=(), tag=""):
        def emit(eng):
            return eng.matmul(out_ap, lhsT_ap, rhs_ap, start=start, stop=stop)
        self.p.op("pe", emit, reads=tuple(reads), writes=(psum_key,), tag=tag)

    def tr(self, out_ap, in_ap, ident_ap, psum_key, reads=(), tag=""):
        def emit(eng):
            return eng.transpose(out_ap, in_ap, ident_ap)
        self.p.op("pe", emit, reads=tuple(reads), writes=(psum_key,), tag=tag)

    def act(self, out_ap, in_ap, func, reads=(), writes=(), scale=None, bias=None,
            accum=None, tag=""):
        if func == AF.Copy and bias is not None and not isinstance(bias, float):
            func = AF.Identity
        kw = {}
        if scale is not None:
            kw["scale"] = scale
        if bias is not None:
            kw["bias"] = bias
        if accum is not None:
            kw["accum_out"] = accum

        def emit(eng):
            return eng.activation(out_ap, in_ap, func, **kw)
        self.p.op("act", emit, reads=tuple(reads), writes=tuple(writes), tag=tag)

    def ve(self, method, *args, reads=(), writes=(), eng_name="dve", tag="", **kw):
        def emit(eng):
            return getattr(eng, method)(*args, **kw)
        self.p.op(eng_name, emit, reads=tuple(reads), writes=tuple(writes), tag=tag)

    def dbg(self, name, ap, shape, dt=F32, reads=()):
        if name not in self.dbg_names:
            return
        o = self.dout("dbg_" + name, shape, dt)
        self.dma("sp", o, ap, reads=tuple(reads), sgrp="dbg_" + name, tag="dbg" + name)



def ln_stats(kb, src_fn, n_tiles, tag, skey, keys_read):
    sm = kb.sb["stats"]
    bn = kb.sb["spB"]
    K = "stats"
    for tt in range(n_tiles):
        full = src_fn(tt)
        kb.ve("bn_stats", bn[:, tt * 12:tt * 12 + 6], full[:, 0:512],
              reads=keys_read + ("spB",), writes=("spB",), tag=tag + "bs")
        kb.ve("bn_stats", bn[:, tt * 12 + 6:tt * 12 + 12], full[:, 512:1024],
              reads=keys_read + ("spB",), writes=("spB",), tag=tag + "bs")
        kb.ve("bn_aggr", sm[:, 4 * tt:4 * tt + 2], bn[:, tt * 12:tt * 12 + 12],
              reads=("spB",), writes=(K,), tag=tag + "ba")
    var_v = sm[:, :].rearrange("p (t c) -> p t c", c=4)[:, 0:n_tiles, 1]
    lnv_v = sm[:, :].rearrange("p (t c) -> p t c", c=4)[:, 0:n_tiles, 2]
    rst_v = sm[:, :].rearrange("p (t c) -> p t c", c=4)[:, 0:n_tiles, 3]
    kb.ve("tensor_scalar_add", lnv_v, var_v, EPS, reads=(K,), writes=(K,), tag=tag + "ep")
    kb.act(lnv_v, lnv_v, AF.Ln, reads=(K,), writes=(K,), tag=tag + "ln")
    kb.act(rst_v, lnv_v, AF.Exp, scale=-0.5, reads=(K,), writes=(K,), tag=tag + "ex")
    return K


def ln_mu_rstd(kb, skey, tt):
    sm = kb.sb["stats"]
    return sm[:, 4 * tt:4 * tt + 1], sm[:, 4 * tt + 3:4 * tt + 4]


def tm_to_fm(kb, src_ap_fn, dst, dst_key, n_tt, tt0_tok, ndt, tag, keys_read,
             scale=None, bias=None, dt0=0, view_d=None):
    ident = kb.sb["identb"]
    for tt in range(n_tt):
        for dt in range(ndt):
            pk = "pt%d" % ((tt * ndt + dt) % 2)
            pt = kb.ps[pk]
            kb.tr(pt[:, 0:128], src_ap_fn(tt, dt), ident[:, :], pk,
                  reads=keys_read + ("identb",), tag=tag + "tr")
            dv = dst[:, :].rearrange("p (d t) -> p d t", d=(view_d or ndt))[
                :, dt0 + dt, tt0_tok + tt * 128: tt0_tok + (tt + 1) * 128]
            kb.act(dv, pt[:, 0:128], AF.Copy, scale=scale, bias=bias,
                   reads=(pk,), writes=(dst_key, pk), tag=tag + "ev")


def stage_a(kb, P):
    T, NTT = kb.T, kb.NTT
    TcA = min(512, T)
    NCH = T // TcA
    NTC = TcA // 128
    sb = kb.sb
    ps = kb.ps
    xtm = sb["xtm"]
    G = {k: sb[k] for k in ("G1", "G2", "G3", "G4", "G5", "G6", "G7", "G8", "G9")}
    u = G["G1"]
    du = G["G2"]
    xsr = G["G7"]
    bc = G["G4"]
    ya = G["G5"]
    xnf = G["G3"]
    zb = G["G6"]
    xnt = G["G8"]

    uv = u[:, :].rearrange("p (d t) -> p d t", d=16)
    duv = du[:, :].rearrange("p (d t) -> p d t", d=16)
    xsrv = xsr[:, :].rearrange("p (d t) -> p d t", d=16)
    bcv = bc[:, :].rearrange("p (d t) -> p d t", d=16)
    yav = ya[:, :].rearrange("p (d t) -> p d t", d=16)
    xnfv = xnf[:, :].rearrange("p (d t) -> p d t", d=16)
    hsl = xnf[:, 4096:8192].rearrange("p (n t) -> p n t", n=8)
    zbv = zb[:, :].rearrange("p (d t) -> p d t", d=16)
    ball = zb[:, 0:8 * TcA].rearrange("p (n t) -> p n t", n=8)
    asl = zb[:, 4096:4096 + 2 * TcA].rearrange("p (n t) -> p n t", n=2)
    xntv = xnt[:, :].rearrange("p (t d) -> p t d", d=1024)
    dlt = lambda dt, w: xsrv[:, dt, 3:3 + w]
    yak = lambda dt: G5K[dt // 4]
    zbk = lambda ot: G6K[ot // 4]

    stk = ln_stats(kb, lambda tt: xtm[:, :].rearrange("p (t d) -> p t d", d=1024)[:, tt, :],
                   NTT, "lnA", "A", tuple("xt%d" % i for i in range(NTT)))

    for c in range(NCH):
        for tl in range(NTC):
            tt = c * NTC + tl
            mu, rs = ln_mu_rstd(kb, "A", tt)
            src = xtm[:, :].rearrange("p (t d) -> p t d", d=1024)[:, tt, :]
            kb.ve("tensor_scalar", xntv[:, tl, :], src, mu, rs, ALU.subtract, ALU.mult,
                  reads=(xtk(tt), stk), writes=("G8r%d" % tl,), tag="axn")
        tm_to_fm(kb, lambda tl, dt: xntv[:, tl, dt * 128:(dt + 1) * 128],
                 xnf, "G3a", NTC, 0, 8, "axf",
                 ("G8r0", "G8r1", "G8r2", "G8r3"), view_d=16)

        for kt in range(8):
            kb.mm(ps["p4"][0:16, 0:TcA], sb["laiT"][:, :].rearrange(
                "p (k r) -> p k r", k=8)[:, kt, :], xnfv[:, kt, 0:TcA],
                "p4", kt == 0, kt == 7, reads=("laiT", "G3a"), tag="axl")
        kb.ve("tensor_copy", sb["xl"][0:16, 0:TcA], ps["p4"][0:16, 0:TcA],
              reads=("p4",), writes=("xl", "p4"), tag="axl_ev")

        for ot in range(16):
            wbuf = "wsA%d" % (ot % 3)
            kb.dma("pool", sb[wbuf][:, :].rearrange("p (k o) -> p k o", k=8),
                   P["w_inT"][ot], writes=(wbuf,), tag="awin")
            pk = "p%d" % (ot % 2)
            for kt in range(8):
                kb.mm(ps[pk][:, 0:TcA], sb[wbuf][:, :].rearrange(
                    "p (k o) -> p k o", k=8)[:, kt, :], xnfv[:, kt, 0:TcA],
                    pk, kt == 0, False, reads=(wbuf, "G3a"), tag="axz")
            kb.mm(ps[pk][:, 0:TcA], sb["lbi_s"][0:16, ot * 128:(ot + 1) * 128],
                  sb["xl"][0:16, 0:TcA], pk, False, True,
                  reads=("lbi_s", "xl"), tag="axzl")
            kb.act(xsrv[:, ot, 3:3 + TcA], ps[pk][:, 0:TcA], AF.Copy,
                   reads=(pk,), writes=("G7", pk), tag="axz_ev")

        if c == 0:
            kb.ve("memset", xsrv[:, :, 0:3], 0.0, writes=("G7",), tag="aconv0")
        for dt in range(16):
            dacc = sb["dacc"]
            cwt = sb["cw"][:, :].rearrange("p (d k) -> p d k", d=16)
            kb.ve("tensor_scalar", dacc[:, 0:TcA], xsrv[:, dt, 0:TcA],
                  cwt[:, dt, 0:1], None, ALU.mult,
                  reads=("G7", "cw"), writes=("spA",), tag="acv")
            for k in range(1, 4):
                kb.ve("scalar_tensor_tensor", dacc[:, 0:TcA], xsrv[:, dt, k:k + TcA],
                      cwt[:, dt, k:k + 1], dacc[:, 0:TcA], ALU.mult, ALU.add,
                      reads=("G7", "cw", "spA"), writes=("spA",), tag="acv")
            kb.act(uv[:, dt, 0:TcA], dacc[:, 0:TcA], AF.Silu,
                   bias=sb["cb"][:, dt:dt + 1],
                   reads=("spA", "cb"), writes=("G1",), tag="acs")
        if c + 1 < NCH:
            kb.ve("tensor_copy", xsrv[:, :, 0:3], xsrv[:, :, TcA:TcA + 3],
                  reads=("G7",), writes=("G7",), tag="ahalo")

        for kt in range(16):
            kb.mm(ps["p4"][0:96, 0:TcA], sb["wxpT"][:, :].rearrange(
                "p (d r) -> p d r", d=16)[:, kt, :], uv[:, kt, 0:TcA],
                "p4", kt == 0, kt == 15, reads=("wxpT", "G1"), tag="axd")
        kb.ve("tensor_copy", sb["xdbl"][0:96, 0:TcA], ps["p4"][0:96, 0:TcA],
              reads=("p4",), writes=("xdbl", "p4"), tag="axd_ev")

        for dt in range(16):
            pk = "p%d" % (dt % 2)
            kb.mm(ps[pk][:, 0:TcA], sb["wdtT"][0:64, dt * 128:(dt + 1) * 128],
                  sb["xdbl"][0:64, 0:TcA], pk, True, True,
                  reads=("wdtT", "xdbl"), tag="adl")
            spA, spB = sb["spA"], sb["spB"]
            bdt1 = sb["bdt"][:, dt:dt + 1]
            kb.act(spA[:, 0:TcA], ps[pk][:, 0:TcA], AF.Abs, bias=bdt1,
                   reads=(pk, "bdt"), writes=("spA",), tag="asp1")
            kb.act(spB[:, 0:TcA], spA[:, 0:TcA], AF.Exp, scale=-1.0,
                   reads=("spA",), writes=("spB",), tag="asp2")
            kb.act(spA[:, 0:TcA], spB[:, 0:TcA], AF.Ln, bias=1.0,
                   reads=("spB",), writes=("spA",), tag="asp3")
            kb.act(spB[:, 0:TcA], ps[pk][:, 0:TcA], AF.Relu, bias=bdt1,
                   reads=(pk, "bdt"), writes=("spB", pk), tag="asp4")
            kb.ve("tensor_add", dlt(dt, TcA), spA[:, 0:TcA], spB[:, 0:TcA],
                  reads=("spA", "spB"), writes=("G7",), tag="asp5")
            kb.ve("tensor_mul", duv[:, dt, 0:TcA], dlt(dt, TcA), uv[:, dt, 0:TcA],
                  reads=("G7", "G1"), writes=("G2",), tag="adu")
            kb.ve("tensor_scalar", yav[:, dt, 0:TcA], uv[:, dt, 0:TcA],
                  sb["Dp"][:, dt:dt + 1], None, ALU.mult,
                  reads=("G1", "Dp"), writes=(yak(dt),), tag="ayi")

        if c == 0:
            kb.ve("memset", sb["hcar"][:, :], 0.0, writes=("hcar",), tag="ahc0")
        kb.dma("pool", P["scr_bc"][:, 0:TcA], sb["xdbl"][64:96, 0:TcA],
               reads=("xdbl",), writes=("scrBC",), tag="abc_st")
        pb8 = zb[:, 8 * TcA:16 * TcA].rearrange("p (n t) -> p n t", n=8)
        g6r = lambda lo, hi: tuple(G6K[r] for r in range(lo // 2048, (hi - 1) // 2048 + 1))
        ballk = g6r(0, 8 * TcA)
        pbak = g6r(8 * TcA, 12 * TcA)
        pbbk = g6r(12 * TcA, 16 * TcA)
        aslg = sb["G9"][:, 0:2 * TcA].rearrange("p (n t) -> p n t", n=2)
        pend = []
        for g in range(2):
            kb.dma("pool", bcv[:, 0:8, 0:TcA],
                   P["scr_bc"][g * 8:g * 8 + 8, 0:TcA][None, :, :].broadcast_to([128, 8, TcA]),
                   reads=("scrBC",), writes=("G4a",), tag="abc_ldB")
            kb.dma("pool", bcv[:, 8:16, 0:TcA],
                   P["scr_bc"][16 + g * 8:16 + g * 8 + 8, 0:TcA][None, :, :].broadcast_to([128, 8, TcA]),
                   reads=("scrBC",), writes=("G4b",), tag="abc_ldC")
            for dt in range(16):
                kb.ve("tensor_mul", ball[:, :, 0:TcA],
                      duv[:, dt:dt + 1, 0:TcA].broadcast_to([128, 8, TcA]),
                      bcv[:, 0:8, 0:TcA],
                      reads=("G2", "G4a"), writes=ballk, tag="asb")
                if len(pend) >= 2:
                    pend.pop(0)()
                for n in range(8):
                    nn = g * 8 + n
                    par = n % 2
                    kb.act(aslg[:, par, 0:TcA], dlt(dt, TcA), AF.Exp,
                           scale=sb["Aneg"][:, :].rearrange("p (d n) -> p d n", d=16)[:, dt, nn:nn + 1],
                           reads=("G7", "Aneg"), writes=("g9p%d" % par,), tag="asa")
                    kb.ve("tensor_tensor_scan", hsl[:, n, 0:TcA], aslg[:, par, 0:TcA],
                          ball[:, n, 0:TcA],
                          sb["hcar"][:, dt * 16 + nn:dt * 16 + nn + 1],
                          ALU.mult, ALU.add,
                          reads=("g9p%d" % par,) + ballk + ("hcar",),
                          writes=("hs%d" % n,), tag="asc")
                kb.act(sb["hcar"][:, dt * 16 + g * 8:dt * 16 + g * 8 + 8],
                       hsl[:, :, TcA - 1], AF.Copy,
                       reads=tuple("hs%d" % i for i in range(8)), writes=("hcar",), tag="ahc")
                kb.ve("tensor_mul", pb8[:, 0:4, 0:TcA], hsl[:, 0:4, 0:TcA], bcv[:, 8:12, 0:TcA],
                      reads=("hs0", "hs1", "hs2", "hs3", "G4b"),
                      writes=pbak, eng_name="pool", tag="asya")
                kb.ve("tensor_mul", pb8[:, 4:8, 0:TcA], hsl[:, 4:8, 0:TcA], bcv[:, 12:16, 0:TcA],
                      reads=("hs4", "hs5", "hs6", "hs7", "G4b"),
                      writes=pbbk, eng_name="pool", tag="asyb")
                kb.ve("tensor_add", pb8[:, 0:4, 0:TcA], pb8[:, 0:4, 0:TcA], pb8[:, 4:8, 0:TcA],
                      reads=pbak + pbbk, writes=pbak, eng_name="pool", tag="atr1")
                kb.mm(ps["p2"][0:16, 0:16], sb["onesb"][0:1, 0:16], sb["onesb"][0:1, 0:16],
                      "p2", True, True, reads=("onesb", "hs7"), tag="awarm")

                def tail(dt=dt):
                    kb.ve("tensor_add", pb8[:, 0:2, 0:TcA], pb8[:, 0:2, 0:TcA],
                          pb8[:, 2:4, 0:TcA], reads=pbak, writes=pbak, tag="atr2")
                    kb.ve("tensor_add", pb8[:, 0, 0:TcA], pb8[:, 0, 0:TcA],
                          pb8[:, 1, 0:TcA], reads=pbak, writes=pbak, tag="atr3")
                    kb.ve("tensor_add", yav[:, dt, 0:TcA], yav[:, dt, 0:TcA],
                          pb8[:, 0, 0:TcA], reads=(yak(dt),) + pbak, writes=(yak(dt),), tag="atr4")
                pend.append(tail)
        while pend:
            pend.pop(0)()

        for ot in range(16):
            wbuf = "wsA%d" % (ot % 3)
            kb.dma("pool", sb[wbuf][:, :].rearrange("p (k o) -> p k o", k=8),
                   P["w_inT"][16 + ot], writes=(wbuf,), tag="awz")
            pk = "p%d" % (ot % 2)
            for kt in range(8):
                kb.mm(ps[pk][:, 0:TcA], sb[wbuf][:, :].rearrange(
                    "p (k o) -> p k o", k=8)[:, kt, :], xnfv[:, kt, 0:TcA],
                    pk, kt == 0, False, reads=(wbuf, "G3a"), tag="az")
            kb.mm(ps[pk][:, 0:TcA], sb["lbi_s"][0:16, (16 + ot) * 128:(17 + ot) * 128],
                  sb["xl"][0:16, 0:TcA], pk, False, True,
                  reads=("lbi_s", "xl"), tag="azl")
            kb.act(zbv[:, ot, 0:TcA], ps[pk][:, 0:TcA], AF.Silu,
                   reads=(pk,), writes=(zbk(ot), pk), tag="az_ev")

        for dt in range(16):
            kb.ve("tensor_mul", zbv[:, dt, 0:TcA], yav[:, dt, 0:TcA],
                  zbv[:, dt, 0:TcA],
                  reads=(yak(dt), zbk(dt)), writes=(zbk(dt),), tag="agate")
        for tb in range(NTC):
            for kt in range(16):
                kb.mm(ps["p4"][0:16, 0:128], sb["laoT"][:, :].rearrange(
                    "p (d r) -> p d r", d=16)[:, kt, :],
                    zbv[:, kt, tb * 128:(tb + 1) * 128],
                    "p4", kt == 0, kt == 15, reads=("laoT", zbk(kt)), tag="aylo")
            kb.ve("tensor_copy", sb["xdbl"][0:16, tb * 128:(tb + 1) * 128],
                  ps["p4"][0:16, 0:128], reads=("p4",), writes=("xdbl", "p4"), tag="aylo_ev")
        xv = xtm[:, :].rearrange("p (t d) -> p t d", d=1024)
        for oh in range(2):
            for kt in range(16):
                wbuf = "wsA%d" % ((oh * 16 + kt) % 3)
                kb.dma("pool", sb[wbuf][:, 0:512],
                       P["w_outT"][kt, oh], writes=(wbuf,), tag="awo")
                for tb in range(NTC):
                    kb.mm(ps["p%d" % tb][:, 0:512], zbv[:, kt, tb * 128:(tb + 1) * 128],
                          sb[wbuf][:, 0:512], "p%d" % tb, kt == 0, False,
                          reads=(zbk(kt), wbuf), tag="aop")
            for tb in range(NTC):
                pk = "p%d" % tb
                kb.mm(ps[pk][:, 0:512], sb["xdbl"][0:16, tb * 128:(tb + 1) * 128],
                      sb["lbo_s"][0:16, oh * 512:(oh + 1) * 512], pk, False, True,
                      reads=("xdbl", "lbo_s"), tag="aopl")
                tt = c * NTC + tb
                kb.ve("tensor_add", xv[:, tt, oh * 512:(oh + 1) * 512],
                      xv[:, tt, oh * 512:(oh + 1) * 512], ps[pk][:, 0:512],
                      reads=(xtk(tt), pk), writes=(xtk(tt), pk), tag="aox")
    kb.dbg("x_after_a", xtm[:, :], [128, NTT * 1024], BF16, reads=tuple("xt%d" % i for i in range(NTT)))


def build_program(T, stages=("a",), dbg_names=()):
    nc = bass.Bass("TRN2", target_bir_lowering=False, debug=False)
    kb = KB(nc, T, dbg_names)
    NTT = kb.NTT
    TcA = min(512, T)

    P = {}
    for nm, shp in [
        ("x", [T, D]), ("w_inT", [32, 128, 8, 128]), ("laiT", [D, RK]), ("lbiT", [RK, 2 * DI]),
        ("mask_in", [2 * DI]), ("cw", [128, 16, 4]), ("cb", [128, 16]), ("wxpT", [DI, 96]),
        ("wdtT", [DTR, DI]), ("bdt", [128, 16]), ("Alog", [128, 16, DS]), ("Dp", [128, 16]),
        ("w_outT", [16, 2, 128, 512]), ("laoT", [DI, RK]), ("lboT", [RK, D]), ("mask_out", [D]),
        ("nu", [128, 8]), ("th", [128, 8]), ("gam", [128, 8]), ("Dlru", [1, D]),
        ("BreT", [8, 128, 8, 128]), ("BimT", [8, 128, 8, 128]),
        ("CreT", [8, 2, 128, 512]), ("CimT", [8, 2, 128, 512]),
        ("ln1g", [128, 8]), ("ln1b", [128, 8]), ("ln2g", [128, 8]), ("ln2b", [128, 8]),
        ("Wq", [8, 128, 8, 128]), ("Wk", [8, 128, 8, 128]), ("Wv", [8, 128, 8, 128]),
        ("Wp", [8, 2, 128, 512]),
        ("bq", [128, 8]), ("bk", [128, 8]), ("bv", [1, D]), ("bp", [1, D]),
        ("gw", [D, NE]), ("gb", [NE]), ("W1", [NE, 8, 128, 8, 512]), ("b1", [128, NE, 32]),
        ("W2", [NE, 4, 2, 128, 8, 512]), ("b2", [NE, D]),
    ]:
        P[nm] = kb.din("i_" + nm, shp, F32)
    for nm, shp, dt in [("identb", [128, 128], BF16), ("triN", [128, 128], BF16),
                        ("iota8", [128, NE], F32)]:
        P[nm] = kb.din("i_" + nm, shp, dt)
    OUT = kb.dout("out", [T, D], F32)
    SCR_BC = kb.nc.dram_tensor("scr_bc", [32, min(512, T)], BF16, kind="Internal").ap()
    P["scr_bc"] = SCR_BC

    kb.t("xtm", [128, NTT * 1024], BF16)
    for g in ("G1", "G2", "G3", "G4", "G5", "G6"):
        kb.t(g, [128, 8192], BF16)
    kb.t("G7", [128, 16 * (TcA + 3)], BF16)
    kb.t("G8", [128, 4096], BF16)
    kb.t("G9", [128, 4096], BF16)
    for wi_ in range(3):
        kb.t("wsA%d" % wi_, [128, 8 * 128], BF16)
    kb.t("identb", [128, 128], BF16)
    kb.t("triN", [128, 128], BF16)
    kb.t("iota8", [128, NE], F32)
    kb.t("stats", [128, 4 * NTT], F32)
    kb.t("onesf", [1, 128], F32)
    kb.t("onesb", [1, 128], BF16)
    kb.t("onesn", [1, 128], BF16)
    kb.t("onescol", [128, 1], BF16)
    kb.t("onescf", [128, 1], F32)
    kb.t("laiT", [128, 8 * RK], BF16)
    kb.t("bsml", [128, 4096], BF16)
    kb.t("cw", [128, 16 * 4], F32)
    kb.t("cb", [128, 16], F32)
    kb.t("wxpT", [128, 16 * 96], BF16)
    kb.t("wdtT", [DTR, DI], BF16)
    kb.t("bdt", [128, 16], F32)
    kb.t("Aneg", [128, 16 * DS], F32)
    kb.t("Dp", [128, 16], F32)
    kb.t("laoT", [128, 16 * RK], BF16)
    kb.t("lbo_s", [RK, D], BF16)
    kb.t("xl", [RK, TcA], BF16)
    kb.t("ylo", [RK, 128], BF16)
    kb.t("xdbl", [128, TcA], BF16)
    kb.t("spA", [128, max(TcA, 1024)], F32)
    kb.t("spB", [128, max(TcA, 1024)], F32)
    kb.t("hcar", [128, 16 * DS], F32)
    kb.t("rows", [1, 2048], BF16)
    kb.t("wsV", [128, 8 * 256], BF16)
    kb.t("attsh", [128, 32], BF16)
    kb.t("ascr", [128, 16], F32)
    kb.t("e8", [128, NE], F32)
    kb.t("gwt", [128, 8 * NE], BF16)
    kb.t("bq", [128, 8], F32)
    kb.t("bk8", [128, 8], F32)
    kb.t("ln1g", [128, 8], F32)
    kb.t("ln1b", [128, 8], F32)
    kb.t("ln2g", [128, 8], F32)
    kb.t("ln2b", [128, 8], F32)

    bs = kb.sb["bsml"]
    bsf = bs.bitcast(F32)
    kb.sb["lbi_s"] = bs[0:128, :]
    kb.sb["lpr"] = bs[:, 0:1024]
    kb.sb["lpi"] = bs[:, 1024:2048]
    kb.sb["lh"] = bsf[:, 1024:1280]
    kb.sb["lam"] = bsf[:, 1280:1344]
    kb.sb["nuv"] = bsf[:, 1344:1352]
    kb.sb["thv"] = bsf[:, 1352:1360]
    kb.sb["gamv"] = bsf[:, 1360:1368]
    kb.sb["gexp"] = bsf[:, 1368:1376]
    kb.t("b1pp", [128, NE * 32], BF16)
    kb.sb["dacc"] = kb.sb["spA"][:, 512:1024]
    kb.sb["lgt"] = kb.sb["spB"][:, 0:128]
    kb.sb["wfl"] = kb.sb["spB"][:, 128:256]
    kb.p.alias["lgt"] = "spB"
    kb.p.alias["wfl"] = "spB"
    for k in ("lbi_s", "lpr", "lpi", "lh", "lam", "nuv", "thv", "gamv", "gexp"):
        kb.p.alias[k] = "bsml"
    kb.p.alias["dacc"] = "spA"
    for i in range(6):
        kb.pt("p%d" % i, [128, 512], F32)
    kb.pt("pt0", [128, 128], BF16)
    kb.pt("pt1", [128, 128], BF16)

    pre = [
        ("sp", "identb", P["identb"], None), ("sp", "triN", P["triN"], None),
        ("sp", "iota8", P["iota8"], None),
    ]
    for q, nm, src, re_ in pre:
        kb.dma(q, kb.sb[nm][:, :], src, writes=(nm,), tag="pre" + nm)
    kb.dma("pool", kb.sb["xtm"][:, :].rearrange("p (t d) -> p t d", d=1024),
           P["x"].rearrange("(t p) d -> p t d", p=128),
           writes=tuple("xt%d" % i for i in range(NTT)), sgrp="prex", tag="prex")
    kb.dma("pool", kb.sb["laiT"][:, :].rearrange("p (k r) -> p k r", k=8),
           P["laiT"].rearrange("(k p) r -> p k r", p=128), writes=("laiT",), tag="plai")
    kb.dma("pool", kb.sb["G1"][0:RK, 0:2 * DI], P["lbiT"], writes=("G1",), tag="plbi")
    kb.dma("pool", kb.sb["G2"][0:RK, 0:2 * DI], P["mask_in"][None, :].broadcast_to([RK, 2 * DI]), writes=("G2",), tag="pmin")
    kb.dma("sp", kb.sb["cw"][:, :].rearrange("p (d k) -> p d k", d=16),
           P["cw"], writes=("cw",), tag="pcw")
    kb.dma("sp", kb.sb["cb"][:, :], P["cb"], writes=("cb",), tag="pcb")
    kb.dma("pool", kb.sb["wxpT"][:, :].rearrange("p (d r) -> p d r", d=16),
           P["wxpT"].rearrange("(d p) r -> p d r", p=128), writes=("wxpT",), tag="pwxp")
    kb.dma("pool", kb.sb["wdtT"][0:DTR, :], P["wdtT"], writes=("wdtT",), tag="pwdt")
    kb.dma("sp", kb.sb["bdt"][:, :], P["bdt"], writes=("bdt",), tag="pbdt")
    kb.dma("sp", kb.sb["spA"].bitcast(F32)[:, 0:256].rearrange("p (d n) -> p d n", d=16)
           if False else kb.sb["spA"][:, 0:256].rearrange("p (d n) -> p d n", d=16),
           P["Alog"], writes=("spA",), tag="pal")
    kb.dma("sp", kb.sb["Dp"][:, :], P["Dp"], writes=("Dp",), tag="pdp")
    kb.dma("pool", kb.sb["laoT"][:, :].rearrange("p (k r) -> p k r", k=16),
           P["laoT"].rearrange("(k p) r -> p k r", p=128), writes=("laoT",), tag="plao")
    kb.dma("pool", kb.sb["G1"][0:RK, 4096:4096 + D], P["lboT"], writes=("G1",), tag="plbo")
    kb.dma("pool", kb.sb["G2"][0:RK, 4096:4096 + D], P["mask_out"][None, :].broadcast_to([RK, D]), writes=("G2",), tag="pmout")
    kb.ve("memset", kb.sb["onesf"][:, :], 1.0, writes=("onesf",), tag="ponef")
    kb.ve("memset", kb.sb["e8"][:, :], 8.0, writes=("e8",), tag="pe8")
    kb.dma("sp", kb.sb["bq"][:, :], P["bq"], writes=("bq",), tag="pbq")
    kb.dma("sp", kb.sb["bk8"][:, :], P["bk"], writes=("bk8",), tag="pbk")
    kb.ve("tensor_scalar_mul", kb.sb["bk8"][:, :], kb.sb["bk8"][:, :], 0.125,
          reads=("bk8",), writes=("bk8",), tag="pbk8")
    for nmv in ("ln1g", "ln1b", "ln2g", "ln2b"):
        kb.dma("sp", kb.sb[nmv][:, :], P[nmv], writes=(nmv,), tag="p" + nmv)
    kb.dma("pool", kb.sb["rows"][0:1, 0:D], P["bp"], writes=("rows",), sgrp="r_bp", tag="pbp")
    kb.dma("pool", kb.sb["attsh"][0:1, 24:24 + NE], P["gb"][None, :], writes=("attsh",), sgrp="r_gb", tag="pgb")
    kb.dma("pool", kb.sb["gwt"][:, :].rearrange("p (k e) -> p k e", k=8),
           P["gw"].rearrange("(k p) e -> p k e", p=128), writes=("gwt",), tag="pgw")
    kb.ve("memset", kb.sb["onesb"][:, :], 1.0, writes=("onesb",), tag="poneb")
    kb.ve("memset", kb.sb["onesn"][:, :], -1.0, writes=("onesn",), tag="ponen")
    kb.ve("memset", kb.sb["onescol"][:, :], 1.0, writes=("onescol",), tag="ponec")
    kb.ve("memset", kb.sb["onescf"][:, :], 1.0, writes=("onescf",), tag="ponecf")
    kb.act(kb.sb["Aneg"][:, :], kb.sb["spA"][:, 0:256], AF.Exp, reads=("spA",),
           writes=("Aneg",), tag="pan1")
    kb.ve("tensor_scalar_mul", kb.sb["Aneg"][:, :], kb.sb["Aneg"][:, :], -1.0,
          reads=("Aneg",), writes=("Aneg",), tag="pan2")
    kb.ve("scalar_tensor_tensor", kb.sb["lbi_s"][0:RK, :], kb.sb["G1"][0:RK, 0:2 * DI], 2.0,
          kb.sb["G2"][0:RK, 0:2 * DI], ALU.mult, ALU.mult,
          reads=("G1", "G2"), writes=("lbi_s",), tag="plbis")
    kb.ve("scalar_tensor_tensor", kb.sb["lbo_s"][0:RK, :], kb.sb["G1"][0:RK, 4096:4096 + D], 2.0,
          kb.sb["G2"][0:RK, 4096:4096 + D], ALU.mult, ALU.mult,
          reads=("G1", "G2"), writes=("lbo_s",), tag="plbos")

    if "a" in stages:
        stage_a(kb, P)
    if "b" in stages:
        stage_ffn(kb, P, 1)
    if "c" in stages:
        stage_lru(kb, P)
    if "b2" in stages:
        stage_ffn(kb, P, 2)

    kb.dma("pool", OUT.rearrange("(t p) d -> p t d", p=128),
           kb.sb["xtm"][:, :].rearrange("p (t d) -> p t d", d=1024),
           reads=tuple("xt%d" % i for i in range(NTT)), sgrp="outx", tag="outx")

    esem = {}
    qsem = {}
    for e in ("pe", "act", "dve", "pool", "sp"):
        esem[e] = kb.st.enter_context(nc.semaphore("se_" + e))
    for g in sorted(kb.p.qcount):
        qsem[g] = kb.st.enter_context(nc.semaphore("sq_" + g))
    with nc.Block() as block:
        kb.p.emit_all(block, esem, qsem)
    kb.st.close()
    assert kb.sbuf_bytes <= 213 * 1024, f"sbuf over budget: {kb.sbuf_bytes}"
    return nc, kb


def _til8(w):
    K, N = w.shape
    return np.ascontiguousarray(w.reshape(K // 128, 128, N // 128, 128).transpose(2, 1, 0, 3))


def host_inputs(inputs, T=2048):
    g = lambda k: np.ascontiguousarray(np.asarray(inputs["L0_" + k], np.float32))
    com = {
        "w_inT": np.ascontiguousarray(g("attn_a_w_in").T.reshape(8, 128, 32, 128).transpose(2, 1, 0, 3)),
        "laiT": np.ascontiguousarray(g("attn_a_lora_A_in").T),
        "lbiT": np.ascontiguousarray(g("attn_a_lora_B_in").T),
        "mask_in": g("attn_a_mask_in"),
        "cw": np.ascontiguousarray(g("attn_a_conv_w")[:, 0, :].reshape(16, 128, 4).transpose(1, 0, 2)),
        "cb": np.ascontiguousarray(g("attn_a_conv_b").reshape(16, 128).T),
        "wxpT": np.ascontiguousarray(g("attn_a_w_xproj").T),
        "wdtT": np.ascontiguousarray(g("attn_a_w_dt").T),
        "bdt": np.ascontiguousarray(g("attn_a_b_dt").reshape(16, 128).T),
        "Alog": np.ascontiguousarray(g("attn_a_A_log").reshape(16, 128, 16).transpose(1, 0, 2)),
        "Dp": np.ascontiguousarray(g("attn_a_Dp").reshape(16, 128).T),
        "w_outT": np.ascontiguousarray(g("attn_a_w_out").T.reshape(16, 128, 2, 512).transpose(0, 2, 1, 3)),
        "laoT": np.ascontiguousarray(g("attn_a_lora_A_out").T),
        "lboT": np.ascontiguousarray(g("attn_a_lora_B_out").T),
        "mask_out": g("attn_a_mask_out"),
        "nu": np.ascontiguousarray(g("attn_b_nu_log").reshape(8, 128).T), "th": np.ascontiguousarray(g("attn_b_theta_log").reshape(8, 128).T),
        "gam": np.ascontiguousarray(g("attn_b_gamma_log").reshape(8, 128).T), "Dlru": g("attn_b_D")[None, :],
        "BreT": _til8(np.ascontiguousarray(g("attn_b_B_re").T)),
        "BimT": _til8(np.ascontiguousarray(g("attn_b_B_im").T)),
        "CreT": np.ascontiguousarray(g("attn_b_C_re").T.reshape(8, 128, 2, 512).transpose(0, 2, 1, 3)),
        "CimT": np.ascontiguousarray(g("attn_b_C_im").T.reshape(8, 128, 2, 512).transpose(0, 2, 1, 3)),
        "ln1g": np.ascontiguousarray(g("ffn_ln1_g").reshape(8, 128).T), "ln1b": np.ascontiguousarray(g("ffn_ln1_b").reshape(8, 128).T),
        "ln2g": np.ascontiguousarray(g("ffn_ln2_g").reshape(8, 128).T), "ln2b": np.ascontiguousarray(g("ffn_ln2_b").reshape(8, 128).T),
        "Wq": _til8(g("ffn_Wq")), "Wk": _til8(g("ffn_Wk")), "Wv": _til8(g("ffn_Wv")),
        "Wp": np.ascontiguousarray(g("ffn_Wp").reshape(8, 128, 2, 512).transpose(0, 2, 1, 3)),
        "bq": np.ascontiguousarray(g("ffn_bq").reshape(8, 128).T), "bk": np.ascontiguousarray(g("ffn_bk").reshape(8, 128).T), "bv": g("ffn_bv")[None, :], "bp": g("ffn_bp")[None, :],
        "gw": g("ffn_gate_W"), "gb": g("ffn_gate_b"),
        "W1": np.ascontiguousarray(g("ffn_exp_W1").reshape(NE, 8, 128, 8, 512).transpose(0, 3, 2, 1, 4)), "b1": np.ascontiguousarray(g("ffn_exp_b1").reshape(8, 32, 128).transpose(2, 0, 1)),
        "W2": np.ascontiguousarray(g("ffn_exp_W2").reshape(NE, 4, 8, 128, 2, 512).transpose(0, 1, 4, 3, 2, 5)), "b2": g("ffn_exp_b2"),
        "i_identb": np.eye(128, dtype=ml_dtypes.bfloat16),
        "i_triN": np.where(np.triu(np.ones((128, 128), bool)), 0.0, -1e30).astype(ml_dtypes.bfloat16),
        "i_iota8": np.broadcast_to(np.arange(NE, dtype=np.float32), (128, NE)).copy(),
    }
    com = {("i_" + k if not k.startswith("i_") else k): v for k, v in com.items()}
    x = np.asarray(inputs["L0_attn_a_x"], np.float32)
    maps = []
    for b in range(x.shape[0]):
        m = dict(com)
        m["i_x"] = np.ascontiguousarray(x[b, :T])
        maps.append(m)
    return maps


_CACHE = {}


def kernel(**inputs):
    T = 2048
    if "prog" not in _CACHE:
        _CACHE["prog"] = build_program(T, stages=("a", "b", "c", "b2"))
    nc, kb = _CACHE["prog"]
    in_maps = host_inputs(inputs, T)
    res = run_bass_kernel_spmd(nc, in_maps, core_ids=list(range(8)))
    outs = [np.asarray(r["out"], np.float32) for r in res.results]
    full = np.stack(outs, 0)
    return full


def xtk(tt):
    return "xt%d" % tt


def _htile(kb, tt):
    g = "G1" if tt < 8 else "G2"
    return kb.sb[g], g, (tt % 8) * 1024


def _fm(kb, dt):
    g = "G3" if dt < 4 else "G4"
    k = (("G3a",), ("G3a",), ("hs0", "hs1", "hs2", "hs3"), ("hs4", "hs5", "hs6", "hs7"),
         ("G4a",), ("G4a",), ("G4b",), ("G4b",))[dt]
    return kb.sb[g], k, (dt % 4) * 2048


def stage_ffn(kb, P, ci):
    T, NTT = kb.T, kb.NTT
    sb, ps = kb.sb, kb.ps
    xtm = sb["xtm"]
    xv = xtm[:, :].rearrange("p (t d) -> p t d", d=1024)
    QCW = min(512, T)
    NQC = T // QCW
    cg = "c%d_" % ci

    kb.dma("pool", sb["G7"][0:1, 3 * T + 1024:3 * T + 2048], P["bv"], writes=("G7",),
           sgrp="r_bv", tag=cg + "pbv")
    if ci == 1:
        kb.dma("pool", sb["b1pp"][:, :].rearrange("p (e h) -> p e h", e=NE),
               P["b1"], writes=("b1pp",), tag=cg + "pb1")
    ln_stats(kb, lambda tt: xv[:, tt, :], NTT, cg + "ln0", "F0", tuple("xt%d" % i for i in range(NTT)))
    for tt in range(NTT):
        mu, rs = ln_mu_rstd(kb, "F0", tt)
        ht, hk, co = _htile(kb, tt)
        kb.ve("tensor_scalar", ht[:, co:co + 1024], xv[:, tt, :], mu, rs,
              ALU.subtract, ALU.mult, reads=(xtk(tt), "stats"), writes=(hk,), tag=cg + "h")

    ln_stats(kb, lambda tt: _htile(kb, tt)[0][:, _htile(kb, tt)[2]:_htile(kb, tt)[2] + 1024],
             NTT, cg + "ln1", "F1", ("G1", "G2"))
    tscr = sb["G9"]
    for tt in range(NTT):
        mu, rs = ln_mu_rstd(kb, "F1", tt)
        ht, hk, co = _htile(kb, tt)
        kb.ve("tensor_scalar", tscr[:, 0:1024], ht[:, co:co + 1024], mu, rs,
              ALU.subtract, ALU.mult, reads=(hk, "stats"), writes=("g9t",), tag=cg + "n1")
        for dt in range(8):
            pk = "pt%d" % (dt % 2)
            kb.tr(ps[pk][:, 0:128], tscr[:, dt * 128:(dt + 1) * 128], sb["identb"][:, :],
                  pk, reads=("g9t", "identb"), tag=cg + "t1")
            ft, fk, fo = _fm(kb, dt)
            kb.act(ft[:, fo + tt * 128:fo + (tt + 1) * 128], ps[pk][:, 0:128], AF.Copy,
                   scale=sb["ln1g"][:, dt:dt + 1], bias=sb["ln1b"][:, dt:dt + 1],
                   reads=(pk, "ln1g", "ln1b"), writes=fk + (pk,), tag=cg + "t1e")

    rows = sb["rows"]

    def qkv_dmas(hg):
        for wi_ in range(2):
            for ot2 in range(2):
                og = hg * 2 + ot2
                wbuf = "wsA%d" % ((wi_ * 2 + ot2) % 3)
                kb.dma("pool", sb[wbuf][:, :].rearrange("p (k o) -> p k o", k=8),
                       P["Wq" if wi_ == 0 else "Wk"][og].rearrange("p k o -> p k o"),
                       writes=(wbuf,), tag=cg + "wqk")
        for _vt in range(2):
            kb.dma("pool", sb["wsV"][:, _vt * 1024:(_vt + 1) * 1024].rearrange(
                "p (k o) -> p k o", k=8),
                P["Wv"][hg * 2 + _vt].rearrange("p k o -> p k o"),
                writes=("wsV%d" % _vt,), tag=cg + "wvl")

    qkv_dmas(0)
    for hg in range(4):
        for wi_, (wname, dstk) in enumerate((("Wq", "G5"), ("Wk", "G6"))):
            dkeys = G5K if dstk == "G5" else G6K
            for ot2 in range(2):
                og = hg * 2 + ot2
                wbuf = "wsA%d" % ((wi_ * 2 + ot2) % 3)
                for qc in range(NQC):
                    pk = "p%d" % (qc % 2)
                    for kt in range(8):
                        ft, fk, fo = _fm(kb, kt)
                        kb.mm(ps[pk][:, 0:QCW], sb[wbuf][:, :].rearrange(
                            "p (k o) -> p k o", k=8)[:, kt, :],
                            ft[:, fo + qc * QCW:fo + (qc + 1) * QCW],
                            pk, kt == 0, kt == 7, reads=(wbuf,) + fk, tag=cg + "qk")
                    for hh in range(2):
                        hl = ot2 * 2 + hh
                        if wname == "Wq":
                            sc, bi = 1.0, sb["bq"][hh * 64:hh * 64 + 64, og:og + 1]
                        else:
                            sc, bi = 0.125, sb["bk8"][hh * 64:hh * 64 + 64, og:og + 1]
                        kb.act(sb[dstk][0:64, hl * T + qc * QCW:hl * T + (qc + 1) * QCW],
                               ps[pk][hh * 64:hh * 64 + 64, 0:QCW], AF.Copy,
                               scale=sc, bias=bi,
                               reads=(pk, "bq", "bk8"), writes=dkeys + (pk,), tag=cg + "qke")
        g8v = sb["G8"][:, :].rearrange("p (t o) -> p t o", o=256)
        for tb in range(NTT):
            pk = "p%d" % (tb % 2)
            for kt in range(8):
                ft, fk, fo = _fm(kb, kt)
                for vt in range(2):
                    kb.mm(ps[pk][:, vt * 128:(vt + 1) * 128],
                          ft[:, fo + tb * 128:fo + (tb + 1) * 128],
                          sb["wsV"][:, vt * 1024 + kt * 128:vt * 1024 + (kt + 1) * 128],
                          pk, kt == 0 and vt == 0, False,
                          reads=fk + ("wsV%d" % vt,), tag=cg + "v")
            kb.mm(ps[pk][:, 0:256], sb["onesb"][0:1, :],
                  sb["G7"][0:1, 3 * T + 1024 + hg * 256:3 * T + 1024 + (hg + 1) * 256],
                  pk, False, True, reads=("onesb", "G7"), tag=cg + "vb")
            kb.ve("tensor_copy", g8v[:, tb, :], ps[pk][:, 0:256],
                  reads=(pk,), writes=("G8r%d" % (tb // 4), pk), tag=cg + "ve")

        if hg < 3:
            qkv_dmas(hg + 1)
        for hl in range(4):
            h = hg * 4 + hl
            q0c = hl * T
            g9f = sb["G9"].bitcast(F32)
            kb.act(g9f[0:64, 0:T], sb["G6"][0:64, q0c:q0c + T], AF.Square,
                   reads=G6K, writes=("g9t", "g9p0", "g9p1", "g9p2", "g9p3"), tag=cg + "ksq")
            NQB = T // 512 if T >= 512 else 1
            QW = min(512, T)
            for c_ in range(T // QW):
                kb.mm(ps["p5"][0:1, 0:QW],
                      sb["onescf"][0:64, 0:1], g9f[0:64, c_ * QW:(c_ + 1) * QW],
                      "p5", True, True, reads=("onescf", "g9t"), tag=cg + "ksum")
                kb.ve("reduce_max", sb["spB"][0:1, c_:c_ + 1], ps["p5"][0:1, 0:QW],
                      axis=AX.X, reads=("p5",), writes=("spB",), tag=cg + "kmax")
            kb.ve("reduce_max", sb["spB"][0:1, 8:9], sb["spB"][0:1, 0:T // QW],
                  axis=AX.X, reads=("spB",), writes=("spB",), tag=cg + "kmax2")
            kb.act(g9f[0:64, 0:T], sb["G5"][0:64, q0c:q0c + T], AF.Square,
                   reads=G5K, writes=("g9t", "g9p0", "g9p1", "g9p2", "g9p3"), tag=cg + "qsq")
            for c_ in range(T // QW):
                kb.mm(ps["p5"][0:1, 0:QW],
                      sb["onescf"][0:64, 0:1], g9f[0:64, c_ * QW:(c_ + 1) * QW],
                      "p5", True, True, reads=("onescf", "g9t"), tag=cg + "qsum")
                kb.act(sb["G7"][0:1, 2 * T + c_ * QW:2 * T + (c_ + 1) * QW],
                       ps["p5"][0:1, 0:QW], AF.Sqrt, scale=sb["spB"][0:1, 8:9],
                       reads=("p5", "spB"), writes=("G7",), tag=cg + "qsh")

            for qc in range(NQC):
                qlo = qc * QCW
                po = "p4"
                nkb = qc * (QCW // 128) + (QCW // 128)

                def st_mm(kb_i):
                    k0 = kb_i * 128
                    q0 = max(qlo, k0)
                    w = qlo + QCW - q0
                    pk = "p%d" % (2 + kb_i % 2)
                    kb.mm(ps[pk][:, 0:w], sb["G6"][0:64, q0c + k0:q0c + k0 + 128],
                          sb["G5"][0:64, q0c + q0:q0c + qlo + QCW],
                          pk, True, False, reads=G5K + G6K, tag=cg + "pb")
                    kb.mm(ps[pk][:, 0:w], sb["onesn"][0:1, :],
                          sb["G7"][0:1, 2 * T + q0:2 * T + qlo + QCW], pk, False, q0 != k0,
                          reads=("onesn", "G7"), tag=cg + "pbs")
                    if q0 == k0:
                        kb.mm(ps[pk][:, 0:128], sb["identb"][:, :], sb["triN"][:, :],
                              pk, False, True, reads=("identb", "triN"), tag=cg + "pbm")
                    pb = "g9p%d" % (kb_i % 4)
                    pbuf = sb["G9"][:, 1024 + (kb_i % 4) * 512:1024 + (kb_i % 4) * 512 + 512]
                    kb.act(pbuf[:, 0:w], ps[pk][:, 0:w], AF.Exp,
                           reads=(pk,), writes=(pb, pk), tag=cg + "pbe")

                def pv_mm(kb_i):
                    k0 = kb_i * 128
                    q0 = max(qlo, k0)
                    w = qlo + QCW - q0
                    pb = "g9p%d" % (kb_i % 4)
                    pbuf = sb["G9"][:, 1024 + (kb_i % 4) * 512:1024 + (kb_i % 4) * 512 + 512]
                    kb.mm(ps[po][0:64, q0 - qlo:QCW], g8v[:, kb_i, hl * 64:(hl + 1) * 64],
                          pbuf[:, 0:w], po, kb_i == 0, kb_i == nkb - 1,
                          reads=("G8r%d" % (kb_i // 4), pb), tag=cg + "pv")
                    kb.mm(ps["p5"][0:1, q0 - qlo:QCW], sb["onescol"][:, 0:1],
                          pbuf[:, 0:w], "p5", kb_i == 0, kb_i == nkb - 1,
                          reads=("onescol", pb), tag=cg + "psum")

                for kb_i in range(nkb + 1):
                    if kb_i < nkb:
                        st_mm(kb_i)
                    if kb_i >= 1:
                        pv_mm(kb_i - 1)
                kb.ve("reciprocal", sb["spB"][0:1, 512:512 + QCW],
                      ps["p5"][0:1, 0:QCW], reads=("p5",), writes=("spB", "p5"), tag=cg + "prc")
                pk = "p%d" % (2 + qc % 2)
                kb.mm(ps[pk][0:64, 0:QCW], sb["onesf"][0:1, 0:64],
                      sb["spB"][0:1, 512:512 + QCW], pk, True, True,
                      reads=("onesf", "spB"), tag=cg + "pbc")
                kb.act(sb["spA"][0:64, 0:QCW], ps[pk][0:64, 0:QCW],
                       AF.Copy, reads=(pk,), writes=("spA", pk), tag=cg + "pbce")
                dtg = hl // 2
                rbase = (hl % 2) * 64
                kb.ve("tensor_mul", sb["G7"][rbase:rbase + 64, dtg * T + qlo:dtg * T + qlo + QCW],
                      ps[po][0:64, 0:QCW], sb["spA"][0:64, 0:QCW],
                      reads=(po, "spA"), writes=("G7", po), tag=cg + "oev")
        for par in range(2):
            kb.dma("pool", sb["G5"][:, par * 2048:par * 2048 + 1024].rearrange(
                "p (a o) -> p a o", a=2),
                P["Wp"][2 * hg:2 * hg + 2, par].rearrange("a p o -> p a o"),
                writes=("G5q%d" % par,), tag=cg + "wp")
            for tb in range(NTT):
                pk = "p%d" % (tb % 2)
                for dtg in range(2):
                    kb.mm(ps[pk][:, 0:512],
                          sb["G7"][0:128, dtg * T + tb * 128:dtg * T + (tb + 1) * 128],
                          sb["G5"][:, par * 2048 + dtg * 512:par * 2048 + dtg * 512 + 512],
                          pk, dtg == 0, dtg == 1 and hg != 0,
                          reads=("G7", "G5q%d" % par), tag=cg + "pj")
                if hg == 0:
                    kb.mm(ps[pk][:, 0:512], sb["onesb"][0:1, :],
                          rows[0:1, par * 512:(par + 1) * 512], pk,
                          False, True, reads=("onesb", "rows"), tag=cg + "pjb")
                ht, hk, co = _htile(kb, tb)
                kb.ve("tensor_add", ht[:, co + par * 512:co + (par + 1) * 512],
                      ht[:, co + par * 512:co + (par + 1) * 512], ps[pk][:, 0:512],
                      reads=(hk, pk), writes=(hk, pk), tag=cg + "pje")

    ln_stats(kb, lambda tt: _htile(kb, tt)[0][:, _htile(kb, tt)[2]:_htile(kb, tt)[2] + 1024],
             NTT, cg + "ln2", "F2", ("G1", "G2"))
    for tt in range(NTT):
        mu, rs = ln_mu_rstd(kb, "F2", tt)
        ht, hk, co = _htile(kb, tt)
        kb.ve("tensor_scalar", tscr[:, 0:1024], ht[:, co:co + 1024], mu, rs,
              ALU.subtract, ALU.mult, reads=(hk, "stats"), writes=("g9t",), tag=cg + "n2")
        for dt in range(8):
            pk = "pt%d" % (dt % 2)
            kb.tr(ps[pk][:, 0:128], tscr[:, dt * 128:(dt + 1) * 128], sb["identb"][:, :],
                  pk, reads=("g9t", "identb"), tag=cg + "t2")
            ft, fk, fo = _fm(kb, dt)
            kb.act(ft[:, fo + tt * 128:fo + (tt + 1) * 128], ps[pk][:, 0:128], AF.Copy,
                   scale=sb["ln2g"][:, dt:dt + 1], bias=sb["ln2b"][:, dt:dt + 1],
                   reads=(pk, "ln2g", "ln2b"), writes=fk + (pk,), tag=cg + "t2e")
        kb.ve("tensor_add", xv[:, tt, :], xv[:, tt, :], ht[:, co:co + 1024],
              reads=(xtk(tt), hk), writes=(xtk(tt),), tag=cg + "xf")

    for tt in range(NTT):
        pk = "p%d" % (tt % 2)
        for kt in range(8):
            ft, fk, fo = _fm(kb, kt)
            kb.mm(ps[pk][:, 0:NE], ft[:, fo + tt * 128:fo + (tt + 1) * 128],
                  sb["gwt"][:, :].rearrange("p (k e) -> p k e", k=8)[:, kt, :],
                  pk, kt == 0, False, reads=("gwt",) + fk, tag=cg + "g")
        kb.mm(ps[pk][:, 0:NE], sb["onesb"][0:1, :], sb["attsh"][0:1, 24:24 + NE],
              pk, False, True, reads=("onesb", "attsh"), tag=cg + "gb")
        kb.ve("tensor_copy", sb["lgt"][:, tt * NE:(tt + 1) * NE], ps[pk][:, 0:NE],
              reads=(pk,), writes=("lgt", pk), tag=cg + "ge")
    kb.ve("tensor_scalar_add", sb["ascr"][:, 8:16], sb["iota8"][:, :], -8.0,
          reads=("iota8",), writes=("ascr",), tag=cg + "iom8")
    for tt in range(NTT):
        g_ = sb["spA"][:, 960:1008]
        lg = sb["lgt"][:, tt * NE:(tt + 1) * NE]
        wf = sb["wfl"][:, tt * NE:(tt + 1) * NE]
        kb.ve("reduce_max", g_[:, 0:1], lg, axis=AX.X, reads=("lgt",), writes=("spA",), tag=cg + "m1")
        kb.ve("tensor_scalar", g_[:, 8:16], lg, g_[:, 0:1], None, ALU.is_equal,
              reads=("lgt", "spA"), writes=("spA",), tag=cg + "eq1")
        kb.ve("scalar_tensor_tensor", g_[:, 16:24], g_[:, 8:16], 1.0, sb["ascr"][:, 8:16],
              ALU.mult, ALU.mult, reads=("spA", "ascr"), writes=("spA",), tag=cg + "sel1a")
        kb.ve("tensor_scalar_add", g_[:, 16:24], g_[:, 16:24], 8.0,
              reads=("spA",), writes=("spA",), tag=cg + "sel1b")
        kb.ve("tensor_reduce", g_[:, 1:2], g_[:, 16:24], AX.X, ALU.min,
              reads=("spA",), writes=("spA",), tag=cg + "i1")
        kb.ve("tensor_scalar", g_[:, 8:16], sb["iota8"][:, :], g_[:, 1:2], None, ALU.is_equal,
              reads=("spA", "iota8"), writes=("spA",), tag=cg + "es1")
        kb.ve("scalar_tensor_tensor", g_[:, 24:32], g_[:, 8:16], NEG, lg,
              ALU.mult, ALU.add, reads=("spA", "lgt"), writes=("spA",), tag=cg + "lg2")
        kb.ve("reduce_max", g_[:, 2:3], g_[:, 24:32], axis=AX.X,
              reads=("spA",), writes=("spA",), tag=cg + "m2")
        kb.ve("tensor_scalar", g_[:, 32:40], g_[:, 24:32], g_[:, 2:3], None, ALU.is_equal,
              reads=("spA",), writes=("spA",), tag=cg + "eq2")
        kb.ve("scalar_tensor_tensor", g_[:, 16:24], g_[:, 32:40], 1.0, sb["ascr"][:, 8:16],
              ALU.mult, ALU.mult, reads=("spA", "ascr"), writes=("spA",), tag=cg + "sel2a")
        kb.ve("tensor_scalar_add", g_[:, 16:24], g_[:, 16:24], 8.0,
              reads=("spA",), writes=("spA",), tag=cg + "sel2b")
        kb.ve("tensor_reduce", g_[:, 3:4], g_[:, 16:24], AX.X, ALU.min,
              reads=("spA",), writes=("spA",), tag=cg + "i2")
        kb.ve("tensor_scalar", g_[:, 32:40], sb["iota8"][:, :], g_[:, 3:4], None, ALU.is_equal,
              reads=("spA", "iota8"), writes=("spA",), tag=cg + "es2")
        kb.ve("tensor_sub", g_[:, 4:5], g_[:, 0:1], g_[:, 2:3],
              reads=("spA",), writes=("spA",), tag=cg + "d12")
        kb.act(g_[:, 5:6], g_[:, 4:5], AF.Sigmoid, reads=("spA",), writes=("spA",), tag=cg + "w1")
        kb.act(g_[:, 6:7], g_[:, 4:5], AF.Sigmoid, scale=-1.0,
               reads=("spA",), writes=("spA",), tag=cg + "w2")
        kb.ve("tensor_scalar", g_[:, 40:48], g_[:, 8:16], g_[:, 5:6], None, ALU.mult,
              reads=("spA",), writes=("spA",), tag=cg + "wa")
        kb.ve("scalar_tensor_tensor", wf, g_[:, 32:40], g_[:, 6:7], g_[:, 40:48],
              ALU.mult, ALU.add, reads=("spA",), writes=("wfl",), tag=cg + "wb")

    TCW = min(512, T)
    NTCH = T // TCW
    midv1 = sb["G1"][:, :].rearrange("p (h t) -> p h t", h=16)
    midv2 = sb["G2"][:, :].rearrange("p (h t) -> p h t", h=16)
    for e in range(NE):
        kb.dma("pool", sb["G7"][0:1, 3 * T:3 * T + 1024], P["b2"][e:e + 1, :],
               writes=("G7",), sgrp="r_b2", tag=cg + "b2l")
        for tc in range(NTCH):
            for o8 in range(8):
                woff = (o8 % 2) * 4096
                kb.dma("pool", sb["G5"][:, woff:woff + 4096].rearrange(
                    "p (k h) -> p k h", k=8),
                    P["W1"][e, o8], writes=G5K[2 * (o8 % 2):2 * (o8 % 2) + 2],
                    sgrp="w1l%d" % (o8 % 2), tag=cg + "w1l")
                for htl in range(4):
                    ht = o8 * 4 + htl
                    pk = "p%d" % (4 + ht % 2)
                    for kt in range(8):
                        ft, fk, fo = _fm(kb, kt)
                        kb.mm(ps[pk][:, 0:TCW], sb["G5"][:, woff + kt * 512 + htl * 128:
                                                         woff + kt * 512 + (htl + 1) * 128],
                              ft[:, fo + tc * TCW:fo + (tc + 1) * TCW], pk,
                              kt == 0, kt == 7,
                              reads=G5K[2 * (o8 % 2):2 * (o8 % 2) + 2] + fk, tag=cg + "m1")
                    mv = midv1 if ht < 16 else midv2
                    mk = "G1" if ht < 16 else "G2"
                    kb.act(mv[:, ht % 16, 0:TCW], ps[pk][:, 0:TCW], AF.Gelu,
                           bias=sb["b1pp"][:, :].rearrange("p (e h) -> p e h", e=NE)[:, e, ht:ht + 1],
                           reads=(pk, "b1pp"), writes=(mk, pk), tag=cg + "m1e")
            for oh in range(2):
                for ktg in range(4):
                    woff = (ktg % 2) * 4096
                    kb.dma("pool", sb["G6"][:, woff:woff + 4096].rearrange(
                        "p (k o) -> p k o", k=8),
                        P["W2"][e, ktg, oh], writes=G6K[2 * (ktg % 2):2 * (ktg % 2) + 2],
                        sgrp="w2l%d" % (ktg % 2), tag=cg + "w2l")
                    for kt in range(8):
                        htg = ktg * 8 + kt
                        mv = midv1 if htg < 16 else midv2
                        mk = "G1" if htg < 16 else "G2"
                        for tb in range(TCW // 128):
                            pk = "p%d" % (tb % 4)
                            kb.mm(ps[pk][:, 0:512],
                                  mv[:, htg % 16, tb * 128:(tb + 1) * 128],
                                  sb["G6"][:, woff + kt * 512:woff + (kt + 1) * 512],
                                  pk, htg == 0, False,
                                  reads=G6K[2 * (ktg % 2):2 * (ktg % 2) + 2] + (mk,), tag=cg + "m2")
                for tb in range(TCW // 128):
                    pk = "p%d" % (tb % 4)
                    kb.mm(ps[pk][:, 0:512], sb["onesb"][0:1, :],
                          sb["G7"][0:1, 3 * T + oh * 512:3 * T + (oh + 1) * 512], pk,
                          False, True, reads=("onesb", "G7"), tag=cg + "m2b")
                    tt = tc * (TCW // 128) + tb
                    kb.ve("scalar_tensor_tensor", xv[:, tt, oh * 512:(oh + 1) * 512],
                          ps[pk][:, 0:512], sb["wfl"][:, tt * NE + e:tt * NE + e + 1],
                          xv[:, tt, oh * 512:(oh + 1) * 512], ALU.mult, ALU.add,
                          reads=(pk, "wfl", xtk(tt)), writes=(xtk(tt), pk), tag=cg + "m2x")
    if ci == 1:
        kb.dbg("x_after_f1", xtm[:, :], [128, NTT * 1024], BF16, reads=tuple("xt%d" % i for i in range(NTT)))


def _bu(kb, im, pt):
    T = kb.T
    if not im:
        gk = G5K if pt < 4 else G6K
        g = "G5" if pt < 4 else "G6"
        reg = ((pt % 4) * T) // 2048
        return kb.sb[g], gk[reg], (pt % 4) * T
    if pt < 4:
        return kb.sb["G7"], "G7", pt * T
    g = "G8" if pt < 6 else "G9"
    return kb.sb[g], g, (pt % 2) * T


G5K = ("G5q0", "G5q1", "G5q2", "G5q3")
G6K = ("G6q0", "G6q1", "G6q2", "G6q3")

_LREG = [("G5", 0, 4, 0), ("G6", 0, 4, 4), ("G7", 1, 4, 0), ("G8", 1, 2, 4), ("G9", 1, 2, 6)]
_LKEY = {"G5": G5K, "G6": G6K, "G7": ("G7",), "G8": ("G8",), "G9": ("G9",)}


def stage_lru(kb, P):
    T, NTT = kb.T, kb.NTT
    sb, ps = kb.sb, kb.ps
    xtm = sb["xtm"]
    xv = xtm[:, :].rearrange("p (t d) -> p t d", d=1024)
    NCH = T // 128
    lam = sb["lam"]
    cg = "l_"

    kb.dma("sp", sb["nuv"][:, :], P["nu"], writes=("nuv",), sgrp="c_nu", tag=cg + "pnu")
    kb.dma("sp", sb["thv"][:, :], P["th"], writes=("thv",), sgrp="c_th", tag=cg + "pth")
    kb.dma("sp", sb["gamv"][:, :], P["gam"], writes=("gamv",), sgrp="c_gam", tag=cg + "pgam")
    kb.act(lam[:, 16:24], sb["nuv"][:, :], AF.Exp, reads=("nuv",), writes=("lam",), tag=cg + "e1")
    kb.act(lam[:, 24:32], lam[:, 16:24], AF.Exp, scale=-1.0,
           reads=("lam",), writes=("lam",), tag=cg + "r")
    kb.act(lam[:, 32:40], sb["thv"][:, :], AF.Exp, reads=("thv",), writes=("lam",), tag=cg + "th")
    kb.act(lam[:, 40:48], lam[:, 32:40], AF.Sin, scale=0.5,
           reads=("lam",), writes=("lam",), tag=cg + "s2")
    kb.act(lam[:, 48:56], lam[:, 32:40], AF.Sin, scale=0.25,
           reads=("lam",), writes=("lam",), tag=cg + "s4")
    kb.act(lam[:, 56:64], lam[:, 40:48], AF.Square, reads=("lam",), writes=("lam",), tag=cg + "q2")
    kb.ve("tensor_scalar", lam[:, 56:64], lam[:, 56:64], -2.0, 1.0, ALU.mult, ALU.add,
          reads=("lam",), writes=("lam",), tag=cg + "cos")
    kb.act(lam[:, 48:56], lam[:, 48:56], AF.Square, reads=("lam",), writes=("lam",), tag=cg + "q4")
    kb.ve("tensor_scalar", lam[:, 48:56], lam[:, 48:56], -2.0, 1.0, ALU.mult, ALU.add,
          reads=("lam",), writes=("lam",), tag=cg + "c2")
    kb.ve("tensor_mul", lam[:, 40:48], lam[:, 40:48], lam[:, 48:56],
          reads=("lam",), writes=("lam",), tag=cg + "sc")
    kb.ve("tensor_scalar_mul", lam[:, 40:48], lam[:, 40:48], 2.0,
          reads=("lam",), writes=("lam",), tag=cg + "sin")
    kb.ve("tensor_mul", lam[:, 0:8], lam[:, 24:32], lam[:, 56:64],
          reads=("lam",), writes=("lam",), tag=cg + "lr")
    kb.ve("tensor_mul", lam[:, 8:16], lam[:, 24:32], lam[:, 40:48],
          reads=("lam",), writes=("lam",), tag=cg + "li")
    kb.act(sb["gexp"][:, :], sb["gamv"][:, :], AF.Exp, reads=("gamv",), writes=("gexp",), tag=cg + "g")
    kb.dma("pool", kb.sb["rows"][0:1, 1024:2048], P["Dlru"], writes=("rows",), sgrp="r_dl", tag=cg + "dl")
    kb.mm(ps["p5"][:, 0:512], sb["onesb"][0:1, :], sb["rows"][0:1, 1024:1536], "p5",
          True, True, reads=("onesb", "rows"), tag=cg + "db1")
    kb.act(sb["wsV"][:, 0:512], ps["p5"][:, 0:512], AF.Copy, reads=("p5",),
           writes=("wsV0", "p5"), tag=cg + "db1e")
    kb.mm(ps["p5"][:, 0:512], sb["onesb"][0:1, :], sb["rows"][0:1, 1536:2048], "p5",
          True, True, reads=("onesb", "rows"), tag=cg + "db2")
    kb.act(sb["wsV"][:, 512:1024], ps["p5"][:, 0:512], AF.Copy, reads=("p5",),
           writes=("wsV0", "p5"), tag=cg + "db2e")

    lprv = sb["lpr"].rearrange("p (e j) -> p e j", e=8)
    lpiv = sb["lpi"].rearrange("p (e j) -> p e j", e=8)
    kb.ve("tensor_copy", lprv[:, :, 0:1], lam[:, 0:8][:, :, None],
          reads=("lam",), writes=("lpr",), tag=cg + "p0r")
    kb.ve("tensor_copy", lpiv[:, :, 0:1], lam[:, 8:16][:, :, None],
          reads=("lam",), writes=("lpi",), tag=cg + "p0i")
    w = 1
    while w < 128:
        m = min(w, 128 - w)
        ar = lprv[:, :, w - 1:w].broadcast_to([128, 8, m])
        ai = lpiv[:, :, w - 1:w].broadcast_to([128, 8, m])
        sA = sb["spA"].rearrange("p (e j) -> p e j", e=8)[:, :, 0:m]
        sB = sb["spB"].rearrange("p (e j) -> p e j", e=8)[:, :, 0:m]
        kb.ve("tensor_mul", sA, lprv[:, :, 0:m], ar, reads=("lpr",), writes=("spA", "sC", "sD"), tag=cg + "pd1")
        kb.ve("tensor_mul", sB, lpiv[:, :, 0:m], ai, reads=("lpr", "lpi"), writes=("spB",), tag=cg + "pd2")
        kb.ve("tensor_sub", sA, sA, sB, reads=("spA", "sC", "sD", "spB"), writes=("spA", "sC", "sD"), tag=cg + "pd3")
        kb.ve("tensor_mul", sB, lprv[:, :, 0:m], ai, reads=("lpr", "lpi"), writes=("spB",), tag=cg + "pd4")
        kb.ve("tensor_copy", lprv[:, :, w:w + m], sA, reads=("spA", "sC", "sD"), writes=("lpr",), tag=cg + "pd5")
        kb.ve("tensor_mul", sA, lpiv[:, :, 0:m], ar, reads=("lpi",), writes=("spA", "sC", "sD"), tag=cg + "pd6")
        kb.ve("tensor_add", lpiv[:, :, w:w + m], sA, sB, reads=("spA", "sC", "sD", "spB"), writes=("lpi",), tag=cg + "pd7")
        w *= 2

    ln_stats(kb, lambda tt: xv[:, tt, :], NTT, cg + "ln", "L", tuple("xt%d" % i for i in range(NTT)))
    for tt in range(NTT):
        mu, rs = ln_mu_rstd(kb, "L", tt)
        ht, hk, co = _htile(kb, tt)
        kb.ve("tensor_scalar", ht[:, co:co + 1024], xv[:, tt, :], mu, rs,
              ALU.subtract, ALU.mult, reads=(xtk(tt), "stats"), writes=(hk,), tag=cg + "n")
        for dt in range(8):
            pk = "pt%d" % (dt % 2)
            kb.tr(ps[pk][:, 0:128], ht[:, co + dt * 128:co + (dt + 1) * 128],
                  sb["identb"][:, :], pk, reads=(hk, "identb"), tag=cg + "t")
            ft, fk, fo = _fm(kb, dt)
            kb.act(ft[:, fo + tt * 128:fo + (tt + 1) * 128], ps[pk][:, 0:128], AF.Copy,
                   reads=(pk,), writes=fk + (pk,), tag=cg + "te")

    for pt in range(8):
        for im in range(2):
            wsb = "wsA%d" % ((pt * 2 + im) % 3)
            src = P["BimT"] if im else P["BreT"]
            kb.dma("pool", sb[wsb][:, :].rearrange("p (k o) -> p k o", k=8),
                   src[pt], writes=(wsb,), tag=cg + "bw")
            bt, bk_, bco = _bu(kb, im, pt)
            BW = min(512, T)
            for tc in range(T // BW):
                pk = "p%d" % (2 * im + tc % 2)
                for kt in range(8):
                    ft, fk, fo = _fm(kb, kt)
                    kb.mm(ps[pk][:, 0:BW], sb[wsb][:, :].rearrange(
                        "p (k o) -> p k o", k=8)[:, kt, :],
                        ft[:, fo + tc * BW:fo + (tc + 1) * BW], pk, kt == 0, kt == 7,
                        reads=(wsb,) + fk, tag=cg + "bu")
                kb.act(bt[:, bco + tc * BW:bco + (tc + 1) * BW], ps[pk][:, 0:BW],
                       AF.Copy, scale=sb["gexp"][:, pt:pt + 1],
                       reads=(pk, "gexp"), writes=(bk_, pk), tag=cg + "bue")

    kb.dbg("bu_re", sb["G5"][:, 0:4 * T], [128, 4 * T], BF16, reads=G5K)
    kb.dbg("bu_im", sb["G7"][:, 0:4 * T], [128, 4 * T], BF16, reads=("G7",))
    lh = sb["lh"]
    kb.ve("memset", lh[:, :], 0.0, writes=("lhr", "lhi"), tag=cg + "h0")
    hre = lh[:, 0:128].rearrange("p (e c) -> p e c", e=8)[:, :, 0:NCH]
    him = lh[:, 128:256].rearrange("p (e c) -> p e c", e=8)[:, :, 0:NCH]
    lrv = lam[:, 0:8][:, :, None].broadcast_to([128, 8, NCH])
    liv = lam[:, 8:16][:, :, None].broadcast_to([128, 8, NCH])
    sA8 = sb["spA"][:, 0:128].rearrange("p (e c) -> p e c", e=8)[:, :, 0:NCH]
    sB8 = sb["spB"][:, 0:128].rearrange("p (e c) -> p e c", e=8)[:, :, 0:NCH]
    sC8 = sb["spA"][:, 128:256].rearrange("p (e c) -> p e c", e=8)[:, :, 0:NCH]

    def rview(g, npt, bpt, j):
        return sb[g][:, :].rearrange("p (a c j) -> p a c j", a=8192 // 128 // 8 if False else None)

    def bu_col(g, npt, j):
        return sb[g][:, 0:npt * T].rearrange("p (a c j) -> p a c j", a=npt, j=128)[:, :, 0:NCH, j]

    sD8 = sb["dacc"][:, 128:256].rearrange("p (e c) -> p e c", e=8)[:, :, 0:NCH]
    for j in range(128):
        kb.ve("tensor_mul", sA8, hre, lrv, reads=("lhr", "lam"), writes=("spA",), tag=cg + "a1")
        kb.ve("tensor_mul", sC8, him, liv, reads=("lhi", "lam"), writes=("sC",), tag=cg + "a2")
        kb.ve("tensor_sub", sA8, sA8, sC8, reads=("spA", "sC"), writes=("spA",), tag=cg + "a3")
        kb.ve("tensor_mul", sB8, him, lrv, reads=("lhi", "lam"), writes=("spB",),
              eng_name="pool", tag=cg + "b1")
        kb.ve("tensor_mul", sD8, hre, liv, reads=("lhr", "lam"), writes=("sD",),
              eng_name="pool", tag=cg + "b2")
        kb.ve("tensor_add", sB8, sB8, sD8, reads=("spB", "sD"), writes=("spB",),
              eng_name="pool", tag=cg + "b3")
        for g, im, npt, bpt in _LREG:
            hv = (him if im else hre)[:, bpt:bpt + npt, :]
            tv = (sB8 if im else sA8)[:, bpt:bpt + npt, :]
            bj = bu_col(g, npt, j)
            kb.ve("tensor_add", hv, tv, bj, reads=("spB" if im else "spA",) + _LKEY[g],
                  writes=("lhi" if im else "lhr",),
                  eng_name="pool" if im else "dve", tag=cg + "hu")
            kb.act(bj, hv, AF.Copy, reads=("lhi" if im else "lhr",),
                   writes=_LKEY[g], tag=cg + "xw")
        if j % 4 == 0:
            kb.mm(ps["p5"][0:16, 0:16], sb["onesb"][0:1, 0:16], sb["onesb"][0:1, 0:16],
                  "p5", True, True, reads=("onesb", "lhr"), tag=cg + "warm")

    xer = hre
    xei = him
    kb.ve("tensor_copy", lam[:, 16:24], lam[:, 0:8], reads=("lam",), writes=("lam",), tag=cg + "q0")
    kb.ve("tensor_copy", lam[:, 24:32], lam[:, 8:16], reads=("lam",), writes=("lam",), tag=cg + "q0i")
    for _sq in range(7):
        kb.ve("tensor_mul", lam[:, 32:40], lam[:, 16:24], lam[:, 16:24],
              reads=("lam",), writes=("lam",), tag=cg + "sq1")
        kb.ve("tensor_mul", lam[:, 40:48], lam[:, 24:32], lam[:, 24:32],
              reads=("lam",), writes=("lam",), tag=cg + "sq2")
        kb.ve("tensor_mul", lam[:, 48:56], lam[:, 16:24], lam[:, 24:32],
              reads=("lam",), writes=("lam",), tag=cg + "sq3")
        kb.ve("tensor_sub", lam[:, 16:24], lam[:, 32:40], lam[:, 40:48],
              reads=("lam",), writes=("lam",), tag=cg + "sq4")
        kb.ve("tensor_scalar_mul", lam[:, 24:32], lam[:, 48:56], 2.0,
              reads=("lam",), writes=("lam",), tag=cg + "sq5")
    l128r = lam[:, 16:24][:, :, None]
    l128i = lam[:, 24:32][:, :, None]
    xer = hre
    xei = him
    for c in range(1, NCH):
        sA1 = sb["spA"].rearrange("p (e c) -> p e c", e=8)[:, :, 0:1]
        sB1 = sb["spB"].rearrange("p (e c) -> p e c", e=8)[:, :, 0:1]
        sA2 = sb["spA"].rearrange("p (e c) -> p e c", e=8)[:, :, 1:2]
        kb.ve("tensor_mul", sA1, xer[:, :, c - 1:c], l128r, reads=("lhr", "lam"), writes=("spA", "sC", "sD"), tag=cg + "xc1")
        kb.ve("tensor_mul", sB1, xei[:, :, c - 1:c], l128i, reads=("lhi", "lam"), writes=("spB",), tag=cg + "xc2")
        kb.ve("tensor_sub", sA1, sA1, sB1, reads=("spA", "sC", "sD", "spB"), writes=("spA", "sC", "sD"), tag=cg + "xc3")
        kb.ve("tensor_mul", sB1, xei[:, :, c - 1:c], l128r, reads=("lhi", "lam"), writes=("spB",), tag=cg + "xc5")
        kb.ve("tensor_mul", sA2, xer[:, :, c - 1:c], l128i, reads=("lhr", "lam"), writes=("spA", "sC", "sD"), tag=cg + "xc6")
        kb.ve("tensor_add", xer[:, :, c:c + 1], xer[:, :, c:c + 1], sA1,
              reads=("lhr", "spA", "sC", "sD"), writes=("lhr",), tag=cg + "xc4")
        kb.ve("tensor_add", sA2, sA2, sB1, reads=("spA", "sC", "sD", "spB"), writes=("spA", "sC", "sD"), tag=cg + "xc7")
        kb.ve("tensor_add", xei[:, :, c:c + 1], xei[:, :, c:c + 1], sA2,
              reads=("lhi", "spA", "sC", "sD"), writes=("lhi",), tag=cg + "xc8")

    for c in range(1, NCH):
        for g, im, npt, bpt in _LREG:
            xcol = sb[g][:, 0:npt * T].rearrange("p (a c j) -> p a c j", a=npt, j=128)[:, :, c, :]
            pr = lprv[:, bpt:bpt + npt, :]
            pi = lpiv[:, bpt:bpt + npt, :]
            XsR = hre[:, bpt:bpt + npt, c - 1:c].broadcast_to([128, npt, 128])
            XsI = him[:, bpt:bpt + npt, c - 1:c].broadcast_to([128, npt, 128])
            sA = sb["spA"].rearrange("p (e j) -> p e j", e=8)[:, 0:npt, :]
            sB = sb["spB"].rearrange("p (e j) -> p e j", e=8)[:, 0:npt, :]
            if not im:
                kb.ve("tensor_mul", sA, pr, XsR, reads=("lpr", "lhr"), writes=("spA", "sC", "sD"), tag=cg + "c1")
                kb.ve("tensor_mul", sB, pi, XsI, reads=("lpi", "lhi"), writes=("spB",), tag=cg + "c2")
                kb.ve("tensor_sub", sA, sA, sB, reads=("spA", "sC", "sD", "spB"), writes=("spA", "sC", "sD"), tag=cg + "c3")
            else:
                kb.ve("tensor_mul", sA, pr, XsI, reads=("lpr", "lhi"), writes=("spA", "sC", "sD"), tag=cg + "c4")
                kb.ve("tensor_mul", sB, pi, XsR, reads=("lpi", "lhr"), writes=("spB",), tag=cg + "c5")
                kb.ve("tensor_add", sA, sA, sB, reads=("spA", "sC", "sD", "spB"), writes=("spA", "sC", "sD"), tag=cg + "c6")
            kb.ve("tensor_add", xcol, xcol, sA, reads=_LKEY[g] + ("spA", "sC", "sD"),
                  writes=_LKEY[g], tag=cg + "c7")

    kb.dbg("xsc_re", sb["G5"][:, 0:4 * T], [128, 4 * T], BF16, reads=G5K)
    kb.dbg("xsc_im", sb["G7"][:, 0:4 * T], [128, 4 * T], BF16, reads=("G7",))
    for tt in range(NTT):
        ht, hk, co = _htile(kb, tt)
        kb.ve("tensor_mul", sb["spA"][:, 0:1024], ht[:, co:co + 1024],
              sb["wsV"][:, 0:1024], reads=(hk, "wsV0"), writes=("spA", "sC", "sD"), tag=cg + "du")
        kb.ve("tensor_add", xv[:, tt, :], xv[:, tt, :], sb["spA"][:, 0:1024],
              reads=(xtk(tt), "spA", "sC", "sD"), writes=(xtk(tt),), tag=cg + "dua")
    for tbg in range(NTT // 4 + (1 if NTT % 4 else 0)):
        tbs = range(tbg * 4, min(NTT, tbg * 4 + 4))
        for oh in range(2):
            for pt in range(8):
                wpar = pt % 2
                kb.dma("pool", sb["wsV"][:, wpar * 1024:wpar * 1024 + 512],
                       P["CreT"][pt, oh], writes=("wsV%d" % wpar,), tag=cg + "cr")
                kb.dma("pool", sb["wsV"][:, wpar * 1024 + 512:wpar * 1024 + 1024],
                       P["CimT"][pt, oh], writes=("wsV%d" % wpar,), tag=cg + "ci")
                kb.ve("tensor_scalar_mul", sb["wsV"][:, wpar * 1024 + 512:wpar * 1024 + 1024],
                      sb["wsV"][:, wpar * 1024 + 512:wpar * 1024 + 1024], -1.0,
                      reads=("wsV%d" % wpar,), writes=("wsV%d" % wpar,), tag=cg + "cin")
                for tb in tbs:
                    pk = "p%d" % (tb % 4)
                    rt, rk, rco = _bu(kb, 0, pt)
                    it, ik, ico = _bu(kb, 1, pt)
                    kb.mm(ps[pk][:, 0:512], rt[:, rco + tb * 128:rco + (tb + 1) * 128],
                          sb["wsV"][:, wpar * 1024:wpar * 1024 + 512], pk,
                          pt == 0, False, reads=(rk, "wsV%d" % wpar), tag=cg + "cx1")
                    kb.mm(ps[pk][:, 0:512], it[:, ico + tb * 128:ico + (tb + 1) * 128],
                          sb["wsV"][:, wpar * 1024 + 512:wpar * 1024 + 1024], pk,
                          False, pt == 7, reads=(ik, "wsV%d" % wpar), tag=cg + "cx2")
            for tb in tbs:
                pk = "p%d" % (tb % 4)
                kb.ve("tensor_add", xv[:, tb, oh * 512:(oh + 1) * 512],
                      xv[:, tb, oh * 512:(oh + 1) * 512], ps[pk][:, 0:512],
                      reads=(xtk(tb), pk), writes=(xtk(tb), pk), tag=cg + "cxe")
    kb.dbg("x_after_b", xtm[:, :], [128, NTT * 1024], BF16, reads=tuple("xt%d" % i for i in range(NTT)))


# version 20
# speedup vs baseline: 1.2082x; 1.0071x over previous
import numpy as np
import ml_dtypes
import concourse.bass as bass
import concourse.mybir as mybir
from concourse.bass_utils import run_bass_kernel_spmd
from contextlib import ExitStack

F32 = mybir.dt.float32
BF16 = mybir.dt.bfloat16
AF = mybir.ActivationFunctionType
ALU = mybir.AluOpType
AX = mybir.AxisListType

D = 1024
DI = 2048
DS = 16
DTR = 64
RK = 16
NH = 16
HD = 64
NE = 8
FF = 4096
EPS = 1e-5
NEG = -1e30



class Prog:

    def __init__(self):
        self.ops = []
        self.lastw = {}
        self.readers = {}
        self.qcount = {}
        self.alias = {}

    def op(self, eng, emit, reads=(), writes=(), q=None, sgrp=None, tag=""):
        reads = tuple(self.alias.get(r, r) for r in reads)
        writes = tuple(self.alias.get(w, w) for w in writes)
        i = len(self.ops)
        rec = dict(i=i, eng=eng, emit=emit, q=q, sgrp=sgrp, qidx=None, inc=False,
                   qwaits={}, ewaits={}, sdep=-1, tag=tag)
        if q is not None:
            if sgrp is None:
                sgrp = q + ":" + (writes[0] if writes else "misc")
            rec["sgrp"] = sgrp
            self.qcount[sgrp] = self.qcount.get(sgrp, 0) + 1
            rec["qidx"] = self.qcount[sgrp]
        deps = set()
        for r in reads:
            d = self.lastw.get(r)
            if d is not None:
                deps.add(d)
        for w in writes:
            d = self.lastw.get(w)
            if d is not None:
                deps.add(d)
            deps.update(self.readers.get(w, ()))
        deps.discard(i)
        for d in deps:
            dr = self.ops[d]
            if dr["q"] is not None:
                rec["qwaits"][dr["sgrp"]] = max(rec["qwaits"].get(dr["sgrp"], 0), 16 * dr["qidx"])
            elif dr["eng"] == eng and q is None and eng == "pe":
                pass
            elif dr["eng"] == eng:
                rec["sdep"] = max(rec["sdep"], d)
            else:
                dr["inc"] = True
                rec["ewaits"][dr["eng"]] = max(rec["ewaits"].get(dr["eng"], -1), d)
        for r in reads:
            self.readers.setdefault(r, []).append(i)
        for w in writes:
            self.lastw[w] = i
            self.readers[w] = []
        self.ops.append(rec)
        return i

    def emit_all(self, block, esem, qsem):
        incnum = {}
        counts = {}
        for rec in self.ops:
            if rec["inc"] and rec["q"] is None:
                counts[rec["eng"]] = counts.get(rec["eng"], 0) + 1
                incnum[rec["i"]] = counts[rec["eng"]]
        self.final_inc_counts = counts
        self.final_q_counts = {g: 16 * n for g, n in self.qcount.items()}

        streams = {e: [rec for rec in self.ops if rec["eng"] == e]
                   for e in ("pe", "act", "dve", "pool", "sp")}

        def run_stream(eng_name):
            def body(eng):
                wm_q = {}
                wm_e = {}
                last_drain = -1
                prev_idx = -1
                for rec in streams[eng_name]:
                    for qn, val in sorted(rec["qwaits"].items()):
                        if wm_q.get(qn, 0) < val:
                            eng.wait_ge(qsem[qn], val)
                            wm_q[qn] = val
                    for en, didx in sorted(rec["ewaits"].items()):
                        val = incnum[didx]
                        if wm_e.get(en, 0) < val:
                            eng.wait_ge(esem[en], val)
                            wm_e[en] = val
                    if rec["sdep"] > last_drain:
                        eng.drain()
                        last_drain = prev_idx
                    inst = rec["emit"](eng)
                    if rec["q"] is not None:
                        inst.then_inc(qsem[rec["sgrp"]], 16)
                    elif rec["inc"]:
                        inst.then_inc(esem[eng_name], 1)
                    prev_idx = rec["i"]
            return body

        block.tensor(run_stream("pe"))
        block.scalar(run_stream("act"))
        block.vector(run_stream("dve"))
        block.gpsimd(run_stream("pool"))
        block.sync(run_stream("sp"))

        def finale(eng):
            for qn, val in sorted(self.final_q_counts.items()):
                if val:
                    eng.wait_ge(qsem[qn], val)
            for en, val in sorted(self.final_inc_counts.items()):
                eng.wait_ge(esem[en], val)
        block.sync(finale)


class KB:

    def __init__(self, nc, T, dbg_names=()):
        self.nc = nc
        self.T = T
        self.NTT = T // 128
        self.p = Prog()
        self.st = ExitStack()
        self.sb = {}
        self.ps = {}
        self.dram_in = {}
        self.dram_out = {}
        self.dbg_names = set(dbg_names)
        self.sbuf_bytes = 0

    def din(self, name, shape, dt=F32):
        h = self.nc.dram_tensor(name, list(shape), dt, kind="ExternalInput")
        self.dram_in[name] = h.ap()
        return self.dram_in[name]

    def dout(self, name, shape, dt=F32):
        h = self.nc.dram_tensor(name, list(shape), dt, kind="ExternalOutput")
        self.dram_out[name] = h.ap()
        return self.dram_out[name]

    def t(self, name, shape, dt=BF16):
        h = self.st.enter_context(self.nc.sbuf_tensor(name, list(shape), dt))
        self.sb[name] = h
        esz = {F32: 4, BF16: 2}[dt]
        self.sbuf_bytes += int(np.prod(shape[1:])) * esz
        return h

    def pt(self, name, shape, dt=F32):
        h = self.st.enter_context(self.nc.psum_tensor(name, list(shape), dt))
        self.ps[name] = h
        return h

    def dma(self, qname, out_ap, in_ap, reads=(), writes=(), sgrp=None, tag=""):
        def emit(eng):
            return eng.dma_start(out=out_ap, in_=in_ap)
        self.p.op(qname, emit, reads=reads, writes=writes, q=qname, sgrp=sgrp, tag=tag)

    def mm(self, out_ap, lhsT_ap, rhs_ap, psum_key, start, stop, reads=(), tag=""):
        def emit(eng):
            return eng.matmul(out_ap, lhsT_ap, rhs_ap, start=start, stop=stop)
        self.p.op("pe", emit, reads=tuple(reads), writes=(psum_key,), tag=tag)

    def tr(self, out_ap, in_ap, ident_ap, psum_key, reads=(), tag=""):
        def emit(eng):
            return eng.transpose(out_ap, in_ap, ident_ap)
        self.p.op("pe", emit, reads=tuple(reads), writes=(psum_key,), tag=tag)

    def act(self, out_ap, in_ap, func, reads=(), writes=(), scale=None, bias=None,
            accum=None, tag=""):
        if func == AF.Copy and bias is not None and not isinstance(bias, float):
            func = AF.Identity
        kw = {}
        if scale is not None:
            kw["scale"] = scale
        if bias is not None:
            kw["bias"] = bias
        if accum is not None:
            kw["accum_out"] = accum

        def emit(eng):
            return eng.activation(out_ap, in_ap, func, **kw)
        self.p.op("act", emit, reads=tuple(reads), writes=tuple(writes), tag=tag)

    def ve(self, method, *args, reads=(), writes=(), eng_name="dve", tag="", **kw):
        def emit(eng):
            return getattr(eng, method)(*args, **kw)
        self.p.op(eng_name, emit, reads=tuple(reads), writes=tuple(writes), tag=tag)

    def dbg(self, name, ap, shape, dt=F32, reads=()):
        if name not in self.dbg_names:
            return
        o = self.dout("dbg_" + name, shape, dt)
        self.dma("sp", o, ap, reads=tuple(reads), sgrp="dbg_" + name, tag="dbg" + name)



def ln_stats(kb, src_fn, n_tiles, tag, skey, keys_read):
    sm = kb.sb["stats"]
    bn = kb.sb["spB"]
    K = "stats"
    for tt in range(n_tiles):
        full = src_fn(tt)
        kb.ve("bn_stats", bn[:, tt * 12:tt * 12 + 6], full[:, 0:512],
              reads=keys_read + ("spB",), writes=("spB",), tag=tag + "bs")
        kb.ve("bn_stats", bn[:, tt * 12 + 6:tt * 12 + 12], full[:, 512:1024],
              reads=keys_read + ("spB",), writes=("spB",), tag=tag + "bs")
        kb.ve("bn_aggr", sm[:, 4 * tt:4 * tt + 2], bn[:, tt * 12:tt * 12 + 12],
              reads=("spB",), writes=(K,), tag=tag + "ba")
    var_v = sm[:, :].rearrange("p (t c) -> p t c", c=4)[:, 0:n_tiles, 1]
    lnv_v = sm[:, :].rearrange("p (t c) -> p t c", c=4)[:, 0:n_tiles, 2]
    rst_v = sm[:, :].rearrange("p (t c) -> p t c", c=4)[:, 0:n_tiles, 3]
    kb.ve("tensor_scalar_add", lnv_v, var_v, EPS, reads=(K,), writes=(K,), tag=tag + "ep")
    kb.act(lnv_v, lnv_v, AF.Ln, reads=(K,), writes=(K,), tag=tag + "ln")
    kb.act(rst_v, lnv_v, AF.Exp, scale=-0.5, reads=(K,), writes=(K,), tag=tag + "ex")
    return K


def ln_mu_rstd(kb, skey, tt):
    sm = kb.sb["stats"]
    return sm[:, 4 * tt:4 * tt + 1], sm[:, 4 * tt + 3:4 * tt + 4]


def tm_to_fm(kb, src_ap_fn, dst, dst_key, n_tt, tt0_tok, ndt, tag, keys_read,
             scale=None, bias=None, dt0=0, view_d=None):
    ident = kb.sb["identb"]
    for tt in range(n_tt):
        for dt in range(ndt):
            pk = "pt%d" % ((tt * ndt + dt) % 2)
            pt = kb.ps[pk]
            kb.tr(pt[:, 0:128], src_ap_fn(tt, dt), ident[:, :], pk,
                  reads=keys_read + ("identb",), tag=tag + "tr")
            dv = dst[:, :].rearrange("p (d t) -> p d t", d=(view_d or ndt))[
                :, dt0 + dt, tt0_tok + tt * 128: tt0_tok + (tt + 1) * 128]
            kb.act(dv, pt[:, 0:128], AF.Copy, scale=scale, bias=bias,
                   reads=(pk,), writes=(dst_key, pk), tag=tag + "ev")


def stage_a(kb, P):
    T, NTT = kb.T, kb.NTT
    TcA = min(512, T)
    NCH = T // TcA
    NTC = TcA // 128
    sb = kb.sb
    ps = kb.ps
    xtm = sb["xtm"]
    G = {k: sb[k] for k in ("G1", "G2", "G3", "G4", "G5", "G6", "G7", "G8", "G9")}
    u = G["G1"]
    du = G["G2"]
    xsr = G["G7"]
    bc = G["G4"]
    ya = G["G5"]
    xnf = G["G3"]
    zb = G["G6"]
    xnt = G["G8"]

    uv = u[:, :].rearrange("p (d t) -> p d t", d=16)
    duv = du[:, :].rearrange("p (d t) -> p d t", d=16)
    xsrv = xsr[:, :].rearrange("p (d t) -> p d t", d=16)
    bcv = bc[:, :].rearrange("p (d t) -> p d t", d=16)
    yav = ya[:, :].rearrange("p (d t) -> p d t", d=16)
    xnfv = xnf[:, :].rearrange("p (d t) -> p d t", d=16)
    hsl = xnf[:, 4096:8192].rearrange("p (n t) -> p n t", n=8)
    zbv = zb[:, :].rearrange("p (d t) -> p d t", d=16)
    ball = zb[:, 0:8 * TcA].rearrange("p (n t) -> p n t", n=8)
    asl = zb[:, 4096:4096 + 2 * TcA].rearrange("p (n t) -> p n t", n=2)
    xntv = xnt[:, :].rearrange("p (t d) -> p t d", d=1024)
    dlt = lambda dt, w: xsrv[:, dt, 3:3 + w]
    yak = lambda dt: G5K[dt // 4]
    zbk = lambda ot: G6K[ot // 4]

    stk = ln_stats(kb, lambda tt: xtm[:, :].rearrange("p (t d) -> p t d", d=1024)[:, tt, :],
                   NTT, "lnA", "A", tuple("xt%d" % i for i in range(NTT)))

    for c in range(NCH):
        for tl in range(NTC):
            tt = c * NTC + tl
            mu, rs = ln_mu_rstd(kb, "A", tt)
            src = xtm[:, :].rearrange("p (t d) -> p t d", d=1024)[:, tt, :]
            kb.ve("tensor_scalar", xntv[:, tl, :], src, mu, rs, ALU.subtract, ALU.mult,
                  reads=(xtk(tt), stk), writes=("G8r%d" % tl,), tag="axn")
        tm_to_fm(kb, lambda tl, dt: xntv[:, tl, dt * 128:(dt + 1) * 128],
                 xnf, "G3a", NTC, 0, 8, "axf",
                 ("G8r0", "G8r1", "G8r2", "G8r3"), view_d=16)

        for kt in range(8):
            kb.mm(ps["p4"][0:16, 0:TcA], sb["laiT"][:, :].rearrange(
                "p (k r) -> p k r", k=8)[:, kt, :], xnfv[:, kt, 0:TcA],
                "p4", kt == 0, kt == 7, reads=("laiT", "G3a"), tag="axl")
        kb.ve("tensor_copy", sb["xl"][0:16, 0:TcA], ps["p4"][0:16, 0:TcA],
              reads=("p4",), writes=("xl", "p4"), tag="axl_ev")

        for ot in range(16):
            wbuf = "wsA%d" % (ot % 3)
            kb.dma("pool", sb[wbuf][:, :].rearrange("p (k o) -> p k o", k=8),
                   P["w_inT"][ot], writes=(wbuf,), tag="awin")
            pk = "p%d" % (ot % 2)
            for kt in range(8):
                kb.mm(ps[pk][:, 0:TcA], sb[wbuf][:, :].rearrange(
                    "p (k o) -> p k o", k=8)[:, kt, :], xnfv[:, kt, 0:TcA],
                    pk, kt == 0, False, reads=(wbuf, "G3a"), tag="axz")
            kb.mm(ps[pk][:, 0:TcA], sb["lbi_s"][0:16, ot * 128:(ot + 1) * 128],
                  sb["xl"][0:16, 0:TcA], pk, False, True,
                  reads=("lbi_s", "xl"), tag="axzl")
            kb.act(xsrv[:, ot, 3:3 + TcA], ps[pk][:, 0:TcA], AF.Copy,
                   reads=(pk,), writes=("G7", pk), tag="axz_ev")

        if c == 0:
            kb.ve("memset", xsrv[:, :, 0:3], 0.0, writes=("G7",), tag="aconv0")
        for dt in range(16):
            dacc = sb["dacc"]
            cwt = sb["cw"][:, :].rearrange("p (d k) -> p d k", d=16)
            kb.ve("tensor_scalar", dacc[:, 0:TcA], xsrv[:, dt, 0:TcA],
                  cwt[:, dt, 0:1], None, ALU.mult,
                  reads=("G7", "cw"), writes=("spA",), tag="acv")
            for k in range(1, 4):
                kb.ve("scalar_tensor_tensor", dacc[:, 0:TcA], xsrv[:, dt, k:k + TcA],
                      cwt[:, dt, k:k + 1], dacc[:, 0:TcA], ALU.mult, ALU.add,
                      reads=("G7", "cw", "spA"), writes=("spA",), tag="acv")
            kb.act(uv[:, dt, 0:TcA], dacc[:, 0:TcA], AF.Silu,
                   bias=sb["cb"][:, dt:dt + 1],
                   reads=("spA", "cb"), writes=("G1",), tag="acs")
        if c + 1 < NCH:
            kb.ve("tensor_copy", xsrv[:, :, 0:3], xsrv[:, :, TcA:TcA + 3],
                  reads=("G7",), writes=("G7",), tag="ahalo")

        for kt in range(16):
            kb.mm(ps["p4"][0:96, 0:TcA], sb["wxpT"][:, :].rearrange(
                "p (d r) -> p d r", d=16)[:, kt, :], uv[:, kt, 0:TcA],
                "p4", kt == 0, kt == 15, reads=("wxpT", "G1"), tag="axd")
        kb.ve("tensor_copy", sb["xdbl"][0:96, 0:TcA], ps["p4"][0:96, 0:TcA],
              reads=("p4",), writes=("xdbl", "p4"), tag="axd_ev")

        for dt in range(16):
            pk = "p%d" % (dt % 2)
            kb.mm(ps[pk][:, 0:TcA], sb["wdtT"][0:64, dt * 128:(dt + 1) * 128],
                  sb["xdbl"][0:64, 0:TcA], pk, True, True,
                  reads=("wdtT", "xdbl"), tag="adl")
            spA, spB = sb["spA"], sb["spB"]
            bdt1 = sb["bdt"][:, dt:dt + 1]
            kb.act(spA[:, 0:TcA], ps[pk][:, 0:TcA], AF.Abs, bias=bdt1,
                   reads=(pk, "bdt"), writes=("spA",), tag="asp1")
            kb.act(spB[:, 0:TcA], spA[:, 0:TcA], AF.Exp, scale=-1.0,
                   reads=("spA",), writes=("spB",), tag="asp2")
            kb.act(spA[:, 0:TcA], spB[:, 0:TcA], AF.Ln, bias=1.0,
                   reads=("spB",), writes=("spA",), tag="asp3")
            kb.act(spB[:, 0:TcA], ps[pk][:, 0:TcA], AF.Relu, bias=bdt1,
                   reads=(pk, "bdt"), writes=("spB", pk), tag="asp4")
            kb.ve("tensor_add", dlt(dt, TcA), spA[:, 0:TcA], spB[:, 0:TcA],
                  reads=("spA", "spB"), writes=("G7",), tag="asp5")
            kb.ve("tensor_mul", duv[:, dt, 0:TcA], dlt(dt, TcA), uv[:, dt, 0:TcA],
                  reads=("G7", "G1"), writes=("G2",), tag="adu")
            kb.ve("tensor_scalar", yav[:, dt, 0:TcA], uv[:, dt, 0:TcA],
                  sb["Dp"][:, dt:dt + 1], None, ALU.mult,
                  reads=("G1", "Dp"), writes=(yak(dt),), tag="ayi")

        if c == 0:
            kb.ve("memset", sb["hcar"][:, :], 0.0, writes=("hcar",), tag="ahc0")
        kb.dma("pool", P["scr_bc"][:, 0:TcA], sb["xdbl"][64:96, 0:TcA],
               reads=("xdbl",), writes=("scrBC",), tag="abc_st")
        pb8 = zb[:, 8 * TcA:16 * TcA].rearrange("p (n t) -> p n t", n=8)
        g6r = lambda lo, hi: tuple(G6K[r] for r in range(lo // 2048, (hi - 1) // 2048 + 1))
        ballk = g6r(0, 8 * TcA)
        pbak = g6r(8 * TcA, 12 * TcA)
        pbbk = g6r(12 * TcA, 16 * TcA)
        aslg = sb["G9"][:, 0:2 * TcA].rearrange("p (n t) -> p n t", n=2)
        pend = []
        for g in range(2):
            kb.dma("pool", bcv[:, 0:8, 0:TcA],
                   P["scr_bc"][g * 8:g * 8 + 8, 0:TcA][None, :, :].broadcast_to([128, 8, TcA]),
                   reads=("scrBC",), writes=("G4a",), tag="abc_ldB")
            kb.dma("pool", bcv[:, 8:16, 0:TcA],
                   P["scr_bc"][16 + g * 8:16 + g * 8 + 8, 0:TcA][None, :, :].broadcast_to([128, 8, TcA]),
                   reads=("scrBC",), writes=("G4b",), tag="abc_ldC")
            for dt in range(16):
                kb.ve("tensor_mul", ball[:, :, 0:TcA],
                      duv[:, dt:dt + 1, 0:TcA].broadcast_to([128, 8, TcA]),
                      bcv[:, 0:8, 0:TcA],
                      reads=("G2", "G4a"), writes=ballk, tag="asb")
                if len(pend) >= 2:
                    pend.pop(0)()
                for n in range(8):
                    nn = g * 8 + n
                    par = n % 2
                    kb.act(aslg[:, par, 0:TcA], dlt(dt, TcA), AF.Exp,
                           scale=sb["Aneg"][:, :].rearrange("p (d n) -> p d n", d=16)[:, dt, nn:nn + 1],
                           reads=("G7", "Aneg"), writes=("g9p%d" % par,), tag="asa")
                    kb.ve("tensor_tensor_scan", hsl[:, n, 0:TcA], aslg[:, par, 0:TcA],
                          ball[:, n, 0:TcA],
                          sb["hcar"][:, dt * 16 + nn:dt * 16 + nn + 1],
                          ALU.mult, ALU.add,
                          reads=("g9p%d" % par,) + ballk + ("hcar",),
                          writes=("hs%d" % n,), tag="asc")
                kb.act(sb["hcar"][:, dt * 16 + g * 8:dt * 16 + g * 8 + 8],
                       hsl[:, :, TcA - 1], AF.Copy,
                       reads=tuple("hs%d" % i for i in range(8)), writes=("hcar",), tag="ahc")
                kb.ve("tensor_mul", pb8[:, 0:4, 0:TcA], hsl[:, 0:4, 0:TcA], bcv[:, 8:12, 0:TcA],
                      reads=("hs0", "hs1", "hs2", "hs3", "G4b"),
                      writes=pbak, eng_name="pool", tag="asya")
                kb.ve("tensor_mul", pb8[:, 4:8, 0:TcA], hsl[:, 4:8, 0:TcA], bcv[:, 12:16, 0:TcA],
                      reads=("hs4", "hs5", "hs6", "hs7", "G4b"),
                      writes=pbbk, eng_name="pool", tag="asyb")
                kb.ve("tensor_add", pb8[:, 0:4, 0:TcA], pb8[:, 0:4, 0:TcA], pb8[:, 4:8, 0:TcA],
                      reads=pbak + pbbk, writes=pbak, eng_name="pool", tag="atr1")

                def tail(dt=dt):
                    kb.ve("tensor_add", pb8[:, 0:2, 0:TcA], pb8[:, 0:2, 0:TcA],
                          pb8[:, 2:4, 0:TcA], reads=pbak, writes=pbak, tag="atr2")
                    kb.ve("tensor_add", pb8[:, 0, 0:TcA], pb8[:, 0, 0:TcA],
                          pb8[:, 1, 0:TcA], reads=pbak, writes=pbak, tag="atr3")
                    kb.ve("tensor_add", yav[:, dt, 0:TcA], yav[:, dt, 0:TcA],
                          pb8[:, 0, 0:TcA], reads=(yak(dt),) + pbak, writes=(yak(dt),), tag="atr4")
                pend.append(tail)
        while pend:
            pend.pop(0)()

        for ot in range(16):
            wbuf = "wsA%d" % (ot % 3)
            kb.dma("pool", sb[wbuf][:, :].rearrange("p (k o) -> p k o", k=8),
                   P["w_inT"][16 + ot], writes=(wbuf,), tag="awz")
            pk = "p%d" % (ot % 2)
            for kt in range(8):
                kb.mm(ps[pk][:, 0:TcA], sb[wbuf][:, :].rearrange(
                    "p (k o) -> p k o", k=8)[:, kt, :], xnfv[:, kt, 0:TcA],
                    pk, kt == 0, False, reads=(wbuf, "G3a"), tag="az")
            kb.mm(ps[pk][:, 0:TcA], sb["lbi_s"][0:16, (16 + ot) * 128:(17 + ot) * 128],
                  sb["xl"][0:16, 0:TcA], pk, False, True,
                  reads=("lbi_s", "xl"), tag="azl")
            kb.act(zbv[:, ot, 0:TcA], ps[pk][:, 0:TcA], AF.Silu,
                   reads=(pk,), writes=(zbk(ot), pk), tag="az_ev")

        for dt in range(16):
            kb.ve("tensor_mul", zbv[:, dt, 0:TcA], yav[:, dt, 0:TcA],
                  zbv[:, dt, 0:TcA],
                  reads=(yak(dt), zbk(dt)), writes=(zbk(dt),), tag="agate")
        for tb in range(NTC):
            for kt in range(16):
                kb.mm(ps["p4"][0:16, 0:128], sb["laoT"][:, :].rearrange(
                    "p (d r) -> p d r", d=16)[:, kt, :],
                    zbv[:, kt, tb * 128:(tb + 1) * 128],
                    "p4", kt == 0, kt == 15, reads=("laoT", zbk(kt)), tag="aylo")
            kb.ve("tensor_copy", sb["xdbl"][0:16, tb * 128:(tb + 1) * 128],
                  ps["p4"][0:16, 0:128], reads=("p4",), writes=("xdbl", "p4"), tag="aylo_ev")
        xv = xtm[:, :].rearrange("p (t d) -> p t d", d=1024)
        for oh in range(2):
            for kt in range(16):
                wbuf = "wsA%d" % ((oh * 16 + kt) % 3)
                kb.dma("pool", sb[wbuf][:, 0:512],
                       P["w_outT"][kt, oh], writes=(wbuf,), tag="awo")
                for tb in range(NTC):
                    kb.mm(ps["p%d" % tb][:, 0:512], zbv[:, kt, tb * 128:(tb + 1) * 128],
                          sb[wbuf][:, 0:512], "p%d" % tb, kt == 0, False,
                          reads=(zbk(kt), wbuf), tag="aop")
            for tb in range(NTC):
                pk = "p%d" % tb
                kb.mm(ps[pk][:, 0:512], sb["xdbl"][0:16, tb * 128:(tb + 1) * 128],
                      sb["lbo_s"][0:16, oh * 512:(oh + 1) * 512], pk, False, True,
                      reads=("xdbl", "lbo_s"), tag="aopl")
                tt = c * NTC + tb
                kb.ve("tensor_add", xv[:, tt, oh * 512:(oh + 1) * 512],
                      xv[:, tt, oh * 512:(oh + 1) * 512], ps[pk][:, 0:512],
                      reads=(xtk(tt), pk), writes=(xtk(tt), pk), tag="aox")
    kb.dbg("x_after_a", xtm[:, :], [128, NTT * 1024], BF16, reads=tuple("xt%d" % i for i in range(NTT)))


def build_program(T, stages=("a",), dbg_names=()):
    nc = bass.Bass("TRN2", target_bir_lowering=False, debug=False)
    kb = KB(nc, T, dbg_names)
    NTT = kb.NTT
    TcA = min(512, T)

    P = {}
    for nm, shp in [
        ("x", [T, D]), ("w_inT", [32, 128, 8, 128]), ("laiT", [D, RK]), ("lbiT", [RK, 2 * DI]),
        ("mask_in", [2 * DI]), ("cw", [128, 16, 4]), ("cb", [128, 16]), ("wxpT", [DI, 96]),
        ("wdtT", [DTR, DI]), ("bdt", [128, 16]), ("Alog", [128, 16, DS]), ("Dp", [128, 16]),
        ("w_outT", [16, 2, 128, 512]), ("laoT", [DI, RK]), ("lboT", [RK, D]), ("mask_out", [D]),
        ("nu", [128, 8]), ("th", [128, 8]), ("gam", [128, 8]), ("Dlru", [1, D]),
        ("BreT", [8, 128, 8, 128]), ("BimT", [8, 128, 8, 128]),
        ("CreT", [8, 2, 128, 512]), ("CimT", [8, 2, 128, 512]),
        ("ln1g", [128, 8]), ("ln1b", [128, 8]), ("ln2g", [128, 8]), ("ln2b", [128, 8]),
        ("Wq", [8, 128, 8, 128]), ("Wk", [8, 128, 8, 128]), ("Wv", [8, 128, 8, 128]),
        ("Wp", [8, 2, 128, 512]),
        ("bq", [128, 8]), ("bk", [128, 8]), ("bv", [1, D]), ("bp", [1, D]),
        ("gw", [D, NE]), ("gb", [NE]), ("W1", [NE, 8, 128, 8, 512]), ("b1", [128, NE, 32]),
        ("W2", [NE, 4, 2, 128, 8, 512]), ("b2", [NE, D]),
    ]:
        P[nm] = kb.din("i_" + nm, shp, F32)
    for nm, shp, dt in [("identb", [128, 128], BF16), ("triN", [128, 128], BF16),
                        ("iota8", [128, NE], F32)]:
        P[nm] = kb.din("i_" + nm, shp, dt)
    OUT = kb.dout("out", [T, D], F32)
    SCR_BC = kb.nc.dram_tensor("scr_bc", [32, min(512, T)], BF16, kind="Internal").ap()
    P["scr_bc"] = SCR_BC

    kb.t("xtm", [128, NTT * 1024], BF16)
    for g in ("G1", "G2", "G3", "G4", "G5", "G6"):
        kb.t(g, [128, 8192], BF16)
    kb.t("G7", [128, 16 * (TcA + 3)], BF16)
    kb.t("G8", [128, 4096], BF16)
    kb.t("G9", [128, 4096], BF16)
    for wi_ in range(3):
        kb.t("wsA%d" % wi_, [128, 8 * 128], BF16)
    kb.t("identb", [128, 128], BF16)
    kb.t("triN", [128, 128], BF16)
    kb.t("iota8", [128, NE], F32)
    kb.t("stats", [128, 4 * NTT], F32)
    kb.t("onesf", [1, 128], F32)
    kb.t("onesb", [1, 128], BF16)
    kb.t("onesn", [1, 128], BF16)
    kb.t("onescol", [128, 1], BF16)
    kb.t("onescf", [128, 1], F32)
    kb.t("laiT", [128, 8 * RK], BF16)
    kb.t("bsml", [128, 4096], BF16)
    kb.t("cw", [128, 16 * 4], F32)
    kb.t("cb", [128, 16], F32)
    kb.t("wxpT", [128, 16 * 96], BF16)
    kb.t("wdtT", [DTR, DI], BF16)
    kb.t("bdt", [128, 16], F32)
    kb.t("Aneg", [128, 16 * DS], F32)
    kb.t("Dp", [128, 16], F32)
    kb.t("laoT", [128, 16 * RK], BF16)
    kb.t("lbo_s", [RK, D], BF16)
    kb.t("xl", [RK, TcA], BF16)
    kb.t("ylo", [RK, 128], BF16)
    kb.t("xdbl", [128, TcA], BF16)
    kb.t("spA", [128, max(TcA, 1024)], F32)
    kb.t("spB", [128, max(TcA, 1024)], F32)
    kb.t("hcar", [128, 16 * DS], F32)
    kb.t("rows", [1, 2048], BF16)
    kb.t("wsV", [128, 8 * 256], BF16)
    kb.t("attsh", [128, 32], BF16)
    kb.t("ascr", [128, 16], F32)
    kb.t("e8", [128, NE], F32)
    kb.t("gwt", [128, 8 * NE], BF16)
    kb.t("bq", [128, 8], F32)
    kb.t("bk8", [128, 8], F32)
    kb.t("ln1g", [128, 8], F32)
    kb.t("ln1b", [128, 8], F32)
    kb.t("ln2g", [128, 8], F32)
    kb.t("ln2b", [128, 8], F32)

    bs = kb.sb["bsml"]
    bsf = bs.bitcast(F32)
    kb.sb["lbi_s"] = bs[0:128, :]
    kb.sb["lpr"] = bs[:, 0:1024]
    kb.sb["lpi"] = bs[:, 1024:2048]
    kb.sb["lh"] = bsf[:, 1024:1280]
    kb.sb["lam"] = bsf[:, 1280:1344]
    kb.sb["nuv"] = bsf[:, 1344:1352]
    kb.sb["thv"] = bsf[:, 1352:1360]
    kb.sb["gamv"] = bsf[:, 1360:1368]
    kb.sb["gexp"] = bsf[:, 1368:1376]
    kb.t("b1pp", [128, NE * 32], BF16)
    kb.sb["dacc"] = kb.sb["spA"][:, 512:1024]
    kb.sb["lgt"] = kb.sb["spB"][:, 0:128]
    kb.sb["wfl"] = kb.sb["spB"][:, 128:256]
    kb.p.alias["lgt"] = "spB"
    kb.p.alias["wfl"] = "spB"
    for k in ("lbi_s", "lpr", "lpi", "lh", "lam", "nuv", "thv", "gamv", "gexp"):
        kb.p.alias[k] = "bsml"
    kb.p.alias["dacc"] = "spA"
    for i in range(6):
        kb.pt("p%d" % i, [128, 512], F32)
    kb.pt("pt0", [128, 128], BF16)
    kb.pt("pt1", [128, 128], BF16)

    pre = [
        ("sp", "identb", P["identb"], None), ("sp", "triN", P["triN"], None),
        ("sp", "iota8", P["iota8"], None),
    ]
    for q, nm, src, re_ in pre:
        kb.dma(q, kb.sb[nm][:, :], src, writes=(nm,), tag="pre" + nm)
    kb.dma("pool", kb.sb["xtm"][:, :].rearrange("p (t d) -> p t d", d=1024),
           P["x"].rearrange("(t p) d -> p t d", p=128),
           writes=tuple("xt%d" % i for i in range(NTT)), sgrp="prex", tag="prex")
    kb.dma("pool", kb.sb["laiT"][:, :].rearrange("p (k r) -> p k r", k=8),
           P["laiT"].rearrange("(k p) r -> p k r", p=128), writes=("laiT",), tag="plai")
    kb.dma("pool", kb.sb["G1"][0:RK, 0:2 * DI], P["lbiT"], writes=("G1",), tag="plbi")
    kb.dma("pool", kb.sb["G2"][0:RK, 0:2 * DI], P["mask_in"][None, :].broadcast_to([RK, 2 * DI]), writes=("G2",), tag="pmin")
    kb.dma("sp", kb.sb["cw"][:, :].rearrange("p (d k) -> p d k", d=16),
           P["cw"], writes=("cw",), tag="pcw")
    kb.dma("sp", kb.sb["cb"][:, :], P["cb"], writes=("cb",), tag="pcb")
    kb.dma("pool", kb.sb["wxpT"][:, :].rearrange("p (d r) -> p d r", d=16),
           P["wxpT"].rearrange("(d p) r -> p d r", p=128), writes=("wxpT",), tag="pwxp")
    kb.dma("pool", kb.sb["wdtT"][0:DTR, :], P["wdtT"], writes=("wdtT",), tag="pwdt")
    kb.dma("sp", kb.sb["bdt"][:, :], P["bdt"], writes=("bdt",), tag="pbdt")
    kb.dma("sp", kb.sb["spA"].bitcast(F32)[:, 0:256].rearrange("p (d n) -> p d n", d=16)
           if False else kb.sb["spA"][:, 0:256].rearrange("p (d n) -> p d n", d=16),
           P["Alog"], writes=("spA",), tag="pal")
    kb.dma("sp", kb.sb["Dp"][:, :], P["Dp"], writes=("Dp",), tag="pdp")
    kb.dma("pool", kb.sb["laoT"][:, :].rearrange("p (k r) -> p k r", k=16),
           P["laoT"].rearrange("(k p) r -> p k r", p=128), writes=("laoT",), tag="plao")
    kb.dma("pool", kb.sb["G1"][0:RK, 4096:4096 + D], P["lboT"], writes=("G1",), tag="plbo")
    kb.dma("pool", kb.sb["G2"][0:RK, 4096:4096 + D], P["mask_out"][None, :].broadcast_to([RK, D]), writes=("G2",), tag="pmout")
    kb.ve("memset", kb.sb["onesf"][:, :], 1.0, writes=("onesf",), tag="ponef")
    kb.ve("memset", kb.sb["e8"][:, :], 8.0, writes=("e8",), tag="pe8")
    kb.dma("sp", kb.sb["bq"][:, :], P["bq"], writes=("bq",), tag="pbq")
    kb.dma("sp", kb.sb["bk8"][:, :], P["bk"], writes=("bk8",), tag="pbk")
    kb.ve("tensor_scalar_mul", kb.sb["bk8"][:, :], kb.sb["bk8"][:, :], 0.125,
          reads=("bk8",), writes=("bk8",), tag="pbk8")
    for nmv in ("ln1g", "ln1b", "ln2g", "ln2b"):
        kb.dma("sp", kb.sb[nmv][:, :], P[nmv], writes=(nmv,), tag="p" + nmv)
    kb.dma("pool", kb.sb["rows"][0:1, 0:D], P["bp"], writes=("rows",), sgrp="r_bp", tag="pbp")
    kb.dma("pool", kb.sb["attsh"][0:1, 24:24 + NE], P["gb"][None, :], writes=("attsh",), sgrp="r_gb", tag="pgb")
    kb.dma("pool", kb.sb["gwt"][:, :].rearrange("p (k e) -> p k e", k=8),
           P["gw"].rearrange("(k p) e -> p k e", p=128), writes=("gwt",), tag="pgw")
    kb.ve("memset", kb.sb["onesb"][:, :], 1.0, writes=("onesb",), tag="poneb")
    kb.ve("memset", kb.sb["onesn"][:, :], -1.0, writes=("onesn",), tag="ponen")
    kb.ve("memset", kb.sb["onescol"][:, :], 1.0, writes=("onescol",), tag="ponec")
    kb.ve("memset", kb.sb["onescf"][:, :], 1.0, writes=("onescf",), tag="ponecf")
    kb.act(kb.sb["Aneg"][:, :], kb.sb["spA"][:, 0:256], AF.Exp, reads=("spA",),
           writes=("Aneg",), tag="pan1")
    kb.ve("tensor_scalar_mul", kb.sb["Aneg"][:, :], kb.sb["Aneg"][:, :], -1.0,
          reads=("Aneg",), writes=("Aneg",), tag="pan2")
    kb.ve("scalar_tensor_tensor", kb.sb["lbi_s"][0:RK, :], kb.sb["G1"][0:RK, 0:2 * DI], 2.0,
          kb.sb["G2"][0:RK, 0:2 * DI], ALU.mult, ALU.mult,
          reads=("G1", "G2"), writes=("lbi_s",), tag="plbis")
    kb.ve("scalar_tensor_tensor", kb.sb["lbo_s"][0:RK, :], kb.sb["G1"][0:RK, 4096:4096 + D], 2.0,
          kb.sb["G2"][0:RK, 4096:4096 + D], ALU.mult, ALU.mult,
          reads=("G1", "G2"), writes=("lbo_s",), tag="plbos")

    if "a" in stages:
        stage_a(kb, P)
    if "b" in stages:
        stage_ffn(kb, P, 1)
    if "c" in stages:
        stage_lru(kb, P)
    if "b2" in stages:
        stage_ffn(kb, P, 2)

    kb.dma("pool", OUT.rearrange("(t p) d -> p t d", p=128),
           kb.sb["xtm"][:, :].rearrange("p (t d) -> p t d", d=1024),
           reads=tuple("xt%d" % i for i in range(NTT)), sgrp="outx", tag="outx")

    esem = {}
    qsem = {}
    for e in ("pe", "act", "dve", "pool", "sp"):
        esem[e] = kb.st.enter_context(nc.semaphore("se_" + e))
    for g in sorted(kb.p.qcount):
        qsem[g] = kb.st.enter_context(nc.semaphore("sq_" + g))
    with nc.Block() as block:
        kb.p.emit_all(block, esem, qsem)
    kb.st.close()
    assert kb.sbuf_bytes <= 213 * 1024, f"sbuf over budget: {kb.sbuf_bytes}"
    return nc, kb


def _til8(w):
    K, N = w.shape
    return np.ascontiguousarray(w.reshape(K // 128, 128, N // 128, 128).transpose(2, 1, 0, 3))


def host_inputs(inputs, T=2048):
    g = lambda k: np.ascontiguousarray(np.asarray(inputs["L0_" + k], np.float32))
    com = {
        "w_inT": np.ascontiguousarray(g("attn_a_w_in").T.reshape(8, 128, 32, 128).transpose(2, 1, 0, 3)),
        "laiT": np.ascontiguousarray(g("attn_a_lora_A_in").T),
        "lbiT": np.ascontiguousarray(g("attn_a_lora_B_in").T),
        "mask_in": g("attn_a_mask_in"),
        "cw": np.ascontiguousarray(g("attn_a_conv_w")[:, 0, :].reshape(16, 128, 4).transpose(1, 0, 2)),
        "cb": np.ascontiguousarray(g("attn_a_conv_b").reshape(16, 128).T),
        "wxpT": np.ascontiguousarray(g("attn_a_w_xproj").T),
        "wdtT": np.ascontiguousarray(g("attn_a_w_dt").T),
        "bdt": np.ascontiguousarray(g("attn_a_b_dt").reshape(16, 128).T),
        "Alog": np.ascontiguousarray(g("attn_a_A_log").reshape(16, 128, 16).transpose(1, 0, 2)),
        "Dp": np.ascontiguousarray(g("attn_a_Dp").reshape(16, 128).T),
        "w_outT": np.ascontiguousarray(g("attn_a_w_out").T.reshape(16, 128, 2, 512).transpose(0, 2, 1, 3)),
        "laoT": np.ascontiguousarray(g("attn_a_lora_A_out").T),
        "lboT": np.ascontiguousarray(g("attn_a_lora_B_out").T),
        "mask_out": g("attn_a_mask_out"),
        "nu": np.ascontiguousarray(g("attn_b_nu_log").reshape(8, 128).T), "th": np.ascontiguousarray(g("attn_b_theta_log").reshape(8, 128).T),
        "gam": np.ascontiguousarray(g("attn_b_gamma_log").reshape(8, 128).T), "Dlru": g("attn_b_D")[None, :],
        "BreT": _til8(np.ascontiguousarray(g("attn_b_B_re").T)),
        "BimT": _til8(np.ascontiguousarray(g("attn_b_B_im").T)),
        "CreT": np.ascontiguousarray(g("attn_b_C_re").T.reshape(8, 128, 2, 512).transpose(0, 2, 1, 3)),
        "CimT": np.ascontiguousarray(g("attn_b_C_im").T.reshape(8, 128, 2, 512).transpose(0, 2, 1, 3)),
        "ln1g": np.ascontiguousarray(g("ffn_ln1_g").reshape(8, 128).T), "ln1b": np.ascontiguousarray(g("ffn_ln1_b").reshape(8, 128).T),
        "ln2g": np.ascontiguousarray(g("ffn_ln2_g").reshape(8, 128).T), "ln2b": np.ascontiguousarray(g("ffn_ln2_b").reshape(8, 128).T),
        "Wq": _til8(g("ffn_Wq")), "Wk": _til8(g("ffn_Wk")), "Wv": _til8(g("ffn_Wv")),
        "Wp": np.ascontiguousarray(g("ffn_Wp").reshape(8, 128, 2, 512).transpose(0, 2, 1, 3)),
        "bq": np.ascontiguousarray(g("ffn_bq").reshape(8, 128).T), "bk": np.ascontiguousarray(g("ffn_bk").reshape(8, 128).T), "bv": g("ffn_bv")[None, :], "bp": g("ffn_bp")[None, :],
        "gw": g("ffn_gate_W"), "gb": g("ffn_gate_b"),
        "W1": np.ascontiguousarray(g("ffn_exp_W1").reshape(NE, 8, 128, 8, 512).transpose(0, 3, 2, 1, 4)), "b1": np.ascontiguousarray(g("ffn_exp_b1").reshape(8, 32, 128).transpose(2, 0, 1)),
        "W2": np.ascontiguousarray(g("ffn_exp_W2").reshape(NE, 4, 8, 128, 2, 512).transpose(0, 1, 4, 3, 2, 5)), "b2": g("ffn_exp_b2"),
        "i_identb": np.eye(128, dtype=ml_dtypes.bfloat16),
        "i_triN": np.where(np.triu(np.ones((128, 128), bool)), 0.0, -1e30).astype(ml_dtypes.bfloat16),
        "i_iota8": np.broadcast_to(np.arange(NE, dtype=np.float32), (128, NE)).copy(),
    }
    com = {("i_" + k if not k.startswith("i_") else k): v for k, v in com.items()}
    x = np.asarray(inputs["L0_attn_a_x"], np.float32)
    maps = []
    for b in range(x.shape[0]):
        m = dict(com)
        m["i_x"] = np.ascontiguousarray(x[b, :T])
        maps.append(m)
    return maps


_CACHE = {}


def kernel(**inputs):
    T = 2048
    if "prog" not in _CACHE:
        _CACHE["prog"] = build_program(T, stages=("a", "b", "c", "b2"))
    nc, kb = _CACHE["prog"]
    in_maps = host_inputs(inputs, T)
    res = run_bass_kernel_spmd(nc, in_maps, core_ids=list(range(8)))
    outs = [np.asarray(r["out"], np.float32) for r in res.results]
    full = np.stack(outs, 0)
    return full


def xtk(tt):
    return "xt%d" % tt


def _htile(kb, tt):
    g = "G1" if tt < 8 else "G2"
    return kb.sb[g], g, (tt % 8) * 1024


def _fm(kb, dt):
    g = "G3" if dt < 4 else "G4"
    k = (("G3a",), ("G3a",), ("hs0", "hs1", "hs2", "hs3"), ("hs4", "hs5", "hs6", "hs7"),
         ("G4a",), ("G4a",), ("G4b",), ("G4b",))[dt]
    return kb.sb[g], k, (dt % 4) * 2048


def stage_ffn(kb, P, ci):
    T, NTT = kb.T, kb.NTT
    sb, ps = kb.sb, kb.ps
    xtm = sb["xtm"]
    xv = xtm[:, :].rearrange("p (t d) -> p t d", d=1024)
    QCW = min(512, T)
    NQC = T // QCW
    cg = "c%d_" % ci

    kb.dma("pool", sb["G7"][0:1, 3 * T + 1024:3 * T + 2048], P["bv"], writes=("G7",),
           sgrp="r_bv", tag=cg + "pbv")
    if ci == 1:
        kb.dma("pool", sb["b1pp"][:, :].rearrange("p (e h) -> p e h", e=NE),
               P["b1"], writes=("b1pp",), tag=cg + "pb1")
    ln_stats(kb, lambda tt: xv[:, tt, :], NTT, cg + "ln0", "F0", tuple("xt%d" % i for i in range(NTT)))
    for tt in range(NTT):
        mu, rs = ln_mu_rstd(kb, "F0", tt)
        ht, hk, co = _htile(kb, tt)
        kb.ve("tensor_scalar", ht[:, co:co + 1024], xv[:, tt, :], mu, rs,
              ALU.subtract, ALU.mult, reads=(xtk(tt), "stats"), writes=(hk,), tag=cg + "h")

    ln_stats(kb, lambda tt: _htile(kb, tt)[0][:, _htile(kb, tt)[2]:_htile(kb, tt)[2] + 1024],
             NTT, cg + "ln1", "F1", ("G1", "G2"))
    tscr = sb["G9"]
    for tt in range(NTT):
        mu, rs = ln_mu_rstd(kb, "F1", tt)
        ht, hk, co = _htile(kb, tt)
        kb.ve("tensor_scalar", tscr[:, 0:1024], ht[:, co:co + 1024], mu, rs,
              ALU.subtract, ALU.mult, reads=(hk, "stats"), writes=("g9t",), tag=cg + "n1")
        for dt in range(8):
            pk = "pt%d" % (dt % 2)
            kb.tr(ps[pk][:, 0:128], tscr[:, dt * 128:(dt + 1) * 128], sb["identb"][:, :],
                  pk, reads=("g9t", "identb"), tag=cg + "t1")
            ft, fk, fo = _fm(kb, dt)
            kb.act(ft[:, fo + tt * 128:fo + (tt + 1) * 128], ps[pk][:, 0:128], AF.Copy,
                   scale=sb["ln1g"][:, dt:dt + 1], bias=sb["ln1b"][:, dt:dt + 1],
                   reads=(pk, "ln1g", "ln1b"), writes=fk + (pk,), tag=cg + "t1e")

    rows = sb["rows"]
    for hg in range(4):
        for wi_, (wname, dstk) in enumerate((("Wq", "G5"), ("Wk", "G6"))):
            dkeys = G5K if dstk == "G5" else G6K
            for ot2 in range(2):
                og = hg * 2 + ot2
                wbuf = "wsA%d" % ((wi_ * 2 + ot2) % 3)
                kb.dma("pool", sb[wbuf][:, :].rearrange("p (k o) -> p k o", k=8),
                       P[wname][og].rearrange("p k o -> p k o"), writes=(wbuf,), tag=cg + "wqk")
                for qc in range(NQC):
                    pk = "p%d" % (qc % 2)
                    for kt in range(8):
                        ft, fk, fo = _fm(kb, kt)
                        kb.mm(ps[pk][:, 0:QCW], sb[wbuf][:, :].rearrange(
                            "p (k o) -> p k o", k=8)[:, kt, :],
                            ft[:, fo + qc * QCW:fo + (qc + 1) * QCW],
                            pk, kt == 0, kt == 7, reads=(wbuf,) + fk, tag=cg + "qk")
                    for hh in range(2):
                        hl = ot2 * 2 + hh
                        if wname == "Wq":
                            sc, bi = 1.0, sb["bq"][hh * 64:hh * 64 + 64, og:og + 1]
                        else:
                            sc, bi = 0.125, sb["bk8"][hh * 64:hh * 64 + 64, og:og + 1]
                        kb.act(sb[dstk][0:64, hl * T + qc * QCW:hl * T + (qc + 1) * QCW],
                               ps[pk][hh * 64:hh * 64 + 64, 0:QCW], AF.Copy,
                               scale=sc, bias=bi,
                               reads=(pk, "bq", "bk8"), writes=dkeys + (pk,), tag=cg + "qke")
        g8v = sb["G8"][:, :].rearrange("p (t o) -> p t o", o=256)
        for _vt in range(2):
            kb.dma("pool", sb["wsV"][:, _vt * 1024:(_vt + 1) * 1024].rearrange(
                "p (k o) -> p k o", k=8),
                P["Wv"][hg * 2 + _vt].rearrange("p k o -> p k o"),
                writes=("wsV%d" % _vt,), tag=cg + "wvl")
        for tb in range(NTT):
            pk = "p%d" % (tb % 2)
            for kt in range(8):
                ft, fk, fo = _fm(kb, kt)
                for vt in range(2):
                    kb.mm(ps[pk][:, vt * 128:(vt + 1) * 128],
                          ft[:, fo + tb * 128:fo + (tb + 1) * 128],
                          sb["wsV"][:, vt * 1024 + kt * 128:vt * 1024 + (kt + 1) * 128],
                          pk, kt == 0 and vt == 0, False,
                          reads=fk + ("wsV%d" % vt,), tag=cg + "v")
            kb.mm(ps[pk][:, 0:256], sb["onesb"][0:1, :],
                  sb["G7"][0:1, 3 * T + 1024 + hg * 256:3 * T + 1024 + (hg + 1) * 256],
                  pk, False, True, reads=("onesb", "G7"), tag=cg + "vb")
            kb.ve("tensor_copy", g8v[:, tb, :], ps[pk][:, 0:256],
                  reads=(pk,), writes=("G8r%d" % (tb // 4), pk), tag=cg + "ve")

        for hl in range(4):
            h = hg * 4 + hl
            q0c = hl * T
            g9f = sb["G9"].bitcast(F32)
            kb.act(g9f[0:64, 0:T], sb["G6"][0:64, q0c:q0c + T], AF.Square,
                   reads=G6K, writes=("g9t", "g9p0", "g9p1", "g9p2", "g9p3"), tag=cg + "ksq")
            NQB = T // 512 if T >= 512 else 1
            QW = min(512, T)
            for c_ in range(T // QW):
                kb.mm(ps["p5"][0:1, 0:QW],
                      sb["onescf"][0:64, 0:1], g9f[0:64, c_ * QW:(c_ + 1) * QW],
                      "p5", True, True, reads=("onescf", "g9t"), tag=cg + "ksum")
                kb.ve("reduce_max", sb["spB"][0:1, c_:c_ + 1], ps["p5"][0:1, 0:QW],
                      axis=AX.X, reads=("p5",), writes=("spB",), tag=cg + "kmax")
            kb.ve("reduce_max", sb["spB"][0:1, 8:9], sb["spB"][0:1, 0:T // QW],
                  axis=AX.X, reads=("spB",), writes=("spB",), tag=cg + "kmax2")
            kb.act(g9f[0:64, 0:T], sb["G5"][0:64, q0c:q0c + T], AF.Square,
                   reads=G5K, writes=("g9t", "g9p0", "g9p1", "g9p2", "g9p3"), tag=cg + "qsq")
            for c_ in range(T // QW):
                kb.mm(ps["p5"][0:1, 0:QW],
                      sb["onescf"][0:64, 0:1], g9f[0:64, c_ * QW:(c_ + 1) * QW],
                      "p5", True, True, reads=("onescf", "g9t"), tag=cg + "qsum")
                kb.act(sb["G7"][0:1, 2 * T + c_ * QW:2 * T + (c_ + 1) * QW],
                       ps["p5"][0:1, 0:QW], AF.Sqrt, scale=sb["spB"][0:1, 8:9],
                       reads=("p5", "spB"), writes=("G7",), tag=cg + "qsh")

            for qc in range(NQC):
                qlo = qc * QCW
                po = "p4"
                nkb = qc * (QCW // 128) + (QCW // 128)

                def st_mm(kb_i):
                    k0 = kb_i * 128
                    q0 = max(qlo, k0)
                    w = qlo + QCW - q0
                    pk = "p%d" % (2 + kb_i % 2)
                    kb.mm(ps[pk][:, 0:w], sb["G6"][0:64, q0c + k0:q0c + k0 + 128],
                          sb["G5"][0:64, q0c + q0:q0c + qlo + QCW],
                          pk, True, False, reads=G5K + G6K, tag=cg + "pb")
                    kb.mm(ps[pk][:, 0:w], sb["onesn"][0:1, :],
                          sb["G7"][0:1, 2 * T + q0:2 * T + qlo + QCW], pk, False, q0 != k0,
                          reads=("onesn", "G7"), tag=cg + "pbs")
                    if q0 == k0:
                        kb.mm(ps[pk][:, 0:128], sb["identb"][:, :], sb["triN"][:, :],
                              pk, False, True, reads=("identb", "triN"), tag=cg + "pbm")
                    pb = "g9p%d" % (kb_i % 4)
                    pbuf = sb["G9"][:, 1024 + (kb_i % 4) * 512:1024 + (kb_i % 4) * 512 + 512]
                    kb.act(pbuf[:, 0:w], ps[pk][:, 0:w], AF.Exp,
                           reads=(pk,), writes=(pb, pk), tag=cg + "pbe")

                def pv_mm(kb_i):
                    k0 = kb_i * 128
                    q0 = max(qlo, k0)
                    w = qlo + QCW - q0
                    pb = "g9p%d" % (kb_i % 4)
                    pbuf = sb["G9"][:, 1024 + (kb_i % 4) * 512:1024 + (kb_i % 4) * 512 + 512]
                    kb.mm(ps[po][0:64, q0 - qlo:QCW], g8v[:, kb_i, hl * 64:(hl + 1) * 64],
                          pbuf[:, 0:w], po, kb_i == 0, kb_i == nkb - 1,
                          reads=("G8r%d" % (kb_i // 4), pb), tag=cg + "pv")
                    kb.mm(ps["p5"][0:1, q0 - qlo:QCW], sb["onescol"][:, 0:1],
                          pbuf[:, 0:w], "p5", kb_i == 0, kb_i == nkb - 1,
                          reads=("onescol", pb), tag=cg + "psum")

                for kb_i in range(nkb + 1):
                    if kb_i < nkb:
                        st_mm(kb_i)
                    if kb_i >= 1:
                        pv_mm(kb_i - 1)
                kb.ve("reciprocal", sb["spB"][0:1, 512:512 + QCW],
                      ps["p5"][0:1, 0:QCW], reads=("p5",), writes=("spB", "p5"), tag=cg + "prc")
                pk = "p%d" % (2 + qc % 2)
                kb.mm(ps[pk][0:64, 0:QCW], sb["onesf"][0:1, 0:64],
                      sb["spB"][0:1, 512:512 + QCW], pk, True, True,
                      reads=("onesf", "spB"), tag=cg + "pbc")
                kb.act(sb["spA"][0:64, 0:QCW], ps[pk][0:64, 0:QCW],
                       AF.Copy, reads=(pk,), writes=("spA", pk), tag=cg + "pbce")
                dtg = hl // 2
                rbase = (hl % 2) * 64
                kb.ve("tensor_mul", sb["G7"][rbase:rbase + 64, dtg * T + qlo:dtg * T + qlo + QCW],
                      ps[po][0:64, 0:QCW], sb["spA"][0:64, 0:QCW],
                      reads=(po, "spA"), writes=("G7", po), tag=cg + "oev")
        for par in range(2):
            kb.dma("pool", sb["G5"][:, par * 2048:par * 2048 + 1024].rearrange(
                "p (a o) -> p a o", a=2),
                P["Wp"][2 * hg:2 * hg + 2, par].rearrange("a p o -> p a o"),
                writes=("G5q%d" % par,), tag=cg + "wp")
            for tb in range(NTT):
                pk = "p%d" % (tb % 2)
                for dtg in range(2):
                    kb.mm(ps[pk][:, 0:512],
                          sb["G7"][0:128, dtg * T + tb * 128:dtg * T + (tb + 1) * 128],
                          sb["G5"][:, par * 2048 + dtg * 512:par * 2048 + dtg * 512 + 512],
                          pk, dtg == 0, dtg == 1 and hg != 0,
                          reads=("G7", "G5q%d" % par), tag=cg + "pj")
                if hg == 0:
                    kb.mm(ps[pk][:, 0:512], sb["onesb"][0:1, :],
                          rows[0:1, par * 512:(par + 1) * 512], pk,
                          False, True, reads=("onesb", "rows"), tag=cg + "pjb")
                ht, hk, co = _htile(kb, tb)
                kb.ve("tensor_add", ht[:, co + par * 512:co + (par + 1) * 512],
                      ht[:, co + par * 512:co + (par + 1) * 512], ps[pk][:, 0:512],
                      reads=(hk, pk), writes=(hk, pk), tag=cg + "pje")

    ln_stats(kb, lambda tt: _htile(kb, tt)[0][:, _htile(kb, tt)[2]:_htile(kb, tt)[2] + 1024],
             NTT, cg + "ln2", "F2", ("G1", "G2"))
    for tt in range(NTT):
        mu, rs = ln_mu_rstd(kb, "F2", tt)
        ht, hk, co = _htile(kb, tt)
        kb.ve("tensor_scalar", tscr[:, 0:1024], ht[:, co:co + 1024], mu, rs,
              ALU.subtract, ALU.mult, reads=(hk, "stats"), writes=("g9t",), tag=cg + "n2")
        for dt in range(8):
            pk = "pt%d" % (dt % 2)
            kb.tr(ps[pk][:, 0:128], tscr[:, dt * 128:(dt + 1) * 128], sb["identb"][:, :],
                  pk, reads=("g9t", "identb"), tag=cg + "t2")
            ft, fk, fo = _fm(kb, dt)
            kb.act(ft[:, fo + tt * 128:fo + (tt + 1) * 128], ps[pk][:, 0:128], AF.Copy,
                   scale=sb["ln2g"][:, dt:dt + 1], bias=sb["ln2b"][:, dt:dt + 1],
                   reads=(pk, "ln2g", "ln2b"), writes=fk + (pk,), tag=cg + "t2e")
        kb.ve("tensor_add", xv[:, tt, :], xv[:, tt, :], ht[:, co:co + 1024],
              reads=(xtk(tt), hk), writes=(xtk(tt),), tag=cg + "xf")

    for tt in range(NTT):
        pk = "p%d" % (tt % 2)
        for kt in range(8):
            ft, fk, fo = _fm(kb, kt)
            kb.mm(ps[pk][:, 0:NE], ft[:, fo + tt * 128:fo + (tt + 1) * 128],
                  sb["gwt"][:, :].rearrange("p (k e) -> p k e", k=8)[:, kt, :],
                  pk, kt == 0, False, reads=("gwt",) + fk, tag=cg + "g")
        kb.mm(ps[pk][:, 0:NE], sb["onesb"][0:1, :], sb["attsh"][0:1, 24:24 + NE],
              pk, False, True, reads=("onesb", "attsh"), tag=cg + "gb")
        kb.ve("tensor_copy", sb["lgt"][:, tt * NE:(tt + 1) * NE], ps[pk][:, 0:NE],
              reads=(pk,), writes=("lgt", pk), tag=cg + "ge")
    kb.ve("tensor_scalar_add", sb["ascr"][:, 8:16], sb["iota8"][:, :], -8.0,
          reads=("iota8",), writes=("ascr",), tag=cg + "iom8")
    for tt in range(NTT):
        g_ = sb["spA"][:, 960:1008]
        lg = sb["lgt"][:, tt * NE:(tt + 1) * NE]
        wf = sb["wfl"][:, tt * NE:(tt + 1) * NE]
        kb.ve("reduce_max", g_[:, 0:1], lg, axis=AX.X, reads=("lgt",), writes=("spA",), tag=cg + "m1")
        kb.ve("tensor_scalar", g_[:, 8:16], lg, g_[:, 0:1], None, ALU.is_equal,
              reads=("lgt", "spA"), writes=("spA",), tag=cg + "eq1")
        kb.ve("scalar_tensor_tensor", g_[:, 16:24], g_[:, 8:16], 1.0, sb["ascr"][:, 8:16],
              ALU.mult, ALU.mult, reads=("spA", "ascr"), writes=("spA",), tag=cg + "sel1a")
        kb.ve("tensor_scalar_add", g_[:, 16:24], g_[:, 16:24], 8.0,
              reads=("spA",), writes=("spA",), tag=cg + "sel1b")
        kb.ve("tensor_reduce", g_[:, 1:2], g_[:, 16:24], AX.X, ALU.min,
              reads=("spA",), writes=("spA",), tag=cg + "i1")
        kb.ve("tensor_scalar", g_[:, 8:16], sb["iota8"][:, :], g_[:, 1:2], None, ALU.is_equal,
              reads=("spA", "iota8"), writes=("spA",), tag=cg + "es1")
        kb.ve("scalar_tensor_tensor", g_[:, 24:32], g_[:, 8:16], NEG, lg,
              ALU.mult, ALU.add, reads=("spA", "lgt"), writes=("spA",), tag=cg + "lg2")
        kb.ve("reduce_max", g_[:, 2:3], g_[:, 24:32], axis=AX.X,
              reads=("spA",), writes=("spA",), tag=cg + "m2")
        kb.ve("tensor_scalar", g_[:, 32:40], g_[:, 24:32], g_[:, 2:3], None, ALU.is_equal,
              reads=("spA",), writes=("spA",), tag=cg + "eq2")
        kb.ve("scalar_tensor_tensor", g_[:, 16:24], g_[:, 32:40], 1.0, sb["ascr"][:, 8:16],
              ALU.mult, ALU.mult, reads=("spA", "ascr"), writes=("spA",), tag=cg + "sel2a")
        kb.ve("tensor_scalar_add", g_[:, 16:24], g_[:, 16:24], 8.0,
              reads=("spA",), writes=("spA",), tag=cg + "sel2b")
        kb.ve("tensor_reduce", g_[:, 3:4], g_[:, 16:24], AX.X, ALU.min,
              reads=("spA",), writes=("spA",), tag=cg + "i2")
        kb.ve("tensor_scalar", g_[:, 32:40], sb["iota8"][:, :], g_[:, 3:4], None, ALU.is_equal,
              reads=("spA", "iota8"), writes=("spA",), tag=cg + "es2")
        kb.ve("tensor_sub", g_[:, 4:5], g_[:, 0:1], g_[:, 2:3],
              reads=("spA",), writes=("spA",), tag=cg + "d12")
        kb.act(g_[:, 5:6], g_[:, 4:5], AF.Sigmoid, reads=("spA",), writes=("spA",), tag=cg + "w1")
        kb.act(g_[:, 6:7], g_[:, 4:5], AF.Sigmoid, scale=-1.0,
               reads=("spA",), writes=("spA",), tag=cg + "w2")
        kb.ve("tensor_scalar", g_[:, 40:48], g_[:, 8:16], g_[:, 5:6], None, ALU.mult,
              reads=("spA",), writes=("spA",), tag=cg + "wa")
        kb.ve("scalar_tensor_tensor", wf, g_[:, 32:40], g_[:, 6:7], g_[:, 40:48],
              ALU.mult, ALU.add, reads=("spA",), writes=("wfl",), tag=cg + "wb")

    TCW = min(512, T)
    NTCH = T // TCW
    midv1 = sb["G1"][:, :].rearrange("p (h t) -> p h t", h=16)
    midv2 = sb["G2"][:, :].rearrange("p (h t) -> p h t", h=16)
    for e in range(NE):
        kb.dma("pool", sb["G7"][0:1, 3 * T:3 * T + 1024], P["b2"][e:e + 1, :],
               writes=("G7",), sgrp="r_b2", tag=cg + "b2l")
        for tc in range(NTCH):
            for o8 in range(8):
                woff = (o8 % 2) * 4096
                kb.dma("pool", sb["G5"][:, woff:woff + 4096].rearrange(
                    "p (k h) -> p k h", k=8),
                    P["W1"][e, o8], writes=G5K[2 * (o8 % 2):2 * (o8 % 2) + 2],
                    sgrp="w1l%d" % (o8 % 2), tag=cg + "w1l")
                for htl in range(4):
                    ht = o8 * 4 + htl
                    pk = "p%d" % (4 + ht % 2)
                    for kt in range(8):
                        ft, fk, fo = _fm(kb, kt)
                        kb.mm(ps[pk][:, 0:TCW], sb["G5"][:, woff + kt * 512 + htl * 128:
                                                         woff + kt * 512 + (htl + 1) * 128],
                              ft[:, fo + tc * TCW:fo + (tc + 1) * TCW], pk,
                              kt == 0, kt == 7,
                              reads=G5K[2 * (o8 % 2):2 * (o8 % 2) + 2] + fk, tag=cg + "m1")
                    mv = midv1 if ht < 16 else midv2
                    mk = "G1" if ht < 16 else "G2"
                    kb.act(mv[:, ht % 16, 0:TCW], ps[pk][:, 0:TCW], AF.Gelu,
                           bias=sb["b1pp"][:, :].rearrange("p (e h) -> p e h", e=NE)[:, e, ht:ht + 1],
                           reads=(pk, "b1pp"), writes=(mk, pk), tag=cg + "m1e")
            for oh in range(2):
                for ktg in range(4):
                    woff = (ktg % 2) * 4096
                    kb.dma("pool", sb["G6"][:, woff:woff + 4096].rearrange(
                        "p (k o) -> p k o", k=8),
                        P["W2"][e, ktg, oh], writes=G6K[2 * (ktg % 2):2 * (ktg % 2) + 2],
                        sgrp="w2l%d" % (ktg % 2), tag=cg + "w2l")
                    for kt in range(8):
                        htg = ktg * 8 + kt
                        mv = midv1 if htg < 16 else midv2
                        mk = "G1" if htg < 16 else "G2"
                        for tb in range(TCW // 128):
                            pk = "p%d" % (tb % 4)
                            kb.mm(ps[pk][:, 0:512],
                                  mv[:, htg % 16, tb * 128:(tb + 1) * 128],
                                  sb["G6"][:, woff + kt * 512:woff + (kt + 1) * 512],
                                  pk, htg == 0, False,
                                  reads=G6K[2 * (ktg % 2):2 * (ktg % 2) + 2] + (mk,), tag=cg + "m2")
                for tb in range(TCW // 128):
                    pk = "p%d" % (tb % 4)
                    kb.mm(ps[pk][:, 0:512], sb["onesb"][0:1, :],
                          sb["G7"][0:1, 3 * T + oh * 512:3 * T + (oh + 1) * 512], pk,
                          False, True, reads=("onesb", "G7"), tag=cg + "m2b")
                    tt = tc * (TCW // 128) + tb
                    kb.ve("scalar_tensor_tensor", xv[:, tt, oh * 512:(oh + 1) * 512],
                          ps[pk][:, 0:512], sb["wfl"][:, tt * NE + e:tt * NE + e + 1],
                          xv[:, tt, oh * 512:(oh + 1) * 512], ALU.mult, ALU.add,
                          reads=(pk, "wfl", xtk(tt)), writes=(xtk(tt), pk), tag=cg + "m2x")
    if ci == 1:
        kb.dbg("x_after_f1", xtm[:, :], [128, NTT * 1024], BF16, reads=tuple("xt%d" % i for i in range(NTT)))


def _bu(kb, im, pt):
    T = kb.T
    if not im:
        gk = G5K if pt < 4 else G6K
        g = "G5" if pt < 4 else "G6"
        reg = ((pt % 4) * T) // 2048
        return kb.sb[g], gk[reg], (pt % 4) * T
    if pt < 4:
        return kb.sb["G7"], "G7", pt * T
    g = "G8" if pt < 6 else "G9"
    return kb.sb[g], g, (pt % 2) * T


G5K = ("G5q0", "G5q1", "G5q2", "G5q3")
G6K = ("G6q0", "G6q1", "G6q2", "G6q3")

_LREG = [("G5", 0, 4, 0), ("G6", 0, 4, 4), ("G7", 1, 4, 0), ("G8", 1, 2, 4), ("G9", 1, 2, 6)]
_LKEY = {"G5": G5K, "G6": G6K, "G7": ("G7",), "G8": ("G8",), "G9": ("G9",)}


def stage_lru(kb, P):
    T, NTT = kb.T, kb.NTT
    sb, ps = kb.sb, kb.ps
    xtm = sb["xtm"]
    xv = xtm[:, :].rearrange("p (t d) -> p t d", d=1024)
    NCH = T // 128
    lam = sb["lam"]
    cg = "l_"

    kb.dma("sp", sb["nuv"][:, :], P["nu"], writes=("nuv",), sgrp="c_nu", tag=cg + "pnu")
    kb.dma("sp", sb["thv"][:, :], P["th"], writes=("thv",), sgrp="c_th", tag=cg + "pth")
    kb.dma("sp", sb["gamv"][:, :], P["gam"], writes=("gamv",), sgrp="c_gam", tag=cg + "pgam")
    kb.act(lam[:, 16:24], sb["nuv"][:, :], AF.Exp, reads=("nuv",), writes=("lam",), tag=cg + "e1")
    kb.act(lam[:, 24:32], lam[:, 16:24], AF.Exp, scale=-1.0,
           reads=("lam",), writes=("lam",), tag=cg + "r")
    kb.act(lam[:, 32:40], sb["thv"][:, :], AF.Exp, reads=("thv",), writes=("lam",), tag=cg + "th")
    kb.act(lam[:, 40:48], lam[:, 32:40], AF.Sin, scale=0.5,
           reads=("lam",), writes=("lam",), tag=cg + "s2")
    kb.act(lam[:, 48:56], lam[:, 32:40], AF.Sin, scale=0.25,
           reads=("lam",), writes=("lam",), tag=cg + "s4")
    kb.act(lam[:, 56:64], lam[:, 40:48], AF.Square, reads=("lam",), writes=("lam",), tag=cg + "q2")
    kb.ve("tensor_scalar", lam[:, 56:64], lam[:, 56:64], -2.0, 1.0, ALU.mult, ALU.add,
          reads=("lam",), writes=("lam",), tag=cg + "cos")
    kb.act(lam[:, 48:56], lam[:, 48:56], AF.Square, reads=("lam",), writes=("lam",), tag=cg + "q4")
    kb.ve("tensor_scalar", lam[:, 48:56], lam[:, 48:56], -2.0, 1.0, ALU.mult, ALU.add,
          reads=("lam",), writes=("lam",), tag=cg + "c2")
    kb.ve("tensor_mul", lam[:, 40:48], lam[:, 40:48], lam[:, 48:56],
          reads=("lam",), writes=("lam",), tag=cg + "sc")
    kb.ve("tensor_scalar_mul", lam[:, 40:48], lam[:, 40:48], 2.0,
          reads=("lam",), writes=("lam",), tag=cg + "sin")
    kb.ve("tensor_mul", lam[:, 0:8], lam[:, 24:32], lam[:, 56:64],
          reads=("lam",), writes=("lam",), tag=cg + "lr")
    kb.ve("tensor_mul", lam[:, 8:16], lam[:, 24:32], lam[:, 40:48],
          reads=("lam",), writes=("lam",), tag=cg + "li")
    kb.act(sb["gexp"][:, :], sb["gamv"][:, :], AF.Exp, reads=("gamv",), writes=("gexp",), tag=cg + "g")
    kb.dma("pool", kb.sb["rows"][0:1, 1024:2048], P["Dlru"], writes=("rows",), sgrp="r_dl", tag=cg + "dl")
    kb.mm(ps["p5"][:, 0:512], sb["onesb"][0:1, :], sb["rows"][0:1, 1024:1536], "p5",
          True, True, reads=("onesb", "rows"), tag=cg + "db1")
    kb.act(sb["wsV"][:, 0:512], ps["p5"][:, 0:512], AF.Copy, reads=("p5",),
           writes=("wsV0", "p5"), tag=cg + "db1e")
    kb.mm(ps["p5"][:, 0:512], sb["onesb"][0:1, :], sb["rows"][0:1, 1536:2048], "p5",
          True, True, reads=("onesb", "rows"), tag=cg + "db2")
    kb.act(sb["wsV"][:, 512:1024], ps["p5"][:, 0:512], AF.Copy, reads=("p5",),
           writes=("wsV0", "p5"), tag=cg + "db2e")

    lprv = sb["lpr"].rearrange("p (e j) -> p e j", e=8)
    lpiv = sb["lpi"].rearrange("p (e j) -> p e j", e=8)
    kb.ve("tensor_copy", lprv[:, :, 0:1], lam[:, 0:8][:, :, None],
          reads=("lam",), writes=("lpr",), tag=cg + "p0r")
    kb.ve("tensor_copy", lpiv[:, :, 0:1], lam[:, 8:16][:, :, None],
          reads=("lam",), writes=("lpi",), tag=cg + "p0i")
    w = 1
    while w < 128:
        m = min(w, 128 - w)
        ar = lprv[:, :, w - 1:w].broadcast_to([128, 8, m])
        ai = lpiv[:, :, w - 1:w].broadcast_to([128, 8, m])
        sA = sb["spA"].rearrange("p (e j) -> p e j", e=8)[:, :, 0:m]
        sB = sb["spB"].rearrange("p (e j) -> p e j", e=8)[:, :, 0:m]
        kb.ve("tensor_mul", sA, lprv[:, :, 0:m], ar, reads=("lpr",), writes=("spA", "sC", "sD"), tag=cg + "pd1")
        kb.ve("tensor_mul", sB, lpiv[:, :, 0:m], ai, reads=("lpr", "lpi"), writes=("spB",), tag=cg + "pd2")
        kb.ve("tensor_sub", sA, sA, sB, reads=("spA", "sC", "sD", "spB"), writes=("spA", "sC", "sD"), tag=cg + "pd3")
        kb.ve("tensor_mul", sB, lprv[:, :, 0:m], ai, reads=("lpr", "lpi"), writes=("spB",), tag=cg + "pd4")
        kb.ve("tensor_copy", lprv[:, :, w:w + m], sA, reads=("spA", "sC", "sD"), writes=("lpr",), tag=cg + "pd5")
        kb.ve("tensor_mul", sA, lpiv[:, :, 0:m], ar, reads=("lpi",), writes=("spA", "sC", "sD"), tag=cg + "pd6")
        kb.ve("tensor_add", lpiv[:, :, w:w + m], sA, sB, reads=("spA", "sC", "sD", "spB"), writes=("lpi",), tag=cg + "pd7")
        w *= 2

    ln_stats(kb, lambda tt: xv[:, tt, :], NTT, cg + "ln", "L", tuple("xt%d" % i for i in range(NTT)))
    for tt in range(NTT):
        mu, rs = ln_mu_rstd(kb, "L", tt)
        ht, hk, co = _htile(kb, tt)
        kb.ve("tensor_scalar", ht[:, co:co + 1024], xv[:, tt, :], mu, rs,
              ALU.subtract, ALU.mult, reads=(xtk(tt), "stats"), writes=(hk,), tag=cg + "n")
        for dt in range(8):
            pk = "pt%d" % (dt % 2)
            kb.tr(ps[pk][:, 0:128], ht[:, co + dt * 128:co + (dt + 1) * 128],
                  sb["identb"][:, :], pk, reads=(hk, "identb"), tag=cg + "t")
            ft, fk, fo = _fm(kb, dt)
            kb.act(ft[:, fo + tt * 128:fo + (tt + 1) * 128], ps[pk][:, 0:128], AF.Copy,
                   reads=(pk,), writes=fk + (pk,), tag=cg + "te")

    for pt in range(8):
        for im in range(2):
            wsb = "wsA%d" % ((pt * 2 + im) % 3)
            src = P["BimT"] if im else P["BreT"]
            kb.dma("pool", sb[wsb][:, :].rearrange("p (k o) -> p k o", k=8),
                   src[pt], writes=(wsb,), tag=cg + "bw")
            bt, bk_, bco = _bu(kb, im, pt)
            BW = min(512, T)
            for tc in range(T // BW):
                pk = "p%d" % (2 * im + tc % 2)
                for kt in range(8):
                    ft, fk, fo = _fm(kb, kt)
                    kb.mm(ps[pk][:, 0:BW], sb[wsb][:, :].rearrange(
                        "p (k o) -> p k o", k=8)[:, kt, :],
                        ft[:, fo + tc * BW:fo + (tc + 1) * BW], pk, kt == 0, kt == 7,
                        reads=(wsb,) + fk, tag=cg + "bu")
                kb.act(bt[:, bco + tc * BW:bco + (tc + 1) * BW], ps[pk][:, 0:BW],
                       AF.Copy, scale=sb["gexp"][:, pt:pt + 1],
                       reads=(pk, "gexp"), writes=(bk_, pk), tag=cg + "bue")

    kb.dbg("bu_re", sb["G5"][:, 0:4 * T], [128, 4 * T], BF16, reads=G5K)
    kb.dbg("bu_im", sb["G7"][:, 0:4 * T], [128, 4 * T], BF16, reads=("G7",))
    lh = sb["lh"]
    kb.ve("memset", lh[:, :], 0.0, writes=("lhr", "lhi"), tag=cg + "h0")
    hre = lh[:, 0:128].rearrange("p (e c) -> p e c", e=8)[:, :, 0:NCH]
    him = lh[:, 128:256].rearrange("p (e c) -> p e c", e=8)[:, :, 0:NCH]
    lrv = lam[:, 0:8][:, :, None].broadcast_to([128, 8, NCH])
    liv = lam[:, 8:16][:, :, None].broadcast_to([128, 8, NCH])
    sA8 = sb["spA"][:, 0:128].rearrange("p (e c) -> p e c", e=8)[:, :, 0:NCH]
    sB8 = sb["spB"][:, 0:128].rearrange("p (e c) -> p e c", e=8)[:, :, 0:NCH]
    sC8 = sb["spA"][:, 128:256].rearrange("p (e c) -> p e c", e=8)[:, :, 0:NCH]

    def rview(g, npt, bpt, j):
        return sb[g][:, :].rearrange("p (a c j) -> p a c j", a=8192 // 128 // 8 if False else None)

    def bu_col(g, npt, j):
        return sb[g][:, 0:npt * T].rearrange("p (a c j) -> p a c j", a=npt, j=128)[:, :, 0:NCH, j]

    sD8 = sb["dacc"][:, 128:256].rearrange("p (e c) -> p e c", e=8)[:, :, 0:NCH]
    for j in range(128):
        kb.ve("tensor_mul", sA8, hre, lrv, reads=("lhr", "lam"), writes=("spA",), tag=cg + "a1")
        kb.ve("tensor_mul", sC8, him, liv, reads=("lhi", "lam"), writes=("sC",), tag=cg + "a2")
        kb.ve("tensor_sub", sA8, sA8, sC8, reads=("spA", "sC"), writes=("spA",), tag=cg + "a3")
        kb.ve("tensor_mul", sB8, him, lrv, reads=("lhi", "lam"), writes=("spB",),
              eng_name="pool", tag=cg + "b1")
        kb.ve("tensor_mul", sD8, hre, liv, reads=("lhr", "lam"), writes=("sD",),
              eng_name="pool", tag=cg + "b2")
        kb.ve("tensor_add", sB8, sB8, sD8, reads=("spB", "sD"), writes=("spB",),
              eng_name="pool", tag=cg + "b3")
        for g, im, npt, bpt in _LREG:
            hv = (him if im else hre)[:, bpt:bpt + npt, :]
            tv = (sB8 if im else sA8)[:, bpt:bpt + npt, :]
            bj = bu_col(g, npt, j)
            kb.ve("tensor_add", hv, tv, bj, reads=("spB" if im else "spA",) + _LKEY[g],
                  writes=("lhi" if im else "lhr",),
                  eng_name="pool" if im else "dve", tag=cg + "hu")
            kb.act(bj, hv, AF.Copy, reads=("lhi" if im else "lhr",),
                   writes=_LKEY[g], tag=cg + "xw")

    xer = hre
    xei = him
    kb.ve("tensor_copy", lam[:, 16:24], lam[:, 0:8], reads=("lam",), writes=("lam",), tag=cg + "q0")
    kb.ve("tensor_copy", lam[:, 24:32], lam[:, 8:16], reads=("lam",), writes=("lam",), tag=cg + "q0i")
    for _sq in range(7):
        kb.ve("tensor_mul", lam[:, 32:40], lam[:, 16:24], lam[:, 16:24],
              reads=("lam",), writes=("lam",), tag=cg + "sq1")
        kb.ve("tensor_mul", lam[:, 40:48], lam[:, 24:32], lam[:, 24:32],
              reads=("lam",), writes=("lam",), tag=cg + "sq2")
        kb.ve("tensor_mul", lam[:, 48:56], lam[:, 16:24], lam[:, 24:32],
              reads=("lam",), writes=("lam",), tag=cg + "sq3")
        kb.ve("tensor_sub", lam[:, 16:24], lam[:, 32:40], lam[:, 40:48],
              reads=("lam",), writes=("lam",), tag=cg + "sq4")
        kb.ve("tensor_scalar_mul", lam[:, 24:32], lam[:, 48:56], 2.0,
              reads=("lam",), writes=("lam",), tag=cg + "sq5")
    l128r = lam[:, 16:24][:, :, None]
    l128i = lam[:, 24:32][:, :, None]
    xer = hre
    xei = him
    for c in range(1, NCH):
        sA1 = sb["spA"].rearrange("p (e c) -> p e c", e=8)[:, :, 0:1]
        sB1 = sb["spB"].rearrange("p (e c) -> p e c", e=8)[:, :, 0:1]
        sA2 = sb["spA"].rearrange("p (e c) -> p e c", e=8)[:, :, 1:2]
        kb.ve("tensor_mul", sA1, xer[:, :, c - 1:c], l128r, reads=("lhr", "lam"), writes=("spA", "sC", "sD"), tag=cg + "xc1")
        kb.ve("tensor_mul", sB1, xei[:, :, c - 1:c], l128i, reads=("lhi", "lam"), writes=("spB",), tag=cg + "xc2")
        kb.ve("tensor_sub", sA1, sA1, sB1, reads=("spA", "sC", "sD", "spB"), writes=("spA", "sC", "sD"), tag=cg + "xc3")
        kb.ve("tensor_mul", sB1, xei[:, :, c - 1:c], l128r, reads=("lhi", "lam"), writes=("spB",), tag=cg + "xc5")
        kb.ve("tensor_mul", sA2, xer[:, :, c - 1:c], l128i, reads=("lhr", "lam"), writes=("spA", "sC", "sD"), tag=cg + "xc6")
        kb.ve("tensor_add", xer[:, :, c:c + 1], xer[:, :, c:c + 1], sA1,
              reads=("lhr", "spA", "sC", "sD"), writes=("lhr",), tag=cg + "xc4")
        kb.ve("tensor_add", sA2, sA2, sB1, reads=("spA", "sC", "sD", "spB"), writes=("spA", "sC", "sD"), tag=cg + "xc7")
        kb.ve("tensor_add", xei[:, :, c:c + 1], xei[:, :, c:c + 1], sA2,
              reads=("lhi", "spA", "sC", "sD"), writes=("lhi",), tag=cg + "xc8")

    for c in range(1, NCH):
        for g, im, npt, bpt in _LREG:
            xcol = sb[g][:, 0:npt * T].rearrange("p (a c j) -> p a c j", a=npt, j=128)[:, :, c, :]
            pr = lprv[:, bpt:bpt + npt, :]
            pi = lpiv[:, bpt:bpt + npt, :]
            XsR = hre[:, bpt:bpt + npt, c - 1:c].broadcast_to([128, npt, 128])
            XsI = him[:, bpt:bpt + npt, c - 1:c].broadcast_to([128, npt, 128])
            sA = sb["spA"].rearrange("p (e j) -> p e j", e=8)[:, 0:npt, :]
            sB = sb["spB"].rearrange("p (e j) -> p e j", e=8)[:, 0:npt, :]
            if not im:
                kb.ve("tensor_mul", sA, pr, XsR, reads=("lpr", "lhr"), writes=("spA", "sC", "sD"), tag=cg + "c1")
                kb.ve("tensor_mul", sB, pi, XsI, reads=("lpi", "lhi"), writes=("spB",), tag=cg + "c2")
                kb.ve("tensor_sub", sA, sA, sB, reads=("spA", "sC", "sD", "spB"), writes=("spA", "sC", "sD"), tag=cg + "c3")
            else:
                kb.ve("tensor_mul", sA, pr, XsI, reads=("lpr", "lhi"), writes=("spA", "sC", "sD"), tag=cg + "c4")
                kb.ve("tensor_mul", sB, pi, XsR, reads=("lpi", "lhr"), writes=("spB",), tag=cg + "c5")
                kb.ve("tensor_add", sA, sA, sB, reads=("spA", "sC", "sD", "spB"), writes=("spA", "sC", "sD"), tag=cg + "c6")
            kb.ve("tensor_add", xcol, xcol, sA, reads=_LKEY[g] + ("spA", "sC", "sD"),
                  writes=_LKEY[g], tag=cg + "c7")

    kb.dbg("xsc_re", sb["G5"][:, 0:4 * T], [128, 4 * T], BF16, reads=G5K)
    kb.dbg("xsc_im", sb["G7"][:, 0:4 * T], [128, 4 * T], BF16, reads=("G7",))
    for tt in range(NTT):
        ht, hk, co = _htile(kb, tt)
        kb.ve("tensor_mul", sb["spA"][:, 0:1024], ht[:, co:co + 1024],
              sb["wsV"][:, 0:1024], reads=(hk, "wsV0"), writes=("spA", "sC", "sD"), tag=cg + "du")
        kb.ve("tensor_add", xv[:, tt, :], xv[:, tt, :], sb["spA"][:, 0:1024],
              reads=(xtk(tt), "spA", "sC", "sD"), writes=(xtk(tt),), tag=cg + "dua")
    for tbg in range(NTT // 4 + (1 if NTT % 4 else 0)):
        tbs = range(tbg * 4, min(NTT, tbg * 4 + 4))
        for oh in range(2):
            for pt in range(8):
                wpar = pt % 2
                kb.dma("pool", sb["wsV"][:, wpar * 1024:wpar * 1024 + 512],
                       P["CreT"][pt, oh], writes=("wsV%d" % wpar,), tag=cg + "cr")
                kb.dma("pool", sb["wsV"][:, wpar * 1024 + 512:wpar * 1024 + 1024],
                       P["CimT"][pt, oh], writes=("wsV%d" % wpar,), tag=cg + "ci")
                kb.ve("tensor_scalar_mul", sb["wsV"][:, wpar * 1024 + 512:wpar * 1024 + 1024],
                      sb["wsV"][:, wpar * 1024 + 512:wpar * 1024 + 1024], -1.0,
                      reads=("wsV%d" % wpar,), writes=("wsV%d" % wpar,), tag=cg + "cin")
                for tb in tbs:
                    pk = "p%d" % (tb % 4)
                    rt, rk, rco = _bu(kb, 0, pt)
                    it, ik, ico = _bu(kb, 1, pt)
                    kb.mm(ps[pk][:, 0:512], rt[:, rco + tb * 128:rco + (tb + 1) * 128],
                          sb["wsV"][:, wpar * 1024:wpar * 1024 + 512], pk,
                          pt == 0, False, reads=(rk, "wsV%d" % wpar), tag=cg + "cx1")
                    kb.mm(ps[pk][:, 0:512], it[:, ico + tb * 128:ico + (tb + 1) * 128],
                          sb["wsV"][:, wpar * 1024 + 512:wpar * 1024 + 1024], pk,
                          False, pt == 7, reads=(ik, "wsV%d" % wpar), tag=cg + "cx2")
            for tb in tbs:
                pk = "p%d" % (tb % 4)
                kb.ve("tensor_add", xv[:, tb, oh * 512:(oh + 1) * 512],
                      xv[:, tb, oh * 512:(oh + 1) * 512], ps[pk][:, 0:512],
                      reads=(xtk(tb), pk), writes=(xtk(tb), pk), tag=cg + "cxe")
    kb.dbg("x_after_b", xtm[:, :], [128, NTT * 1024], BF16, reads=tuple("xt%d" % i for i in range(NTT)))


# version 21 (final)
# speedup vs baseline: 1.2174x; 1.0076x over previous
import numpy as np
import ml_dtypes
import concourse.bass as bass
import concourse.mybir as mybir
from concourse.bass_utils import run_bass_kernel_spmd
from contextlib import ExitStack

F32 = mybir.dt.float32
BF16 = mybir.dt.bfloat16
AF = mybir.ActivationFunctionType
ALU = mybir.AluOpType
AX = mybir.AxisListType

D = 1024
DI = 2048
DS = 16
DTR = 64
RK = 16
NH = 16
HD = 64
NE = 8
FF = 4096
EPS = 1e-5
NEG = -1e30



class Prog:

    def __init__(self):
        self.ops = []
        self.lastw = {}
        self.readers = {}
        self.qcount = {}
        self.alias = {}

    def op(self, eng, emit, reads=(), writes=(), q=None, sgrp=None, tag=""):
        reads = tuple(self.alias.get(r, r) for r in reads)
        writes = tuple(self.alias.get(w, w) for w in writes)
        i = len(self.ops)
        rec = dict(i=i, eng=eng, emit=emit, q=q, sgrp=sgrp, qidx=None, inc=False,
                   qwaits={}, ewaits={}, sdep=-1, tag=tag)
        if q is not None:
            if sgrp is None:
                sgrp = q + ":" + (writes[0] if writes else "misc")
            rec["sgrp"] = sgrp
            self.qcount[sgrp] = self.qcount.get(sgrp, 0) + 1
            rec["qidx"] = self.qcount[sgrp]
        deps = set()
        for r in reads:
            d = self.lastw.get(r)
            if d is not None:
                deps.add(d)
        for w in writes:
            d = self.lastw.get(w)
            if d is not None:
                deps.add(d)
            deps.update(self.readers.get(w, ()))
        deps.discard(i)
        for d in deps:
            dr = self.ops[d]
            if dr["q"] is not None:
                rec["qwaits"][dr["sgrp"]] = max(rec["qwaits"].get(dr["sgrp"], 0), 16 * dr["qidx"])
            elif dr["eng"] == eng and q is None and eng == "pe":
                pass
            elif dr["eng"] == eng:
                rec["sdep"] = max(rec["sdep"], d)
            else:
                dr["inc"] = True
                rec["ewaits"][dr["eng"]] = max(rec["ewaits"].get(dr["eng"], -1), d)
        for r in reads:
            self.readers.setdefault(r, []).append(i)
        for w in writes:
            self.lastw[w] = i
            self.readers[w] = []
        self.ops.append(rec)
        return i

    def emit_all(self, block, esem, qsem):
        incnum = {}
        counts = {}
        for rec in self.ops:
            if rec["inc"] and rec["q"] is None:
                counts[rec["eng"]] = counts.get(rec["eng"], 0) + 1
                incnum[rec["i"]] = counts[rec["eng"]]
        self.final_inc_counts = counts
        self.final_q_counts = {g: 16 * n for g, n in self.qcount.items()}

        streams = {e: [rec for rec in self.ops if rec["eng"] == e]
                   for e in ("pe", "act", "dve", "pool", "sp")}

        def run_stream(eng_name):
            def body(eng):
                wm_q = {}
                wm_e = {}
                last_drain = -1
                prev_idx = -1
                for rec in streams[eng_name]:
                    for qn, val in sorted(rec["qwaits"].items()):
                        if wm_q.get(qn, 0) < val:
                            eng.wait_ge(qsem[qn], val)
                            wm_q[qn] = val
                    for en, didx in sorted(rec["ewaits"].items()):
                        val = incnum[didx]
                        if wm_e.get(en, 0) < val:
                            eng.wait_ge(esem[en], val)
                            wm_e[en] = val
                    if rec["sdep"] > last_drain:
                        eng.drain()
                        last_drain = prev_idx
                    inst = rec["emit"](eng)
                    if rec["q"] is not None:
                        inst.then_inc(qsem[rec["sgrp"]], 16)
                    elif rec["inc"]:
                        inst.then_inc(esem[eng_name], 1)
                    prev_idx = rec["i"]
            return body

        block.tensor(run_stream("pe"))
        block.scalar(run_stream("act"))
        block.vector(run_stream("dve"))
        block.gpsimd(run_stream("pool"))
        block.sync(run_stream("sp"))

        def finale(eng):
            for qn, val in sorted(self.final_q_counts.items()):
                if val:
                    eng.wait_ge(qsem[qn], val)
            for en, val in sorted(self.final_inc_counts.items()):
                eng.wait_ge(esem[en], val)
        block.sync(finale)


class KB:

    def __init__(self, nc, T, dbg_names=()):
        self.nc = nc
        self.T = T
        self.NTT = T // 128
        self.p = Prog()
        self.st = ExitStack()
        self.sb = {}
        self.ps = {}
        self.dram_in = {}
        self.dram_out = {}
        self.dbg_names = set(dbg_names)
        self.sbuf_bytes = 0

    def din(self, name, shape, dt=F32):
        h = self.nc.dram_tensor(name, list(shape), dt, kind="ExternalInput")
        self.dram_in[name] = h.ap()
        return self.dram_in[name]

    def dout(self, name, shape, dt=F32):
        h = self.nc.dram_tensor(name, list(shape), dt, kind="ExternalOutput")
        self.dram_out[name] = h.ap()
        return self.dram_out[name]

    def t(self, name, shape, dt=BF16):
        h = self.st.enter_context(self.nc.sbuf_tensor(name, list(shape), dt))
        self.sb[name] = h
        esz = {F32: 4, BF16: 2}[dt]
        self.sbuf_bytes += int(np.prod(shape[1:])) * esz
        return h

    def pt(self, name, shape, dt=F32):
        h = self.st.enter_context(self.nc.psum_tensor(name, list(shape), dt))
        self.ps[name] = h
        return h

    def dma(self, qname, out_ap, in_ap, reads=(), writes=(), sgrp=None, tag=""):
        def emit(eng):
            return eng.dma_start(out=out_ap, in_=in_ap)
        self.p.op(qname, emit, reads=reads, writes=writes, q=qname, sgrp=sgrp, tag=tag)

    def mm(self, out_ap, lhsT_ap, rhs_ap, psum_key, start, stop, reads=(), tag=""):
        def emit(eng):
            return eng.matmul(out_ap, lhsT_ap, rhs_ap, start=start, stop=stop)
        self.p.op("pe", emit, reads=tuple(reads), writes=(psum_key,), tag=tag)

    def tr(self, out_ap, in_ap, ident_ap, psum_key, reads=(), tag=""):
        def emit(eng):
            return eng.transpose(out_ap, in_ap, ident_ap)
        self.p.op("pe", emit, reads=tuple(reads), writes=(psum_key,), tag=tag)

    def act(self, out_ap, in_ap, func, reads=(), writes=(), scale=None, bias=None,
            accum=None, tag=""):
        if func == AF.Copy and bias is not None and not isinstance(bias, float):
            func = AF.Identity
        kw = {}
        if scale is not None:
            kw["scale"] = scale
        if bias is not None:
            kw["bias"] = bias
        if accum is not None:
            kw["accum_out"] = accum

        def emit(eng):
            return eng.activation(out_ap, in_ap, func, **kw)
        self.p.op("act", emit, reads=tuple(reads), writes=tuple(writes), tag=tag)

    def ve(self, method, *args, reads=(), writes=(), eng_name="dve", tag="", **kw):
        def emit(eng):
            return getattr(eng, method)(*args, **kw)
        self.p.op(eng_name, emit, reads=tuple(reads), writes=tuple(writes), tag=tag)

    def dbg(self, name, ap, shape, dt=F32, reads=()):
        if name not in self.dbg_names:
            return
        o = self.dout("dbg_" + name, shape, dt)
        self.dma("sp", o, ap, reads=tuple(reads), sgrp="dbg_" + name, tag="dbg" + name)



def ln_stats(kb, src_fn, n_tiles, tag, skey, keys_read):
    sm = kb.sb["stats"]
    bn = kb.sb["spB"]
    K = "stats"
    for tt in range(n_tiles):
        full = src_fn(tt)
        kb.ve("bn_stats", bn[:, tt * 12:tt * 12 + 6], full[:, 0:512],
              reads=keys_read + ("spB",), writes=("spB",), tag=tag + "bs")
        kb.ve("bn_stats", bn[:, tt * 12 + 6:tt * 12 + 12], full[:, 512:1024],
              reads=keys_read + ("spB",), writes=("spB",), tag=tag + "bs")
        kb.ve("bn_aggr", sm[:, 4 * tt:4 * tt + 2], bn[:, tt * 12:tt * 12 + 12],
              reads=("spB",), writes=(K,), tag=tag + "ba")
    var_v = sm[:, :].rearrange("p (t c) -> p t c", c=4)[:, 0:n_tiles, 1]
    lnv_v = sm[:, :].rearrange("p (t c) -> p t c", c=4)[:, 0:n_tiles, 2]
    rst_v = sm[:, :].rearrange("p (t c) -> p t c", c=4)[:, 0:n_tiles, 3]
    kb.ve("tensor_scalar_add", lnv_v, var_v, EPS, reads=(K,), writes=(K,), tag=tag + "ep")
    kb.act(lnv_v, lnv_v, AF.Ln, reads=(K,), writes=(K,), tag=tag + "ln")
    kb.act(rst_v, lnv_v, AF.Exp, scale=-0.5, reads=(K,), writes=(K,), tag=tag + "ex")
    return K


def ln_mu_rstd(kb, skey, tt):
    sm = kb.sb["stats"]
    return sm[:, 4 * tt:4 * tt + 1], sm[:, 4 * tt + 3:4 * tt + 4]


def tm_to_fm(kb, src_ap_fn, dst, dst_key, n_tt, tt0_tok, ndt, tag, keys_read,
             scale=None, bias=None, dt0=0, view_d=None):
    ident = kb.sb["identb"]
    for tt in range(n_tt):
        for dt in range(ndt):
            pk = "pt%d" % ((tt * ndt + dt) % 2)
            pt = kb.ps[pk]
            kb.tr(pt[:, 0:128], src_ap_fn(tt, dt), ident[:, :], pk,
                  reads=keys_read + ("identb",), tag=tag + "tr")
            dv = dst[:, :].rearrange("p (d t) -> p d t", d=(view_d or ndt))[
                :, dt0 + dt, tt0_tok + tt * 128: tt0_tok + (tt + 1) * 128]
            kb.act(dv, pt[:, 0:128], AF.Copy, scale=scale, bias=bias,
                   reads=(pk,), writes=(dst_key, pk), tag=tag + "ev")


def stage_a(kb, P):
    T, NTT = kb.T, kb.NTT
    TcA = min(512, T)
    NCH = T // TcA
    NTC = TcA // 128
    sb = kb.sb
    ps = kb.ps
    xtm = sb["xtm"]
    G = {k: sb[k] for k in ("G1", "G2", "G3", "G4", "G5", "G6", "G7", "G8", "G9")}
    u = G["G1"]
    du = G["G2"]
    xsr = G["G7"]
    bc = G["G4"]
    ya = G["G5"]
    xnf = G["G3"]
    zb = G["G6"]
    xnt = G["G8"]

    uv = u[:, :].rearrange("p (d t) -> p d t", d=16)
    duv = du[:, :].rearrange("p (d t) -> p d t", d=16)
    xsrv = xsr[:, :].rearrange("p (d t) -> p d t", d=16)
    bcv = bc[:, :].rearrange("p (d t) -> p d t", d=16)
    yav = ya[:, :].rearrange("p (d t) -> p d t", d=16)
    xnfv = xnf[:, :].rearrange("p (d t) -> p d t", d=16)
    hsl = xnf[:, 4096:8192].rearrange("p (n t) -> p n t", n=8)
    zbv = zb[:, :].rearrange("p (d t) -> p d t", d=16)
    ball = zb[:, 0:8 * TcA].rearrange("p (n t) -> p n t", n=8)
    asl = zb[:, 4096:4096 + 2 * TcA].rearrange("p (n t) -> p n t", n=2)
    xntv = xnt[:, :].rearrange("p (t d) -> p t d", d=1024)
    dlt = lambda dt, w: xsrv[:, dt, 3:3 + w]
    yak = lambda dt: G5K[dt // 4]
    zbk = lambda ot: G6K[ot // 4]

    stk = ln_stats(kb, lambda tt: xtm[:, :].rearrange("p (t d) -> p t d", d=1024)[:, tt, :],
                   NTT, "lnA", "A", tuple("xt%d" % i for i in range(NTT)))

    def ln_front(c):
        for tl in range(NTC):
            tt = c * NTC + tl
            mu, rs = ln_mu_rstd(kb, "A", tt)
            src = xtm[:, :].rearrange("p (t d) -> p t d", d=1024)[:, tt, :]
            kb.ve("tensor_scalar", xntv[:, tl, :], src, mu, rs, ALU.subtract, ALU.mult,
                  reads=(xtk(tt), stk), writes=("G8r%d" % tl,), tag="axn")
        tm_to_fm(kb, lambda tl, dt: xntv[:, tl, dt * 128:(dt + 1) * 128],
                 xnf, "G3a", NTC, 0, 8, "axf",
                 ("G8r0", "G8r1", "G8r2", "G8r3"), view_d=16)

    ln_front(0)
    for c in range(NCH):
        for kt in range(8):
            kb.mm(ps["p4"][0:16, 0:TcA], sb["laiT"][:, :].rearrange(
                "p (k r) -> p k r", k=8)[:, kt, :], xnfv[:, kt, 0:TcA],
                "p4", kt == 0, kt == 7, reads=("laiT", "G3a"), tag="axl")
        kb.ve("tensor_copy", sb["xl"][0:16, 0:TcA], ps["p4"][0:16, 0:TcA],
              reads=("p4",), writes=("xl", "p4"), tag="axl_ev")

        for ot in range(16):
            wbuf = "wsA%d" % (ot % 3)
            kb.dma("pool", sb[wbuf][:, :].rearrange("p (k o) -> p k o", k=8),
                   P["w_inT"][ot], writes=(wbuf,), tag="awin")
            pk = "p%d" % (ot % 2)
            for kt in range(8):
                kb.mm(ps[pk][:, 0:TcA], sb[wbuf][:, :].rearrange(
                    "p (k o) -> p k o", k=8)[:, kt, :], xnfv[:, kt, 0:TcA],
                    pk, kt == 0, False, reads=(wbuf, "G3a"), tag="axz")
            kb.mm(ps[pk][:, 0:TcA], sb["lbi_s"][0:16, ot * 128:(ot + 1) * 128],
                  sb["xl"][0:16, 0:TcA], pk, False, True,
                  reads=("lbi_s", "xl"), tag="axzl")
            kb.act(xsrv[:, ot, 3:3 + TcA], ps[pk][:, 0:TcA], AF.Copy,
                   reads=(pk,), writes=("G7", pk), tag="axz_ev")

        if c == 0:
            kb.ve("memset", xsrv[:, :, 0:3], 0.0, writes=("G7",), tag="aconv0")
        for dt in range(16):
            dacc = sb["dacc"]
            cwt = sb["cw"][:, :].rearrange("p (d k) -> p d k", d=16)
            kb.ve("tensor_scalar", dacc[:, 0:TcA], xsrv[:, dt, 0:TcA],
                  cwt[:, dt, 0:1], None, ALU.mult,
                  reads=("G7", "cw"), writes=("spA",), tag="acv")
            for k in range(1, 4):
                kb.ve("scalar_tensor_tensor", dacc[:, 0:TcA], xsrv[:, dt, k:k + TcA],
                      cwt[:, dt, k:k + 1], dacc[:, 0:TcA], ALU.mult, ALU.add,
                      reads=("G7", "cw", "spA"), writes=("spA",), tag="acv")
            kb.act(uv[:, dt, 0:TcA], dacc[:, 0:TcA], AF.Silu,
                   bias=sb["cb"][:, dt:dt + 1],
                   reads=("spA", "cb"), writes=("G1",), tag="acs")
        if c + 1 < NCH:
            kb.ve("tensor_copy", xsrv[:, :, 0:3], xsrv[:, :, TcA:TcA + 3],
                  reads=("G7",), writes=("G7",), tag="ahalo")

        for kt in range(16):
            kb.mm(ps["p4"][0:96, 0:TcA], sb["wxpT"][:, :].rearrange(
                "p (d r) -> p d r", d=16)[:, kt, :], uv[:, kt, 0:TcA],
                "p4", kt == 0, kt == 15, reads=("wxpT", "G1"), tag="axd")
        kb.ve("tensor_copy", sb["xdbl"][0:96, 0:TcA], ps["p4"][0:96, 0:TcA],
              reads=("p4",), writes=("xdbl", "p4"), tag="axd_ev")

        for dt in range(16):
            pk = "p%d" % (dt % 2)
            kb.mm(ps[pk][:, 0:TcA], sb["wdtT"][0:64, dt * 128:(dt + 1) * 128],
                  sb["xdbl"][0:64, 0:TcA], pk, True, True,
                  reads=("wdtT", "xdbl"), tag="adl")
            spA, spB = sb["spA"], sb["spB"]
            bdt1 = sb["bdt"][:, dt:dt + 1]
            kb.act(spA[:, 0:TcA], ps[pk][:, 0:TcA], AF.Abs, bias=bdt1,
                   reads=(pk, "bdt"), writes=("spA",), tag="asp1")
            kb.act(spB[:, 0:TcA], spA[:, 0:TcA], AF.Exp, scale=-1.0,
                   reads=("spA",), writes=("spB",), tag="asp2")
            kb.act(spA[:, 0:TcA], spB[:, 0:TcA], AF.Ln, bias=1.0,
                   reads=("spB",), writes=("spA",), tag="asp3")
            kb.act(spB[:, 0:TcA], ps[pk][:, 0:TcA], AF.Relu, bias=bdt1,
                   reads=(pk, "bdt"), writes=("spB", pk), tag="asp4")
            kb.ve("tensor_add", dlt(dt, TcA), spA[:, 0:TcA], spB[:, 0:TcA],
                  reads=("spA", "spB"), writes=("G7",), tag="asp5")
            kb.ve("tensor_mul", duv[:, dt, 0:TcA], dlt(dt, TcA), uv[:, dt, 0:TcA],
                  reads=("G7", "G1"), writes=("G2",), tag="adu")
            kb.ve("tensor_scalar", yav[:, dt, 0:TcA], uv[:, dt, 0:TcA],
                  sb["Dp"][:, dt:dt + 1], None, ALU.mult,
                  reads=("G1", "Dp"), writes=(yak(dt),), tag="ayi")

        if c == 0:
            kb.ve("memset", sb["hcar"][:, :], 0.0, writes=("hcar",), tag="ahc0")
        kb.dma("pool", P["scr_bc"][:, 0:TcA], sb["xdbl"][64:96, 0:TcA],
               reads=("xdbl",), writes=("scrBC",), tag="abc_st")
        pb8 = zb[:, 8 * TcA:16 * TcA].rearrange("p (n t) -> p n t", n=8)
        g6r = lambda lo, hi: tuple(G6K[r] for r in range(lo // 2048, (hi - 1) // 2048 + 1))
        ballk = g6r(0, 8 * TcA)
        pbak = g6r(8 * TcA, 12 * TcA)
        pbbk = g6r(12 * TcA, 16 * TcA)
        aslg = sb["G9"][:, 0:2 * TcA].rearrange("p (n t) -> p n t", n=2)
        pend = []
        for g in range(2):
            kb.dma("pool", bcv[:, 0:8, 0:TcA],
                   P["scr_bc"][g * 8:g * 8 + 8, 0:TcA][None, :, :].broadcast_to([128, 8, TcA]),
                   reads=("scrBC",), writes=("G4a",), tag="abc_ldB")
            kb.dma("pool", bcv[:, 8:16, 0:TcA],
                   P["scr_bc"][16 + g * 8:16 + g * 8 + 8, 0:TcA][None, :, :].broadcast_to([128, 8, TcA]),
                   reads=("scrBC",), writes=("G4b",), tag="abc_ldC")
            for dt in range(16):
                kb.ve("tensor_mul", ball[:, :, 0:TcA],
                      duv[:, dt:dt + 1, 0:TcA].broadcast_to([128, 8, TcA]),
                      bcv[:, 0:8, 0:TcA],
                      reads=("G2", "G4a"), writes=ballk, tag="asb")
                if len(pend) >= 2:
                    pend.pop(0)()
                for n in range(8):
                    nn = g * 8 + n
                    par = n % 2
                    kb.act(aslg[:, par, 0:TcA], dlt(dt, TcA), AF.Exp,
                           scale=sb["Aneg"][:, :].rearrange("p (d n) -> p d n", d=16)[:, dt, nn:nn + 1],
                           reads=("G7", "Aneg"), writes=("g9p%d" % par,), tag="asa")
                    kb.ve("tensor_tensor_scan", hsl[:, n, 0:TcA], aslg[:, par, 0:TcA],
                          ball[:, n, 0:TcA],
                          sb["hcar"][:, dt * 16 + nn:dt * 16 + nn + 1],
                          ALU.mult, ALU.add,
                          reads=("g9p%d" % par,) + ballk + ("hcar",),
                          writes=("hs%d" % n,), tag="asc")
                kb.act(sb["hcar"][:, dt * 16 + g * 8:dt * 16 + g * 8 + 8],
                       hsl[:, :, TcA - 1], AF.Copy,
                       reads=tuple("hs%d" % i for i in range(8)), writes=("hcar",), tag="ahc")
                kb.ve("tensor_mul", pb8[:, 0:4, 0:TcA], hsl[:, 0:4, 0:TcA], bcv[:, 8:12, 0:TcA],
                      reads=("hs0", "hs1", "hs2", "hs3", "G4b"),
                      writes=pbak, eng_name="pool", tag="asya")
                kb.ve("tensor_mul", pb8[:, 4:8, 0:TcA], hsl[:, 4:8, 0:TcA], bcv[:, 12:16, 0:TcA],
                      reads=("hs4", "hs5", "hs6", "hs7", "G4b"),
                      writes=pbbk, eng_name="pool", tag="asyb")
                kb.ve("tensor_add", pb8[:, 0:4, 0:TcA], pb8[:, 0:4, 0:TcA], pb8[:, 4:8, 0:TcA],
                      reads=pbak + pbbk, writes=pbak, eng_name="pool", tag="atr1")

                def tail(dt=dt):
                    kb.ve("tensor_add", pb8[:, 0:2, 0:TcA], pb8[:, 0:2, 0:TcA],
                          pb8[:, 2:4, 0:TcA], reads=pbak, writes=pbak, tag="atr2")
                    kb.ve("tensor_add", pb8[:, 0, 0:TcA], pb8[:, 0, 0:TcA],
                          pb8[:, 1, 0:TcA], reads=pbak, writes=pbak, tag="atr3")
                    kb.ve("tensor_add", yav[:, dt, 0:TcA], yav[:, dt, 0:TcA],
                          pb8[:, 0, 0:TcA], reads=(yak(dt),) + pbak, writes=(yak(dt),), tag="atr4")
                pend.append(tail)
        while pend:
            pend.pop(0)()

        for ot in range(16):
            wbuf = "wsA%d" % (ot % 3)
            kb.dma("pool", sb[wbuf][:, :].rearrange("p (k o) -> p k o", k=8),
                   P["w_inT"][16 + ot], writes=(wbuf,), tag="awz")
            pk = "p%d" % (ot % 2)
            for kt in range(8):
                kb.mm(ps[pk][:, 0:TcA], sb[wbuf][:, :].rearrange(
                    "p (k o) -> p k o", k=8)[:, kt, :], xnfv[:, kt, 0:TcA],
                    pk, kt == 0, False, reads=(wbuf, "G3a"), tag="az")
            kb.mm(ps[pk][:, 0:TcA], sb["lbi_s"][0:16, (16 + ot) * 128:(17 + ot) * 128],
                  sb["xl"][0:16, 0:TcA], pk, False, True,
                  reads=("lbi_s", "xl"), tag="azl")
            kb.act(zbv[:, ot, 0:TcA], ps[pk][:, 0:TcA], AF.Silu,
                   reads=(pk,), writes=(zbk(ot), pk), tag="az_ev")

        for dt in range(16):
            kb.ve("tensor_mul", zbv[:, dt, 0:TcA], yav[:, dt, 0:TcA],
                  zbv[:, dt, 0:TcA],
                  reads=(yak(dt), zbk(dt)), writes=(zbk(dt),), tag="agate")
        for tb in range(NTC):
            for kt in range(16):
                kb.mm(ps["p4"][0:16, 0:128], sb["laoT"][:, :].rearrange(
                    "p (d r) -> p d r", d=16)[:, kt, :],
                    zbv[:, kt, tb * 128:(tb + 1) * 128],
                    "p4", kt == 0, kt == 15, reads=("laoT", zbk(kt)), tag="aylo")
            kb.ve("tensor_copy", sb["xdbl"][0:16, tb * 128:(tb + 1) * 128],
                  ps["p4"][0:16, 0:128], reads=("p4",), writes=("xdbl", "p4"), tag="aylo_ev")
        if c + 1 < NCH:
            ln_front(c + 1)
        xv = xtm[:, :].rearrange("p (t d) -> p t d", d=1024)
        for oh in range(2):
            for kt in range(16):
                wbuf = "wsA%d" % ((oh * 16 + kt) % 3)
                kb.dma("pool", sb[wbuf][:, 0:512],
                       P["w_outT"][kt, oh], writes=(wbuf,), tag="awo")
                for tb in range(NTC):
                    kb.mm(ps["p%d" % tb][:, 0:512], zbv[:, kt, tb * 128:(tb + 1) * 128],
                          sb[wbuf][:, 0:512], "p%d" % tb, kt == 0, False,
                          reads=(zbk(kt), wbuf), tag="aop")
            for tb in range(NTC):
                pk = "p%d" % tb
                kb.mm(ps[pk][:, 0:512], sb["xdbl"][0:16, tb * 128:(tb + 1) * 128],
                      sb["lbo_s"][0:16, oh * 512:(oh + 1) * 512], pk, False, True,
                      reads=("xdbl", "lbo_s"), tag="aopl")
                tt = c * NTC + tb
                kb.ve("tensor_add", xv[:, tt, oh * 512:(oh + 1) * 512],
                      xv[:, tt, oh * 512:(oh + 1) * 512], ps[pk][:, 0:512],
                      reads=(xtk(tt), pk), writes=(xtk(tt), pk), tag="aox")
    kb.dbg("x_after_a", xtm[:, :], [128, NTT * 1024], BF16, reads=tuple("xt%d" % i for i in range(NTT)))


def build_program(T, stages=("a",), dbg_names=()):
    nc = bass.Bass("TRN2", target_bir_lowering=False, debug=False)
    kb = KB(nc, T, dbg_names)
    NTT = kb.NTT
    TcA = min(512, T)

    P = {}
    for nm, shp in [
        ("x", [T, D]), ("w_inT", [32, 128, 8, 128]), ("laiT", [D, RK]), ("lbiT", [RK, 2 * DI]),
        ("mask_in", [2 * DI]), ("cw", [128, 16, 4]), ("cb", [128, 16]), ("wxpT", [DI, 96]),
        ("wdtT", [DTR, DI]), ("bdt", [128, 16]), ("Alog", [128, 16, DS]), ("Dp", [128, 16]),
        ("w_outT", [16, 2, 128, 512]), ("laoT", [DI, RK]), ("lboT", [RK, D]), ("mask_out", [D]),
        ("nu", [128, 8]), ("th", [128, 8]), ("gam", [128, 8]), ("Dlru", [1, D]),
        ("BreT", [8, 128, 8, 128]), ("BimT", [8, 128, 8, 128]),
        ("CreT", [8, 2, 128, 512]), ("CimT", [8, 2, 128, 512]),
        ("ln1g", [128, 8]), ("ln1b", [128, 8]), ("ln2g", [128, 8]), ("ln2b", [128, 8]),
        ("Wq", [8, 128, 8, 128]), ("Wk", [8, 128, 8, 128]), ("Wv", [8, 128, 8, 128]),
        ("Wp", [8, 2, 128, 512]),
        ("bq", [128, 8]), ("bk", [128, 8]), ("bv", [1, D]), ("bp", [1, D]),
        ("gw", [D, NE]), ("gb", [NE]), ("W1", [NE, 8, 128, 8, 512]), ("b1", [128, NE, 32]),
        ("W2", [NE, 4, 2, 128, 8, 512]), ("b2", [NE, D]),
    ]:
        P[nm] = kb.din("i_" + nm, shp, F32)
    for nm, shp, dt in [("identb", [128, 128], BF16), ("triN", [128, 128], BF16),
                        ("iota8", [128, NE], F32)]:
        P[nm] = kb.din("i_" + nm, shp, dt)
    OUT = kb.dout("out", [T, D], F32)
    SCR_BC = kb.nc.dram_tensor("scr_bc", [32, min(512, T)], BF16, kind="Internal").ap()
    P["scr_bc"] = SCR_BC

    kb.t("xtm", [128, NTT * 1024], BF16)
    for g in ("G1", "G2", "G3", "G4", "G5", "G6"):
        kb.t(g, [128, 8192], BF16)
    kb.t("G7", [128, 16 * (TcA + 3)], BF16)
    kb.t("G8", [128, 4096], BF16)
    kb.t("G9", [128, 4096], BF16)
    for wi_ in range(3):
        kb.t("wsA%d" % wi_, [128, 8 * 128], BF16)
    kb.t("identb", [128, 128], BF16)
    kb.t("triN", [128, 128], BF16)
    kb.t("iota8", [128, NE], F32)
    kb.t("stats", [128, 4 * NTT], F32)
    kb.t("onesf", [1, 128], F32)
    kb.t("onesb", [1, 128], BF16)
    kb.t("onesn", [1, 128], BF16)
    kb.t("onescol", [128, 1], BF16)
    kb.t("onescf", [128, 1], F32)
    kb.t("laiT", [128, 8 * RK], BF16)
    kb.t("bsml", [128, 4096], BF16)
    kb.t("cw", [128, 16 * 4], F32)
    kb.t("cb", [128, 16], F32)
    kb.t("wxpT", [128, 16 * 96], BF16)
    kb.t("wdtT", [DTR, DI], BF16)
    kb.t("bdt", [128, 16], F32)
    kb.t("Aneg", [128, 16 * DS], F32)
    kb.t("Dp", [128, 16], F32)
    kb.t("laoT", [128, 16 * RK], BF16)
    kb.t("lbo_s", [RK, D], BF16)
    kb.t("xl", [RK, TcA], BF16)
    kb.t("ylo", [RK, 128], BF16)
    kb.t("xdbl", [128, TcA], BF16)
    kb.t("spA", [128, max(TcA, 1024)], F32)
    kb.t("spB", [128, max(TcA, 1024)], F32)
    kb.t("hcar", [128, 16 * DS], F32)
    kb.t("rows", [1, 2048], BF16)
    kb.t("wsV", [128, 8 * 256], BF16)
    kb.t("attsh", [128, 32], BF16)
    kb.t("ascr", [128, 16], F32)
    kb.t("e8", [128, NE], F32)
    kb.t("gwt", [128, 8 * NE], BF16)
    kb.t("bq", [128, 8], F32)
    kb.t("bk8", [128, 8], F32)
    kb.t("ln1g", [128, 8], F32)
    kb.t("ln1b", [128, 8], F32)
    kb.t("ln2g", [128, 8], F32)
    kb.t("ln2b", [128, 8], F32)

    bs = kb.sb["bsml"]
    bsf = bs.bitcast(F32)
    kb.sb["lbi_s"] = bs[0:128, :]
    kb.sb["lpr"] = bs[:, 0:1024]
    kb.sb["lpi"] = bs[:, 1024:2048]
    kb.sb["lh"] = bsf[:, 1024:1280]
    kb.sb["lam"] = bsf[:, 1280:1344]
    kb.sb["nuv"] = bsf[:, 1344:1352]
    kb.sb["thv"] = bsf[:, 1352:1360]
    kb.sb["gamv"] = bsf[:, 1360:1368]
    kb.sb["gexp"] = bsf[:, 1368:1376]
    kb.t("b1pp", [128, NE * 32], BF16)
    kb.sb["dacc"] = kb.sb["spA"][:, 512:1024]
    kb.sb["lgt"] = kb.sb["spB"][:, 0:128]
    kb.sb["wfl"] = kb.sb["spB"][:, 128:256]
    kb.p.alias["lgt"] = "spB"
    kb.p.alias["wfl"] = "spB"
    for k in ("lbi_s", "lpr", "lpi", "lh", "lam", "nuv", "thv", "gamv", "gexp"):
        kb.p.alias[k] = "bsml"
    kb.p.alias["dacc"] = "spA"
    for i in range(6):
        kb.pt("p%d" % i, [128, 512], F32)
    kb.pt("pt0", [128, 128], BF16)
    kb.pt("pt1", [128, 128], BF16)

    pre = [
        ("sp", "identb", P["identb"], None), ("sp", "triN", P["triN"], None),
        ("sp", "iota8", P["iota8"], None),
    ]
    for q, nm, src, re_ in pre:
        kb.dma(q, kb.sb[nm][:, :], src, writes=(nm,), tag="pre" + nm)
    kb.dma("pool", kb.sb["xtm"][:, :].rearrange("p (t d) -> p t d", d=1024),
           P["x"].rearrange("(t p) d -> p t d", p=128),
           writes=tuple("xt%d" % i for i in range(NTT)), sgrp="prex", tag="prex")
    kb.dma("pool", kb.sb["laiT"][:, :].rearrange("p (k r) -> p k r", k=8),
           P["laiT"].rearrange("(k p) r -> p k r", p=128), writes=("laiT",), tag="plai")
    kb.dma("pool", kb.sb["G1"][0:RK, 0:2 * DI], P["lbiT"], writes=("G1",), tag="plbi")
    kb.dma("pool", kb.sb["G2"][0:RK, 0:2 * DI], P["mask_in"][None, :].broadcast_to([RK, 2 * DI]), writes=("G2",), tag="pmin")
    kb.dma("sp", kb.sb["cw"][:, :].rearrange("p (d k) -> p d k", d=16),
           P["cw"], writes=("cw",), tag="pcw")
    kb.dma("sp", kb.sb["cb"][:, :], P["cb"], writes=("cb",), tag="pcb")
    kb.dma("pool", kb.sb["wxpT"][:, :].rearrange("p (d r) -> p d r", d=16),
           P["wxpT"].rearrange("(d p) r -> p d r", p=128), writes=("wxpT",), tag="pwxp")
    kb.dma("pool", kb.sb["wdtT"][0:DTR, :], P["wdtT"], writes=("wdtT",), tag="pwdt")
    kb.dma("sp", kb.sb["bdt"][:, :], P["bdt"], writes=("bdt",), tag="pbdt")
    kb.dma("sp", kb.sb["spA"].bitcast(F32)[:, 0:256].rearrange("p (d n) -> p d n", d=16)
           if False else kb.sb["spA"][:, 0:256].rearrange("p (d n) -> p d n", d=16),
           P["Alog"], writes=("spA",), tag="pal")
    kb.dma("sp", kb.sb["Dp"][:, :], P["Dp"], writes=("Dp",), tag="pdp")
    kb.dma("pool", kb.sb["laoT"][:, :].rearrange("p (k r) -> p k r", k=16),
           P["laoT"].rearrange("(k p) r -> p k r", p=128), writes=("laoT",), tag="plao")
    kb.dma("pool", kb.sb["G1"][0:RK, 4096:4096 + D], P["lboT"], writes=("G1",), tag="plbo")
    kb.dma("pool", kb.sb["G2"][0:RK, 4096:4096 + D], P["mask_out"][None, :].broadcast_to([RK, D]), writes=("G2",), tag="pmout")
    kb.ve("memset", kb.sb["onesf"][:, :], 1.0, writes=("onesf",), tag="ponef")
    kb.ve("memset", kb.sb["e8"][:, :], 8.0, writes=("e8",), tag="pe8")
    kb.dma("sp", kb.sb["bq"][:, :], P["bq"], writes=("bq",), tag="pbq")
    kb.dma("sp", kb.sb["bk8"][:, :], P["bk"], writes=("bk8",), tag="pbk")
    kb.ve("tensor_scalar_mul", kb.sb["bk8"][:, :], kb.sb["bk8"][:, :], 0.125,
          reads=("bk8",), writes=("bk8",), tag="pbk8")
    for nmv in ("ln1g", "ln1b", "ln2g", "ln2b"):
        kb.dma("sp", kb.sb[nmv][:, :], P[nmv], writes=(nmv,), tag="p" + nmv)
    kb.dma("pool", kb.sb["rows"][0:1, 0:D], P["bp"], writes=("rows",), sgrp="r_bp", tag="pbp")
    kb.dma("pool", kb.sb["attsh"][0:1, 24:24 + NE], P["gb"][None, :], writes=("attsh",), sgrp="r_gb", tag="pgb")
    kb.dma("pool", kb.sb["gwt"][:, :].rearrange("p (k e) -> p k e", k=8),
           P["gw"].rearrange("(k p) e -> p k e", p=128), writes=("gwt",), tag="pgw")
    kb.ve("memset", kb.sb["onesb"][:, :], 1.0, writes=("onesb",), tag="poneb")
    kb.ve("memset", kb.sb["onesn"][:, :], -1.0, writes=("onesn",), tag="ponen")
    kb.ve("memset", kb.sb["onescol"][:, :], 1.0, writes=("onescol",), tag="ponec")
    kb.ve("memset", kb.sb["onescf"][:, :], 1.0, writes=("onescf",), tag="ponecf")
    kb.act(kb.sb["Aneg"][:, :], kb.sb["spA"][:, 0:256], AF.Exp, reads=("spA",),
           writes=("Aneg",), tag="pan1")
    kb.ve("tensor_scalar_mul", kb.sb["Aneg"][:, :], kb.sb["Aneg"][:, :], -1.0,
          reads=("Aneg",), writes=("Aneg",), tag="pan2")
    kb.ve("scalar_tensor_tensor", kb.sb["lbi_s"][0:RK, :], kb.sb["G1"][0:RK, 0:2 * DI], 2.0,
          kb.sb["G2"][0:RK, 0:2 * DI], ALU.mult, ALU.mult,
          reads=("G1", "G2"), writes=("lbi_s",), tag="plbis")
    kb.ve("scalar_tensor_tensor", kb.sb["lbo_s"][0:RK, :], kb.sb["G1"][0:RK, 4096:4096 + D], 2.0,
          kb.sb["G2"][0:RK, 4096:4096 + D], ALU.mult, ALU.mult,
          reads=("G1", "G2"), writes=("lbo_s",), tag="plbos")

    if "a" in stages:
        stage_a(kb, P)
    if "b" in stages:
        stage_ffn(kb, P, 1)
    if "c" in stages:
        stage_lru(kb, P)
    if "b2" in stages:
        stage_ffn(kb, P, 2)

    kb.dma("pool", OUT.rearrange("(t p) d -> p t d", p=128),
           kb.sb["xtm"][:, :].rearrange("p (t d) -> p t d", d=1024),
           reads=tuple("xt%d" % i for i in range(NTT)), sgrp="outx", tag="outx")

    esem = {}
    qsem = {}
    for e in ("pe", "act", "dve", "pool", "sp"):
        esem[e] = kb.st.enter_context(nc.semaphore("se_" + e))
    for g in sorted(kb.p.qcount):
        qsem[g] = kb.st.enter_context(nc.semaphore("sq_" + g))
    with nc.Block() as block:
        kb.p.emit_all(block, esem, qsem)
    kb.st.close()
    assert kb.sbuf_bytes <= 213 * 1024, f"sbuf over budget: {kb.sbuf_bytes}"
    return nc, kb


def _til8(w):
    K, N = w.shape
    return np.ascontiguousarray(w.reshape(K // 128, 128, N // 128, 128).transpose(2, 1, 0, 3))


def host_inputs(inputs, T=2048):
    g = lambda k: np.ascontiguousarray(np.asarray(inputs["L0_" + k], np.float32))
    com = {
        "w_inT": np.ascontiguousarray(g("attn_a_w_in").T.reshape(8, 128, 32, 128).transpose(2, 1, 0, 3)),
        "laiT": np.ascontiguousarray(g("attn_a_lora_A_in").T),
        "lbiT": np.ascontiguousarray(g("attn_a_lora_B_in").T),
        "mask_in": g("attn_a_mask_in"),
        "cw": np.ascontiguousarray(g("attn_a_conv_w")[:, 0, :].reshape(16, 128, 4).transpose(1, 0, 2)),
        "cb": np.ascontiguousarray(g("attn_a_conv_b").reshape(16, 128).T),
        "wxpT": np.ascontiguousarray(g("attn_a_w_xproj").T),
        "wdtT": np.ascontiguousarray(g("attn_a_w_dt").T),
        "bdt": np.ascontiguousarray(g("attn_a_b_dt").reshape(16, 128).T),
        "Alog": np.ascontiguousarray(g("attn_a_A_log").reshape(16, 128, 16).transpose(1, 0, 2)),
        "Dp": np.ascontiguousarray(g("attn_a_Dp").reshape(16, 128).T),
        "w_outT": np.ascontiguousarray(g("attn_a_w_out").T.reshape(16, 128, 2, 512).transpose(0, 2, 1, 3)),
        "laoT": np.ascontiguousarray(g("attn_a_lora_A_out").T),
        "lboT": np.ascontiguousarray(g("attn_a_lora_B_out").T),
        "mask_out": g("attn_a_mask_out"),
        "nu": np.ascontiguousarray(g("attn_b_nu_log").reshape(8, 128).T), "th": np.ascontiguousarray(g("attn_b_theta_log").reshape(8, 128).T),
        "gam": np.ascontiguousarray(g("attn_b_gamma_log").reshape(8, 128).T), "Dlru": g("attn_b_D")[None, :],
        "BreT": _til8(np.ascontiguousarray(g("attn_b_B_re").T)),
        "BimT": _til8(np.ascontiguousarray(g("attn_b_B_im").T)),
        "CreT": np.ascontiguousarray(g("attn_b_C_re").T.reshape(8, 128, 2, 512).transpose(0, 2, 1, 3)),
        "CimT": np.ascontiguousarray(g("attn_b_C_im").T.reshape(8, 128, 2, 512).transpose(0, 2, 1, 3)),
        "ln1g": np.ascontiguousarray(g("ffn_ln1_g").reshape(8, 128).T), "ln1b": np.ascontiguousarray(g("ffn_ln1_b").reshape(8, 128).T),
        "ln2g": np.ascontiguousarray(g("ffn_ln2_g").reshape(8, 128).T), "ln2b": np.ascontiguousarray(g("ffn_ln2_b").reshape(8, 128).T),
        "Wq": _til8(g("ffn_Wq")), "Wk": _til8(g("ffn_Wk")), "Wv": _til8(g("ffn_Wv")),
        "Wp": np.ascontiguousarray(g("ffn_Wp").reshape(8, 128, 2, 512).transpose(0, 2, 1, 3)),
        "bq": np.ascontiguousarray(g("ffn_bq").reshape(8, 128).T), "bk": np.ascontiguousarray(g("ffn_bk").reshape(8, 128).T), "bv": g("ffn_bv")[None, :], "bp": g("ffn_bp")[None, :],
        "gw": g("ffn_gate_W"), "gb": g("ffn_gate_b"),
        "W1": np.ascontiguousarray(g("ffn_exp_W1").reshape(NE, 8, 128, 8, 512).transpose(0, 3, 2, 1, 4)), "b1": np.ascontiguousarray(g("ffn_exp_b1").reshape(8, 32, 128).transpose(2, 0, 1)),
        "W2": np.ascontiguousarray(g("ffn_exp_W2").reshape(NE, 4, 8, 128, 2, 512).transpose(0, 1, 4, 3, 2, 5)), "b2": g("ffn_exp_b2"),
        "i_identb": np.eye(128, dtype=ml_dtypes.bfloat16),
        "i_triN": np.where(np.triu(np.ones((128, 128), bool)), 0.0, -1e30).astype(ml_dtypes.bfloat16),
        "i_iota8": np.broadcast_to(np.arange(NE, dtype=np.float32), (128, NE)).copy(),
    }
    com = {("i_" + k if not k.startswith("i_") else k): v for k, v in com.items()}
    x = np.asarray(inputs["L0_attn_a_x"], np.float32)
    maps = []
    for b in range(x.shape[0]):
        m = dict(com)
        m["i_x"] = np.ascontiguousarray(x[b, :T])
        maps.append(m)
    return maps


_CACHE = {}


def kernel(**inputs):
    T = 2048
    if "prog" not in _CACHE:
        _CACHE["prog"] = build_program(T, stages=("a", "b", "c", "b2"))
    nc, kb = _CACHE["prog"]
    in_maps = host_inputs(inputs, T)
    res = run_bass_kernel_spmd(nc, in_maps, core_ids=list(range(8)))
    outs = [np.asarray(r["out"], np.float32) for r in res.results]
    full = np.stack(outs, 0)
    return full


def xtk(tt):
    return "xt%d" % tt


def _htile(kb, tt):
    g = "G1" if tt < 8 else "G2"
    return kb.sb[g], g, (tt % 8) * 1024


def _fm(kb, dt):
    g = "G3" if dt < 4 else "G4"
    k = (("G3a",), ("G3a",), ("hs0", "hs1", "hs2", "hs3"), ("hs4", "hs5", "hs6", "hs7"),
         ("G4a",), ("G4a",), ("G4b",), ("G4b",))[dt]
    return kb.sb[g], k, (dt % 4) * 2048


def stage_ffn(kb, P, ci):
    T, NTT = kb.T, kb.NTT
    sb, ps = kb.sb, kb.ps
    xtm = sb["xtm"]
    xv = xtm[:, :].rearrange("p (t d) -> p t d", d=1024)
    QCW = min(512, T)
    NQC = T // QCW
    cg = "c%d_" % ci

    kb.dma("pool", sb["G7"][0:1, 3 * T + 1024:3 * T + 2048], P["bv"], writes=("G7",),
           sgrp="r_bv", tag=cg + "pbv")
    if ci == 1:
        kb.dma("pool", sb["b1pp"][:, :].rearrange("p (e h) -> p e h", e=NE),
               P["b1"], writes=("b1pp",), tag=cg + "pb1")
    ln_stats(kb, lambda tt: xv[:, tt, :], NTT, cg + "ln0", "F0", tuple("xt%d" % i for i in range(NTT)))
    for tt in range(NTT):
        mu, rs = ln_mu_rstd(kb, "F0", tt)
        ht, hk, co = _htile(kb, tt)
        kb.ve("tensor_scalar", ht[:, co:co + 1024], xv[:, tt, :], mu, rs,
              ALU.subtract, ALU.mult, reads=(xtk(tt), "stats"), writes=(hk,), tag=cg + "h")

    ln_stats(kb, lambda tt: _htile(kb, tt)[0][:, _htile(kb, tt)[2]:_htile(kb, tt)[2] + 1024],
             NTT, cg + "ln1", "F1", ("G1", "G2"))
    tscr = sb["G9"]
    for tt in range(NTT):
        mu, rs = ln_mu_rstd(kb, "F1", tt)
        ht, hk, co = _htile(kb, tt)
        kb.ve("tensor_scalar", tscr[:, 0:1024], ht[:, co:co + 1024], mu, rs,
              ALU.subtract, ALU.mult, reads=(hk, "stats"), writes=("g9t",), tag=cg + "n1")
        for dt in range(8):
            pk = "pt%d" % (dt % 2)
            kb.tr(ps[pk][:, 0:128], tscr[:, dt * 128:(dt + 1) * 128], sb["identb"][:, :],
                  pk, reads=("g9t", "identb"), tag=cg + "t1")
            ft, fk, fo = _fm(kb, dt)
            kb.act(ft[:, fo + tt * 128:fo + (tt + 1) * 128], ps[pk][:, 0:128], AF.Copy,
                   scale=sb["ln1g"][:, dt:dt + 1], bias=sb["ln1b"][:, dt:dt + 1],
                   reads=(pk, "ln1g", "ln1b"), writes=fk + (pk,), tag=cg + "t1e")

    rows = sb["rows"]
    for hg in range(4):
        for wi_, (wname, dstk) in enumerate((("Wq", "G5"), ("Wk", "G6"))):
            dkeys = G5K if dstk == "G5" else G6K
            for ot2 in range(2):
                og = hg * 2 + ot2
                wbuf = "wsA%d" % ((wi_ * 2 + ot2) % 3)
                kb.dma("pool", sb[wbuf][:, :].rearrange("p (k o) -> p k o", k=8),
                       P[wname][og].rearrange("p k o -> p k o"), writes=(wbuf,), tag=cg + "wqk")
                for qc in range(NQC):
                    pk = "p%d" % (qc % 2)
                    for kt in range(8):
                        ft, fk, fo = _fm(kb, kt)
                        kb.mm(ps[pk][:, 0:QCW], sb[wbuf][:, :].rearrange(
                            "p (k o) -> p k o", k=8)[:, kt, :],
                            ft[:, fo + qc * QCW:fo + (qc + 1) * QCW],
                            pk, kt == 0, kt == 7, reads=(wbuf,) + fk, tag=cg + "qk")
                    for hh in range(2):
                        hl = ot2 * 2 + hh
                        if wname == "Wq":
                            sc, bi = 1.0, sb["bq"][hh * 64:hh * 64 + 64, og:og + 1]
                        else:
                            sc, bi = 0.125, sb["bk8"][hh * 64:hh * 64 + 64, og:og + 1]
                        kb.act(sb[dstk][0:64, hl * T + qc * QCW:hl * T + (qc + 1) * QCW],
                               ps[pk][hh * 64:hh * 64 + 64, 0:QCW], AF.Copy,
                               scale=sc, bias=bi,
                               reads=(pk, "bq", "bk8"), writes=dkeys + (pk,), tag=cg + "qke")
        g8v = sb["G8"][:, :].rearrange("p (t o) -> p t o", o=256)
        for _vt in range(2):
            kb.dma("pool", sb["wsV"][:, _vt * 1024:(_vt + 1) * 1024].rearrange(
                "p (k o) -> p k o", k=8),
                P["Wv"][hg * 2 + _vt].rearrange("p k o -> p k o"),
                writes=("wsV%d" % _vt,), tag=cg + "wvl")
        for tb in range(NTT):
            pk = "p%d" % (tb % 2)
            for kt in range(8):
                ft, fk, fo = _fm(kb, kt)
                for vt in range(2):
                    kb.mm(ps[pk][:, vt * 128:(vt + 1) * 128],
                          ft[:, fo + tb * 128:fo + (tb + 1) * 128],
                          sb["wsV"][:, vt * 1024 + kt * 128:vt * 1024 + (kt + 1) * 128],
                          pk, kt == 0 and vt == 0, False,
                          reads=fk + ("wsV%d" % vt,), tag=cg + "v")
            kb.mm(ps[pk][:, 0:256], sb["onesb"][0:1, :],
                  sb["G7"][0:1, 3 * T + 1024 + hg * 256:3 * T + 1024 + (hg + 1) * 256],
                  pk, False, True, reads=("onesb", "G7"), tag=cg + "vb")
            kb.ve("tensor_copy", g8v[:, tb, :], ps[pk][:, 0:256],
                  reads=(pk,), writes=("G8r%d" % (tb // 4), pk), tag=cg + "ve")

        for hl in range(4):
            h = hg * 4 + hl
            q0c = hl * T
            g9f = sb["G9"].bitcast(F32)
            kb.act(g9f[0:64, 0:T], sb["G6"][0:64, q0c:q0c + T], AF.Square,
                   reads=G6K, writes=("g9t", "g9p0", "g9p1", "g9p2", "g9p3"), tag=cg + "ksq")
            NQB = T // 512 if T >= 512 else 1
            QW = min(512, T)
            for c_ in range(T // QW):
                kb.mm(ps["p5"][0:1, 0:QW],
                      sb["onescf"][0:64, 0:1], g9f[0:64, c_ * QW:(c_ + 1) * QW],
                      "p5", True, True, reads=("onescf", "g9t"), tag=cg + "ksum")
                kb.ve("reduce_max", sb["spB"][0:1, c_:c_ + 1], ps["p5"][0:1, 0:QW],
                      axis=AX.X, reads=("p5",), writes=("spB",), tag=cg + "kmax")
            kb.ve("reduce_max", sb["spB"][0:1, 8:9], sb["spB"][0:1, 0:T // QW],
                  axis=AX.X, reads=("spB",), writes=("spB",), tag=cg + "kmax2")
            kb.act(g9f[0:64, 0:T], sb["G5"][0:64, q0c:q0c + T], AF.Square,
                   reads=G5K, writes=("g9t", "g9p0", "g9p1", "g9p2", "g9p3"), tag=cg + "qsq")
            for c_ in range(T // QW):
                kb.mm(ps["p5"][0:1, 0:QW],
                      sb["onescf"][0:64, 0:1], g9f[0:64, c_ * QW:(c_ + 1) * QW],
                      "p5", True, True, reads=("onescf", "g9t"), tag=cg + "qsum")
                kb.act(sb["G7"][0:1, 2 * T + c_ * QW:2 * T + (c_ + 1) * QW],
                       ps["p5"][0:1, 0:QW], AF.Sqrt, scale=sb["spB"][0:1, 8:9],
                       reads=("p5", "spB"), writes=("G7",), tag=cg + "qsh")

            for qc in range(NQC):
                qlo = qc * QCW
                po = "p4"
                nkb = qc * (QCW // 128) + (QCW // 128)

                def st_mm(kb_i):
                    k0 = kb_i * 128
                    q0 = max(qlo, k0)
                    w = qlo + QCW - q0
                    pk = "p%d" % (2 + kb_i % 2)
                    kb.mm(ps[pk][:, 0:w], sb["G6"][0:64, q0c + k0:q0c + k0 + 128],
                          sb["G5"][0:64, q0c + q0:q0c + qlo + QCW],
                          pk, True, False, reads=G5K + G6K, tag=cg + "pb")
                    kb.mm(ps[pk][:, 0:w], sb["onesn"][0:1, :],
                          sb["G7"][0:1, 2 * T + q0:2 * T + qlo + QCW], pk, False, q0 != k0,
                          reads=("onesn", "G7"), tag=cg + "pbs")
                    if q0 == k0:
                        kb.mm(ps[pk][:, 0:128], sb["identb"][:, :], sb["triN"][:, :],
                              pk, False, True, reads=("identb", "triN"), tag=cg + "pbm")
                    pb = "g9p%d" % (kb_i % 4)
                    pbuf = sb["G9"][:, 1024 + (kb_i % 4) * 512:1024 + (kb_i % 4) * 512 + 512]
                    kb.act(pbuf[:, 0:w], ps[pk][:, 0:w], AF.Exp,
                           reads=(pk,), writes=(pb, pk), tag=cg + "pbe")

                def pv_mm(kb_i):
                    k0 = kb_i * 128
                    q0 = max(qlo, k0)
                    w = qlo + QCW - q0
                    pb = "g9p%d" % (kb_i % 4)
                    pbuf = sb["G9"][:, 1024 + (kb_i % 4) * 512:1024 + (kb_i % 4) * 512 + 512]
                    kb.mm(ps[po][0:64, q0 - qlo:QCW], g8v[:, kb_i, hl * 64:(hl + 1) * 64],
                          pbuf[:, 0:w], po, kb_i == 0, kb_i == nkb - 1,
                          reads=("G8r%d" % (kb_i // 4), pb), tag=cg + "pv")
                    kb.mm(ps["p5"][0:1, q0 - qlo:QCW], sb["onescol"][:, 0:1],
                          pbuf[:, 0:w], "p5", kb_i == 0, kb_i == nkb - 1,
                          reads=("onescol", pb), tag=cg + "psum")

                for kb_i in range(nkb + 1):
                    if kb_i < nkb:
                        st_mm(kb_i)
                    if kb_i >= 1:
                        pv_mm(kb_i - 1)
                kb.ve("reciprocal", sb["spB"][0:1, 512:512 + QCW],
                      ps["p5"][0:1, 0:QCW], reads=("p5",), writes=("spB", "p5"), tag=cg + "prc")
                pk = "p%d" % (2 + qc % 2)
                kb.mm(ps[pk][0:64, 0:QCW], sb["onesf"][0:1, 0:64],
                      sb["spB"][0:1, 512:512 + QCW], pk, True, True,
                      reads=("onesf", "spB"), tag=cg + "pbc")
                kb.act(sb["spA"][0:64, 0:QCW], ps[pk][0:64, 0:QCW],
                       AF.Copy, reads=(pk,), writes=("spA", pk), tag=cg + "pbce")
                dtg = hl // 2
                rbase = (hl % 2) * 64
                kb.ve("tensor_mul", sb["G7"][rbase:rbase + 64, dtg * T + qlo:dtg * T + qlo + QCW],
                      ps[po][0:64, 0:QCW], sb["spA"][0:64, 0:QCW],
                      reads=(po, "spA"), writes=("G7", po), tag=cg + "oev")
        for par in range(2):
            kb.dma("pool", sb["G5"][:, par * 2048:par * 2048 + 1024].rearrange(
                "p (a o) -> p a o", a=2),
                P["Wp"][2 * hg:2 * hg + 2, par].rearrange("a p o -> p a o"),
                writes=("G5q%d" % par,), tag=cg + "wp")
            for tb in range(NTT):
                pk = "p%d" % (tb % 2)
                for dtg in range(2):
                    kb.mm(ps[pk][:, 0:512],
                          sb["G7"][0:128, dtg * T + tb * 128:dtg * T + (tb + 1) * 128],
                          sb["G5"][:, par * 2048 + dtg * 512:par * 2048 + dtg * 512 + 512],
                          pk, dtg == 0, dtg == 1 and hg != 0,
                          reads=("G7", "G5q%d" % par), tag=cg + "pj")
                if hg == 0:
                    kb.mm(ps[pk][:, 0:512], sb["onesb"][0:1, :],
                          rows[0:1, par * 512:(par + 1) * 512], pk,
                          False, True, reads=("onesb", "rows"), tag=cg + "pjb")
                ht, hk, co = _htile(kb, tb)
                kb.ve("tensor_add", ht[:, co + par * 512:co + (par + 1) * 512],
                      ht[:, co + par * 512:co + (par + 1) * 512], ps[pk][:, 0:512],
                      reads=(hk, pk), writes=(hk, pk), tag=cg + "pje")

    ln_stats(kb, lambda tt: _htile(kb, tt)[0][:, _htile(kb, tt)[2]:_htile(kb, tt)[2] + 1024],
             NTT, cg + "ln2", "F2", ("G1", "G2"))
    for tt in range(NTT):
        mu, rs = ln_mu_rstd(kb, "F2", tt)
        ht, hk, co = _htile(kb, tt)
        kb.ve("tensor_scalar", tscr[:, 0:1024], ht[:, co:co + 1024], mu, rs,
              ALU.subtract, ALU.mult, reads=(hk, "stats"), writes=("g9t",), tag=cg + "n2")
        for dt in range(8):
            pk = "pt%d" % (dt % 2)
            kb.tr(ps[pk][:, 0:128], tscr[:, dt * 128:(dt + 1) * 128], sb["identb"][:, :],
                  pk, reads=("g9t", "identb"), tag=cg + "t2")
            ft, fk, fo = _fm(kb, dt)
            kb.act(ft[:, fo + tt * 128:fo + (tt + 1) * 128], ps[pk][:, 0:128], AF.Copy,
                   scale=sb["ln2g"][:, dt:dt + 1], bias=sb["ln2b"][:, dt:dt + 1],
                   reads=(pk, "ln2g", "ln2b"), writes=fk + (pk,), tag=cg + "t2e")
        kb.ve("tensor_add", xv[:, tt, :], xv[:, tt, :], ht[:, co:co + 1024],
              reads=(xtk(tt), hk), writes=(xtk(tt),), tag=cg + "xf")

    for tt in range(NTT):
        pk = "p%d" % (tt % 2)
        for kt in range(8):
            ft, fk, fo = _fm(kb, kt)
            kb.mm(ps[pk][:, 0:NE], ft[:, fo + tt * 128:fo + (tt + 1) * 128],
                  sb["gwt"][:, :].rearrange("p (k e) -> p k e", k=8)[:, kt, :],
                  pk, kt == 0, False, reads=("gwt",) + fk, tag=cg + "g")
        kb.mm(ps[pk][:, 0:NE], sb["onesb"][0:1, :], sb["attsh"][0:1, 24:24 + NE],
              pk, False, True, reads=("onesb", "attsh"), tag=cg + "gb")
        kb.ve("tensor_copy", sb["lgt"][:, tt * NE:(tt + 1) * NE], ps[pk][:, 0:NE],
              reads=(pk,), writes=("lgt", pk), tag=cg + "ge")
    kb.ve("tensor_scalar_add", sb["ascr"][:, 8:16], sb["iota8"][:, :], -8.0,
          reads=("iota8",), writes=("ascr",), tag=cg + "iom8")
    for tt in range(NTT):
        g_ = sb["spA"][:, 960:1008]
        lg = sb["lgt"][:, tt * NE:(tt + 1) * NE]
        wf = sb["wfl"][:, tt * NE:(tt + 1) * NE]
        kb.ve("reduce_max", g_[:, 0:1], lg, axis=AX.X, reads=("lgt",), writes=("spA",), tag=cg + "m1")
        kb.ve("tensor_scalar", g_[:, 8:16], lg, g_[:, 0:1], None, ALU.is_equal,
              reads=("lgt", "spA"), writes=("spA",), tag=cg + "eq1")
        kb.ve("scalar_tensor_tensor", g_[:, 16:24], g_[:, 8:16], 1.0, sb["ascr"][:, 8:16],
              ALU.mult, ALU.mult, reads=("spA", "ascr"), writes=("spA",), tag=cg + "sel1a")
        kb.ve("tensor_scalar_add", g_[:, 16:24], g_[:, 16:24], 8.0,
              reads=("spA",), writes=("spA",), tag=cg + "sel1b")
        kb.ve("tensor_reduce", g_[:, 1:2], g_[:, 16:24], AX.X, ALU.min,
              reads=("spA",), writes=("spA",), tag=cg + "i1")
        kb.ve("tensor_scalar", g_[:, 8:16], sb["iota8"][:, :], g_[:, 1:2], None, ALU.is_equal,
              reads=("spA", "iota8"), writes=("spA",), tag=cg + "es1")
        kb.ve("scalar_tensor_tensor", g_[:, 24:32], g_[:, 8:16], NEG, lg,
              ALU.mult, ALU.add, reads=("spA", "lgt"), writes=("spA",), tag=cg + "lg2")
        kb.ve("reduce_max", g_[:, 2:3], g_[:, 24:32], axis=AX.X,
              reads=("spA",), writes=("spA",), tag=cg + "m2")
        kb.ve("tensor_scalar", g_[:, 32:40], g_[:, 24:32], g_[:, 2:3], None, ALU.is_equal,
              reads=("spA",), writes=("spA",), tag=cg + "eq2")
        kb.ve("scalar_tensor_tensor", g_[:, 16:24], g_[:, 32:40], 1.0, sb["ascr"][:, 8:16],
              ALU.mult, ALU.mult, reads=("spA", "ascr"), writes=("spA",), tag=cg + "sel2a")
        kb.ve("tensor_scalar_add", g_[:, 16:24], g_[:, 16:24], 8.0,
              reads=("spA",), writes=("spA",), tag=cg + "sel2b")
        kb.ve("tensor_reduce", g_[:, 3:4], g_[:, 16:24], AX.X, ALU.min,
              reads=("spA",), writes=("spA",), tag=cg + "i2")
        kb.ve("tensor_scalar", g_[:, 32:40], sb["iota8"][:, :], g_[:, 3:4], None, ALU.is_equal,
              reads=("spA", "iota8"), writes=("spA",), tag=cg + "es2")
        kb.ve("tensor_sub", g_[:, 4:5], g_[:, 0:1], g_[:, 2:3],
              reads=("spA",), writes=("spA",), tag=cg + "d12")
        kb.act(g_[:, 5:6], g_[:, 4:5], AF.Sigmoid, reads=("spA",), writes=("spA",), tag=cg + "w1")
        kb.act(g_[:, 6:7], g_[:, 4:5], AF.Sigmoid, scale=-1.0,
               reads=("spA",), writes=("spA",), tag=cg + "w2")
        kb.ve("tensor_scalar", g_[:, 40:48], g_[:, 8:16], g_[:, 5:6], None, ALU.mult,
              reads=("spA",), writes=("spA",), tag=cg + "wa")
        kb.ve("scalar_tensor_tensor", wf, g_[:, 32:40], g_[:, 6:7], g_[:, 40:48],
              ALU.mult, ALU.add, reads=("spA",), writes=("wfl",), tag=cg + "wb")

    TCW = min(512, T)
    NTCH = T // TCW
    midv1 = sb["G1"][:, :].rearrange("p (h t) -> p h t", h=16)
    midv2 = sb["G2"][:, :].rearrange("p (h t) -> p h t", h=16)
    for e in range(NE):
        kb.dma("pool", sb["G7"][0:1, 3 * T:3 * T + 1024], P["b2"][e:e + 1, :],
               writes=("G7",), sgrp="r_b2", tag=cg + "b2l")
        for tc in range(NTCH):
            for o8 in range(8):
                woff = (o8 % 2) * 4096
                kb.dma("pool", sb["G5"][:, woff:woff + 4096].rearrange(
                    "p (k h) -> p k h", k=8),
                    P["W1"][e, o8], writes=G5K[2 * (o8 % 2):2 * (o8 % 2) + 2],
                    sgrp="w1l%d" % (o8 % 2), tag=cg + "w1l")
                for htl in range(4):
                    ht = o8 * 4 + htl
                    pk = "p%d" % (4 + ht % 2)
                    for kt in range(8):
                        ft, fk, fo = _fm(kb, kt)
                        kb.mm(ps[pk][:, 0:TCW], sb["G5"][:, woff + kt * 512 + htl * 128:
                                                         woff + kt * 512 + (htl + 1) * 128],
                              ft[:, fo + tc * TCW:fo + (tc + 1) * TCW], pk,
                              kt == 0, kt == 7,
                              reads=G5K[2 * (o8 % 2):2 * (o8 % 2) + 2] + fk, tag=cg + "m1")
                    mv = midv1 if ht < 16 else midv2
                    mk = "G1" if ht < 16 else "G2"
                    kb.act(mv[:, ht % 16, 0:TCW], ps[pk][:, 0:TCW], AF.Gelu,
                           bias=sb["b1pp"][:, :].rearrange("p (e h) -> p e h", e=NE)[:, e, ht:ht + 1],
                           reads=(pk, "b1pp"), writes=(mk, pk), tag=cg + "m1e")
            for oh in range(2):
                for ktg in range(4):
                    woff = (ktg % 2) * 4096
                    kb.dma("pool", sb["G6"][:, woff:woff + 4096].rearrange(
                        "p (k o) -> p k o", k=8),
                        P["W2"][e, ktg, oh], writes=G6K[2 * (ktg % 2):2 * (ktg % 2) + 2],
                        sgrp="w2l%d" % (ktg % 2), tag=cg + "w2l")
                    for kt in range(8):
                        htg = ktg * 8 + kt
                        mv = midv1 if htg < 16 else midv2
                        mk = "G1" if htg < 16 else "G2"
                        for tb in range(TCW // 128):
                            pk = "p%d" % (tb % 4)
                            kb.mm(ps[pk][:, 0:512],
                                  mv[:, htg % 16, tb * 128:(tb + 1) * 128],
                                  sb["G6"][:, woff + kt * 512:woff + (kt + 1) * 512],
                                  pk, htg == 0, False,
                                  reads=G6K[2 * (ktg % 2):2 * (ktg % 2) + 2] + (mk,), tag=cg + "m2")
                for tb in range(TCW // 128):
                    pk = "p%d" % (tb % 4)
                    kb.mm(ps[pk][:, 0:512], sb["onesb"][0:1, :],
                          sb["G7"][0:1, 3 * T + oh * 512:3 * T + (oh + 1) * 512], pk,
                          False, True, reads=("onesb", "G7"), tag=cg + "m2b")
                    tt = tc * (TCW // 128) + tb
                    kb.ve("scalar_tensor_tensor", xv[:, tt, oh * 512:(oh + 1) * 512],
                          ps[pk][:, 0:512], sb["wfl"][:, tt * NE + e:tt * NE + e + 1],
                          xv[:, tt, oh * 512:(oh + 1) * 512], ALU.mult, ALU.add,
                          reads=(pk, "wfl", xtk(tt)), writes=(xtk(tt), pk), tag=cg + "m2x")
    if ci == 1:
        kb.dbg("x_after_f1", xtm[:, :], [128, NTT * 1024], BF16, reads=tuple("xt%d" % i for i in range(NTT)))


def _bu(kb, im, pt):
    T = kb.T
    if not im:
        gk = G5K if pt < 4 else G6K
        g = "G5" if pt < 4 else "G6"
        reg = ((pt % 4) * T) // 2048
        return kb.sb[g], gk[reg], (pt % 4) * T
    if pt < 4:
        return kb.sb["G7"], "G7", pt * T
    g = "G8" if pt < 6 else "G9"
    return kb.sb[g], g, (pt % 2) * T


G5K = ("G5q0", "G5q1", "G5q2", "G5q3")
G6K = ("G6q0", "G6q1", "G6q2", "G6q3")

_LREG = [("G5", 0, 4, 0), ("G6", 0, 4, 4), ("G7", 1, 4, 0), ("G8", 1, 2, 4), ("G9", 1, 2, 6)]
_LKEY = {"G5": G5K, "G6": G6K, "G7": ("G7",), "G8": ("G8",), "G9": ("G9",)}


def stage_lru(kb, P):
    T, NTT = kb.T, kb.NTT
    sb, ps = kb.sb, kb.ps
    xtm = sb["xtm"]
    xv = xtm[:, :].rearrange("p (t d) -> p t d", d=1024)
    NCH = T // 128
    lam = sb["lam"]
    cg = "l_"

    kb.dma("sp", sb["nuv"][:, :], P["nu"], writes=("nuv",), sgrp="c_nu", tag=cg + "pnu")
    kb.dma("sp", sb["thv"][:, :], P["th"], writes=("thv",), sgrp="c_th", tag=cg + "pth")
    kb.dma("sp", sb["gamv"][:, :], P["gam"], writes=("gamv",), sgrp="c_gam", tag=cg + "pgam")
    kb.act(lam[:, 16:24], sb["nuv"][:, :], AF.Exp, reads=("nuv",), writes=("lam",), tag=cg + "e1")
    kb.act(lam[:, 24:32], lam[:, 16:24], AF.Exp, scale=-1.0,
           reads=("lam",), writes=("lam",), tag=cg + "r")
    kb.act(lam[:, 32:40], sb["thv"][:, :], AF.Exp, reads=("thv",), writes=("lam",), tag=cg + "th")
    kb.act(lam[:, 40:48], lam[:, 32:40], AF.Sin, scale=0.5,
           reads=("lam",), writes=("lam",), tag=cg + "s2")
    kb.act(lam[:, 48:56], lam[:, 32:40], AF.Sin, scale=0.25,
           reads=("lam",), writes=("lam",), tag=cg + "s4")
    kb.act(lam[:, 56:64], lam[:, 40:48], AF.Square, reads=("lam",), writes=("lam",), tag=cg + "q2")
    kb.ve("tensor_scalar", lam[:, 56:64], lam[:, 56:64], -2.0, 1.0, ALU.mult, ALU.add,
          reads=("lam",), writes=("lam",), tag=cg + "cos")
    kb.act(lam[:, 48:56], lam[:, 48:56], AF.Square, reads=("lam",), writes=("lam",), tag=cg + "q4")
    kb.ve("tensor_scalar", lam[:, 48:56], lam[:, 48:56], -2.0, 1.0, ALU.mult, ALU.add,
          reads=("lam",), writes=("lam",), tag=cg + "c2")
    kb.ve("tensor_mul", lam[:, 40:48], lam[:, 40:48], lam[:, 48:56],
          reads=("lam",), writes=("lam",), tag=cg + "sc")
    kb.ve("tensor_scalar_mul", lam[:, 40:48], lam[:, 40:48], 2.0,
          reads=("lam",), writes=("lam",), tag=cg + "sin")
    kb.ve("tensor_mul", lam[:, 0:8], lam[:, 24:32], lam[:, 56:64],
          reads=("lam",), writes=("lam",), tag=cg + "lr")
    kb.ve("tensor_mul", lam[:, 8:16], lam[:, 24:32], lam[:, 40:48],
          reads=("lam",), writes=("lam",), tag=cg + "li")
    kb.act(sb["gexp"][:, :], sb["gamv"][:, :], AF.Exp, reads=("gamv",), writes=("gexp",), tag=cg + "g")
    kb.dma("pool", kb.sb["rows"][0:1, 1024:2048], P["Dlru"], writes=("rows",), sgrp="r_dl", tag=cg + "dl")
    kb.mm(ps["p5"][:, 0:512], sb["onesb"][0:1, :], sb["rows"][0:1, 1024:1536], "p5",
          True, True, reads=("onesb", "rows"), tag=cg + "db1")
    kb.act(sb["wsV"][:, 0:512], ps["p5"][:, 0:512], AF.Copy, reads=("p5",),
           writes=("wsV0", "p5"), tag=cg + "db1e")
    kb.mm(ps["p5"][:, 0:512], sb["onesb"][0:1, :], sb["rows"][0:1, 1536:2048], "p5",
          True, True, reads=("onesb", "rows"), tag=cg + "db2")
    kb.act(sb["wsV"][:, 512:1024], ps["p5"][:, 0:512], AF.Copy, reads=("p5",),
           writes=("wsV0", "p5"), tag=cg + "db2e")

    lprv = sb["lpr"].rearrange("p (e j) -> p e j", e=8)
    lpiv = sb["lpi"].rearrange("p (e j) -> p e j", e=8)
    kb.ve("tensor_copy", lprv[:, :, 0:1], lam[:, 0:8][:, :, None],
          reads=("lam",), writes=("lpr",), tag=cg + "p0r")
    kb.ve("tensor_copy", lpiv[:, :, 0:1], lam[:, 8:16][:, :, None],
          reads=("lam",), writes=("lpi",), tag=cg + "p0i")
    w = 1
    while w < 128:
        m = min(w, 128 - w)
        ar = lprv[:, :, w - 1:w].broadcast_to([128, 8, m])
        ai = lpiv[:, :, w - 1:w].broadcast_to([128, 8, m])
        sA = sb["spA"].rearrange("p (e j) -> p e j", e=8)[:, :, 0:m]
        sB = sb["spB"].rearrange("p (e j) -> p e j", e=8)[:, :, 0:m]
        kb.ve("tensor_mul", sA, lprv[:, :, 0:m], ar, reads=("lpr",), writes=("spA", "sC", "sD"), tag=cg + "pd1")
        kb.ve("tensor_mul", sB, lpiv[:, :, 0:m], ai, reads=("lpr", "lpi"), writes=("spB",), tag=cg + "pd2")
        kb.ve("tensor_sub", sA, sA, sB, reads=("spA", "sC", "sD", "spB"), writes=("spA", "sC", "sD"), tag=cg + "pd3")
        kb.ve("tensor_mul", sB, lprv[:, :, 0:m], ai, reads=("lpr", "lpi"), writes=("spB",), tag=cg + "pd4")
        kb.ve("tensor_copy", lprv[:, :, w:w + m], sA, reads=("spA", "sC", "sD"), writes=("lpr",), tag=cg + "pd5")
        kb.ve("tensor_mul", sA, lpiv[:, :, 0:m], ar, reads=("lpi",), writes=("spA", "sC", "sD"), tag=cg + "pd6")
        kb.ve("tensor_add", lpiv[:, :, w:w + m], sA, sB, reads=("spA", "sC", "sD", "spB"), writes=("lpi",), tag=cg + "pd7")
        w *= 2

    ln_stats(kb, lambda tt: xv[:, tt, :], NTT, cg + "ln", "L", tuple("xt%d" % i for i in range(NTT)))
    for tt in range(NTT):
        mu, rs = ln_mu_rstd(kb, "L", tt)
        ht, hk, co = _htile(kb, tt)
        kb.ve("tensor_scalar", ht[:, co:co + 1024], xv[:, tt, :], mu, rs,
              ALU.subtract, ALU.mult, reads=(xtk(tt), "stats"), writes=(hk,), tag=cg + "n")
        for dt in range(8):
            pk = "pt%d" % (dt % 2)
            kb.tr(ps[pk][:, 0:128], ht[:, co + dt * 128:co + (dt + 1) * 128],
                  sb["identb"][:, :], pk, reads=(hk, "identb"), tag=cg + "t")
            ft, fk, fo = _fm(kb, dt)
            kb.act(ft[:, fo + tt * 128:fo + (tt + 1) * 128], ps[pk][:, 0:128], AF.Copy,
                   reads=(pk,), writes=fk + (pk,), tag=cg + "te")

    for pt in range(8):
        for im in range(2):
            wsb = "wsA%d" % ((pt * 2 + im) % 3)
            src = P["BimT"] if im else P["BreT"]
            kb.dma("pool", sb[wsb][:, :].rearrange("p (k o) -> p k o", k=8),
                   src[pt], writes=(wsb,), tag=cg + "bw")
            bt, bk_, bco = _bu(kb, im, pt)
            BW = min(512, T)
            for tc in range(T // BW):
                pk = "p%d" % (2 * im + tc % 2)
                for kt in range(8):
                    ft, fk, fo = _fm(kb, kt)
                    kb.mm(ps[pk][:, 0:BW], sb[wsb][:, :].rearrange(
                        "p (k o) -> p k o", k=8)[:, kt, :],
                        ft[:, fo + tc * BW:fo + (tc + 1) * BW], pk, kt == 0, kt == 7,
                        reads=(wsb,) + fk, tag=cg + "bu")
                kb.act(bt[:, bco + tc * BW:bco + (tc + 1) * BW], ps[pk][:, 0:BW],
                       AF.Copy, scale=sb["gexp"][:, pt:pt + 1],
                       reads=(pk, "gexp"), writes=(bk_, pk), tag=cg + "bue")

    kb.dbg("bu_re", sb["G5"][:, 0:4 * T], [128, 4 * T], BF16, reads=G5K)
    kb.dbg("bu_im", sb["G7"][:, 0:4 * T], [128, 4 * T], BF16, reads=("G7",))
    lh = sb["lh"]
    kb.ve("memset", lh[:, :], 0.0, writes=("lhr", "lhi"), tag=cg + "h0")
    hre = lh[:, 0:128].rearrange("p (e c) -> p e c", e=8)[:, :, 0:NCH]
    him = lh[:, 128:256].rearrange("p (e c) -> p e c", e=8)[:, :, 0:NCH]
    lrv = lam[:, 0:8][:, :, None].broadcast_to([128, 8, NCH])
    liv = lam[:, 8:16][:, :, None].broadcast_to([128, 8, NCH])
    sA8 = sb["spA"][:, 0:128].rearrange("p (e c) -> p e c", e=8)[:, :, 0:NCH]
    sB8 = sb["spB"][:, 0:128].rearrange("p (e c) -> p e c", e=8)[:, :, 0:NCH]
    sC8 = sb["spA"][:, 128:256].rearrange("p (e c) -> p e c", e=8)[:, :, 0:NCH]

    def rview(g, npt, bpt, j):
        return sb[g][:, :].rearrange("p (a c j) -> p a c j", a=8192 // 128 // 8 if False else None)

    def bu_col(g, npt, j):
        return sb[g][:, 0:npt * T].rearrange("p (a c j) -> p a c j", a=npt, j=128)[:, :, 0:NCH, j]

    sD8 = sb["dacc"][:, 128:256].rearrange("p (e c) -> p e c", e=8)[:, :, 0:NCH]
    for j in range(128):
        kb.ve("tensor_mul", sA8, hre, lrv, reads=("lhr", "lam"), writes=("spA",), tag=cg + "a1")
        kb.ve("tensor_mul", sC8, him, liv, reads=("lhi", "lam"), writes=("sC",), tag=cg + "a2")
        kb.ve("tensor_sub", sA8, sA8, sC8, reads=("spA", "sC"), writes=("spA",), tag=cg + "a3")
        kb.ve("tensor_mul", sB8, him, lrv, reads=("lhi", "lam"), writes=("spB",),
              eng_name="pool", tag=cg + "b1")
        kb.ve("tensor_mul", sD8, hre, liv, reads=("lhr", "lam"), writes=("sD",),
              eng_name="pool", tag=cg + "b2")
        kb.ve("tensor_add", sB8, sB8, sD8, reads=("spB", "sD"), writes=("spB",),
              eng_name="pool", tag=cg + "b3")
        for g, im, npt, bpt in _LREG:
            hv = (him if im else hre)[:, bpt:bpt + npt, :]
            tv = (sB8 if im else sA8)[:, bpt:bpt + npt, :]
            bj = bu_col(g, npt, j)
            kb.ve("tensor_add", hv, tv, bj, reads=("spB" if im else "spA",) + _LKEY[g],
                  writes=("lhi" if im else "lhr",),
                  eng_name="pool" if im else "dve", tag=cg + "hu")
            kb.act(bj, hv, AF.Copy, reads=("lhi" if im else "lhr",),
                   writes=_LKEY[g], tag=cg + "xw")

    xer = hre
    xei = him
    kb.ve("tensor_copy", lam[:, 16:24], lam[:, 0:8], reads=("lam",), writes=("lam",), tag=cg + "q0")
    kb.ve("tensor_copy", lam[:, 24:32], lam[:, 8:16], reads=("lam",), writes=("lam",), tag=cg + "q0i")
    for _sq in range(7):
        kb.ve("tensor_mul", lam[:, 32:40], lam[:, 16:24], lam[:, 16:24],
              reads=("lam",), writes=("lam",), tag=cg + "sq1")
        kb.ve("tensor_mul", lam[:, 40:48], lam[:, 24:32], lam[:, 24:32],
              reads=("lam",), writes=("lam",), tag=cg + "sq2")
        kb.ve("tensor_mul", lam[:, 48:56], lam[:, 16:24], lam[:, 24:32],
              reads=("lam",), writes=("lam",), tag=cg + "sq3")
        kb.ve("tensor_sub", lam[:, 16:24], lam[:, 32:40], lam[:, 40:48],
              reads=("lam",), writes=("lam",), tag=cg + "sq4")
        kb.ve("tensor_scalar_mul", lam[:, 24:32], lam[:, 48:56], 2.0,
              reads=("lam",), writes=("lam",), tag=cg + "sq5")
    l128r = lam[:, 16:24][:, :, None]
    l128i = lam[:, 24:32][:, :, None]
    xer = hre
    xei = him
    for c in range(1, NCH):
        sA1 = sb["spA"].rearrange("p (e c) -> p e c", e=8)[:, :, 0:1]
        sB1 = sb["spB"].rearrange("p (e c) -> p e c", e=8)[:, :, 0:1]
        sA2 = sb["spA"].rearrange("p (e c) -> p e c", e=8)[:, :, 1:2]
        kb.ve("tensor_mul", sA1, xer[:, :, c - 1:c], l128r, reads=("lhr", "lam"), writes=("spA", "sC", "sD"), tag=cg + "xc1")
        kb.ve("tensor_mul", sB1, xei[:, :, c - 1:c], l128i, reads=("lhi", "lam"), writes=("spB",), tag=cg + "xc2")
        kb.ve("tensor_sub", sA1, sA1, sB1, reads=("spA", "sC", "sD", "spB"), writes=("spA", "sC", "sD"), tag=cg + "xc3")
        kb.ve("tensor_mul", sB1, xei[:, :, c - 1:c], l128r, reads=("lhi", "lam"), writes=("spB",), tag=cg + "xc5")
        kb.ve("tensor_mul", sA2, xer[:, :, c - 1:c], l128i, reads=("lhr", "lam"), writes=("spA", "sC", "sD"), tag=cg + "xc6")
        kb.ve("tensor_add", xer[:, :, c:c + 1], xer[:, :, c:c + 1], sA1,
              reads=("lhr", "spA", "sC", "sD"), writes=("lhr",), tag=cg + "xc4")
        kb.ve("tensor_add", sA2, sA2, sB1, reads=("spA", "sC", "sD", "spB"), writes=("spA", "sC", "sD"), tag=cg + "xc7")
        kb.ve("tensor_add", xei[:, :, c:c + 1], xei[:, :, c:c + 1], sA2,
              reads=("lhi", "spA", "sC", "sD"), writes=("lhi",), tag=cg + "xc8")

    for c in range(1, NCH):
        for g, im, npt, bpt in _LREG:
            xcol = sb[g][:, 0:npt * T].rearrange("p (a c j) -> p a c j", a=npt, j=128)[:, :, c, :]
            pr = lprv[:, bpt:bpt + npt, :]
            pi = lpiv[:, bpt:bpt + npt, :]
            XsR = hre[:, bpt:bpt + npt, c - 1:c].broadcast_to([128, npt, 128])
            XsI = him[:, bpt:bpt + npt, c - 1:c].broadcast_to([128, npt, 128])
            sA = sb["spA"].rearrange("p (e j) -> p e j", e=8)[:, 0:npt, :]
            sB = sb["spB"].rearrange("p (e j) -> p e j", e=8)[:, 0:npt, :]
            if not im:
                kb.ve("tensor_mul", sA, pr, XsR, reads=("lpr", "lhr"), writes=("spA", "sC", "sD"), tag=cg + "c1")
                kb.ve("tensor_mul", sB, pi, XsI, reads=("lpi", "lhi"), writes=("spB",), tag=cg + "c2")
                kb.ve("tensor_sub", sA, sA, sB, reads=("spA", "sC", "sD", "spB"), writes=("spA", "sC", "sD"), tag=cg + "c3")
            else:
                kb.ve("tensor_mul", sA, pr, XsI, reads=("lpr", "lhi"), writes=("spA", "sC", "sD"), tag=cg + "c4")
                kb.ve("tensor_mul", sB, pi, XsR, reads=("lpi", "lhr"), writes=("spB",), tag=cg + "c5")
                kb.ve("tensor_add", sA, sA, sB, reads=("spA", "sC", "sD", "spB"), writes=("spA", "sC", "sD"), tag=cg + "c6")
            kb.ve("tensor_add", xcol, xcol, sA, reads=_LKEY[g] + ("spA", "sC", "sD"),
                  writes=_LKEY[g], tag=cg + "c7")

    kb.dbg("xsc_re", sb["G5"][:, 0:4 * T], [128, 4 * T], BF16, reads=G5K)
    kb.dbg("xsc_im", sb["G7"][:, 0:4 * T], [128, 4 * T], BF16, reads=("G7",))
    for tt in range(NTT):
        ht, hk, co = _htile(kb, tt)
        kb.ve("tensor_mul", sb["spA"][:, 0:1024], ht[:, co:co + 1024],
              sb["wsV"][:, 0:1024], reads=(hk, "wsV0"), writes=("spA", "sC", "sD"), tag=cg + "du")
        kb.ve("tensor_add", xv[:, tt, :], xv[:, tt, :], sb["spA"][:, 0:1024],
              reads=(xtk(tt), "spA", "sC", "sD"), writes=(xtk(tt),), tag=cg + "dua")
    for tbg in range(NTT // 4 + (1 if NTT % 4 else 0)):
        tbs = range(tbg * 4, min(NTT, tbg * 4 + 4))
        for oh in range(2):
            for pt in range(8):
                wpar = pt % 2
                kb.dma("pool", sb["wsV"][:, wpar * 1024:wpar * 1024 + 512],
                       P["CreT"][pt, oh], writes=("wsV%d" % wpar,), tag=cg + "cr")
                kb.dma("pool", sb["wsV"][:, wpar * 1024 + 512:wpar * 1024 + 1024],
                       P["CimT"][pt, oh], writes=("wsV%d" % wpar,), tag=cg + "ci")
                kb.ve("tensor_scalar_mul", sb["wsV"][:, wpar * 1024 + 512:wpar * 1024 + 1024],
                      sb["wsV"][:, wpar * 1024 + 512:wpar * 1024 + 1024], -1.0,
                      reads=("wsV%d" % wpar,), writes=("wsV%d" % wpar,), tag=cg + "cin")
                for tb in tbs:
                    pk = "p%d" % (tb % 4)
                    rt, rk, rco = _bu(kb, 0, pt)
                    it, ik, ico = _bu(kb, 1, pt)
                    kb.mm(ps[pk][:, 0:512], rt[:, rco + tb * 128:rco + (tb + 1) * 128],
                          sb["wsV"][:, wpar * 1024:wpar * 1024 + 512], pk,
                          pt == 0, False, reads=(rk, "wsV%d" % wpar), tag=cg + "cx1")
                    kb.mm(ps[pk][:, 0:512], it[:, ico + tb * 128:ico + (tb + 1) * 128],
                          sb["wsV"][:, wpar * 1024 + 512:wpar * 1024 + 1024], pk,
                          False, pt == 7, reads=(ik, "wsV%d" % wpar), tag=cg + "cx2")
            for tb in tbs:
                pk = "p%d" % (tb % 4)
                kb.ve("tensor_add", xv[:, tb, oh * 512:(oh + 1) * 512],
                      xv[:, tb, oh * 512:(oh + 1) * 512], ps[pk][:, 0:512],
                      reads=(xtk(tb), pk), writes=(xtk(tb), pk), tag=cg + "cxe")
    kb.dbg("x_after_b", xtm[:, :], [128, NTT * 1024], BF16, reads=tuple("xt%d" % i for i in range(NTT)))
